# version 04
# baseline (speedup 1.0000x reference)
import numpy as np
import ml_dtypes
import concourse.bacc as bacc
import concourse.bass as bass
import concourse.mybir as mybir
from concourse.bass import ts, IndirectOffsetOnAxis
from concourse.bass_utils import run_bass_kernel_spmd
from concourse.expressions import smin

F32 = mybir.dt.float32
BF16 = mybir.dt.bfloat16
I32 = mybir.dt.int32
AF = mybir.ActivationFunctionType
ALU = mybir.AluOpType
AX = mybir.AxisListType

NCORES = 8
B_LOC = 4
S = 2048
D = 1024
T = B_LOC * S
NT = T // 128
H1 = 16
HD = 64
E = 8
CAP = 3072
LNEPS = 1e-5
RMSEPS = 1e-6
KVEPS = 1e-5

BF = np.dtype(ml_dtypes.bfloat16)



def _rope_tables():
    inv = 1.0 / 10000.0 ** (np.arange(0, HD, 2, dtype=np.float32) / HD)
    tpos = np.arange(S, dtype=np.float32)
    fr = np.outer(tpos, inv)
    emb = np.concatenate([fr, fr], axis=1)
    cos = np.cos(emb).T.astype(np.float32)
    sin = np.sin(emb).T.astype(np.float32)
    cos128 = np.concatenate([cos, cos], axis=0)
    sin128 = np.concatenate([sin, sin], axis=0)
    return np.ascontiguousarray(cos128), np.ascontiguousarray(sin128)


def _rot_mat128():
    P = np.zeros((64, 64), np.float32)
    for i in range(32):
        P[i, i + 32] = -1.0
        P[i + 32, i] = 1.0
    blk = P.T
    M = np.zeros((128, 128), np.float32)
    M[:64, :64] = blk
    M[64:, 64:] = blk
    return M


def _prep(inputs):
    g = {}
    f32 = np.float32

    def c(a):
        return np.ascontiguousarray(a, dtype=f32)

    g["ident"] = np.eye(128, dtype=f32)
    g["tri"] = np.triu(np.ones((128, 128), f32), 1)
    cos128, sin128 = _rope_tables()
    g["rope_cos"] = cos128
    g["rope_sin"] = sin128
    g["rope_cos_q"] = c(cos128 * (HD ** -0.5))
    g["rope_sin_q"] = c(sin128 * (HD ** -0.5))
    g["rotmat"] = _rot_mat128()
    g["iota8"] = np.broadcast_to(np.arange(8, dtype=f32), (128, 8)).copy()

    g["l0f_router"] = c(inputs["L0_ffn_w_router"])
    g["l0f_w1"] = c(inputs["L0_ffn_w1"])
    g["l0f_b1"] = c(inputs["L0_ffn_b1"].reshape(E, 32, 128).transpose(0, 2, 1))
    g["l0f_w2"] = c(inputs["L0_ffn_w2"])
    g["l0f_b2"] = c(inputs["L0_ffn_b2"])
    g["l0a_inwT"] = c(inputs["L0_attn_in_w"].T)
    g["l0a_inb"] = c(inputs["L0_attn_in_b"].reshape(16, 128).T)
    g["l0a_convw"] = c(inputs["L0_attn_conv_w"][:, 0, :].reshape(16, 128, 4))
    g["l0a_convb"] = c(inputs["L0_attn_conv_b"].reshape(16, 128).T)
    g["l0a_xpwT"] = c(inputs["L0_attn_xp_w"].T)
    xpp = np.zeros((2048, 65), np.float32)
    xpp[:, 0:16] = inputs["L0_attn_xp_w"].T[:, 0:16]
    xpp[:, 32:48] = inputs["L0_attn_xp_w"].T[:, 16:32]
    xpp[:, 64] = 1.0
    g["l0a_xpwT_pad"] = xpp
    g["l0a_xpb"] = c(inputs["L0_attn_xp_b"].reshape(32, 1))
    g["l0a_outwT"] = c(inputs["L0_attn_out_w"].T)
    g["l0a_outb"] = c(inputs["L0_attn_out_b"].reshape(1, D))
    g["l1f_n1"] = c(np.broadcast_to(inputs["L1_ffn_norm1_w"], (128, D)))
    g["l1f_n2"] = c(np.broadcast_to(inputs["L1_ffn_norm2_w"], (128, D)))
    g["l1f_qkvwT"] = c(inputs["L1_ffn_qkv_w"].T)
    g["l1f_qkvb"] = c(inputs["L1_ffn_qkv_b"].reshape(1, 3 * D))
    g["l1f_qkvb_h"] = c(inputs["L1_ffn_qkv_b"].reshape(24, 128).T)
    g["l1f_outwT"] = c(inputs["L1_ffn_out_w"].T)
    g["l1f_outb"] = c(inputs["L1_ffn_out_b"].reshape(1, D))
    g["l1f_gatewT"] = c(inputs["L1_ffn_gate_w"].T)
    g["l1f_w1T"] = c(inputs["L1_ffn_w1"].transpose(0, 2, 1))
    g["l1f_w2T"] = c(inputs["L1_ffn_w2"].transpose(0, 2, 1))
    g["l1f_b1"] = c(inputs["L1_ffn_b1"].reshape(E, 16, 128).transpose(0, 2, 1))
    g["l1f_b2"] = c(inputs["L1_ffn_b2"].reshape(E, 16, 128).transpose(0, 2, 1))
    g["l1f_w3T"] = c(inputs["L1_ffn_w3"].transpose(0, 2, 1))
    g["l1f_b3"] = c(inputs["L1_ffn_b3"])
    g["l1a_qkvwT"] = c(inputs["L1_attn_qkv_w"].T)
    g["l1a_qkvb"] = c(inputs["L1_attn_qkv_b"].reshape(1, 3 * D))
    g["l1a_qkvb_h"] = c(inputs["L1_attn_qkv_b"].reshape(24, 128).T)
    g["l1a_kng"] = c(np.broadcast_to(np.tile(inputs["L1_attn_kn_g"], H1), (128, D)))
    g["l1a_knb"] = c(np.broadcast_to(np.tile(inputs["L1_attn_kn_b"], H1), (128, D)))
    g["l1a_vng"] = c(np.broadcast_to(np.tile(inputs["L1_attn_vn_g"], H1), (128, D)))
    g["l1a_vnb"] = c(np.broadcast_to(np.tile(inputs["L1_attn_vn_b"], H1), (128, D)))
    g["l1a_temp"] = c(inputs["L1_attn_temperature"].reshape(1, H1))
    g["l1a_bias"] = c(inputs["L1_attn_attn_bias"])
    g["l1a_p1wT"] = c(inputs["L1_attn_p1_w"].T)
    g["l1a_p1b"] = c(inputs["L1_attn_p1_b"].reshape(8, 128).T)
    g["l1a_p2wT"] = c(inputs["L1_attn_p2_w"].T)
    g["l1a_p2b"] = c(inputs["L1_attn_p2_b"].reshape(1, D))

    x = np.ascontiguousarray(inputs["L0_attn_x"], dtype=f32)
    per_core = []
    for j in range(NCORES):
        bsel = [4 * j, 4 * j + 1, 4 * j + 2, 4 * j + 3]
        m = {"x_in": np.ascontiguousarray(x[bsel].reshape(T, D))}
        onehot = np.zeros((1, 64), f32)
        onehot[0, j] = 1.0
        mask = np.zeros((64, 1), f32)
        mask[:j, 0] = 1.0
        m["core_onehot"] = onehot
        m["core_mask"] = mask
        per_core.append(m)
    return g, per_core


def _unshard(outs):
    full = np.empty((32, S, D), np.float32)
    for j in range(NCORES):
        r = outs[j].reshape(B_LOC, S, D)
        for bi, b in enumerate([4 * j, 4 * j + 1, 4 * j + 2, 4 * j + 3]):
            full[b] = r[bi]
    return full



class P:

    def __init__(self, nc):
        self.nc = nc
        self._exit = []
        self._sem_i = 0
        self._reg_i = 0

    def sem(self):
        cm = self.nc.semaphore(f"s{self._sem_i}")
        self._sem_i += 1
        h = cm.__enter__()
        self._exit.append(("sem", cm, h))
        return h

    def sbuf(self, shape, dtype, name=None):
        self._name_i = getattr(self, "_name_i", 0) + 1
        cm = self.nc.sbuf_tensor(f"{name or 'sb'}_{self._name_i}", shape, dtype)
        h = cm.__enter__()
        self._exit.append(("buf", cm, h))
        return h

    def psum(self, shape, dtype=F32, name=None):
        self._name_i = getattr(self, "_name_i", 0) + 1
        cm = self.nc.psum_tensor(f"{name or 'ps'}_{self._name_i}", shape, dtype)
        h = cm.__enter__()
        self._exit.append(("buf", cm, h))
        return h

    def reg(self, eng, val=None):
        r = eng.alloc_register(f"r{self._reg_i}")
        self._reg_i += 1
        if val is not None:
            eng.reg_mov(r, val)
        return r

    def close(self):
        self.free_scope(0)

    def scoped(self):
        return len(self._exit)

    def free_scope(self, mark):
        sems = [h for (kind, cm, h) in self._exit[mark:] if kind == "sem"]
        if sems:
            self.barrier()
            self.nc.gpsimd.dma_reset()
            for h in sems:
                self.nc.gpsimd.sem_clear(h)
            self.barrier()
        for kind, cm, h in reversed(self._exit[mark:]):
            cm.__exit__(None, None, None)
        del self._exit[mark:]

    def barrier(self):
        self.nc.all_engine_barrier()


def emit_prologue_casts(p, pairs):
    nc = p.nc
    mk_pc = p.scoped()
    ld = p.sem()
    st = p.sem()
    nld = 0
    nst = 0
    stage = p.sbuf([128, 4096], BF16, "cast_stage")
    cols_max = 4096
    for src, dst in pairs:
        rows, cols = src.shape
        assert dst.shape == (rows, cols)
        r = 0
        while r < rows:
            rr = min(128, rows - r)
            cc0 = 0
            while cc0 < cols:
                cc = min(cols_max, cols - cc0)
                buf = stage[0:rr, 0:cc]
                if nst > 0:
                    nc.gpsimd.wait_ge(st, nst * 16)
                nc.gpsimd.dma_start(out=buf, in_=src[r:r + rr, cc0:cc0 + cc]).then_inc(ld, 16)
                nld += 1
                nc.gpsimd.wait_ge(ld, nld * 16)
                nc.gpsimd.dma_start(out=dst[r:r + rr, cc0:cc0 + cc], in_=buf).then_inc(st, 16)
                nst += 1
                cc0 += cc
            r += rr
    nc.gpsimd.wait_ge(st, nst * 16)
    p.barrier()
    p.free_scope(mk_pc)




def emit_norm_pass(p, x_in, z_f32, zln, zlnT, kind, gamma_rep=None, ident_bf=None,
                   eps=LNEPS, nt=None):
    nc = p.nc
    v, sc, g, t, sy = nc.vector, nc.scalar, nc.gpsimd, nc.tensor, nc.sync
    if nt is None:
        nt = NT
    mk_np = p.scoped()
    xr = x_in.rearrange("(n p) d -> n p d", p=128)
    ld = [p.sem(), p.sem()]
    rdy = p.sem()
    stok = p.sem()
    sc_mean = p.sem(); vc_xc = p.sem(); sc_sq = p.sem(); vc_v = p.sem()
    sc_rt = p.sem(); vc_rc = p.sem(); sc_zf = p.sem(); vn = p.sem()
    tp = p.sem(); ev = p.sem(); st = p.sem(); evok = p.sem()

    xt = p.sbuf([128, 2, D], F32, "np_x")
    xc = p.sbuf([128, 2, D], F32, "np_xc")
    zf = p.sbuf([128, 2, D], F32, "np_zf")
    sqs = p.sbuf([128, D], F32, "np_sqs")
    stat = p.sbuf([128, 8], F32, "np_stat")
    lnt = p.sbuf([128, 2, D], BF16, "np_ln")
    g_t = p.sbuf([128, D], F32, "np_gam") if gamma_rep is not None else None
    zT_sb = p.sbuf([128, 8, S], BF16, "np_zT") if zlnT is not None else None
    ps_tr = p.psum([128, 8, 128], BF16, "np_tr") if zlnT is not None else None

    if gamma_rep is not None:
        sy.dma_start(out=g_t[:], in_=gamma_rep).then_inc(ld[0], 16)
        sy.wait_ge(ld[0], 16)
    p.barrier()
    ld_base = [16 if gamma_rep is not None else 0, 0]
    sy.dma_start(out=xt[:, 0, :], in_=xr[0]).then_inc(ld[0], 16)
    r_ld = [p.reg(sy, ld_base[0] + 16), p.reg(sy, ld_base[1])]
    r_g = p.reg(sy, 0)
    r_sv = p.reg(sy, 1)
    r_st = p.reg(sy, 0)

    r_s_rdy = p.reg(sc, 0)
    r_v_m = p.reg(v, 0)
    r_s_xc = p.reg(sc, 0)
    r_v_sq = p.reg(v, 0)
    r_s_v = p.reg(sc, 0)
    r_v_rt = p.reg(v, 0)
    r_s_rc = p.reg(sc, 0)
    r_v_zf = p.reg(v, 0)
    r_v_stok = p.reg(v, 0)
    r_t_vn = p.reg(t, 0) if zlnT is not None else None
    r_t_ev = p.reg(t, 0) if zlnT is not None else None
    r_v_tp = p.reg(v, 0) if zlnT is not None else None
    r_sy_ev = p.reg(sy, 0)

    TPB = S // 128
    ln = kind == "ln"
    nst_per = (1 if zln is not None else 0) + (1 if z_f32 is not None else 0)

    def tile_body(tg_expr, t_expr, par):
        nxt = 1 - par
        sy.wait_ge(vn, r_g)
        sy.reg_add(r_g, r_g, 1)
        sy.dma_start(out=xt[:, nxt, :], in_=xr[smin(tg_expr + 1, nt - 1)]).then_inc(ld[nxt], 16)
        sy.reg_add(r_ld[nxt], r_ld[nxt], 16)
        sy.wait_ge(ld[par], r_ld[par])
        sy.sem_inc(rdy, 1)

        sc.reg_add(r_s_rdy, r_s_rdy, 1)
        sc.wait_ge(rdy, r_s_rdy)
        if ln:
            sc.activation(out=sqs[:], in_=xt[:, par, :], func=AF.Identity,
                          scale=1.0 / D, accum_out=stat[:, 0:1]).then_inc(sc_mean, 1)
            v.reg_add(r_v_m, r_v_m, 1)
            v.wait_ge(sc_mean, r_v_m)
            v.tensor_scalar(xc[:, par, :], xt[:, par, :], stat[:, 0:1], None,
                            ALU.subtract).then_inc(vc_xc, 1)
            csrc = xc
            sc.reg_add(r_s_xc, r_s_xc, 1)
            sc.wait_ge(vc_xc, r_s_xc)
        else:
            csrc = xt
        sc.activation(out=sqs[:], in_=csrc[:, par, :], func=AF.Square,
                      accum_out=stat[:, 1:2]).then_inc(sc_sq, 1)
        v.reg_add(r_v_sq, r_v_sq, 1)
        v.wait_ge(sc_sq, r_v_sq)
        v.tensor_scalar(stat[:, 2:3], stat[:, 1:2], 1.0 / D, eps, ALU.mult,
                        ALU.add).then_inc(vc_v, 1)
        sc.reg_add(r_s_v, r_s_v, 1)
        sc.wait_ge(vc_v, r_s_v)
        sc.activation(out=stat[:, 3:4], in_=stat[:, 2:3], func=AF.Sqrt).then_inc(sc_rt, 1)
        v.reg_add(r_v_rt, r_v_rt, 1)
        v.wait_ge(sc_rt, r_v_rt)
        v.reciprocal(stat[:, 4:5], stat[:, 3:4]).then_inc(vc_rc, 1)
        sc.reg_add(r_s_rc, r_s_rc, 1)
        sc.wait_ge(vc_rc, r_s_rc)
        sc.activation(out=zf[:, par, :], in_=csrc[:, par, :], func=AF.Identity,
                      scale=stat[:, 4:5]).then_inc(sc_zf, 1)
        v.reg_add(r_v_zf, r_v_zf, 1)
        v.wait_ge(sc_zf, r_v_zf)
        if nst_per:
            v.wait_ge(stok, r_v_stok)
            v.reg_add(r_v_stok, r_v_stok, 1)
        if gamma_rep is not None:
            v.tensor_tensor(lnt[:, par, :], zf[:, par, :], g_t[:], ALU.mult).then_inc(vn, 1)
        else:
            v.tensor_copy(lnt[:, par, :], zf[:, par, :]).then_inc(vn, 1)

        if zlnT is not None:
            t.reg_add(r_t_vn, r_t_vn, 1)
            t.wait_ge(vn, r_t_vn)
            t.wait_ge(ev, r_t_ev)
            for c in range(8):
                ins = t.transpose(ps_tr[:, c, :], lnt[:, par, ts(c, 128)], ident_bf[:])
            ins.then_inc(tp, 1)
            t.reg_add(r_t_ev, r_t_ev, 1)
            v.reg_add(r_v_tp, r_v_tp, 1)
            v.wait_ge(tp, r_v_tp)
            v.tensor_copy(zT_sb[:, :, bass.ds(t_expr * 128, 128)], ps_tr[:]).then_inc(ev, 1)
        sy.wait_ge(vn, r_sv)
        sy.reg_add(r_sv, r_sv, 1)
        sy_waited = False
        if zln is not None:
            sy.dma_start(out=zln.rearrange("(n p) d -> n p d", p=128)[tg_expr],
                         in_=lnt[:, par, :]).then_inc(st, 16)
            sy.reg_add(r_st, r_st, 16)
            sy_waited = True
        if z_f32 is not None:
            sy.dma_start(out=z_f32.rearrange("(n p) d -> n p d", p=128)[tg_expr],
                         in_=zf[:, par, :]).then_inc(st, 16)
            sy.reg_add(r_st, r_st, 16)
            sy_waited = True
        if sy_waited:
            sy.wait_ge(st, r_st)
            sy.sem_inc(stok, 1)
        else:
            sy.sem_inc(stok, 1)

    nb = nt // TPB
    for b in range(nb):
        with nc.Fori(0, TPB // 2) as th:
            for par in (0, 1):
                tile_body(b * TPB + th * 2 + par, th * 2 + par, par)
        if zlnT is not None:
            sy.reg_add(r_sy_ev, r_sy_ev, TPB)
            sy.wait_ge(ev, r_sy_ev)
            for c in range(8):
                sy.dma_start(out=zlnT[ts(c, 128), b * S:(b + 1) * S],
                             in_=zT_sb[:, c, :]).then_inc(st, 16)
                sy.reg_add(r_st, r_st, 16)
            sy.wait_ge(st, r_st)
            p.barrier()
    sy.wait_ge(st, r_st)
    sy.wait_ge(ld[0], r_ld[0])
    sy.wait_ge(ld[1], r_ld[1])
    p.barrier()
    p.free_scope(mk_np)

def ds_(i, w_):
    return bass.ds(i * w_, w_)


def emit_head_ln(p, nc, src, dst_bf, gam_t, bet_t, aux):
    v, s, g = nc.vector, nc.scalar, nc.gpsimd
    xcg, sqg, t1g, st16 = aux["xcg"], aux["sqg"], aux["t1g"], aux["st16"]
    sem_v, sem_s, sem_g = aux["sem_v"], aux["sem_s"], aux["sem_g"]
    rv, rs, rg, rv2, rvg = aux["rv"], aux["rs"], aux["rg"], aux["rv2"], aux["rvg"]
    gsrc = src.rearrange("p (g w) -> p g w", g=16)
    v.wait_ge(sem_g, rv2)
    v.reg_add(rv2, rv2, 2)
    v.tensor_reduce(st16[:, 0:16], gsrc, AX.X, ALU.add).then_inc(sem_v, 1)
    s.reg_add(rs, rs, 1); s.wait_ge(sem_v, rs)
    s.activation(out=st16[:, 16:32], in_=st16[:, 0:16], func=AF.Identity,
                 scale=1.0 / HD).then_inc(sem_s, 1)
    v.reg_add(rv, rv, 1); v.wait_ge(sem_s, rv)
    v.tensor_tensor(xcg[:].rearrange("p (g w) -> p g w", g=16), gsrc,
                    st16[:, 16:32].broadcast_to([128, 16, HD]),
                    ALU.subtract).then_inc(sem_v, 1)
    s.reg_add(rs, rs, 1); s.wait_ge(sem_v, rs)
    s.activation(out=sqg[:], in_=xcg[:], func=AF.Square).then_inc(sem_s, 1)
    v.reg_add(rv, rv, 1); v.wait_ge(sem_s, rv)
    v.tensor_reduce(st16[:, 32:48], sqg[:].rearrange("p (g w) -> p g w", g=16),
                    AX.X, ALU.add).then_inc(sem_v, 1)
    s.reg_add(rs, rs, 1); s.wait_ge(sem_v, rs)
    s.activation(out=st16[:, 48:64], in_=st16[:, 32:48], func=AF.Sqrt,
                 scale=1.0 / HD, bias=aux["eps_col"][:, 0:1]).then_inc(sem_s, 1)
    s.reg_add(rs, rs, 2)
    v.reg_add(rv, rv, 1); v.wait_ge(sem_s, rv)
    v.reciprocal(st16[:, 64:80], st16[:, 48:64]).then_inc(sem_v, 1)
    g.reg_add(rg, rg, 4); g.wait_ge(sem_v, rg)
    g.tensor_tensor(t1g[:].rearrange("p (g w) -> p g w", g=16),
                    xcg[:].rearrange("p (g w) -> p g w", g=16),
                    st16[:, 64:80].broadcast_to([128, 16, HD]), ALU.mult).then_inc(sem_g, 1)
    v.wait_ge(sem_g, rvg)
    v.reg_add(rvg, rvg, 2)
    v.tensor_tensor(xcg[:], t1g[:], gam_t[:], ALU.mult).then_inc(sem_v, 1)
    g.reg_add(rg, rg, 1); g.wait_ge(sem_v, rg)
    g.tensor_tensor(dst_bf, xcg[:], bet_t[:], ALU.add).then_inc(sem_g, 1)


def emit_l1attn(p, dram, w):
    import os
    _PH = int(os.environ.get("L1A_PHASES", "9"))
    nc = p.nc
    v, sc, g, t, sy = nc.vector, nc.scalar, nc.gpsimd, nc.tensor, nc.sync
    SPT = S // 128
    SST = S // 512

    mk_all = p.scoped()
    ident_bf = p.sbuf([128, 128], BF16, "la_ident")
    ones_row = p.sbuf([1, 128], BF16, "la_ones")
    qkvb_h = p.sbuf([128, 24], F32, "la_qkvbh")
    p1b_t = p.sbuf([128, 8], F32, "la_p1b")
    eps_col = p.sbuf([128, 1], F32, "la_eps")
    kng_t = p.sbuf([128, D], BF16, "la_kng")
    knb_t = p.sbuf([128, D], BF16, "la_knb")
    vng_t = p.sbuf([128, D], BF16, "la_vng")
    vnb_t = p.sbuf([128, D], BF16, "la_vnb")
    temp_sb = p.sbuf([1, 16], F32, "la_temp")
    tempcol = p.sbuf([64, 16], F32, "la_tempcol")
    biasall = p.sbuf([64, 16 * 64], F32, "la_biasall")
    kv_bf = p.sbuf([64, 16 * 64], BF16, "la_kv")
    zT_b = p.sbuf([128, 8 * S], BF16, "la_zT")
    onesq_col = p.sbuf([1, 64], BF16, "la_onesqc")
    qkvb_row_sb = p.sbuf([1, 3 * D], BF16, "la_qkvbrow")
    tmp_bf = p.sbuf([1, 16], BF16, "la_tmpbf")
    ps_misc = p.psum([64, 16], F32, "la_psmisc")

    s0 = p.sem()
    g.dma_start(out=ident_bf[:], in_=w["ident"]).then_inc(s0, 16)
    g.memset(ones_row[:], 1.0)
    g.memset(eps_col[:], KVEPS)
    g.memset(onesq_col[:], 1.0)
    g.dma_start(out=qkvb_h[:], in_=w["l1a_qkvb_h"]).then_inc(s0, 16)
    g.dma_start(out=p1b_t[:], in_=w["l1a_p1b"]).then_inc(s0, 16)
    g.dma_start(out=kng_t[:], in_=w["l1a_kng"]).then_inc(s0, 16)
    g.dma_start(out=knb_t[:], in_=w["l1a_knb"]).then_inc(s0, 16)
    g.dma_start(out=vng_t[:], in_=w["l1a_vng"]).then_inc(s0, 16)
    g.dma_start(out=vnb_t[:], in_=w["l1a_vnb"]).then_inc(s0, 16)
    g.dma_start(out=temp_sb[:], in_=w["l1a_temp"]).then_inc(s0, 16)
    g.dma_start(out=biasall[:].rearrange("p (h e) -> p h e", h=16), in_=w["l1a_bias"].rearrange("h d e -> d h e")).then_inc(s0, 16)
    g.dma_start(out=qkvb_row_sb[:], in_=w["l1a_qkvb_row"]).then_inc(s0, 16)
    g.wait_ge(s0, 10 * 16)
    p.barrier()
    v.tensor_scalar(tmp_bf[:], temp_sb[:], 1.0 / S, None, ALU.mult)
    p.barrier()
    t.matmul(ps_misc[:], onesq_col[:], tmp_bf[:], start=True, stop=True)
    p.barrier()
    v.tensor_copy(tempcol[:], ps_misc[:])
    p.barrier()

    xi_r = dram["x_in"].rearrange("(n p) d -> n p d", p=128)
    z0_r = dram["z0"].rearrange("(n p) d -> n p d", p=128)
    xo_r = dram["x_out"].rearrange("(n p) d -> n p d", p=128)

    for b in range(B_LOC):
        mk1 = p.scoped()
        qkvw_sb = p.sbuf([128, 8 * 2048], BF16, f"la_qkvw{b}")
        kln_sb = p.sbuf([128, SPT * D], BF16, f"la_kln{b}")
        vln_sb = p.sbuf([128, SPT * D], BF16, f"la_vln{b}")
        zstage = p.sbuf([128, 2, 1024], BF16, f"la_zstg{b}")
        kstage = p.sbuf([128, 2, D], BF16, f"la_kstg{b}")
        xcg = p.sbuf([128, D], F32, f"la_xcg{b}")
        sqg = p.sbuf([128, D], F32, f"la_sqg{b}")
        t1g = p.sbuf([128, D], F32, f"la_t1g{b}")
        st16 = p.sbuf([128, 96], F32, f"la_st16{b}")
        qstg = p.sbuf([128, 2, 512], BF16, f"la_qstg{b}")
        ps_q = p.psum([128, 512], F32, f"la_psq{b}")
        ps_kv4 = p.psum([128, 4, 512], F32, f"la_pskv{b}")
        ps_ein = p.psum([64, 16 * 64], F32, f"la_psein{b}")
        aux = {"xcg": xcg, "sqg": sqg, "t1g": t1g, "st16": st16, "eps_col": eps_col}

        ld = p.sem()
        nld = 0
        for dc in range(8):
            sy.dma_start(out=qkvw_sb[:, ds_(dc, 1024)][:, 0:1024],
                         in_=w["l1a_qkvwT_bf"][ts(dc, 128), 0:D]).then_inc(ld, 16)
            nld += 16
        for c in range(8):
            sy.dma_start(out=zT_b[:, ds_(c, S)], in_=dram["zlnT"][ts(c, 128), b * S:(b + 1) * S]).then_inc(ld, 16)
            nld += 16
        sy.wait_ge(ld, nld)
        p.barrier()

        mmq = p.sem(); evq = p.sem(); stq = p.sem(); stfree = p.sem()
        r_v_q = p.reg(v, 0); r_sy_q = p.reg(sy, 0); r_t_q = p.reg(t, 0)
        r_v_stf = p.reg(v, 0); r_sy_st = p.reg(sy, 0)
        sy.sem_inc(stfree, 1)
        for fc in range(8):
            with nc.Fori(0, SST) as sti:
                t.wait_ge(evq, r_t_q)
                for dc in range(8):
                    ins = t.matmul(ps_q[:], qkvw_sb[:, dc * 1024 + fc * 128: dc * 1024 + (fc + 1) * 128],
                                   zT_b[:, bass.ds(dc * S + sti * 512, 512)],
                                   start=(dc == 0), stop=(dc == 7))
                ins.then_inc(mmq, 1)
                t.reg_add(r_t_q, r_t_q, 1)
                v.reg_add(r_v_stf, r_v_stf, 1)
                v.wait_ge(stfree, r_v_stf)
                v.reg_add(r_v_q, r_v_q, 1)
                v.wait_ge(mmq, r_v_q)
                v.tensor_scalar(qstg[:].rearrange("p c f -> p (c f)")[:, bass.ds((sti % 2) * 512, 512)],
                                ps_q[:], qkvb_h[:, fc:fc + 1], None, ALU.add).then_inc(evq, 1)
                sy.reg_add(r_sy_q, r_sy_q, 1)
                sy.wait_ge(evq, r_sy_q)
                sy.dma_start(out=dram["qT"][b, ts(fc, 128)][:, bass.ds(sti * 512, 512)],
                             in_=qstg[:].rearrange("p c f -> p (c f)")[:, bass.ds((sti % 2) * 512, 512)]).then_inc(stq, 16)
                sy.reg_add(r_sy_st, r_sy_st, 16)
                sy.wait_ge(stq, r_sy_st)
                sy.sem_inc(stfree, 1)
        p.barrier()

        if _PH < 2:
            p.free_scope(mk1)
            continue
        ldkv = p.sem()
        for dc in range(8):
            sy.dma_start(out=qkvw_sb[:, ds_(dc, 2048)],
                         in_=w["l1a_qkvwT_bf"][ts(dc, 128), D:3 * D]).then_inc(ldkv, 16)
        sy.wait_ge(ldkv, 8 * 16)
        p.barrier()
        sem_v = p.sem(); sem_s = p.sem(); sem_g = p.sem(); mmkv = p.sem(); sem_z = p.sem()
        rv = p.reg(v, 0); rs = p.reg(sc, 0); rg = p.reg(g, 0)
        rv2 = p.reg(v, 0); rvg = p.reg(v, 1)
        r_v_mm = p.reg(v, 0); r_t_z = p.reg(t, 1); r_t_sv = p.reg(t, 0)
        aux.update({"sem_v": sem_v, "sem_s": sem_s, "sem_g": sem_g,
                    "rv": rv, "rs": rs, "rg": rg, "rv2": rv2, "rvg": rvg})
        with nc.Fori(0, SPT // 2) as th:
            for par in (0, 1):
                tt = th * 2 + par
                v.tensor_copy(zstage[:, par, :],
                              zT_b[:].rearrange("p (c s) -> p c s", c=8)[:, :, bass.ds(tt * 128, 128)]).then_inc(sem_z, 1)
                t.wait_ge(sem_z, r_t_z)
                t.wait_ge(sem_v, r_t_sv)
                for part in range(4):
                    for dc in range(8):
                        t.matmul(ps_kv4[:, part, :], zstage[:, par, ts(dc, 128)],
                                 qkvw_sb[:, dc * 2048 + part * 512: dc * 2048 + (part + 1) * 512],
                                 start=(dc == 0), stop=False)
                    ins = t.matmul(ps_kv4[:, part, :], ones_row[:],
                                   qkvb_row_sb[0:1, D + part * 512: D + (part + 1) * 512],
                                   start=False, stop=True)
                ins.then_inc(mmkv, 1)
                t.reg_add(r_t_z, r_t_z, 1)
                t.reg_add(r_t_sv, r_t_sv, 10)
                v.reg_add(r_v_mm, r_v_mm, 1)
                v.wait_ge(mmkv, r_v_mm)
                emit_head_ln(p, nc, ps_kv4[:, 0:2, :].rearrange("p c f -> p (c f)"),
                             kln_sb[:, bass.ds(tt * D, D)], kng_t, knb_t, aux)
                emit_head_ln(p, nc, ps_kv4[:, 2:4, :].rearrange("p c f -> p (c f)"),
                             vln_sb[:, bass.ds(tt * D, D)], vng_t, vnb_t, aux)
        p.barrier()

        if _PH < 3:
            p.free_scope(mk1)
            continue
        mm_e = p.sem(); cp_e = p.sem()
        for c in range(SPT):
            if c >= 2:
                v.wait_ge(mm_e, c - 1)
            v.tensor_copy(kstage[:, c % 2, :], kln_sb[:, ds_(c, D)]).then_inc(cp_e, 1)
            t.wait_ge(cp_e, c + 1)
            for h in range(16):
                ins = t.matmul(ps_ein[:, ts(h, 64)], kstage[:, c % 2, ts(h, 64)],
                               vln_sb[:, c * D + h * 64: c * D + (h + 1) * 64],
                               start=(c == 0 and h % 8 == 0),
                               stop=(c == SPT - 1 and h % 8 == 7))
            ins.then_inc(mm_e, 1)
        v.wait_ge(mm_e, SPT)
        for h in range(16):
            v.scalar_tensor_tensor(kv_bf[:, ts(h, 64)], ps_ein[:, ts(h, 64)],
                                   tempcol[:, h:h + 1], biasall[:, ts(h, 64)],
                                   ALU.mult, ALU.add)
        p.barrier()
        p.free_scope(mk1)

        mk2 = p.scoped()
        attT_sb = p.sbuf([128, 8 * S], BF16, f"la_att{b}")
        h1T_sb = p.sbuf([128, 8 * S], BF16, f"la_h1{b}")
        p1w_sb = p.sbuf([128, 64 * 128], BF16, f"la_p1w{b}")
        p2w_sb = p.sbuf([128, 8 * 1024], BF16, f"la_p2w{b}")
        p2b_row = p.sbuf([1, D], BF16, f"la_p2b{b}")
        qh_sb = p.sbuf([64, 2, S], BF16, f"la_qh{b}")
        h1stage = p.sbuf([128, 2, 1024], BF16, f"la_h1s{b}")
        zt_f = p.sbuf([128, 2, D], F32, f"la_ztf{b}")
        xt_f = p.sbuf([128, 2, D], F32, f"la_xtf{b}")
        tmpo = p.sbuf([128, 2, D], F32, f"la_tmpo{b}")
        xo_t = p.sbuf([128, 2, D], F32, f"la_xo{b}")
        ps_att = p.psum([64, 512], F32, f"la_psa{b}")
        ps_p1 = p.psum([128, 512], F32, f"la_psp1{b}")
        ps_o = p.psum([128, 2, 512], F32, f"la_pso{b}")

        ld2 = p.sem()
        n2 = 0
        for dc in range(8):
            sy.dma_start(out=p1w_sb[:, ds_(dc, 1024)],
                         in_=w["l1a_p1wT_bf"][ts(dc, 128), :].rearrange("p (f c) -> p (f c)") if False else w["l1a_p1wT_bf"][ts(dc, 128), :]).then_inc(ld2, 16)
            sy.dma_start(out=p2w_sb[:, ds_(dc, 1024)], in_=w["l1a_p2wT_bf"][ts(dc, 128), :]).then_inc(ld2, 16)
            n2 += 32
        sy.dma_start(out=p2b_row[:], in_=w["l1a_p2b_bf"]).then_inc(ld2, 16)
        n2 += 16
        sy.wait_ge(ld2, n2)
        p.barrier()

        if _PH < 4:
            p.free_scope(mk2)
            continue
        NH_DBG = int(os.environ.get("L1A_NH", "16"))
        mha = p.sem(); eva = p.sem(); qrdy = p.sem()
        ldq = [p.sem(), p.sem()]
        r_v_a = p.reg(v, 0); r_t_ev2 = p.reg(t, 0); r_t_qr = p.reg(t, 0)
        for h in range(NH_DBG):
            hp = h % 2
            if h >= 2:
                sy.wait_ge(mha, (h - 1) * SST)
            sy.dma_start(out=qh_sb[:, hp, :], in_=dram["qT"][b, h * 64:(h + 1) * 64, :]).then_inc(ldq[hp], 16)
            sy.wait_ge(ldq[hp], (h // 2 + 1) * 16)
            sy.sem_inc(qrdy, 1)
            t.reg_add(r_t_qr, r_t_qr, 1)
            t.wait_ge(qrdy, r_t_qr)
            for sti in range(SST):
                t.wait_ge(eva, r_t_ev2)
                t.matmul(ps_att[:], kv_bf[:, ts(h, 64)],
                         qh_sb[:, hp, sti * 512:(sti + 1) * 512],
                         start=True, stop=True).then_inc(mha, 1)
                t.reg_add(r_t_ev2, r_t_ev2, 1)
                v.reg_add(r_v_a, r_v_a, 1)
                v.wait_ge(mha, r_v_a)
                dr = (h % 2) * 64
                off = (h // 2) * S + sti * 512
                v.tensor_tensor(attT_sb[dr:dr + 64, off:off + 512],
                                ps_att[:],
                                zT_b[dr:dr + 64, off:off + 512],
                                ALU.add).then_inc(eva, 1)
        p.barrier()

        if _PH < 5:
            p.free_scope(mk2)
            continue
        mm1 = p.sem(); ev1 = p.sem()
        r_s_1 = p.reg(sc, 0); r_t_1 = p.reg(t, 0)
        for fc in range(8):
            with nc.Fori(0, SST) as sti:
                t.wait_ge(ev1, r_t_1)
                for dc in range(8):
                    ins = t.matmul(ps_p1[:], p1w_sb[:, (dc * 8 + fc) * 128:(dc * 8 + fc + 1) * 128],
                                   attT_sb[:, bass.ds(dc * S + sti * 512, 512)],
                                   start=(dc == 0), stop=(dc == 7))
                ins.then_inc(mm1, 1)
                t.reg_add(r_t_1, r_t_1, 1)
                sc.reg_add(r_s_1, r_s_1, 1)
                sc.wait_ge(mm1, r_s_1)
                sc.activation(out=h1T_sb[:, bass.ds(fc * S + sti * 512, 512)], in_=ps_p1[:],
                              func=AF.Gelu, bias=p1b_t[:, fc:fc + 1]).then_inc(ev1, 1)
        p.barrier()

        if _PH < 6:
            p.free_scope(mk2)
            continue
        mm2 = p.sem(); cp2 = p.sem(); ev2 = p.sem(); st2 = p.sem(); po2 = p.sem()
        ldx = [p.sem(), p.sem()]
        xrdy = p.sem(); sfree = p.sem()
        r_v_cp = p.reg(v, 0)
        r_t_cp2 = p.reg(t, 0); r_v_m2 = p.reg(v, 0); r_g_e2 = p.reg(g, 0)
        r_sy_st2 = p.reg(sy, 0); r_v_xr = p.reg(v, 0); r_sy_po = p.reg(sy, 0)
        r_sy_po2 = p.reg(sy, 0); r_t_ev3 = p.reg(t, 0)
        r_ldx = [p.reg(sy, 32), p.reg(sy, 0)]
        r_g_sf = p.reg(g, 0)
        sy.sem_inc(sfree, 2)
        sy.dma_start(out=zt_f[:, 0, :], in_=z0_r[b * SPT]).then_inc(ldx[0], 16)
        sy.dma_start(out=xt_f[:, 0, :], in_=xi_r[b * SPT]).then_inc(ldx[0], 16)
        with nc.Fori(0, SPT // 2) as th:
            for par in (0, 1):
                tt = th * 2 + par
                nxt = 1 - par
                sy.wait_ge(po2, r_sy_po)
                sy.reg_add(r_sy_po, r_sy_po, 1)
                sy.dma_start(out=zt_f[:, nxt, :], in_=z0_r[smin(b * SPT + tt + 1, NT - 1)]).then_inc(ldx[nxt], 16)
                sy.dma_start(out=xt_f[:, nxt, :], in_=xi_r[smin(b * SPT + tt + 1, NT - 1)]).then_inc(ldx[nxt], 16)
                sy.reg_add(r_ldx[nxt], r_ldx[nxt], 32)
                sy.wait_ge(ldx[par], r_ldx[par])
                sy.sem_inc(xrdy, 1)
                v.wait_ge(mm2, r_v_cp)
                v.tensor_copy(h1stage[:, par, :],
                              h1T_sb[:].rearrange("p (c s) -> p c s", c=8)[:, :, bass.ds(tt * 128, 128)]).then_inc(cp2, 1)
                v.reg_add(r_v_cp, r_v_cp, 1)
                t.reg_add(r_t_cp2, r_t_cp2, 1)
                t.wait_ge(cp2, r_t_cp2)
                t.wait_ge(ev2, r_t_ev3)
                for half in (0, 1):
                    for hc in range(8):
                        t.matmul(ps_o[:, half, :], h1stage[:, par, ts(hc, 128)],
                                 p2w_sb[:, hc * 1024 + half * 512: hc * 1024 + (half + 1) * 512],
                                 start=(hc == 0), stop=False)
                    ins = t.matmul(ps_o[:, half, :], ones_row[:],
                                   p2b_row[0:1, half * 512:(half + 1) * 512],
                                   start=False, stop=True)
                ins.then_inc(mm2, 1)
                t.reg_add(r_t_ev3, r_t_ev3, 1)
                v.reg_add(r_v_m2, r_v_m2, 1)
                v.wait_ge(mm2, r_v_m2)
                v.reg_add(r_v_xr, r_v_xr, 1)
                v.wait_ge(xrdy, r_v_xr)
                v.tensor_tensor(tmpo[:, par, :], ps_o[:].rearrange("p c f -> p (c f)"),
                                zt_f[:, par, :], ALU.add).then_inc(ev2, 1)
                g.reg_add(r_g_e2, r_g_e2, 1)
                g.wait_ge(ev2, r_g_e2)
                g.reg_add(r_g_sf, r_g_sf, 1)
                g.wait_ge(sfree, r_g_sf)
                g.tensor_tensor(xo_t[:, par, :], tmpo[:, par, :], xt_f[:, par, :], ALU.add).then_inc(po2, 1)
                sy.reg_add(r_sy_po2, r_sy_po2, 1)
                sy.wait_ge(po2, r_sy_po2)
                sy.dma_start(out=xo_r[b * SPT + tt], in_=xo_t[:, par, :]).then_inc(st2, 16)
                sy.reg_add(r_sy_st2, r_sy_st2, 16)
                sy.wait_ge(st2, r_sy_st2)
                sy.sem_inc(sfree, 1)
        sy.wait_ge(ldx[0], r_ldx[0])
        sy.wait_ge(ldx[1], r_ldx[1])
        p.barrier()
        p.free_scope(mk2)
    p.free_scope(mk_all)
    p.barrier()



def emit_l0attn(p, dram, w):
    nc = p.nc
    v, sc, g, t, sy = nc.vector, nc.scalar, nc.gpsimd, nc.tensor, nc.sync
    SST = S // 512

    mk = p.scoped()
    inw_sb = p.sbuf([128, 8 * 2048], BF16, "sa_inw")
    xpw_sb = p.sbuf([128, 16 * 65], BF16, "sa_xpw")
    inb_sb = p.sbuf([128, 16], F32, "sa_inb")
    convw_sb = p.sbuf([128, 16 * 4], F32, "sa_convw")
    convb_sb = p.sbuf([128, 16], F32, "sa_convb")
    xpba_sb = p.sbuf([16, 1], F32, "sa_xpba")
    xpbb_sb = p.sbuf([16, 1], F32, "sa_xpbb")
    outw_aug = p.sbuf([17, 1024], BF16, "sa_outw")
    onescol = p.sbuf([128, 1], BF16, "sa_ones")
    onescol_f = p.sbuf([1, 16], F32, "sa_onesf")
    zT_b = p.sbuf([128, 8 * S], BF16, "sa_zT")
    hx = p.sbuf([128, 2, 3 + S], F32, "sa_hx")
    cacc = p.sbuf([128, 2, S], F32, "sa_cacc")
    hs_bf = p.sbuf([128, 2, S], BF16, "sa_hs")
    e1 = p.sbuf([16, S], F32, "sa_e1")
    e1n = p.sbuf([16, S], F32, "sa_e1n")
    scanA = p.sbuf([16, S], F32, "sa_scanA")
    sb1 = p.sbuf([16, S], F32, "sa_sb1")
    rs_sb = p.sbuf([1, S], F32, "sa_rs")
    scanB = p.sbuf([16, S], F32, "sa_scanB")
    ys_f = p.sbuf([16, S], F32, "sa_ysf")
    ysbf = p.sbuf([17, S], BF16, "sa_ysbf")
    xt_f = p.sbuf([128, 2, D], F32, "sa_xt")
    xo_t = p.sbuf([128, 2, D], F32, "sa_xo")
    ps_h = p.psum([128, 2, 512], F32, "sa_psh")
    ps_xpF = p.psum([128, S], F32, "sa_psxp")
    ps_xp = ps_xpF[0:65, :]
    ps_bc = ps_h[0:16, 0, :]

    s0 = p.sem()
    n0 = 0
    for dc in range(8):
        sy.dma_start(out=inw_sb[:, ds_(dc, 2048)], in_=w["l0a_inwT_bf"][ts(dc, 128), :]).then_inc(s0, 16)
        n0 += 16
    for c in range(16):
        sy.dma_start(out=xpw_sb[:, ds_(c, 65)],
                     in_=w["l0a_xpwT_pad_bf"][ts(c, 128), :]).then_inc(s0, 16)
        n0 += 16
    sy.dma_start(out=inb_sb[:], in_=w["l0a_inb"]).then_inc(s0, 16)
    n0 += 16
    for c in range(16):
        sy.dma_start(out=convw_sb[:, ds_(c, 4)], in_=w["l0a_convw"][c]).then_inc(s0, 16)
        n0 += 16
    sy.dma_start(out=convb_sb[:], in_=w["l0a_convb"]).then_inc(s0, 16)
    n0 += 16
    sy.dma_start(out=xpba_sb[:], in_=w["l0a_xpb"][0:16, :]).then_inc(s0, 16)
    n0 += 16
    sy.dma_start(out=xpbb_sb[:], in_=w["l0a_xpb"][16:32, :]).then_inc(s0, 16)
    n0 += 16
    sy.dma_start(out=outw_aug[0:16, :], in_=w["l0a_outwT_bf"]).then_inc(s0, 16)
    n0 += 16
    sy.dma_start(out=outw_aug[16:17, :], in_=w["l0a_outb_bf"]).then_inc(s0, 16)
    n0 += 16
    sy.wait_ge(s0, n0)
    g.memset(onescol[:], 1.0)
    g.memset(onescol_f[:], 1.0)
    g.memset(hx[:, 0, 0:3], 0.0)
    g.memset(hx[:, 1, 0:3], 0.0)
    g.memset(ysbf[:], 1.0)
    p.barrier()

    xi_r = dram["x_in"].rearrange("(n p) d -> n p d", p=128)
    xo_r = dram["x_out"].rearrange("(n p) d -> n p d", p=128)

    for b in range(B_LOC):
        mkb = p.scoped()
        ld = p.sem(); mmh = p.sem(); evh = p.sem(); cvA = p.sem(); cvB = p.sem()
        svu = p.sem(); xpd = p.sem()
        nld = 0
        for c in range(8):
            sy.dma_start(out=zT_b[:, ds_(c, S)], in_=dram["zlnT"][ts(c, 128), b * S:(b + 1) * S]).then_inc(ld, 16)
            nld += 16
        sy.wait_ge(ld, nld)
        p.barrier()

        n_mmh = 0; n_evh = 0; n_cvA = 0; n_cvB = 0; n_svu = 0
        for fc in range(16):
            par = fc % 2
            for st in range(SST):
                if n_evh - 2 + 1 > 0:
                    t.wait_ge(evh, n_evh - 1)
                for dc in range(8):
                    ins = t.matmul(ps_h[:, st % 2, :],
                                   inw_sb[:, dc * 2048 + fc * 128: dc * 2048 + (fc + 1) * 128],
                                   zT_b[:, dc * S + st * 512: dc * S + (st + 1) * 512],
                                   start=(dc == 0), stop=(dc == 7))
                ins.then_inc(mmh, 1)
                n_mmh += 1
                sc.wait_ge(mmh, n_mmh)
                sc.activation(out=hx[:, par, 3 + st * 512: 3 + (st + 1) * 512],
                              in_=ps_h[:, st % 2, :], func=AF.Identity,
                              bias=inb_sb[:, fc:fc + 1]).then_inc(evh, 1)
                n_evh += 1
            v.wait_ge(evh, n_evh)
            if fc >= 1:
                v.wait_ge(svu, n_svu)
            v.tensor_scalar(cacc[:, 0, :], hx[:, par, 0:S], convw_sb[:, fc * 4 + 0: fc * 4 + 1],
                            None, ALU.mult).then_inc(cvA, 1)
            n_cvA += 1
            v.wait_ge(cvA, n_cvA)
            v.scalar_tensor_tensor(cacc[:, 1, :], hx[:, par, 1:1 + S],
                                   convw_sb[:, fc * 4 + 1: fc * 4 + 2], cacc[:, 0, :],
                                   ALU.mult, ALU.add).then_inc(cvB, 1)
            n_cvB += 1
            v.wait_ge(cvB, n_cvB)
            v.scalar_tensor_tensor(cacc[:, 0, :], hx[:, par, 2:2 + S],
                                   convw_sb[:, fc * 4 + 2: fc * 4 + 3], cacc[:, 1, :],
                                   ALU.mult, ALU.add).then_inc(cvA, 1)
            n_cvA += 1
            v.wait_ge(cvA, n_cvA)
            v.scalar_tensor_tensor(cacc[:, 1, :], hx[:, par, 3:3 + S],
                                   convw_sb[:, fc * 4 + 3: fc * 4 + 4], cacc[:, 0, :],
                                   ALU.mult, ALU.add).then_inc(cvB, 1)
            n_cvB += 1
            sc.wait_ge(cvB, n_cvB)
            if fc >= 2:
                sc.wait_ge(xpd, fc - 1)
            sc.activation(out=hs_bf[:, par, :], in_=cacc[:, 1, :], func=AF.Silu,
                          bias=convb_sb[:, fc:fc + 1]).then_inc(svu, 1)
            n_svu += 1
            t.wait_ge(svu, n_svu)
            for st in range(SST):
                ins = t.matmul(ps_xpF[0:65, st * 512:(st + 1) * 512],
                               xpw_sb[:, fc * 65:(fc + 1) * 65],
                               hs_bf[:, par, st * 512:(st + 1) * 512],
                               start=(fc == 0), stop=(fc == 15))
            ins.then_inc(xpd, 1)

        s_rs = p.sem(); s_e1 = p.sem(); s_e1n = p.sem(); s_A = p.sem()
        s_sb1 = p.sem(); mmbc = p.sem(); s_B = p.sem(); scn = p.sem()
        v.wait_ge(xpd, 16)
        v.tensor_copy(rs_sb[:], ps_xp[64:65, :]).then_inc(s_rs, 1)
        sc.wait_ge(xpd, 16)
        sc.activation(out=e1[:], in_=ps_xp[0:16, :], func=AF.Exp,
                      bias=xpba_sb[:, 0:1]).then_inc(s_e1, 1)
        v.wait_ge(s_e1, 1)
        v.tensor_scalar(e1n[:], e1[:], -1.0, None, ALU.mult).then_inc(s_e1n, 1)
        sc.wait_ge(s_e1n, 1)
        sc.activation(out=scanA[:], in_=e1n[:], func=AF.Exp).then_inc(s_A, 1)
        sc.activation(out=sb1[:], in_=ps_xp[32:48, :], func=AF.Silu,
                      bias=xpbb_sb[:, 0:1]).then_inc(s_sb1, 1)
        t.wait_ge(s_rs, 1)
        t.wait_ge(evh, n_evh)
        for st in range(SST):
            if st > 0:
                t.wait_ge(s_B, st)
            t.matmul(ps_bc, onescol_f[0:1, :], rs_sb[0:1, st * 512:(st + 1) * 512],
                     start=True, stop=True).then_inc(mmbc, 1)
            v.wait_ge(mmbc, st + 1)
            v.wait_ge(s_sb1, 1)
            v.tensor_tensor(scanB[:, st * 512:(st + 1) * 512],
                            sb1[:, st * 512:(st + 1) * 512], ps_bc,
                            ALU.mult).then_inc(s_B, 1)
        v.wait_ge(s_A, 1)
        v.wait_ge(s_B, SST)
        v.tensor_tensor_scan(ys_f[:], scanA[:], scanB[:], 0.0, ALU.mult,
                             ALU.add).then_inc(scn, 1)
        g.wait_ge(scn, 1)
        g.tensor_copy(ysbf[0:16, :], ys_f[:]).then_inc(scn, 1)
        p.barrier()

        ldx = [p.sem(), p.sem()]
        mmo = p.sem(); evo = p.sem(); po = p.sem(); st2 = p.sem(); sfree = p.sem()
        nldx = [0, 0]
        n_mmo = 0; n_evo = 0; n_st2 = 0; n_po = 0
        sy.sem_inc(sfree, 2)
        sy.dma_start(out=xt_f[:, 0, :], in_=xi_r[b * 16]).then_inc(ldx[0], 16)
        nldx[0] += 16
        for tt in range(16):
            par = tt % 2
            nxt = 1 - par
            if tt < 15:
                sy.dma_start(out=xt_f[:, nxt, :], in_=xi_r[b * 16 + tt + 1]).then_inc(ldx[nxt], 16)
                nldx[nxt] += 16
            sy.wait_ge(ldx[par], nldx[par])
            sy.sem_inc(po, 1)
            n_po += 1
            if tt >= 2:
                t.wait_ge(evo, n_evo - 1)
            pso = ps_h[:].rearrange("p c f -> p (c f)") if tt % 2 == 0 else ps_xpF[:, 0:1024]
            for half in (0, 1):
                ins = t.matmul(pso[:, half * 512:(half + 1) * 512],
                               ysbf[0:17, tt * 128:(tt + 1) * 128],
                               outw_aug[0:17, half * 512:(half + 1) * 512],
                               start=True, stop=True)
            ins.then_inc(mmo, 1)
            n_mmo += 1
            v.wait_ge(mmo, n_mmo)
            v.wait_ge(po, n_po)
            v.wait_ge(sfree, tt)
            v.tensor_tensor(xo_t[:, par, :], pso,
                            xt_f[:, par, :], ALU.add).then_inc(evo, 1)
            n_evo += 1
            sy.wait_ge(evo, n_evo)
            sy.dma_start(out=xo_r[b * 16 + tt], in_=xo_t[:, par, :]).then_inc(st2, 16)
            n_st2 += 16
            sy.wait_ge(st2, n_st2)
            sy.sem_inc(sfree, 1)
        p.barrier()
        p.free_scope(mkb)
    p.free_scope(mk)
    p.barrier()



def emit_l1ffn_qkv(p, dram, w, consts):
    nc = p.nc
    v, sc, g, t, sy = nc.vector, nc.scalar, nc.gpsimd, nc.tensor, nc.sync
    SST = S // 512
    mk = p.scoped()
    qkvw_sb = p.sbuf([128, 8 * 3072], BF16, "lf_qkvw")
    zT_b = p.sbuf([128, 8 * S], BF16, "lf_zT")
    qtmp = p.sbuf([128, 2, 512], BF16, "lf_qtmp")
    t1s = p.sbuf([128, 2, 512], F32, "lf_t1")
    t2s = p.sbuf([128, 2, 512], F32, "lf_t2")
    ostg = p.sbuf([128, 2, 512], BF16, "lf_ostg")
    zstage = p.sbuf([128, 2, 1024], BF16, "lf_zstg")
    vstg = p.sbuf([128, 2, 1024], BF16, "lf_vstg")
    qkvb_h = p.sbuf([128, 24], F32, "lf_qkvbh")
    qkvb_row = p.sbuf([1, 3 * D], BF16, "lf_qkvbr")
    ones_row = consts["ones_row"]
    rot_t = p.sbuf([128, 128], BF16, "lf_rot")
    rc_t = p.sbuf([128, S], BF16, "lf_rc")
    rs_t = p.sbuf([128, S], BF16, "lf_rs")
    rcq_t = p.sbuf([128, S], BF16, "lf_rcq")
    rsq_t = p.sbuf([128, S], BF16, "lf_rsq")
    ps_q = p.psum([128, 2, 512], F32, "lf_psq")
    ps_rot = p.psum([128, 2, 512], F32, "lf_psrot")
    ps_kv = p.psum([128, 2, 1024], F32, "lf_pskv")

    s0 = p.sem()
    sy.dma_start(out=qkvb_h[:], in_=w["l1f_qkvb_h"]).then_inc(s0, 16)
    sy.dma_start(out=qkvb_row[:], in_=w["l1f_qkvb_row"]).then_inc(s0, 16)
    g.dma_start(out=rot_t[:], in_=w["rotmat"]).then_inc(s0, 16)
    g.dma_start(out=rc_t[:], in_=w["rope_cos"]).then_inc(s0, 16)
    g.dma_start(out=rs_t[:], in_=w["rope_sin"]).then_inc(s0, 16)
    g.dma_start(out=rcq_t[:], in_=w["rope_cos_q"]).then_inc(s0, 16)
    g.dma_start(out=rsq_t[:], in_=w["rope_sin_q"]).then_inc(s0, 16)
    sy.wait_ge(s0, 32)
    g.wait_ge(s0, 112)
    p.barrier()
    lconsts = {"rotmat": rot_t[:], "rope_cos": rc_t[:], "rope_sin": rs_t[:],
               "ropeq_cos": rcq_t[:], "ropeq_sin": rsq_t[:]}

    for b in range(B_LOC):
        mkb = p.scoped()
        ld = p.sem()
        nld = 0
        for dc in range(8):
            sy.dma_start(out=qkvw_sb[:, ds_(dc, 3072)], in_=w["l1f_qkvwT_bf"][ts(dc, 128), :]).then_inc(ld, 16)
            nld += 16
        for c in range(8):
            sy.dma_start(out=zT_b[:, ds_(c, S)], in_=dram["zlnT"][ts(c, 128), b * S:(b + 1) * S]).then_inc(ld, 16)
            nld += 16
        sy.wait_ge(ld, nld)
        p.barrier()

        for which, dst, cosname, sinname in [("q", dram["qT"], "ropeq_cos", "ropeq_sin"),
                                             ("k", dram["kT"], "rope_cos", "rope_sin")]:
            fbase = 0 if which == "q" else D
            cos_t, sin_t = lconsts[cosname], lconsts[sinname]
            mmq = p.sem(); eva = p.sem(); evr = p.sem(); mr = p.sem()
            evf = p.sem(); stq = p.sem(); stdone = p.sem()
            r_t_ev = p.reg(t, 0); r_s_mm = p.reg(sc, 0); r_t_ea = p.reg(t, 0)
            r_v_mr = p.reg(v, 0); r_g_ev = p.reg(g, 0); r_sy_f = p.reg(sy, 0)
            r_sy_st = p.reg(sy, 0); r_v_std = p.reg(v, 0); r_s_evf = p.reg(sc, 0)
            sy.sem_inc(stdone, 2)
            for fc in range(8):
                bcol = fbase // 128 + fc
                with nc.Fori(0, SST) as sti:
                    par = sti % 2
                    t.wait_ge(evf, r_t_ev)
                    for dc in range(8):
                        ins = t.matmul(ps_q[:, par, :],
                                       qkvw_sb[:, dc * 3072 + fbase + fc * 128: dc * 3072 + fbase + (fc + 1) * 128],
                                       zT_b[:, bass.ds(dc * S + sti * 512, 512)],
                                       start=(dc == 0), stop=(dc == 7))
                    ins.then_inc(mmq, 1)
                    sc.reg_add(r_s_mm, r_s_mm, 1)
                    sc.wait_ge(mmq, r_s_mm)
                    sc.wait_ge(evf, r_s_evf)
                    sc.reg_add(r_s_evf, r_s_evf, 2)
                    sc.activation(out=qtmp[:, par, :], in_=ps_q[:, par, :], func=AF.Identity,
                                  bias=qkvb_h[:, bcol:bcol + 1]).then_inc(eva, 1)
                    t.reg_add(r_t_ea, r_t_ea, 1)
                    t.wait_ge(eva, r_t_ea)
                    t.matmul(ps_rot[:, par, :], lconsts["rotmat"], qtmp[:, par, :],
                             start=True, stop=True).then_inc(mr, 1)
                    t.reg_add(r_t_ev, r_t_ev, 2)
                    v.reg_add(r_v_mr, r_v_mr, 1)
                    v.wait_ge(mr, r_v_mr)
                    v.reg_add(r_v_std, r_v_std, 1)
                    v.wait_ge(stdone, r_v_std)
                    v.tensor_tensor(t1s[:, par, :], qtmp[:, par, :],
                                    cos_t[:, bass.ds(sti * 512, 512)], ALU.mult)
                    v.tensor_tensor(t2s[:, par, :], ps_rot[:, par, :],
                                    sin_t[:, bass.ds(sti * 512, 512)], ALU.mult).then_inc(evr, 2)
                    g.reg_add(r_g_ev, r_g_ev, 2)
                    g.wait_ge(evr, r_g_ev)
                    g.tensor_tensor(ostg[:, par, :], t1s[:, par, :], t2s[:, par, :],
                                    ALU.add).then_inc(evf, 2)
                    sy.reg_add(r_sy_f, r_sy_f, 2)
                    sy.wait_ge(evf, r_sy_f)
                    sy.dma_start(out=dst[b, ts(fc, 128)][:, bass.ds(sti * 512, 512)],
                                 in_=ostg[:, par, :]).then_inc(stq, 16)
                    sy.reg_add(r_sy_st, r_sy_st, 16)
                    sy.wait_ge(stq, r_sy_st)
                    sy.sem_inc(stdone, 1)
            p.barrier()

        sem_z = p.sem(); mmv = p.sem(); evv = p.sem(); stv = p.sem(); svd = p.sem()
        r_t_z = p.reg(t, 1); r_t_ev = p.reg(t, 0); r_s_mm = p.reg(sc, 0)
        r_sy_ev = p.reg(sy, 0); r_sy_st = p.reg(sy, 0); r_s_svd = p.reg(sc, 0)
        sy.sem_inc(svd, 2)
        r_v_mm = p.reg(v, 0)
        with nc.Fori(0, S // 128 // 2) as th:
            for par in (0, 1):
                tt = th * 2 + par
                v.wait_ge(mmv, r_v_mm)
                v.reg_add(r_v_mm, r_v_mm, 1)
                v.tensor_copy(zstage[:, par, :],
                              zT_b[:].rearrange("p (c s) -> p c s", c=8)[:, :, bass.ds(tt * 128, 128)]).then_inc(sem_z, 1)
                t.wait_ge(sem_z, r_t_z)
                t.wait_ge(evv, r_t_ev)
                for half in range(2):
                    for dc in range(8):
                        t.matmul(ps_kv[:, par, ts(half, 512)], zstage[:, par, ts(dc, 128)],
                                 qkvw_sb[:, dc * 3072 + 2 * D + half * 512: dc * 3072 + 2 * D + (half + 1) * 512],
                                 start=(dc == 0), stop=False)
                    ins = t.matmul(ps_kv[:, par, ts(half, 512)], ones_row,
                                   qkvb_row[0:1, 2 * D + half * 512: 2 * D + (half + 1) * 512],
                                   start=False, stop=True)
                ins.then_inc(mmv, 1)
                t.reg_add(r_t_z, r_t_z, 1)
                t.reg_add(r_t_ev, r_t_ev, 1)
                sc.reg_add(r_s_mm, r_s_mm, 1)
                sc.wait_ge(mmv, r_s_mm)
                sc.reg_add(r_s_svd, r_s_svd, 1)
                sc.wait_ge(svd, r_s_svd)
                sc.activation(out=vstg[:, par, :], in_=ps_kv[:, par, :],
                              func=AF.Identity).then_inc(evv, 1)
                sy.reg_add(r_sy_ev, r_sy_ev, 1)
                sy.wait_ge(evv, r_sy_ev)
                sy.dma_start(out=dram["v_tok"][b, bass.ds(tt * 128, 128), :],
                             in_=vstg[:, par, :]).then_inc(stv, 16)
                sy.reg_add(r_sy_st, r_sy_st, 16)
                sy.wait_ge(stv, r_sy_st)
                sy.sem_inc(svd, 1)
        p.barrier()
        p.free_scope(mkb)
    p.free_scope(mk)
    p.barrier()


def emit_l1ffn_attn(p, dram, w, consts):
    nc = p.nc
    v, sc, g, t, sy = nc.vector, nc.scalar, nc.gpsimd, nc.tensor, nc.sync
    mk = p.scoped()
    q_aug = p.sbuf([72, S], BF16, "lf_qaug")
    k_aug = p.sbuf([72, S], BF16, "lf_kaug")
    v_sb = p.sbuf([128, 16 * 65], BF16, "lf_vsb")
    sqv = p.sbuf([64, 2, S], F32, "lf_sqv")
    nrm = p.sbuf([1, S + 16], F32, "lf_nrm")
    expT = p.sbuf([128, 16 * 512], BF16, "lf_expT")
    attnout = p.sbuf([128, 16 * 1024], BF16, "lf_ao")
    rcp = p.sbuf([128, 4], F32, "lf_rcp")
    astage = p.sbuf([128, 1024], BF16, "lf_astg")
    aT_st = p.sbuf([128, 1024], BF16, "lf_aTst")
    outw_sb = p.sbuf([128, 8 * 1024], BF16, "lf_outw")
    outb_row = p.sbuf([1, D], BF16, "lf_outbr")
    zt_f = p.sbuf([128, 2, D], F32, "lf_ztf")
    z1_t = p.sbuf([128, 2, D], F32, "lf_z1t")
    ps_e = p.psum([128, 2, 512], F32, "lf_pse")
    ps_pv = p.psum([128, 4, 65], F32, "lf_pspv")
    ps_n = p.psum([33, 512], F32, "lf_psn")
    ps_tr = p.psum([128, 8, 128], BF16, "lf_pstr")
    ps_o = p.psum([128, 2, 512], F32, "lf_pso")

    s0 = p.sem()
    n0 = 0
    for dc in range(8):
        sy.dma_start(out=outw_sb[:, ds_(dc, 1024)], in_=w["l1f_outwT_bf"][ts(dc, 128), :]).then_inc(s0, 16)
        n0 += 16
    sy.dma_start(out=outb_row[:], in_=w["l1f_outb_row"]).then_inc(s0, 16)
    n0 += 16
    sy.wait_ge(s0, n0)
    g.memset(v_sb[:].rearrange("p (c w) -> p c w", c=16)[:, :, 64:65], 1.0)
    g.memset(k_aug[64:65, :], 1.0)
    p.barrier()

    z0_r = dram["z0"].rearrange("(n p) d -> n p d", p=128)
    z1_r = dram["z1"].rearrange("(n p) d -> n p d", p=128)

    for b in range(B_LOC):
        mkb = p.scoped()
        ld = p.sem(); hrdy = p.sem()
        mmn = p.sem(); s_sq = p.sem(); s_km = p.sem(); s_rt = p.sem(); vbr = p.sem()
        sq1 = p.sem(); sq2 = p.sem()
        mme = p.sem(); eve = [p.sem(), p.sem()]; mmpv = p.sem(); v_rc = p.sem(); evpv = p.sem()
        r_sy_pv = p.reg(sy, 0); r_sy_ld = p.reg(sy, 0)
        r_s_hr = p.reg(sc, 0)
        r_t_sq1 = p.reg(t, 0); r_t_sq2 = p.reg(t, 0); r_t_nc = p.reg(t, 0)
        r_t_nc2 = p.reg(t, 0); r_v_ssq = p.reg(v, 0); r_s_vbr = p.reg(sc, 0)
        r_s_mmn = p.reg(sc, 0); r_v_mmn = p.reg(v, 0)
        r_v_skc = p.reg(v, 0); r_s_km = p.reg(sc, 0); r_v_rt = p.reg(v, 0)
        r_s_nq = p.reg(sc, 0)
        r_t_vbr = p.reg(t, 0)
        r_t_eve = [p.reg(t, 0), p.reg(t, 0)]; r_s_mme = p.reg(sc, 0); r_s_pv = p.reg(sc, 0)
        r_t_pve = [p.reg(t, 1), p.reg(t, 1)]; r_t_evp = p.reg(t, 0)
        r_v_pv = p.reg(v, 0); r_s_rc = p.reg(sc, 0)
        with nc.Fori(0, 16) as h:
            sy.wait_ge(mmpv, r_sy_pv)
            sy.reg_add(r_sy_pv, r_sy_pv, 16)
            sy.dma_start(out=q_aug[0:64, :], in_=dram["qT"][b, bass.ds(h * 64, 64), :]).then_inc(ld, 16)
            sy.dma_start(out=k_aug[0:64, :], in_=dram["kT"][b, bass.ds(h * 64, 64), :]).then_inc(ld, 16)
            sy.dma_start(out=v_sb[:].rearrange("p (c w) -> p c w", c=16)[:, :, 0:64],
                         in_=dram["v_tok"][b][:, bass.ds(h * 64, 64)].rearrange("(c p) w -> p c w", p=128)).then_inc(ld, 16)
            sy.reg_add(r_sy_ld, r_sy_ld, 48)
            sy.wait_ge(ld, r_sy_ld)
            sy.sem_inc(hrdy, 1)
            sc.reg_add(r_s_hr, r_s_hr, 1)
            sc.wait_ge(hrdy, r_s_hr)
            sc.activation(out=sqv[:, 0, :], in_=q_aug[0:64, :], func=AF.Square).then_inc(sq1, 1)
            sc.activation(out=sqv[:, 1, :], in_=k_aug[0:64, :], func=AF.Square).then_inc(sq2, 1)
            t.reg_add(r_t_sq1, r_t_sq1, 1)
            t.wait_ge(sq1, r_t_sq1)
            t.reg_add(r_t_sq2, r_t_sq2, 1)
            t.wait_ge(sq2, r_t_sq2)
            for stq in range(4):
                t.wait_ge(s_sq, r_t_nc)
                t.wait_ge(s_km, r_t_nc2)
                t.reg_add(r_t_nc, r_t_nc, 1)
                t.reg_add(r_t_nc2, r_t_nc2, 1)
                t.matmul(ps_n[0:1, :], consts["ones64c"], sqv[0:64, 0, :][:, ts(stq, 512)],
                         start=True, stop=True)
                t.matmul(ps_n[32:33, :], consts["ones64c"], sqv[0:64, 1, :][:, ts(stq, 512)],
                         start=True, stop=True).then_inc(mmn, 1)
                sc.reg_add(r_s_mmn, r_s_mmn, 1)
                sc.wait_ge(mmn, r_s_mmn)
                if stq == 0:
                    sc.wait_ge(vbr, r_s_vbr)
                    sc.reg_add(r_s_vbr, r_s_vbr, 1)
                sc.activation(out=nrm[0:1, ts(stq, 512)], in_=ps_n[0:1, :],
                              func=AF.Sqrt).then_inc(s_sq, 1)
                v.reg_add(r_v_mmn, r_v_mmn, 1)
                v.wait_ge(mmn, r_v_mmn)
                v.tensor_reduce(nrm[0:1, S + stq:S + stq + 1], ps_n[32:33, :], AX.X,
                                ALU.max).then_inc(s_km, 1)
            t.reg_add(r_t_nc2, r_t_nc2, 1)
            v.reg_add(r_v_skc, r_v_skc, 4)
            v.wait_ge(s_km, r_v_skc)
            v.tensor_reduce(nrm[0:1, S + 8:S + 9], nrm[0:1, S:S + 4], AX.X,
                            ALU.max).then_inc(s_km, 1)
            v.reg_add(r_v_skc, r_v_skc, 1)
            sc.reg_add(r_s_km, r_s_km, 5)
            sc.wait_ge(s_km, r_s_km)
            sc.activation(out=nrm[0:1, S + 9:S + 10], in_=nrm[0:1, S + 8:S + 9],
                          func=AF.Sqrt).then_inc(s_rt, 1)
            v.reg_add(r_v_rt, r_v_rt, 1)
            v.wait_ge(s_rt, r_v_rt)
            v.reg_add(r_v_ssq, r_v_ssq, 4)
            v.wait_ge(s_sq, r_v_ssq)
            v.tensor_scalar(q_aug[64:65, :], nrm[0:1, 0:S], nrm[0:1, S + 9:S + 10],
                            -1.0, ALU.mult, ALU.mult).then_inc(vbr, 1)
            t.reg_add(r_t_vbr, r_t_vbr, 1)
            t.wait_ge(vbr, r_t_vbr)
            for qg in range(4):
                sc.wait_ge(mmpv, r_s_pv)
                for kc in range(16):
                    t.wait_ge(eve[kc % 2], r_t_eve[kc % 2])
                    t.reg_add(r_t_eve[kc % 2], r_t_eve[kc % 2], 1)
                    t.matmul(ps_e[:, kc % 2, :], k_aug[0:65, ts(kc, 128)],
                             q_aug[0:65, ts(qg, 512)], start=True, stop=True).then_inc(mme, 1)
                    sc.reg_add(r_s_mme, r_s_mme, 1)
                    sc.wait_ge(mme, r_s_mme)
                    sc.activation(out=expT[:, ds_(kc, 512)], in_=ps_e[:, kc % 2, :],
                                  func=AF.Exp).then_inc(eve[kc % 2], 1)
                sc.reg_add(r_s_pv, r_s_pv, 4)
                t.wait_ge(evpv, r_t_evp)
                t.reg_add(r_t_evp, r_t_evp, 4)
                for kc in range(16):
                    t.wait_ge(eve[kc % 2], r_t_pve[kc % 2])
                    t.reg_add(r_t_pve[kc % 2], r_t_pve[kc % 2], 1)
                    for qs in range(4):
                        ins = t.matmul(ps_pv[:, qs, :], expT[:, kc * 512 + qs * 128: kc * 512 + (qs + 1) * 128],
                                       v_sb[:, ds_(kc, 65)],
                                       start=(kc == 0 and qs == 0),
                                       stop=(kc == 15 and qs == 3))
                        if kc == 15:
                            ins.then_inc(mmpv, 1)
                v.reg_add(r_v_pv, r_v_pv, 4)
                v.wait_ge(mmpv, r_v_pv)
                for qs in range(4):
                    v.reciprocal(rcp[:, qs:qs + 1], ps_pv[:, qs, 64:65]).then_inc(v_rc, 1)
                sc.reg_add(r_s_rc, r_s_rc, 4)
                sc.wait_ge(v_rc, r_s_rc)
                for qs in range(4):
                    sc.activation(out=attnout[:, bass.ds((qg * 4 + qs) * 1024 + h * 64, 64)],
                                  in_=ps_pv[:, qs, 0:64], func=AF.Identity,
                                  scale=rcp[:, qs:qs + 1]).then_inc(evpv, 1)
        p.barrier()

        ldz = [p.sem(), p.sem()]
        zrdy = p.sem(); cpa = p.sem(); tpa = p.sem(); cpt = p.sem()
        mmo = p.sem(); evz = p.sem(); stz = p.sem(); szf = p.sem()
        r_ldz = [p.reg(sy, 16), p.reg(sy, 0)]
        r_sy_ev = p.reg(sy, 0)
        r_sy_st = p.reg(sy, 0)
        r_v_mmo = p.reg(v, 0)
        r_t_cpa = p.reg(t, 0); r_v_tpa = p.reg(v, 0); r_t_cpt = p.reg(t, 0)
        r_v_zr = p.reg(v, 0); r_v_szf = p.reg(v, 0)
        r_t_evz = p.reg(t, 0)
        sy.sem_inc(szf, 2)
        sy.dma_start(out=zt_f[:, 0, :], in_=z0_r[b * 16]).then_inc(ldz[0], 16)
        with nc.Fori(0, 8) as th:
            for par in (0, 1):
                qt = th * 2 + par
                nxt = 1 - par
                sy.dma_start(out=zt_f[:, nxt, :], in_=z0_r[smin(b * 16 + qt + 1, NT - 1)]).then_inc(ldz[nxt], 16)
                sy.reg_add(r_ldz[nxt], r_ldz[nxt], 16)
                sy.wait_ge(ldz[par], r_ldz[par])
                sy.sem_inc(zrdy, 1)
                v.wait_ge(mmo, r_v_mmo)
                v.tensor_copy(astage[:], attnout[:, bass.ds(qt * 1024, 1024)]).then_inc(cpa, 1)
                t.reg_add(r_t_cpa, r_t_cpa, 1)
                t.wait_ge(cpa, r_t_cpa)
                t.wait_ge(cpt, r_t_cpt)
                for c in range(8):
                    ins = t.transpose(ps_tr[:, c, :], astage[:, ts(c, 128)], consts["ident_bf"])
                ins.then_inc(tpa, 1)
                v.reg_add(r_v_tpa, r_v_tpa, 1)
                v.wait_ge(tpa, r_v_tpa)
                v.tensor_copy(aT_st[:], ps_tr[:].rearrange("p c f -> p (c f)")).then_inc(cpt, 1)
                t.reg_add(r_t_cpt, r_t_cpt, 1)
                t.wait_ge(cpt, r_t_cpt)
                t.wait_ge(evz, r_t_evz)
                for half in (0, 1):
                    for c in range(8):
                        t.matmul(ps_o[:, half, :], aT_st[:, ts(c, 128)],
                                 outw_sb[:, c * 1024 + half * 512: c * 1024 + (half + 1) * 512],
                                 start=(c == 0), stop=False)
                    ins = t.matmul(ps_o[:, half, :], consts["ones_row"],
                                   outb_row[0:1, half * 512:(half + 1) * 512],
                                   start=False, stop=True)
                ins.then_inc(mmo, 1)
                t.reg_add(r_t_evz, r_t_evz, 1)
                v.reg_add(r_v_mmo, r_v_mmo, 1)
                v.wait_ge(mmo, r_v_mmo)
                v.reg_add(r_v_zr, r_v_zr, 1)
                v.wait_ge(zrdy, r_v_zr)
                v.wait_ge(szf, r_v_szf)
                v.reg_add(r_v_szf, r_v_szf, 1)
                v.tensor_tensor(z1_t[:, par, :], ps_o[:].rearrange("p c f -> p (c f)"),
                                zt_f[:, par, :], ALU.add).then_inc(evz, 1)
                sy.reg_add(r_sy_ev, r_sy_ev, 1)
                sy.wait_ge(evz, r_sy_ev)
                sy.dma_start(out=z1_r[b * 16 + qt], in_=z1_t[:, par, :]).then_inc(stz, 16)
                sy.reg_add(r_sy_st, r_sy_st, 16)
                sy.wait_ge(stz, r_sy_st)
                sy.sem_inc(szf, 1)
        sy.wait_ge(ldz[0], r_ldz[0])
        sy.wait_ge(ldz[1], r_ldz[1])
        p.barrier()
        p.free_scope(mkb)
    p.free_scope(mk)
    p.barrier()


def emit_l1ffn_moe(p, dram, w, consts):
    nc = p.nc
    v, sc, g, t, sy = nc.vector, nc.scalar, nc.gpsimd, nc.tensor, nc.sync
    mk = p.scoped()
    gstage = p.sbuf([128, 2, 1024], BF16, "lm_gstg")
    gw_sb = p.sbuf([128, 8 * 8], BF16, "lm_gw")
    gsc = p.sbuf([128, 48], F32, "lm_gsc")
    comb_sb = p.sbuf([128, 64 * 8], F32, "lm_comb")
    w1_sb = p.sbuf([128, 8 * 2048], BF16, "lm_w1")
    w2_sb = p.sbuf([128, 8 * 2048], BF16, "lm_w2")
    w3_sb = p.sbuf([128, 16 * 1024], BF16, "lm_w3")
    b1_sb = p.sbuf([128, 16], F32, "lm_b1")
    b2_sb = p.sbuf([128, 16], F32, "lm_b2")
    b3_row = p.sbuf([1, 1024], BF16, "lm_b3")
    zst = p.sbuf([128, 2, 8 * 512], BF16, "lm_zst")
    s1_bf = p.sbuf([128, 2, 512], BF16, "lm_s1")
    s2_bf = p.sbuf([128, 2, 512], BF16, "lm_s2")
    hh_sb = p.sbuf([128, 16 * 512], BF16, "lm_hh")
    hstage = p.sbuf([128, 16 * 128], BF16, "lm_hstg")
    yt = p.sbuf([128, 2, 1024], F32, "lm_yt")
    xt = p.sbuf([128, 2, 1024], F32, "lm_xt")
    xo = p.sbuf([128, 2, 1024], F32, "lm_xo")
    ps_g = p.psum([128, 8], F32, "lm_psg")
    ps1 = p.psum([128, 2, 512], F32, "lm_ps1")
    ps2 = p.psum([128, 2, 512], F32, "lm_ps2")
    ps3 = p.psum([128, 2, 512], F32, "lm_ps3")

    s0 = p.sem()
    sy.dma_start(out=gw_sb[:].rearrange("p (c e) -> p c e", c=8),
                 in_=w["l1f_gatewT_bf"].rearrange("(c p) e -> p c e", p=128)).then_inc(s0, 16)
    sy.wait_ge(s0, 16)
    p.barrier()

    ldg = [p.sem(), p.sem()]
    grdy = p.sem(); mmg = p.sem(); sv = p.sem(); sg = p.sem(); ss = p.sem()
    r_ldg = [p.reg(sy, 0), p.reg(sy, 0)]
    r_sy_g = p.reg(sy, 0)
    r_t_gr = p.reg(t, 0); r_t_sv = p.reg(t, 0)
    r_v_mmg = p.reg(v, 0); r_vs = p.reg(v, 1); r_v_ss = p.reg(v, 0); r_v_sg2 = p.reg(v, 0)
    r_g_sv = p.reg(g, 10); r_s_sv = p.reg(sc, 7)
    zr = dram["zlnT"]
    LG, OH1, LGM, OH2, WS = 0, 8, 16, 24, 32
    M1, M2, DM, G2, G1 = 40, 41, 42, 43, 44
    with nc.Fori(0, NT // 2) as th:
        for par in (0, 1):
            tl = th * 2 + par
            sy.wait_ge(mmg, r_sy_g)
            sy.reg_add(r_sy_g, r_sy_g, 1)
            sy.dma_start(out=gstage[:, par, :].rearrange("p (c q) -> p c q", c=8),
                         in_=zr.rearrange("(c p) t -> p c t", p=128)[:, :, bass.ds(tl * 128, 128)]).then_inc(ldg[par], 16)
            sy.reg_add(r_ldg[par], r_ldg[par], 16)
            sy.wait_ge(ldg[par], r_ldg[par])
            sy.sem_inc(grdy, 1)
            t.reg_add(r_t_gr, r_t_gr, 1)
            t.wait_ge(grdy, r_t_gr)
            t.wait_ge(sv, r_t_sv)
            t.reg_add(r_t_sv, r_t_sv, 10)
            for dc in range(8):
                ins = t.matmul(ps_g[:], gstage[:, par, ts(dc, 128)], gw_sb[:, ds_(dc, 8)],
                               start=(dc == 0), stop=(dc == 7))
            ins.then_inc(mmg, 1)
            v.reg_add(r_v_mmg, r_v_mmg, 1); v.wait_ge(mmg, r_v_mmg)
            v.tensor_copy(gsc[:, LG:LG + 8], ps_g[:]).then_inc(sv, 1)
            v.wait_ge(sv, r_vs)
            v.tensor_reduce(gsc[:, M1:M1 + 1], gsc[:, LG:LG + 8], AX.X, ALU.max).then_inc(sv, 1)
            v.reg_add(r_vs, r_vs, 1); v.wait_ge(sv, r_vs)
            v.tensor_scalar(gsc[:, OH1:OH1 + 8], gsc[:, LG:LG + 8], gsc[:, M1:M1 + 1],
                            None, ALU.is_equal).then_inc(sv, 1)
            v.reg_add(r_vs, r_vs, 1); v.wait_ge(sv, r_vs)
            v.scalar_tensor_tensor(gsc[:, LGM:LGM + 8], gsc[:, OH1:OH1 + 8], -1e30,
                                   gsc[:, LG:LG + 8], ALU.mult, ALU.add).then_inc(sv, 1)
            v.reg_add(r_vs, r_vs, 1); v.wait_ge(sv, r_vs)
            v.tensor_reduce(gsc[:, M2:M2 + 1], gsc[:, LGM:LGM + 8], AX.X, ALU.max).then_inc(sv, 1)
            v.reg_add(r_vs, r_vs, 1); v.wait_ge(sv, r_vs)
            v.tensor_scalar(gsc[:, OH2:OH2 + 8], gsc[:, LGM:LGM + 8], gsc[:, M2:M2 + 1],
                            None, ALU.is_equal).then_inc(sv, 1)
            v.reg_add(r_vs, r_vs, 1); v.wait_ge(sv, r_vs)
            v.tensor_scalar(gsc[:, DM:DM + 1], gsc[:, M2:M2 + 1], gsc[:, M1:M1 + 1],
                            None, ALU.subtract).then_inc(sv, 1)
            sc.wait_ge(sv, r_s_sv)
            sc.reg_add(r_s_sv, r_s_sv, 10)
            sc.activation(out=gsc[:, G2:G2 + 1], in_=gsc[:, DM:DM + 1],
                          func=AF.Sigmoid).then_inc(ss, 1)
            v.reg_add(r_v_ss, r_v_ss, 1); v.wait_ge(ss, r_v_ss)
            v.tensor_scalar(gsc[:, G1:G1 + 1], gsc[:, G2:G2 + 1], -1.0, 1.0,
                            ALU.mult, ALU.add).then_inc(sv, 1)
            v.reg_add(r_vs, r_vs, 2); v.wait_ge(sv, r_vs)
            v.tensor_scalar(gsc[:, LGM:LGM + 8], gsc[:, OH2:OH2 + 8], gsc[:, G2:G2 + 1],
                            None, ALU.mult).then_inc(sv, 1)
            v.reg_add(r_vs, r_vs, 1); v.wait_ge(sv, r_vs)
            v.wait_ge(sg, r_v_sg2)
            v.reg_add(r_v_sg2, r_v_sg2, 1)
            v.scalar_tensor_tensor(gsc[:, WS:WS + 8], gsc[:, OH1:OH1 + 8], gsc[:, G1:G1 + 1],
                                   gsc[:, LGM:LGM + 8], ALU.mult, ALU.add).then_inc(sv, 1)
            g.wait_ge(sv, r_g_sv)
            g.reg_add(r_g_sv, r_g_sv, 10)
            g.tensor_copy(comb_sb[:, bass.ds(tl * 8, 8)], gsc[:, WS:WS + 8]).then_inc(sg, 1)
            v.reg_add(r_vs, r_vs, 2)
    p.barrier()

    cps = p.sem(); cpr = p.sem()
    r_cps = p.reg(sy, 0)
    z1r_ = dram["z1"].rearrange("(n p) d -> n p d", p=128)
    yar_ = dram["yacc"].rearrange("(n p) d -> n p d", p=128)
    with nc.Fori(0, NT // 2) as th:
        for par in (0, 1):
            tl = th * 2 + par
            sy.dma_start(out=yt[:, par, :], in_=z1r_[tl]).then_inc(cps, 16)
            sy.reg_add(r_cps, r_cps, 16)
            sy.wait_ge(cps, r_cps)
            sy.dma_start(out=yar_[tl], in_=yt[:, par, :]).then_inc(cps, 16)
            sy.reg_add(r_cps, r_cps, 16)
            sy.wait_ge(cps, r_cps)
    p.barrier()

    lde = p.sem(); wrdy = p.sem(); mmw = p.sem(); mm12 = p.sem()
    s1d = p.sem(); s2d = p.sem(); hhd = p.sem(); cph = p.sem(); mm3 = p.sem()
    evy = p.sem(); yrdy = p.sem(); sty = p.sem()
    r_sy_lde = p.reg(sy, 0)
    r_sy_w = p.reg(sy, 0)
    r_s_hh = p.reg(sc, 0); r_v_hh2 = p.reg(v, 0)
    r_sty_c = p.reg(sy, 0); r_sy_styw = p.reg(sy, 0)
    r_s_12 = p.reg(sc, 0); r_v_12 = p.reg(v, 0); r_g_12 = p.reg(g, 0)
    r_t_s1 = p.reg(t, 0); r_t_s2 = p.reg(t, 0)
    r_g_m3 = p.reg(g, 0)
    r_v_hh = p.reg(v, 0); r_v_m3 = p.reg(v, 0)
    r_t_cph = p.reg(t, 0); r_t_evy = p.reg(t, 0)
    r_v_yr = p.reg(v, 0); r_v_3 = p.reg(v, 0)
    r_sy_ev = p.reg(sy, 0); r_sy_sty = p.reg(sy, 0)
    yr = dram["yacc"].rearrange("(n p) d -> n p d", p=128)
    with nc.Fori(0, E) as e:
        nc.all_engine_barrier()
        nl = 0
        for dc in range(8):
            sy.dma_start(out=w1_sb[:, ds_(dc, 2048)],
                         in_=w["l1f_w1T_bf"][e, bass.ds(dc * 128, 128), :]).then_inc(lde, 16)
            sy.dma_start(out=w2_sb[:, ds_(dc, 2048)],
                         in_=w["l1f_w2T_bf"][e, bass.ds(dc * 128, 128), :]).then_inc(lde, 16)
            nl += 32
        for hc in range(16):
            sy.dma_start(out=w3_sb[:, ds_(hc, 1024)],
                         in_=w["l1f_w3T_bf"][e, bass.ds(hc * 128, 128), :]).then_inc(lde, 16)
            nl += 16
        sy.dma_start(out=b1_sb[:], in_=w["l1f_b1"][e]).then_inc(lde, 16)
        sy.dma_start(out=b2_sb[:], in_=w["l1f_b2"][e]).then_inc(lde, 16)
        sy.dma_start(out=b3_row[:], in_=w["l1f_b3_bf"][bass.ds(e, 1), :]).then_inc(lde, 16)
        nl += 48
        sy.reg_add(r_sy_lde, r_sy_lde, nl)
        sy.wait_ge(lde, r_sy_lde)
        nc.all_engine_barrier()
        with nc.Fori(0, T // 512) as st:
            stp = st % 2
            sy.wait_ge(mm12, r_sy_w)
            sy.reg_add(r_sy_w, r_sy_w, 32)
            sy.dma_start(out=zst[:, stp, :].rearrange("p (c q) -> p c q", c=8),
                         in_=zr.rearrange("(c p) t -> p c t", p=128)[:, :, bass.ds(st * 512, 512)]).then_inc(ldg[0], 16)
            sy.reg_add(r_ldg[0], r_ldg[0], 16)
            sy.wait_ge(ldg[0], r_ldg[0])
            sy.sem_inc(grdy, 1)
            t.reg_add(r_t_gr, r_t_gr, 1)
            t.wait_ge(grdy, r_t_gr)
            for fc in range(16):
                fp = fc % 2
                t.wait_ge(s1d, r_t_s1)
                t.reg_add(r_t_s1, r_t_s1, 1)
                for dc in range(8):
                    ins = t.matmul(ps1[:, fp, :], w1_sb[:, dc * 2048 + fc * 128: dc * 2048 + (fc + 1) * 128],
                                   zst[:, stp, bass.ds(dc * 512, 512)],
                                   start=(dc == 0), stop=(dc == 7))
                ins.then_inc(mm12, 1)
                t.wait_ge(s2d, r_t_s2)
                t.reg_add(r_t_s2, r_t_s2, 1)
                for dc in range(8):
                    ins = t.matmul(ps2[:, fp, :], w2_sb[:, dc * 2048 + fc * 128: dc * 2048 + (fc + 1) * 128],
                                   zst[:, stp, bass.ds(dc * 512, 512)],
                                   start=(dc == 0), stop=(dc == 7))
                ins.then_inc(mm12, 1)
                sc.reg_add(r_s_12, r_s_12, 2)
                sc.wait_ge(mm12, r_s_12)
                sc.wait_ge(hhd, r_s_hh)
                sc.reg_add(r_s_hh, r_s_hh, 1)
                sc.activation(out=s1_bf[:, fp, :], in_=ps1[:, fp, :], func=AF.Silu,
                              bias=b1_sb[:, fc:fc + 1]).then_inc(s1d, 1)
                v.reg_add(r_v_12, r_v_12, 2)
                v.wait_ge(mm12, r_v_12)
                v.wait_ge(hhd, r_v_hh2)
                v.reg_add(r_v_hh2, r_v_hh2, 1)
                v.tensor_scalar(s2_bf[:, fp, :], ps2[:, fp, :], b2_sb[:, fc:fc + 1],
                                None, ALU.add).then_inc(s2d, 1)
                g.reg_add(r_g_12, r_g_12, 1)
                g.wait_ge(s1d, r_g_12 + 1 if False else r_g_12)
                g.wait_ge(s2d, r_g_12)
                if fc == 0:
                    g.wait_ge(mm3, r_g_m3)
                    g.reg_add(r_g_m3, r_g_m3, 4)
                g.tensor_tensor(hh_sb[:, ds_(fc, 512)], s1_bf[:, fp, :], s2_bf[:, fp, :],
                                ALU.mult).then_inc(hhd, 1)
            for tokt in range(4):
                if tokt == 0:
                    v.reg_add(r_v_hh, r_v_hh, 16)
                v.wait_ge(hhd, r_v_hh)
                v.wait_ge(mm3, r_v_m3)
                v.reg_add(r_v_m3, r_v_m3, 1)
                v.tensor_copy(hstage[:],
                              hh_sb[:].rearrange("p (c q) -> p c q", c=16)[:, :, bass.ds(tokt * 128, 128)]).then_inc(cph, 1)
                t.reg_add(r_t_cph, r_t_cph, 1)
                t.wait_ge(cph, r_t_cph)
                t.wait_ge(evy, r_t_evy)
                t.reg_add(r_t_evy, r_t_evy, 1)
                for half in (0, 1):
                    t.matmul(ps3[:, half, :], consts["ones_row"],
                             b3_row[0:1, ts(half, 512)], start=True, stop=False)
                    for hc in range(16):
                        ins = t.matmul(ps3[:, half, :], hstage[:, ts(hc, 128)],
                                       w3_sb[:, hc * 1024 + half * 512: hc * 1024 + (half + 1) * 512],
                                       start=False, stop=(hc == 15))
                ins.then_inc(mm3, 1)
                sy.wait_ge(evy, r_sy_ev)
                sy.reg_add(r_sy_ev, r_sy_ev, 1)
                sy.wait_ge(sty, r_sy_styw)
                sy.reg_add(r_sy_styw, r_sy_styw, 16)
                sy.dma_start(out=yt[:, tokt % 2, :], in_=yr[st * 4 + tokt]).then_inc(ldg[1], 16)
                sy.reg_add(r_ldg[1], r_ldg[1], 16)
                sy.wait_ge(ldg[1], r_ldg[1])
                sy.sem_inc(yrdy, 1)
                v.reg_add(r_v_yr, r_v_yr, 1)
                v.wait_ge(yrdy, r_v_yr)
                v.reg_add(r_v_3, r_v_3, 1)
                v.wait_ge(mm3, r_v_3)
                v.scalar_tensor_tensor(yt[:, tokt % 2, 0:512], ps3[:, 0, :],
                                       comb_sb[:, bass.ds((st * 4 + tokt) * 8 + e, 1)],
                                       yt[:, tokt % 2, 0:512], ALU.mult, ALU.add)
                v.scalar_tensor_tensor(yt[:, tokt % 2, 512:1024], ps3[:, 1, :],
                                       comb_sb[:, bass.ds((st * 4 + tokt) * 8 + e, 1)],
                                       yt[:, tokt % 2, 512:1024], ALU.mult, ALU.add).then_inc(evy, 1)
                sy.reg_add(r_sy_sty, r_sy_sty, 1)
                sy.wait_ge(evy, r_sy_sty)
                sy.dma_start(out=yr[st * 4 + tokt], in_=yt[:, tokt % 2, :]).then_inc(sty, 16)
                sy.reg_add(r_sty_c, r_sty_c, 16)
        sy.wait_ge(sty, r_sty_c)
        nc.all_engine_barrier()
    p.barrier()

    ldf = [p.sem(), p.sem()]
    frdy = p.sem(); evf = p.sem(); stf = p.sem(); sff = p.sem()
    r_ldf = [p.reg(sy, 0), p.reg(sy, 0)]
    r_sy_e = p.reg(sy, 0); r_sy_sf = p.reg(sy, 0)
    r_v_fr = p.reg(v, 0); r_v_sf = p.reg(v, 0)
    xi_r = dram["x_in"].rearrange("(n p) d -> n p d", p=128)
    xo_r = dram["x_out"].rearrange("(n p) d -> n p d", p=128)
    sy.sem_inc(sff, 2)
    with nc.Fori(0, NT // 2) as th:
        for par in (0, 1):
            tl = th * 2 + par
            sy.wait_ge(evf, r_sy_e)
            sy.dma_start(out=xt[:, par, :], in_=xi_r[tl]).then_inc(ldf[par], 16)
            sy.dma_start(out=yt[:, par, :], in_=yr[tl]).then_inc(ldf[par], 16)
            sy.reg_add(r_ldf[par], r_ldf[par], 32)
            sy.wait_ge(ldf[par], r_ldf[par])
            sy.sem_inc(frdy, 1)
            v.reg_add(r_v_fr, r_v_fr, 1)
            v.wait_ge(frdy, r_v_fr)
            v.wait_ge(sff, r_v_sf)
            v.reg_add(r_v_sf, r_v_sf, 1)
            v.scalar_tensor_tensor(xo[:, par, :], yt[:, par, :], 0.5, xt[:, par, :],
                                   ALU.mult, ALU.add).then_inc(evf, 1)
            sy.reg_add(r_sy_e, r_sy_e, 1)
            sy.wait_ge(evf, r_sy_e)
            sy.dma_start(out=xo_r[tl], in_=xo[:, par, :]).then_inc(stf, 16)
            sy.reg_add(r_sy_sf, r_sy_sf, 16)
            sy.wait_ge(stf, r_sy_sf)
            sy.sem_inc(sff, 1)
    p.barrier()
    p.free_scope(mk)
    p.barrier()


def emit_l1ffn(p, dram, w, consts):
    import os
    ph = int(os.environ.get("L1F_PHASES", "9"))
    emit_norm_pass(p, dram["x_in"], dram["z0"], None, None, "ln")
    emit_norm_pass(p, dram["z0"], None, None, dram["zlnT"], "rms",
                   gamma_rep=w["l1f_n1"], ident_bf=consts["ident_bf"], eps=RMSEPS)
    if ph < 2:
        return
    emit_l1ffn_qkv(p, dram, w, consts)
    if ph < 3:
        return
    emit_l1ffn_attn(p, dram, w, consts)
    if ph < 4:
        return
    emit_norm_pass(p, dram["z1"], None, None, dram["zlnT"], "rms",
                   gamma_rep=w["l1f_n2"], ident_bf=consts["ident_bf"], eps=RMSEPS)
    if ph < 5:
        return
    emit_l1ffn_moe(p, dram, w, consts)



def emit_l0ffn(p, dram, w, consts, nocc=False):
    nc = p.nc
    v, sc, g, t, sy = nc.vector, nc.scalar, nc.gpsimd, nc.tensor, nc.sync
    mk = p.scoped()
    gstage = p.sbuf([128, 2, 1024], BF16, "f0_gstg")
    rw_sb = p.sbuf([128, 8 * 8], BF16, "f0_rw")
    gsc = p.sbuf([128, 64], F32, "f0_gsc")
    Sall = p.sbuf([128, NT * 8], BF16, "f0_S")
    prefx = p.sbuf([128, NT * 8], F32, "f0_pref")
    oh1s = p.sbuf([128, NT * 8], F32, "f0_oh1")
    oh2s = p.sbuf([128, NT * 8], F32, "f0_oh2")
    g12s = p.sbuf([128, NT * 2], F32, "f0_g12")
    comb_sb = p.sbuf([128, NT * 8], F32, "f0_comb")
    totrow = p.sbuf([1, NT * 8], F32, "f0_tot")
    tri_bf = p.sbuf([128, 128], BF16, "f0_tri")
    onescol = p.sbuf([128, 1], BF16, "f0_onesc")
    onesrow_f = p.sbuf([1, 128], F32, "f0_onesrf")
    ident8 = p.sbuf([8, 8], F32, "f0_id8")
    onestile = p.sbuf([8, NT], F32, "f0_onet")
    csum = p.sbuf([8, NT + 2], F32, "f0_csum")
    tot8 = p.sbuf([8, NT], F32, "f0_tot8")
    corehot = p.sbuf([1, 64], F32, "f0_chot")
    coremask = p.sbuf([64, 1], F32, "f0_cmask")
    allc = p.sbuf([64, 8], F32, "f0_allc")
    coff = p.sbuf([1, 8], F32, "f0_coff")
    offs_st = p.sbuf([64, 8], F32, "f0_offs")
    totr8 = p.sbuf([1, 8], F32, "f0_totr8")
    arbuf = p.sbuf([64, 1024], F32, "f0_ar")
    offrow = p.sbuf([1, 8], F32, "f0_offrow")
    s0 = p.sem()
    n0 = 0
    g.dma_start(out=tri_bf[:], in_=w["tri"]).then_inc(s0, 16)
    g.dma_start(out=ident8[:], in_=w["ident"][0:8, 0:8]).then_inc(s0, 16)
    g.dma_start(out=corehot[:], in_=w["core_onehot"]).then_inc(s0, 16)
    g.dma_start(out=coremask[:], in_=w["core_mask"]).then_inc(s0, 16)
    g.dma_start(out=rw_sb[:].rearrange("p (c e) -> p c e", c=8),
                in_=w["l0f_router_bf"].rearrange("(c p) e -> p c e", p=128)).then_inc(s0, 16)
    n0 = 5 * 16
    g.wait_ge(s0, n0)
    g.memset(onescol[:], 1.0)
    g.memset(onesrow_f[:], 1.0)
    g.memset(onestile[:], 1.0)
    g.memset(csum[:, 0:1], 0.0)
    g.memset(arbuf[:], 0.0)
    p.barrier()
    mk_route = p.scoped()
    ps_g = p.psum([128, 8], F32, "f0_psg")
    ps_pf = p.psum([128, 8], F32, "f0_pspf")
    ps_t = p.psum([1, 8], F32, "f0_pst")
    ps_o8 = p.psum([64, 8], F32, "f0_pso8")
    ps_off = p.psum([128, 8], F32, "f0_psoff")

    zr = dram["zlnT"]
    LG, OH1, LGM, OH2, WS = 0, 8, 16, 24, 32
    M1, M2, DM, G2, G1 = 40, 41, 42, 43, 44

    ldg = [p.sem(), p.sem()]
    grdy = p.sem(); mmg = p.sem(); sv = p.sem(); sg = p.sem(); ss = p.sem()
    mmp = p.sem()
    r_ldg = [p.reg(sy, 0), p.reg(sy, 0)]
    r_sy_g = p.reg(sy, 0)
    r_t_gr = p.reg(t, 0); r_t_sv = p.reg(t, 0); r_t_sv2 = p.reg(t, 0)
    r_v_mmg = p.reg(v, 0); r_vs = p.reg(v, 1); r_v_ss = p.reg(v, 0); r_v_mmp = p.reg(v, 0)
    r_s_sv = p.reg(sc, 7)
    with nc.Fori(0, NT // 2) as th:
        for par in (0, 1):
            tl = th * 2 + par
            sy.wait_ge(mmg, r_sy_g)
            sy.reg_add(r_sy_g, r_sy_g, 2)
            sy.dma_start(out=gstage[:, par, :].rearrange("p (c q) -> p c q", c=8),
                         in_=zr.rearrange("(c p) t -> p c t", p=128)[:, :, bass.ds(tl * 128, 128)]).then_inc(ldg[par], 16)
            sy.reg_add(r_ldg[par], r_ldg[par], 16)
            sy.wait_ge(ldg[par], r_ldg[par])
            sy.sem_inc(grdy, 1)
            t.reg_add(r_t_gr, r_t_gr, 1)
            t.wait_ge(grdy, r_t_gr)
            t.wait_ge(sv, r_t_sv)
            t.reg_add(r_t_sv, r_t_sv, 12)
            for dc in range(8):
                ins = t.matmul(ps_g[:], gstage[:, par, ts(dc, 128)], rw_sb[:, ds_(dc, 8)],
                               start=(dc == 0), stop=(dc == 7))
            ins.then_inc(mmg, 1)
            v.reg_add(r_v_mmg, r_v_mmg, 1); v.wait_ge(mmg, r_v_mmg)
            v.tensor_copy(gsc[:, LG:LG + 8], ps_g[:]).then_inc(sv, 1)
            v.wait_ge(sv, r_vs)
            v.tensor_reduce(gsc[:, M1:M1 + 1], gsc[:, LG:LG + 8], AX.X, ALU.max).then_inc(sv, 1)
            v.reg_add(r_vs, r_vs, 1); v.wait_ge(sv, r_vs)
            v.tensor_scalar(gsc[:, OH1:OH1 + 8], gsc[:, LG:LG + 8], gsc[:, M1:M1 + 1],
                            None, ALU.is_equal).then_inc(sv, 1)
            v.reg_add(r_vs, r_vs, 1); v.wait_ge(sv, r_vs)
            v.scalar_tensor_tensor(gsc[:, LGM:LGM + 8], gsc[:, OH1:OH1 + 8], -1e30,
                                   gsc[:, LG:LG + 8], ALU.mult, ALU.add).then_inc(sv, 1)
            v.reg_add(r_vs, r_vs, 1); v.wait_ge(sv, r_vs)
            v.tensor_reduce(gsc[:, M2:M2 + 1], gsc[:, LGM:LGM + 8], AX.X, ALU.max).then_inc(sv, 1)
            v.reg_add(r_vs, r_vs, 1); v.wait_ge(sv, r_vs)
            v.tensor_scalar(gsc[:, OH2:OH2 + 8], gsc[:, LGM:LGM + 8], gsc[:, M2:M2 + 1],
                            None, ALU.is_equal).then_inc(sv, 1)
            v.reg_add(r_vs, r_vs, 1); v.wait_ge(sv, r_vs)
            v.tensor_scalar(gsc[:, DM:DM + 1], gsc[:, M2:M2 + 1], gsc[:, M1:M1 + 1],
                            None, ALU.subtract).then_inc(sv, 1)
            sc.wait_ge(sv, r_s_sv)
            sc.reg_add(r_s_sv, r_s_sv, 12)
            sc.activation(out=gsc[:, G2:G2 + 1], in_=gsc[:, DM:DM + 1],
                          func=AF.Sigmoid).then_inc(ss, 1)
            v.reg_add(r_v_ss, r_v_ss, 1); v.wait_ge(ss, r_v_ss)
            v.tensor_scalar(gsc[:, G1:G1 + 1], gsc[:, G2:G2 + 1], -1.0, 1.0,
                            ALU.mult, ALU.add).then_inc(sv, 1)
            v.reg_add(r_vs, r_vs, 2); v.wait_ge(sv, r_vs)
            v.tensor_copy(oh1s[:, bass.ds(tl * 8, 8)], gsc[:, OH1:OH1 + 8]).then_inc(sv, 1)
            v.reg_add(r_vs, r_vs, 1); v.wait_ge(sv, r_vs)
            v.tensor_copy(oh2s[:, bass.ds(tl * 8, 8)], gsc[:, OH2:OH2 + 8]).then_inc(sv, 1)
            v.reg_add(r_vs, r_vs, 1); v.wait_ge(sv, r_vs)
            v.tensor_copy(g12s[:, bass.ds(tl * 2, 2)], gsc[:, G2:G2 + 2]).then_inc(sv, 1)
            v.reg_add(r_vs, r_vs, 1); v.wait_ge(sv, r_vs)
            v.tensor_tensor(Sall[:, bass.ds(tl * 8, 8)], gsc[:, OH1:OH1 + 8],
                            gsc[:, OH2:OH2 + 8], ALU.add).then_inc(sv, 1)
            v.reg_add(r_vs, r_vs, 2)
            t.wait_ge(sv, r_t_sv)
            t.wait_ge(mmp, r_t_sv2)
            t.reg_add(r_t_sv2, r_t_sv2, 2)
            t.matmul(ps_pf[:], tri_bf[:], Sall[:, bass.ds(tl * 8, 8)],
                     start=True, stop=True)
            t.matmul(ps_t[:], onescol[:], Sall[:, bass.ds(tl * 8, 8)],
                     start=True, stop=True).then_inc(mmg, 1)
            v.reg_add(r_v_mmg, r_v_mmg, 1)
            v.wait_ge(mmg, r_v_mmg)
            v.tensor_copy(prefx[:, bass.ds(tl * 8, 8)], ps_pf[:]).then_inc(mmp, 1)
            v.tensor_copy(totrow[0:1, bass.ds(tl * 8, 8)], ps_t[0:1, :]).then_inc(mmp, 1)
    p.barrier()

    so = p.sem()
    sy.dma_start(out=dram["totdram"], in_=totrow[0:1, :]).then_inc(so, 16)
    sy.wait_ge(so, 16)
    sy.dma_start(out=tot8[:], in_=dram["totdram"].rearrange("t e -> e t"),
                 allow_slow_non_contiguous=True).then_inc(so, 16)
    sy.wait_ge(so, 32)
    p.barrier()
    v.tensor_tensor_scan(csum[:, 1:1 + NT], onestile[:], tot8[:], 0.0,
                         ALU.mult, ALU.add)
    p.barrier()
    t.matmul(ps_t[:], csum[0:8, NT:NT + 1], ident8[:], start=True, stop=True)
    p.barrier()
    v.tensor_copy(totr8[:], ps_t[0:1, :])
    p.barrier()
    t.matmul(ps_o8[:], corehot[:], totr8[:], start=True, stop=True)
    p.barrier()
    v.tensor_copy(arbuf[:, 0:8], ps_o8[:])
    p.barrier()
    sy.dma_start(out=dram["cc_in"], in_=arbuf[:]).then_inc(so, 16)
    sy.wait_ge(so, 48)
    if nocc:
        sy.dma_start(out=dram["cc_out"], in_=dram["cc_in"]).then_inc(so, 16)
        sy.wait_ge(so, 64)
    else:
        ccs = p.sem()
        g.collective_compute("AllReduce", ALU.add, replica_groups=[list(range(NCORES))],
                             ins=[dram["cc_in"]], outs=[dram["cc_out"]]).then_inc(ccs)
        g.wait_ge(ccs, 1)
    p.barrier()
    sy.dma_start(out=allc[:], in_=dram["cc_out"][:, 0:8]).then_inc(so, 16)
    sy.wait_ge(so, 80 if nocc else 64)
    p.barrier()
    t.matmul(ps_t[:], coremask[:], allc[:], start=True, stop=True)
    p.barrier()
    v.tensor_copy(coff[:], ps_t[0:1, :])
    p.barrier()
    t.matmul(ps_o8[0:NT, :], csum[0:8, 0:NT], consts["ident_f32"][0:8, 0:8],
             start=True, stop=False)
    t.matmul(ps_o8[0:NT, :], onesrow_f[0:1, 0:NT], coff[:], start=False, stop=True)
    p.barrier()
    v.tensor_copy(offs_st[0:NT, :], ps_o8[0:NT, :])
    p.barrier()
    sy.dma_start(out=dram["offs_dram"], in_=offs_st[0:NT, :]).then_inc(so, 16)
    sy.wait_ge(so, 96 if nocc else 80)
    p.barrier()

    ldo = p.sem(); ordy = p.sem(); mmo = p.sem(); fv = p.sem()
    r_ldo = p.reg(sy, 0); r_sy_fv = p.reg(sy, 0)
    r_t_or = p.reg(t, 0); r_t_fv = p.reg(t, 0)
    r_v_mmo = p.reg(v, 0); r_fv = p.reg(v, 1)
    FK = 11
    with nc.Fori(0, NT) as tl:
        sy.wait_ge(mmo, r_sy_fv)
        sy.reg_add(r_sy_fv, r_sy_fv, 1)
        sy.dma_start(out=offrow[:], in_=dram["offs_dram"][bass.ds(tl, 1), :]).then_inc(ldo, 16)
        sy.reg_add(r_ldo, r_ldo, 16)
        sy.wait_ge(ldo, r_ldo)
        sy.sem_inc(ordy, 1)
        t.reg_add(r_t_or, r_t_or, 1)
        t.wait_ge(ordy, r_t_or)
        t.wait_ge(fv, r_t_fv)
        t.reg_add(r_t_fv, r_t_fv, FK)
        t.matmul(ps_off[:], onesrow_f[:], offrow[:], start=True, stop=True).then_inc(mmo, 1)
        v.reg_add(r_v_mmo, r_v_mmo, 1)
        v.wait_ge(mmo, r_v_mmo)
        v.tensor_tensor(gsc[:, 0:8], prefx[:, bass.ds(tl * 8, 8)], ps_off[:],
                        ALU.add).then_inc(fv, 1)
        v.wait_ge(fv, r_fv)
        v.tensor_tensor(gsc[:, 8:16], gsc[:, 0:8], oh1s[:, bass.ds(tl * 8, 8)],
                        ALU.mult).then_inc(fv, 1)
        v.reg_add(r_fv, r_fv, 1); v.wait_ge(fv, r_fv)
        v.tensor_reduce(gsc[:, 40:41], gsc[:, 8:16], AX.X, ALU.add).then_inc(fv, 1)
        v.reg_add(r_fv, r_fv, 1); v.wait_ge(fv, r_fv)
        v.tensor_tensor(gsc[:, 16:24], gsc[:, 0:8], oh2s[:, bass.ds(tl * 8, 8)],
                        ALU.mult).then_inc(fv, 1)
        v.reg_add(r_fv, r_fv, 1); v.wait_ge(fv, r_fv)
        v.tensor_reduce(gsc[:, 41:42], gsc[:, 16:24], AX.X, ALU.add).then_inc(fv, 1)
        v.reg_add(r_fv, r_fv, 1); v.wait_ge(fv, r_fv)
        v.tensor_scalar(gsc[:, 42:43], gsc[:, 40:41], float(CAP), None, ALU.is_lt).then_inc(fv, 1)
        v.reg_add(r_fv, r_fv, 1); v.wait_ge(fv, r_fv)
        v.tensor_scalar(gsc[:, 43:44], gsc[:, 41:42], float(CAP), None, ALU.is_lt).then_inc(fv, 1)
        v.reg_add(r_fv, r_fv, 1); v.wait_ge(fv, r_fv)
        v.tensor_tensor(gsc[:, 44:45], g12s[:, bass.ds(tl * 2 + 1, 1)], gsc[:, 42:43],
                        ALU.mult).then_inc(fv, 1)
        v.reg_add(r_fv, r_fv, 1); v.wait_ge(fv, r_fv)
        v.tensor_tensor(gsc[:, 45:46], g12s[:, bass.ds(tl * 2, 1)], gsc[:, 43:44],
                        ALU.mult).then_inc(fv, 1)
        v.reg_add(r_fv, r_fv, 1); v.wait_ge(fv, r_fv)
        v.tensor_scalar(gsc[:, 24:32], oh2s[:, bass.ds(tl * 8, 8)], gsc[:, 45:46],
                        None, ALU.mult).then_inc(fv, 1)
        v.reg_add(r_fv, r_fv, 1); v.wait_ge(fv, r_fv)
        v.scalar_tensor_tensor(comb_sb[:, bass.ds(tl * 8, 8)], oh1s[:, bass.ds(tl * 8, 8)],
                               gsc[:, 44:45], gsc[:, 24:32], ALU.mult, ALU.add).then_inc(fv, 1)
        v.reg_add(r_fv, r_fv, 2)
    p.barrier()

    p.free_scope(mk_route)
    yt0 = p.sbuf([128, 2, 1024], F32, "f0_yt")
    xt0 = p.sbuf([128, 2, 1024], F32, "f0_xt")
    xo0 = p.sbuf([128, 2, 1024], F32, "f0_xo")
    zcp = p.sem(); r_zcp = p.reg(sy, 0)
    g.memset(yt0[:, 0, :], 0.0)
    p.barrier()
    yr = dram["yacc"].rearrange("(n p) d -> n p d", p=128)
    with nc.Fori(0, NT) as tl:
        sy.dma_start(out=yr[tl], in_=yt0[:, 0, :]).then_inc(zcp, 16)
        sy.reg_add(r_zcp, r_zcp, 16)
    sy.wait_ge(zcp, r_zcp)
    p.barrier()

    w1_sb = p.sbuf([128, 8 * 2048], BF16, "f0_w1")
    w2_sb = p.sbuf([128, 16 * 1024], BF16, "f0_w2")
    b1_sb = p.sbuf([128, 16], F32, "f0_b1")
    b2_row = p.sbuf([1, 1024], BF16, "f0_b2")
    zst = p.sbuf([128, 2, 8 * 512], BF16, "f0_zst")
    hh_sb = p.sbuf([128, 16 * 512], BF16, "f0_hh")
    hstage = p.sbuf([128, 16 * 128], BF16, "f0_hstg")
    ps1 = p.psum([128, 2, 512], F32, "f0_ps1")
    ps3 = p.psum([128, 2, 512], F32, "f0_ps3")
    for hh in range(2):
        lde = p.sem(); wrdy2 = p.sem(); mm12 = p.sem(); s1d = p.sem()
        hhd = p.sem(); cph = p.sem(); mm3 = p.sem(); evy = p.sem(); yrdy = p.sem()
        sty = p.sem(); zrd = p.sem(); ldz2 = p.sem(); ldy2 = p.sem()
        r_sy_lde = p.reg(sy, 0); r_sy_w = p.reg(sy, 0); r_ldz2 = p.reg(sy, 0)
        r_t_zr = p.reg(t, 0); r_s_12 = p.reg(sc, 0); r_s_hh = p.reg(sc, 0)
        r_t_s1 = p.reg(t, 0); r_g_m3 = p.reg(g, 0)
        r_v_hh = p.reg(v, 0); r_v_m3 = p.reg(v, 0)
        r_t_cph = p.reg(t, 0); r_t_evy = p.reg(t, 0)
        r_v_yr = p.reg(v, 0); r_v_3 = p.reg(v, 0)
        r_sy_ev = p.reg(sy, 0); r_sy_styw = p.reg(sy, 0); r_ldy2 = p.reg(sy, 0)
        r_sty_c = p.reg(sy, 0); r_styd = p.reg(sy, 0)
        with nc.Fori(0, E) as e:
            nc.all_engine_barrier()
            nl = 0
            for dc in range(8):
                sy.dma_start(out=w1_sb[:, ds_(dc, 2048)],
                             in_=w["l0f_w1_bf"][e, bass.ds(dc * 128, 128),
                                                bass.ds(hh * 2048, 2048)]).then_inc(lde, 16)
                nl += 16
            for hc in range(16):
                sy.dma_start(out=w2_sb[:, ds_(hc, 1024)],
                             in_=w["l0f_w2_bf"][e, bass.ds(hh * 2048 + hc * 128, 128), :]).then_inc(lde, 16)
                nl += 16
            sy.dma_start(out=b1_sb[:], in_=w["l0f_b1"][e][:, bass.ds(hh * 16, 16)]).then_inc(lde, 16)
            sy.dma_start(out=b2_row[:], in_=w["l0f_b2_bf"][bass.ds(e, 1), :]).then_inc(lde, 16)
            nl += 32
            sy.reg_add(r_sy_lde, r_sy_lde, nl)
            sy.wait_ge(lde, r_sy_lde)
            nc.all_engine_barrier()
            with nc.Fori(0, T // 512) as st:
                stp = st % 2
                sy.wait_ge(mm12, r_sy_w)
                sy.reg_add(r_sy_w, r_sy_w, 16)
                sy.dma_start(out=zst[:, stp, :].rearrange("p (c q) -> p c q", c=8),
                             in_=zr.rearrange("(c p) t -> p c t", p=128)[:, :, bass.ds(st * 512, 512)]).then_inc(ldz2, 16)
                sy.reg_add(r_ldz2, r_ldz2, 16)
                sy.wait_ge(ldz2, r_ldz2)
                sy.sem_inc(zrd, 1)
                t.reg_add(r_t_zr, r_t_zr, 1)
                t.wait_ge(zrd, r_t_zr)
                for fc in range(16):
                    fp = fc % 2
                    t.wait_ge(s1d, r_t_s1)
                    t.reg_add(r_t_s1, r_t_s1, 1)
                    for dc in range(8):
                        ins = t.matmul(ps1[:, fp, :],
                                       w1_sb[:, dc * 2048 + fc * 128: dc * 2048 + (fc + 1) * 128],
                                       zst[:, stp, bass.ds(dc * 512, 512)],
                                       start=(dc == 0), stop=(dc == 7))
                    ins.then_inc(mm12, 1)
                    sc.reg_add(r_s_12, r_s_12, 1)
                    sc.wait_ge(mm12, r_s_12)
                    if fc == 0:
                        sc.wait_ge(cph, r_s_hh)
                        sc.reg_add(r_s_hh, r_s_hh, 4)
                    sc.activation(out=hh_sb[:, ds_(fc, 512)], in_=ps1[:, fp, :], func=AF.Relu,
                                  bias=b1_sb[:, fc:fc + 1]).then_inc(s1d, 1)
                for tokt in range(4):
                    if tokt == 0:
                        v.reg_add(r_v_hh, r_v_hh, 16)
                    v.wait_ge(s1d, r_v_hh)
                    v.wait_ge(mm3, r_v_m3)
                    v.reg_add(r_v_m3, r_v_m3, 1)
                    v.tensor_copy(hstage[:],
                                  hh_sb[:].rearrange("p (c q) -> p c q", c=16)[:, :, bass.ds(tokt * 128, 128)]).then_inc(cph, 1)
                    t.reg_add(r_t_cph, r_t_cph, 1)
                    t.wait_ge(cph, r_t_cph)
                    t.wait_ge(evy, r_t_evy)
                    t.reg_add(r_t_evy, r_t_evy, 1)
                    for half in (0, 1):
                        if hh == 0:
                            t.matmul(ps3[:, half, :], consts["ones_row"],
                                     b2_row[0:1, ts(half, 512)], start=True, stop=False)
                        for hc in range(16):
                            ins = t.matmul(ps3[:, half, :], hstage[:, ts(hc, 128)],
                                           w2_sb[:, hc * 1024 + half * 512: hc * 1024 + (half + 1) * 512],
                                           start=(hh != 0 and hc == 0), stop=(hc == 15))
                    ins.then_inc(mm3, 1)
                    sy.wait_ge(evy, r_sy_ev)
                    sy.reg_add(r_sy_ev, r_sy_ev, 1)
                    sy.wait_ge(sty, r_sy_styw)
                    sy.reg_add(r_sy_styw, r_sy_styw, 16)
                    sy.dma_start(out=yt0[:, tokt % 2, :], in_=yr[st * 4 + tokt]).then_inc(ldy2, 16)
                    sy.reg_add(r_ldy2, r_ldy2, 16)
                    sy.wait_ge(ldy2, r_ldy2)
                    sy.sem_inc(yrdy, 1)
                    v.reg_add(r_v_yr, r_v_yr, 1)
                    v.wait_ge(yrdy, r_v_yr)
                    v.reg_add(r_v_3, r_v_3, 1)
                    v.wait_ge(mm3, r_v_3)
                    v.scalar_tensor_tensor(yt0[:, tokt % 2, 0:512], ps3[:, 0, :],
                                           comb_sb[:, bass.ds((st * 4 + tokt) * 8 + e, 1)],
                                           yt0[:, tokt % 2, 0:512], ALU.mult, ALU.add)
                    v.scalar_tensor_tensor(yt0[:, tokt % 2, 512:1024], ps3[:, 1, :],
                                           comb_sb[:, bass.ds((st * 4 + tokt) * 8 + e, 1)],
                                           yt0[:, tokt % 2, 512:1024], ALU.mult, ALU.add).then_inc(evy, 1)
                    sy.reg_add(r_sty_c, r_sty_c, 1)
                    sy.wait_ge(evy, r_sty_c)
                    sy.dma_start(out=yr[st * 4 + tokt], in_=yt0[:, tokt % 2, :]).then_inc(sty, 16)
                    sy.reg_add(r_styd, r_styd, 16)
            sy.wait_ge(sty, r_styd)
            nc.all_engine_barrier()
        p.barrier()
    p.barrier()

    ldf = [p.sem(), p.sem()]
    frdy = p.sem(); evf = p.sem(); stf = p.sem(); sff = p.sem()
    r_ldf = [p.reg(sy, 0), p.reg(sy, 0)]
    r_sy_e2 = p.reg(sy, 0); r_sy_sf = p.reg(sy, 0)
    r_v_fr = p.reg(v, 0); r_v_sf = p.reg(v, 0)
    xi_r = dram["x_in"].rearrange("(n p) d -> n p d", p=128)
    xo_r = dram["x_out"].rearrange("(n p) d -> n p d", p=128)
    sy.sem_inc(sff, 2)
    with nc.Fori(0, NT // 2) as th:
        for par in (0, 1):
            tl = th * 2 + par
            sy.wait_ge(evf, r_sy_e2)
            sy.dma_start(out=xt0[:, par, :], in_=xi_r[tl]).then_inc(ldf[par], 16)
            sy.dma_start(out=yt0[:, par, :], in_=yr[tl]).then_inc(ldf[par], 16)
            sy.reg_add(r_ldf[par], r_ldf[par], 32)
            sy.wait_ge(ldf[par], r_ldf[par])
            sy.sem_inc(frdy, 1)
            v.reg_add(r_v_fr, r_v_fr, 1)
            v.wait_ge(frdy, r_v_fr)
            v.wait_ge(sff, r_v_sf)
            v.reg_add(r_v_sf, r_v_sf, 1)
            v.scalar_tensor_tensor(xo0[:, par, :], yt0[:, par, :], 0.5, xt0[:, par, :],
                                   ALU.mult, ALU.add).then_inc(evf, 1)
            sy.reg_add(r_sy_e2, r_sy_e2, 1)
            sy.wait_ge(evf, r_sy_e2)
            sy.dma_start(out=xo_r[tl], in_=xo0[:, par, :]).then_inc(stf, 16)
            sy.reg_add(r_sy_sf, r_sy_sf, 16)
            sy.wait_ge(stf, r_sy_sf)
            sy.sem_inc(sff, 1)
    p.barrier()
    p.free_scope(mk)
    p.barrier()



def _declare_weight_drams(nc, g):
    w = {}
    for name, arr in g.items():
        w[name] = nc.dram_tensor(name, list(arr.shape), F32, kind="ExternalInput").ap()
    w["core_onehot"] = nc.dram_tensor("core_onehot", [1, 64], F32, kind="ExternalInput").ap()
    w["core_mask"] = nc.dram_tensor("core_mask", [64, 1], F32, kind="ExternalInput").ap()
    BFS = ["l0f_router", "l0f_w1", "l0f_w2", "l0f_b2",
           "l0a_inwT", "l0a_xpwT_pad", "l0a_outwT", "l0a_outb",
           "l1f_qkvwT", "l1f_qkvb", "l1f_outwT", "l1f_outb",
           "l1f_gatewT", "l1f_w1T", "l1f_w2T", "l1f_w3T", "l1f_b3",
           "l1a_qkvwT", "l1a_qkvb", "l1a_p1wT", "l1a_p2wT", "l1a_p2b"]
    ren = {"l1f_qkvb": "l1f_qkvb_row", "l1a_qkvb": "l1a_qkvb_row",
           "l1f_outb": "l1f_outb_row", "l1a_p2b": "l1a_p2b_bf"}
    for name in BFS:
        arr = g[name]
        tgt = ren.get(name, name + "_bf")
        w[tgt] = nc.dram_tensor(tgt, list(arr.shape), BF16, kind="Internal").ap()
    return w


_CAST_PAIRS = [("l0f_router", "l0f_router_bf"), ("l0f_b2", "l0f_b2_bf"),
               ("l0a_inwT", "l0a_inwT_bf"), ("l0a_xpwT_pad", "l0a_xpwT_pad_bf"),
               ("l0a_outwT", "l0a_outwT_bf"), ("l0a_outb", "l0a_outb_bf"),
               ("l1f_qkvwT", "l1f_qkvwT_bf"), ("l1f_qkvb", "l1f_qkvb_row"),
               ("l1f_outwT", "l1f_outwT_bf"), ("l1f_outb", "l1f_outb_row"),
               ("l1f_gatewT", "l1f_gatewT_bf"), ("l1f_b3", "l1f_b3_bf"),
               ("l1a_qkvwT", "l1a_qkvwT_bf"), ("l1a_p1wT", "l1a_p1wT_bf"),
               ("l1a_p2wT", "l1a_p2wT_bf"), ("l1a_p2b", "l1a_p2b_bf"),
               ("l1a_qkvb", "l1a_qkvb_row")]
_CAST_3D = ["l0f_w1", "l0f_w2", "l1f_w1T", "l1f_w2T", "l1f_w3T"]


def _emit_casts_and_consts(p, nc, w):
    pairs = [(w[a], w[b]) for a, b in _CAST_PAIRS]
    for name in _CAST_3D:
        for e in range(E):
            pairs.append((w[name][e], w[name + "_bf"][e]))
    emit_prologue_casts(p, pairs)
    consts = {}
    identb = p.sbuf([128, 128], BF16, "c_identb")
    identf = p.sbuf([128, 128], F32, "c_identf")
    onesr = p.sbuf([1, 128], BF16, "c_ones")
    ones64 = p.sbuf([64, 1], F32, "c_ones64")
    sld = p.sem()
    gg = nc.gpsimd
    gg.dma_start(out=identb[:], in_=w["ident"]).then_inc(sld, 16)
    gg.dma_start(out=identf[:], in_=w["ident"]).then_inc(sld, 16)
    gg.memset(onesr[:], 1.0)
    gg.memset(ones64[:], 1.0)
    gg.wait_ge(sld, 2 * 16)
    p.barrier()
    consts.update({"ident_bf": identb[:], "ident_f32": identf[:], "ones_row": onesr[:],
                   "ones64c": ones64[:]})
    return consts


def _declare_scratch(nc):
    d = {}
    for nm, shape, dt in [("xA", [T, D], F32), ("xB", [T, D], F32),
                          ("z0", [T, D], F32), ("z1", [T, D], F32),
                          ("yacc", [T, D], F32), ("zlnT", [D, T], BF16),
                          ("qT", [B_LOC, D, S], BF16), ("kT", [B_LOC, D, S], BF16),
                          ("v_tok", [B_LOC, S, D], BF16),
                          ("totdram", [NT, 8], F32), ("offs_dram", [NT, 8], F32)]:
        d[nm] = nc.dram_tensor(nm, shape, dt, kind="Internal").ap()
    d["cc_in"] = nc.dram_tensor("cc_in", [64, 1024], F32, kind="Internal").ap()
    d["cc_out"] = nc.dram_tensor("cc_out", [64, 1024], F32, kind="Internal").ap()
    return d


def _emit_sublayer(p, nc, which, x_in, x_out, dram, w, consts, nocc):
    dd = dict(dram)
    dd["x_in"] = x_in
    dd["x_out"] = x_out
    if which == "l0ffn":
        emit_norm_pass(p, x_in, None, None, dd["zlnT"], "ln", ident_bf=consts["ident_bf"])
        emit_l0ffn(p, dd, w, consts, nocc=nocc)
    elif which == "l0attn":
        emit_norm_pass(p, x_in, None, None, dd["zlnT"], "ln", ident_bf=consts["ident_bf"])
        emit_l0attn(p, dd, w)
    elif which == "l1ffn":
        emit_l1ffn(p, dd, w, consts)
    elif which == "l1attn":
        emit_norm_pass(p, x_in, dd["z0"], None, dd["zlnT"], "ln", ident_bf=consts["ident_bf"])
        emit_l1attn(p, dd, w)


def _build_program(sublayers, nocc=False):
    nc = bacc.Bacc("TRN2", target_bir_lowering=False)
    x_in = nc.dram_tensor("x_in", [T, D], F32, kind="ExternalInput").ap()
    x_out = nc.dram_tensor("x_out", [T, D], F32, kind="ExternalOutput").ap()
    z = np_inputs_placeholder = None
    g = _HOST_G
    w = _declare_weight_drams(nc, g)
    dram = _declare_scratch(nc)
    dram["x_in_ext"] = x_in
    dram["x_out_ext"] = x_out
    p = P(nc)
    consts = _emit_casts_and_consts(p, nc, w)
    name2ap = {"x_in": x_in, "x_out": x_out, "xA": dram["xA"], "xB": dram["xB"]}
    for which, a, b in sublayers:
        _emit_sublayer(p, nc, which, name2ap[a], name2ap[b], dram, w, consts, nocc)
    p.close()
    nc.compile()
    return nc


_HOST_G = None
_PROG_CACHE = {}


def kernel(**inputs):
    global _HOST_G
    g, per_core = _prep(inputs)
    _HOST_G = g
    seq = [("l0ffn", "x_in", "xA"), ("l0attn", "xA", "xB"), ("l0ffn", "xB", "xA"),
           ("l1ffn", "xA", "xB"), ("l1attn", "xB", "xA"), ("l1ffn", "xA", "x_out")]
    import os
    fused = os.environ.get("MK_UNFUSED", "0") != "1"
    outs = []
    if fused:
        key = "full"
        if key not in _PROG_CACHE:
            _PROG_CACHE[key] = _build_program(seq)
        nc = _PROG_CACHE[key]
        in_maps = []
        for j in range(NCORES):
            m = {k: v for k, v in g.items()}
            m.update(per_core[j])
            m["x_in"] = m.pop("x_in")
            in_maps.append(m)
        res = run_bass_kernel_spmd(nc, in_maps, core_ids=list(range(NCORES)))
        outs = [res.results[j]["x_out"] for j in range(NCORES)]
    else:
        cur = [per_core[j]["x_in"] for j in range(NCORES)]
        for which, a, b in seq:
            key = which
            if key not in _PROG_CACHE:
                _PROG_CACHE[key] = _build_program([(which, "x_in", "x_out")])
            nc = _PROG_CACHE[key]
            in_maps = []
            for j in range(NCORES):
                m = {k: v for k, v in g.items()}
                m.update(per_core[j])
                m["x_in"] = cur[j]
                in_maps.append(m)
            res = run_bass_kernel_spmd(nc, in_maps, core_ids=list(range(NCORES)))
            cur = [res.results[j]["x_out"] for j in range(NCORES)]
        outs = cur
    return _unshard(outs)


# version 05
# speedup vs baseline: 1.1220x; 1.1220x over previous
import numpy as np
import ml_dtypes
import concourse.bacc as bacc
import concourse.bass as bass
import concourse.mybir as mybir
from concourse.bass import ts, IndirectOffsetOnAxis
from concourse.bass_utils import run_bass_kernel_spmd
from concourse.expressions import smin

F32 = mybir.dt.float32
BF16 = mybir.dt.bfloat16
I32 = mybir.dt.int32
AF = mybir.ActivationFunctionType
ALU = mybir.AluOpType
AX = mybir.AxisListType

NCORES = 8
B_LOC = 4
S = 2048
D = 1024
T = B_LOC * S
NT = T // 128
H1 = 16
HD = 64
E = 8
CAP = 3072
LNEPS = 1e-5
RMSEPS = 1e-6
KVEPS = 1e-5

BF = np.dtype(ml_dtypes.bfloat16)



def _rope_tables():
    inv = 1.0 / 10000.0 ** (np.arange(0, HD, 2, dtype=np.float32) / HD)
    tpos = np.arange(S, dtype=np.float32)
    fr = np.outer(tpos, inv)
    emb = np.concatenate([fr, fr], axis=1)
    cos = np.cos(emb).T.astype(np.float32)
    sin = np.sin(emb).T.astype(np.float32)
    cos128 = np.concatenate([cos, cos], axis=0)
    sin128 = np.concatenate([sin, sin], axis=0)
    return np.ascontiguousarray(cos128), np.ascontiguousarray(sin128)


def _rot_mat128():
    P = np.zeros((64, 64), np.float32)
    for i in range(32):
        P[i, i + 32] = -1.0
        P[i + 32, i] = 1.0
    blk = P.T
    M = np.zeros((128, 128), np.float32)
    M[:64, :64] = blk
    M[64:, 64:] = blk
    return M


def _prep(inputs):
    g = {}
    f32 = np.float32

    def c(a):
        return np.ascontiguousarray(a, dtype=f32)

    g["ident"] = np.eye(128, dtype=f32)
    g["tri"] = np.triu(np.ones((128, 128), f32), 1)
    cos128, sin128 = _rope_tables()
    g["rope_cos"] = cos128
    g["rope_sin"] = sin128
    g["rope_cos_q"] = c(cos128 * (HD ** -0.5))
    g["rope_sin_q"] = c(sin128 * (HD ** -0.5))
    g["rotmat"] = _rot_mat128()
    g["iota8"] = np.broadcast_to(np.arange(8, dtype=f32), (128, 8)).copy()

    g["l0f_router"] = c(inputs["L0_ffn_w_router"])
    g["l0f_w1"] = c(inputs["L0_ffn_w1"])
    g["l0f_b1"] = c(inputs["L0_ffn_b1"].reshape(E, 32, 128).transpose(0, 2, 1))
    g["l0f_w2"] = c(inputs["L0_ffn_w2"])
    g["l0f_b2"] = c(inputs["L0_ffn_b2"])
    g["l0a_inwT"] = c(inputs["L0_attn_in_w"].T)
    g["l0a_inb"] = c(inputs["L0_attn_in_b"].reshape(16, 128).T)
    g["l0a_convw"] = c(inputs["L0_attn_conv_w"][:, 0, :].reshape(16, 128, 4))
    g["l0a_convb"] = c(inputs["L0_attn_conv_b"].reshape(16, 128).T)
    g["l0a_xpwT"] = c(inputs["L0_attn_xp_w"].T)
    xpp = np.zeros((2048, 65), np.float32)
    xpp[:, 0:16] = inputs["L0_attn_xp_w"].T[:, 0:16]
    xpp[:, 32:48] = inputs["L0_attn_xp_w"].T[:, 16:32]
    xpp[:, 64] = 1.0
    g["l0a_xpwT_pad"] = xpp
    g["l0a_xpb"] = c(inputs["L0_attn_xp_b"].reshape(32, 1))
    g["l0a_outwT"] = c(inputs["L0_attn_out_w"].T)
    g["l0a_outb"] = c(inputs["L0_attn_out_b"].reshape(1, D))
    g["l1f_n1"] = c(np.broadcast_to(inputs["L1_ffn_norm1_w"], (128, D)))
    g["l1f_n2"] = c(np.broadcast_to(inputs["L1_ffn_norm2_w"], (128, D)))
    g["l1f_qkvwT"] = c(inputs["L1_ffn_qkv_w"].T)
    g["l1f_qkvb"] = c(inputs["L1_ffn_qkv_b"].reshape(1, 3 * D))
    g["l1f_qkvb_h"] = c(inputs["L1_ffn_qkv_b"].reshape(24, 128).T)
    g["l1f_outwT"] = c(inputs["L1_ffn_out_w"].T)
    g["l1f_outb"] = c(inputs["L1_ffn_out_b"].reshape(1, D))
    g["l1f_gatewT"] = c(inputs["L1_ffn_gate_w"].T)
    g["l1f_w1T"] = c(inputs["L1_ffn_w1"].transpose(0, 2, 1))
    g["l1f_w2T"] = c(inputs["L1_ffn_w2"].transpose(0, 2, 1))
    g["l1f_b1"] = c(inputs["L1_ffn_b1"].reshape(E, 16, 128).transpose(0, 2, 1))
    g["l1f_b2"] = c(inputs["L1_ffn_b2"].reshape(E, 16, 128).transpose(0, 2, 1))
    g["l1f_w3T"] = c(inputs["L1_ffn_w3"].transpose(0, 2, 1))
    g["l1f_b3"] = c(inputs["L1_ffn_b3"])
    g["l1a_qkvwT"] = c(inputs["L1_attn_qkv_w"].T)
    g["l1a_qkvb"] = c(inputs["L1_attn_qkv_b"].reshape(1, 3 * D))
    g["l1a_qkvb_h"] = c(inputs["L1_attn_qkv_b"].reshape(24, 128).T)
    g["l1a_kng"] = c(np.broadcast_to(np.tile(inputs["L1_attn_kn_g"], H1), (128, D)))
    g["l1a_knb"] = c(np.broadcast_to(np.tile(inputs["L1_attn_kn_b"], H1), (128, D)))
    g["l1a_vng"] = c(np.broadcast_to(np.tile(inputs["L1_attn_vn_g"], H1), (128, D)))
    g["l1a_vnb"] = c(np.broadcast_to(np.tile(inputs["L1_attn_vn_b"], H1), (128, D)))
    g["l1a_temp"] = c(inputs["L1_attn_temperature"].reshape(1, H1))
    g["l1a_bias"] = c(inputs["L1_attn_attn_bias"])
    g["l1a_p1wT"] = c(inputs["L1_attn_p1_w"].T)
    g["l1a_p1b"] = c(inputs["L1_attn_p1_b"].reshape(8, 128).T)
    g["l1a_p2wT"] = c(inputs["L1_attn_p2_w"].T)
    g["l1a_p2b"] = c(inputs["L1_attn_p2_b"].reshape(1, D))

    x = np.ascontiguousarray(inputs["L0_attn_x"], dtype=f32)
    per_core = []
    for j in range(NCORES):
        bsel = [4 * j, 4 * j + 1, 4 * j + 2, 4 * j + 3]
        m = {"x_in": np.ascontiguousarray(x[bsel].reshape(T, D))}
        onehot = np.zeros((1, 64), f32)
        onehot[0, j] = 1.0
        mask = np.zeros((64, 1), f32)
        mask[:j, 0] = 1.0
        m["core_onehot"] = onehot
        m["core_mask"] = mask
        per_core.append(m)
    return g, per_core


def _unshard(outs):
    full = np.empty((32, S, D), np.float32)
    for j in range(NCORES):
        r = outs[j].reshape(B_LOC, S, D)
        for bi, b in enumerate([4 * j, 4 * j + 1, 4 * j + 2, 4 * j + 3]):
            full[b] = r[bi]
    return full



class P:

    def __init__(self, nc):
        self.nc = nc
        self._exit = []
        self._sem_i = 0
        self._reg_i = 0

    def sem(self):
        cm = self.nc.semaphore(f"s{self._sem_i}")
        self._sem_i += 1
        h = cm.__enter__()
        self._exit.append(("sem", cm, h))
        return h

    def sbuf(self, shape, dtype, name=None):
        self._name_i = getattr(self, "_name_i", 0) + 1
        cm = self.nc.sbuf_tensor(f"{name or 'sb'}_{self._name_i}", shape, dtype)
        h = cm.__enter__()
        self._exit.append(("buf", cm, h))
        return h

    def psum(self, shape, dtype=F32, name=None):
        self._name_i = getattr(self, "_name_i", 0) + 1
        cm = self.nc.psum_tensor(f"{name or 'ps'}_{self._name_i}", shape, dtype)
        h = cm.__enter__()
        self._exit.append(("buf", cm, h))
        return h

    def reg(self, eng, val=None):
        r = eng.alloc_register(f"r{self._reg_i}")
        self._reg_i += 1
        if val is not None:
            eng.reg_mov(r, val)
        return r

    def close(self):
        self.free_scope(0)

    def scoped(self):
        return len(self._exit)

    def free_scope(self, mark):
        sems = [h for (kind, cm, h) in self._exit[mark:] if kind == "sem"]
        if sems:
            self.barrier()
            self.nc.gpsimd.dma_reset()
            for h in sems:
                self.nc.gpsimd.sem_clear(h)
            self.barrier()
        for kind, cm, h in reversed(self._exit[mark:]):
            cm.__exit__(None, None, None)
        del self._exit[mark:]

    def barrier(self):
        self.nc.all_engine_barrier()


def emit_prologue_casts(p, pairs):
    nc = p.nc
    mk_pc = p.scoped()
    ld = p.sem()
    st = p.sem()
    nld = 0
    nst = 0
    stage = p.sbuf([128, 4096], BF16, "cast_stage")
    cols_max = 4096
    for src, dst in pairs:
        rows, cols = src.shape
        assert dst.shape == (rows, cols)
        r = 0
        while r < rows:
            rr = min(128, rows - r)
            cc0 = 0
            while cc0 < cols:
                cc = min(cols_max, cols - cc0)
                buf = stage[0:rr, 0:cc]
                if nst > 0:
                    nc.gpsimd.wait_ge(st, nst * 16)
                nc.gpsimd.dma_start(out=buf, in_=src[r:r + rr, cc0:cc0 + cc]).then_inc(ld, 16)
                nld += 1
                nc.gpsimd.wait_ge(ld, nld * 16)
                nc.gpsimd.dma_start(out=dst[r:r + rr, cc0:cc0 + cc], in_=buf).then_inc(st, 16)
                nst += 1
                cc0 += cc
            r += rr
    nc.gpsimd.wait_ge(st, nst * 16)
    p.barrier()
    p.free_scope(mk_pc)




def emit_norm_pass(p, x_in, z_f32, zln, zlnT, kind, gamma_rep=None, ident_bf=None,
                   eps=LNEPS, nt=None):
    nc = p.nc
    v, sc, g, t, sy = nc.vector, nc.scalar, nc.gpsimd, nc.tensor, nc.sync
    if nt is None:
        nt = NT
    mk_np = p.scoped()
    xr = x_in.rearrange("(n p) d -> n p d", p=128)
    ld = [p.sem(), p.sem()]
    rdy = p.sem()
    stok = p.sem()
    sc_mean = p.sem(); vc_xc = p.sem(); sc_sq = p.sem(); vc_v = p.sem()
    sc_rt = p.sem(); vc_rc = p.sem(); sc_zf = p.sem(); vn = p.sem()
    tp = p.sem(); ev = p.sem(); st = p.sem(); evok = p.sem()

    xt = p.sbuf([128, 2, D], F32, "np_x")
    xc = p.sbuf([128, 2, D], F32, "np_xc")
    zf = p.sbuf([128, 2, D], F32, "np_zf")
    sqs = p.sbuf([128, D], F32, "np_sqs")
    stat = p.sbuf([128, 8], F32, "np_stat")
    lnt = p.sbuf([128, 2, D], BF16, "np_ln")
    g_t = p.sbuf([128, D], F32, "np_gam") if gamma_rep is not None else None
    zT_sb = p.sbuf([128, 8, S], BF16, "np_zT") if zlnT is not None else None
    ps_tr = p.psum([128, 8, 128], BF16, "np_tr") if zlnT is not None else None

    if gamma_rep is not None:
        sy.dma_start(out=g_t[:], in_=gamma_rep).then_inc(ld[0], 16)
        sy.wait_ge(ld[0], 16)
    p.barrier()
    ld_base = [16 if gamma_rep is not None else 0, 0]
    sy.dma_start(out=xt[:, 0, :], in_=xr[0]).then_inc(ld[0], 16)
    r_ld = [p.reg(sy, ld_base[0] + 16), p.reg(sy, ld_base[1])]
    r_g = p.reg(sy, 0)
    r_sv = p.reg(sy, 1)
    r_st = p.reg(sy, 0)

    r_s_rdy = p.reg(sc, 0)
    r_v_m = p.reg(v, 0)
    r_s_xc = p.reg(sc, 0)
    r_v_sq = p.reg(v, 0)
    r_s_v = p.reg(sc, 0)
    r_v_rt = p.reg(v, 0)
    r_s_rc = p.reg(sc, 0)
    r_v_zf = p.reg(v, 0)
    r_v_stok = p.reg(v, 0)
    r_t_vn = p.reg(t, 0) if zlnT is not None else None
    r_t_ev = p.reg(t, 0) if zlnT is not None else None
    r_v_tp = p.reg(v, 0) if zlnT is not None else None
    r_sy_ev = p.reg(sy, 0)

    TPB = S // 128
    ln = kind == "ln"
    nst_per = (1 if zln is not None else 0) + (1 if z_f32 is not None else 0)

    def tile_body(tg_expr, t_expr, par):
        nxt = 1 - par
        sy.wait_ge(vn, r_g)
        sy.reg_add(r_g, r_g, 1)
        sy.dma_start(out=xt[:, nxt, :], in_=xr[smin(tg_expr + 1, nt - 1)]).then_inc(ld[nxt], 16)
        sy.reg_add(r_ld[nxt], r_ld[nxt], 16)
        sy.wait_ge(ld[par], r_ld[par])
        sy.sem_inc(rdy, 1)

        sc.reg_add(r_s_rdy, r_s_rdy, 1)
        sc.wait_ge(rdy, r_s_rdy)
        if ln:
            sc.activation(out=sqs[:], in_=xt[:, par, :], func=AF.Identity,
                          scale=1.0 / D, accum_out=stat[:, 0:1]).then_inc(sc_mean, 1)
            v.reg_add(r_v_m, r_v_m, 1)
            v.wait_ge(sc_mean, r_v_m)
            v.tensor_scalar(xc[:, par, :], xt[:, par, :], stat[:, 0:1], None,
                            ALU.subtract).then_inc(vc_xc, 1)
            csrc = xc
            sc.reg_add(r_s_xc, r_s_xc, 1)
            sc.wait_ge(vc_xc, r_s_xc)
        else:
            csrc = xt
        sc.activation(out=sqs[:], in_=csrc[:, par, :], func=AF.Square,
                      accum_out=stat[:, 1:2]).then_inc(sc_sq, 1)
        v.reg_add(r_v_sq, r_v_sq, 1)
        v.wait_ge(sc_sq, r_v_sq)
        v.tensor_scalar(stat[:, 2:3], stat[:, 1:2], 1.0 / D, eps, ALU.mult,
                        ALU.add).then_inc(vc_v, 1)
        sc.reg_add(r_s_v, r_s_v, 1)
        sc.wait_ge(vc_v, r_s_v)
        sc.activation(out=stat[:, 3:4], in_=stat[:, 2:3], func=AF.Sqrt).then_inc(sc_rt, 1)
        v.reg_add(r_v_rt, r_v_rt, 1)
        v.wait_ge(sc_rt, r_v_rt)
        v.reciprocal(stat[:, 4:5], stat[:, 3:4]).then_inc(vc_rc, 1)
        sc.reg_add(r_s_rc, r_s_rc, 1)
        sc.wait_ge(vc_rc, r_s_rc)
        sc.activation(out=zf[:, par, :], in_=csrc[:, par, :], func=AF.Identity,
                      scale=stat[:, 4:5]).then_inc(sc_zf, 1)
        v.reg_add(r_v_zf, r_v_zf, 1)
        v.wait_ge(sc_zf, r_v_zf)
        if nst_per:
            v.wait_ge(stok, r_v_stok)
            v.reg_add(r_v_stok, r_v_stok, 1)
        if gamma_rep is not None:
            v.tensor_tensor(lnt[:, par, :], zf[:, par, :], g_t[:], ALU.mult).then_inc(vn, 1)
        else:
            v.tensor_copy(lnt[:, par, :], zf[:, par, :]).then_inc(vn, 1)

        if zlnT is not None:
            t.reg_add(r_t_vn, r_t_vn, 1)
            t.wait_ge(vn, r_t_vn)
            t.wait_ge(ev, r_t_ev)
            for c in range(8):
                ins = t.transpose(ps_tr[:, c, :], lnt[:, par, ts(c, 128)], ident_bf[:])
            ins.then_inc(tp, 1)
            t.reg_add(r_t_ev, r_t_ev, 1)
            v.reg_add(r_v_tp, r_v_tp, 1)
            v.wait_ge(tp, r_v_tp)
            v.tensor_copy(zT_sb[:, :, bass.ds(t_expr * 128, 128)], ps_tr[:]).then_inc(ev, 1)
        sy.wait_ge(vn, r_sv)
        sy.reg_add(r_sv, r_sv, 1)
        sy_waited = False
        if zln is not None:
            sy.dma_start(out=zln.rearrange("(n p) d -> n p d", p=128)[tg_expr],
                         in_=lnt[:, par, :]).then_inc(st, 16)
            sy.reg_add(r_st, r_st, 16)
            sy_waited = True
        if z_f32 is not None:
            sy.dma_start(out=z_f32.rearrange("(n p) d -> n p d", p=128)[tg_expr],
                         in_=zf[:, par, :]).then_inc(st, 16)
            sy.reg_add(r_st, r_st, 16)
            sy_waited = True
        if sy_waited:
            sy.wait_ge(st, r_st)
            sy.sem_inc(stok, 1)
        else:
            sy.sem_inc(stok, 1)

    nb = nt // TPB
    for b in range(nb):
        with nc.Fori(0, TPB // 2) as th:
            for par in (0, 1):
                tile_body(b * TPB + th * 2 + par, th * 2 + par, par)
        if zlnT is not None:
            sy.reg_add(r_sy_ev, r_sy_ev, TPB)
            sy.wait_ge(ev, r_sy_ev)
            for c in range(8):
                sy.dma_start(out=zlnT[ts(c, 128), b * S:(b + 1) * S],
                             in_=zT_sb[:, c, :]).then_inc(st, 16)
                sy.reg_add(r_st, r_st, 16)
            sy.wait_ge(st, r_st)
            p.barrier()
    sy.wait_ge(st, r_st)
    sy.wait_ge(ld[0], r_ld[0])
    sy.wait_ge(ld[1], r_ld[1])
    p.barrier()
    p.free_scope(mk_np)

def ds_(i, w_):
    return bass.ds(i * w_, w_)


def emit_head_ln(p, nc, src, dst_bf, gam_t, bet_t, aux):
    v, s, g = nc.vector, nc.scalar, nc.gpsimd
    xcg, sqg, t1g, st16 = aux["xcg"], aux["sqg"], aux["t1g"], aux["st16"]
    sem_v, sem_s, sem_g = aux["sem_v"], aux["sem_s"], aux["sem_g"]
    rv, rs, rg, rv2, rvg = aux["rv"], aux["rs"], aux["rg"], aux["rv2"], aux["rvg"]
    gsrc = src.rearrange("p (g w) -> p g w", g=16)
    v.wait_ge(sem_g, rv2)
    v.reg_add(rv2, rv2, 2)
    v.tensor_reduce(st16[:, 0:16], gsrc, AX.X, ALU.add).then_inc(sem_v, 1)
    s.reg_add(rs, rs, 1); s.wait_ge(sem_v, rs)
    s.activation(out=st16[:, 16:32], in_=st16[:, 0:16], func=AF.Identity,
                 scale=1.0 / HD).then_inc(sem_s, 1)
    v.reg_add(rv, rv, 1); v.wait_ge(sem_s, rv)
    v.tensor_tensor(xcg[:].rearrange("p (g w) -> p g w", g=16), gsrc,
                    st16[:, 16:32].broadcast_to([128, 16, HD]),
                    ALU.subtract).then_inc(sem_v, 1)
    s.reg_add(rs, rs, 1); s.wait_ge(sem_v, rs)
    s.activation(out=sqg[:], in_=xcg[:], func=AF.Square).then_inc(sem_s, 1)
    v.reg_add(rv, rv, 1); v.wait_ge(sem_s, rv)
    v.tensor_reduce(st16[:, 32:48], sqg[:].rearrange("p (g w) -> p g w", g=16),
                    AX.X, ALU.add).then_inc(sem_v, 1)
    s.reg_add(rs, rs, 1); s.wait_ge(sem_v, rs)
    s.activation(out=st16[:, 48:64], in_=st16[:, 32:48], func=AF.Sqrt,
                 scale=1.0 / HD, bias=aux["eps_col"][:, 0:1]).then_inc(sem_s, 1)
    s.reg_add(rs, rs, 2)
    v.reg_add(rv, rv, 1); v.wait_ge(sem_s, rv)
    v.reciprocal(st16[:, 64:80], st16[:, 48:64]).then_inc(sem_v, 1)
    g.reg_add(rg, rg, 4); g.wait_ge(sem_v, rg)
    g.tensor_tensor(t1g[:].rearrange("p (g w) -> p g w", g=16),
                    xcg[:].rearrange("p (g w) -> p g w", g=16),
                    st16[:, 64:80].broadcast_to([128, 16, HD]), ALU.mult).then_inc(sem_g, 1)
    v.wait_ge(sem_g, rvg)
    v.reg_add(rvg, rvg, 2)
    v.tensor_tensor(xcg[:], t1g[:], gam_t[:], ALU.mult).then_inc(sem_v, 1)
    g.reg_add(rg, rg, 1); g.wait_ge(sem_v, rg)
    g.tensor_tensor(dst_bf, xcg[:], bet_t[:], ALU.add).then_inc(sem_g, 1)


def emit_l1attn(p, dram, w):
    import os
    _PH = int(os.environ.get("L1A_PHASES", "9"))
    nc = p.nc
    v, sc, g, t, sy = nc.vector, nc.scalar, nc.gpsimd, nc.tensor, nc.sync
    SPT = S // 128
    SST = S // 512

    mk_all = p.scoped()
    ident_bf = p.sbuf([128, 128], BF16, "la_ident")
    ones_row = p.sbuf([1, 128], BF16, "la_ones")
    qkvb_h = p.sbuf([128, 24], F32, "la_qkvbh")
    p1b_t = p.sbuf([128, 8], F32, "la_p1b")
    eps_col = p.sbuf([128, 1], F32, "la_eps")
    kng_t = p.sbuf([128, D], BF16, "la_kng")
    knb_t = p.sbuf([128, D], BF16, "la_knb")
    vng_t = p.sbuf([128, D], BF16, "la_vng")
    vnb_t = p.sbuf([128, D], BF16, "la_vnb")
    temp_sb = p.sbuf([1, 16], F32, "la_temp")
    tempcol = p.sbuf([64, 16], F32, "la_tempcol")
    biasall = p.sbuf([64, 16 * 64], F32, "la_biasall")
    kv_bf = p.sbuf([64, 16 * 64], BF16, "la_kv")
    zT_b = p.sbuf([128, 8 * S], BF16, "la_zT")
    onesq_col = p.sbuf([1, 64], BF16, "la_onesqc")
    qkvb_row_sb = p.sbuf([1, 3 * D], BF16, "la_qkvbrow")
    tmp_bf = p.sbuf([1, 16], BF16, "la_tmpbf")
    ps_misc = p.psum([64, 16], F32, "la_psmisc")

    s0 = p.sem()
    g.dma_start(out=ident_bf[:], in_=w["ident"]).then_inc(s0, 16)
    g.memset(ones_row[:], 1.0)
    g.memset(eps_col[:], KVEPS)
    g.memset(onesq_col[:], 1.0)
    g.dma_start(out=qkvb_h[:], in_=w["l1a_qkvb_h"]).then_inc(s0, 16)
    g.dma_start(out=p1b_t[:], in_=w["l1a_p1b"]).then_inc(s0, 16)
    g.dma_start(out=kng_t[:], in_=w["l1a_kng"]).then_inc(s0, 16)
    g.dma_start(out=knb_t[:], in_=w["l1a_knb"]).then_inc(s0, 16)
    g.dma_start(out=vng_t[:], in_=w["l1a_vng"]).then_inc(s0, 16)
    g.dma_start(out=vnb_t[:], in_=w["l1a_vnb"]).then_inc(s0, 16)
    g.dma_start(out=temp_sb[:], in_=w["l1a_temp"]).then_inc(s0, 16)
    g.dma_start(out=biasall[:].rearrange("p (h e) -> p h e", h=16), in_=w["l1a_bias"].rearrange("h d e -> d h e")).then_inc(s0, 16)
    g.dma_start(out=qkvb_row_sb[:], in_=w["l1a_qkvb_row"]).then_inc(s0, 16)
    g.wait_ge(s0, 10 * 16)
    p.barrier()
    v.tensor_scalar(tmp_bf[:], temp_sb[:], 1.0 / S, None, ALU.mult)
    p.barrier()
    t.matmul(ps_misc[:], onesq_col[:], tmp_bf[:], start=True, stop=True)
    p.barrier()
    v.tensor_copy(tempcol[:], ps_misc[:])
    p.barrier()

    xi_r = dram["x_in"].rearrange("(n p) d -> n p d", p=128)
    z0_r = dram["z0"].rearrange("(n p) d -> n p d", p=128)
    xo_r = dram["x_out"].rearrange("(n p) d -> n p d", p=128)

    for b in range(B_LOC):
        mk1 = p.scoped()
        qkvw_sb = p.sbuf([128, 8 * 2048], BF16, f"la_qkvw{b}")
        kln_sb = p.sbuf([128, SPT * D], BF16, f"la_kln{b}")
        vln_sb = p.sbuf([128, SPT * D], BF16, f"la_vln{b}")
        zstage = p.sbuf([128, 2, 1024], BF16, f"la_zstg{b}")
        kstage = p.sbuf([128, 2, D], BF16, f"la_kstg{b}")
        xcg = p.sbuf([128, D], F32, f"la_xcg{b}")
        sqg = p.sbuf([128, D], F32, f"la_sqg{b}")
        t1g = p.sbuf([128, D], F32, f"la_t1g{b}")
        st16 = p.sbuf([128, 96], F32, f"la_st16{b}")
        qstg = p.sbuf([128, 2, 512], BF16, f"la_qstg{b}")
        ps_q = p.psum([128, 512], F32, f"la_psq{b}")
        ps_kv4 = p.psum([128, 4, 512], F32, f"la_pskv{b}")
        ps_ein = p.psum([64, 16 * 64], F32, f"la_psein{b}")
        aux = {"xcg": xcg, "sqg": sqg, "t1g": t1g, "st16": st16, "eps_col": eps_col}

        ld = p.sem()
        nld = 0
        for dc in range(8):
            sy.dma_start(out=qkvw_sb[:, ds_(dc, 1024)][:, 0:1024],
                         in_=w["l1a_qkvwT_bf"][ts(dc, 128), 0:D]).then_inc(ld, 16)
            nld += 16
        for c in range(8):
            sy.dma_start(out=zT_b[:, ds_(c, S)], in_=dram["zlnT"][ts(c, 128), b * S:(b + 1) * S]).then_inc(ld, 16)
            nld += 16
        sy.wait_ge(ld, nld)
        p.barrier()

        mmq = p.sem(); evq = p.sem(); stq = p.sem(); stfree = p.sem()
        r_v_q = p.reg(v, 0); r_sy_q = p.reg(sy, 0); r_t_q = p.reg(t, 0)
        r_v_stf = p.reg(v, 0); r_sy_st = p.reg(sy, 0)
        sy.sem_inc(stfree, 1)
        for fc in range(8):
            with nc.Fori(0, SST) as sti:
                t.wait_ge(evq, r_t_q)
                for dc in range(8):
                    ins = t.matmul(ps_q[:], qkvw_sb[:, dc * 1024 + fc * 128: dc * 1024 + (fc + 1) * 128],
                                   zT_b[:, bass.ds(dc * S + sti * 512, 512)],
                                   start=(dc == 0), stop=(dc == 7))
                ins.then_inc(mmq, 1)
                t.reg_add(r_t_q, r_t_q, 1)
                v.reg_add(r_v_stf, r_v_stf, 1)
                v.wait_ge(stfree, r_v_stf)
                v.reg_add(r_v_q, r_v_q, 1)
                v.wait_ge(mmq, r_v_q)
                v.tensor_scalar(qstg[:].rearrange("p c f -> p (c f)")[:, bass.ds((sti % 2) * 512, 512)],
                                ps_q[:], qkvb_h[:, fc:fc + 1], None, ALU.add).then_inc(evq, 1)
                sy.reg_add(r_sy_q, r_sy_q, 1)
                sy.wait_ge(evq, r_sy_q)
                sy.dma_start(out=dram["qT"][b, ts(fc, 128)][:, bass.ds(sti * 512, 512)],
                             in_=qstg[:].rearrange("p c f -> p (c f)")[:, bass.ds((sti % 2) * 512, 512)]).then_inc(stq, 16)
                sy.reg_add(r_sy_st, r_sy_st, 16)
                sy.wait_ge(stq, r_sy_st)
                sy.sem_inc(stfree, 1)
        p.barrier()

        if _PH < 2:
            p.free_scope(mk1)
            continue
        ldkv = p.sem()
        for dc in range(8):
            sy.dma_start(out=qkvw_sb[:, ds_(dc, 2048)],
                         in_=w["l1a_qkvwT_bf"][ts(dc, 128), D:3 * D]).then_inc(ldkv, 16)
        sy.wait_ge(ldkv, 8 * 16)
        p.barrier()
        sem_v = p.sem(); sem_s = p.sem(); sem_g = p.sem(); mmkv = p.sem(); sem_z = p.sem()
        rv = p.reg(v, 0); rs = p.reg(sc, 0); rg = p.reg(g, 0)
        rv2 = p.reg(v, 0); rvg = p.reg(v, 1)
        r_v_mm = p.reg(v, 0); r_t_z = p.reg(t, 1); r_t_sv = p.reg(t, 0)
        aux.update({"sem_v": sem_v, "sem_s": sem_s, "sem_g": sem_g,
                    "rv": rv, "rs": rs, "rg": rg, "rv2": rv2, "rvg": rvg})
        with nc.Fori(0, SPT // 2) as th:
            for par in (0, 1):
                tt = th * 2 + par
                v.tensor_copy(zstage[:, par, :],
                              zT_b[:].rearrange("p (c s) -> p c s", c=8)[:, :, bass.ds(tt * 128, 128)]).then_inc(sem_z, 1)
                t.wait_ge(sem_z, r_t_z)
                t.wait_ge(sem_v, r_t_sv)
                for part in range(4):
                    for dc in range(8):
                        t.matmul(ps_kv4[:, part, :], zstage[:, par, ts(dc, 128)],
                                 qkvw_sb[:, dc * 2048 + part * 512: dc * 2048 + (part + 1) * 512],
                                 start=(dc == 0), stop=False)
                    ins = t.matmul(ps_kv4[:, part, :], ones_row[:],
                                   qkvb_row_sb[0:1, D + part * 512: D + (part + 1) * 512],
                                   start=False, stop=True)
                ins.then_inc(mmkv, 1)
                t.reg_add(r_t_z, r_t_z, 1)
                t.reg_add(r_t_sv, r_t_sv, 10)
                v.reg_add(r_v_mm, r_v_mm, 1)
                v.wait_ge(mmkv, r_v_mm)
                emit_head_ln(p, nc, ps_kv4[:, 0:2, :].rearrange("p c f -> p (c f)"),
                             kln_sb[:, bass.ds(tt * D, D)], kng_t, knb_t, aux)
                emit_head_ln(p, nc, ps_kv4[:, 2:4, :].rearrange("p c f -> p (c f)"),
                             vln_sb[:, bass.ds(tt * D, D)], vng_t, vnb_t, aux)
        p.barrier()

        if _PH < 3:
            p.free_scope(mk1)
            continue
        mm_e = p.sem(); cp_e = p.sem()
        for c in range(SPT):
            if c >= 2:
                v.wait_ge(mm_e, c - 1)
            v.tensor_copy(kstage[:, c % 2, :], kln_sb[:, ds_(c, D)]).then_inc(cp_e, 1)
            t.wait_ge(cp_e, c + 1)
            for h in range(16):
                ins = t.matmul(ps_ein[:, ts(h, 64)], kstage[:, c % 2, ts(h, 64)],
                               vln_sb[:, c * D + h * 64: c * D + (h + 1) * 64],
                               start=(c == 0 and h % 8 == 0),
                               stop=(c == SPT - 1 and h % 8 == 7))
            ins.then_inc(mm_e, 1)
        v.wait_ge(mm_e, SPT)
        for h in range(16):
            v.scalar_tensor_tensor(kv_bf[:, ts(h, 64)], ps_ein[:, ts(h, 64)],
                                   tempcol[:, h:h + 1], biasall[:, ts(h, 64)],
                                   ALU.mult, ALU.add)
        p.barrier()
        p.free_scope(mk1)

        mk2 = p.scoped()
        attT_sb = p.sbuf([128, 8 * S], BF16, f"la_att{b}")
        h1T_sb = p.sbuf([128, 8 * S], BF16, f"la_h1{b}")
        p1w_sb = p.sbuf([128, 64 * 128], BF16, f"la_p1w{b}")
        p2w_sb = p.sbuf([128, 8 * 1024], BF16, f"la_p2w{b}")
        p2b_row = p.sbuf([1, D], BF16, f"la_p2b{b}")
        qh_sb = p.sbuf([64, 2, S], BF16, f"la_qh{b}")
        h1stage = p.sbuf([128, 2, 1024], BF16, f"la_h1s{b}")
        zt_f = p.sbuf([128, 2, D], F32, f"la_ztf{b}")
        xt_f = p.sbuf([128, 2, D], F32, f"la_xtf{b}")
        tmpo = p.sbuf([128, 2, D], F32, f"la_tmpo{b}")
        xo_t = p.sbuf([128, 2, D], F32, f"la_xo{b}")
        ps_att = p.psum([64, 512], F32, f"la_psa{b}")
        ps_p1 = p.psum([128, 512], F32, f"la_psp1{b}")
        ps_o = p.psum([128, 2, 512], F32, f"la_pso{b}")

        ld2 = p.sem()
        n2 = 0
        for dc in range(8):
            sy.dma_start(out=p1w_sb[:, ds_(dc, 1024)],
                         in_=w["l1a_p1wT_bf"][ts(dc, 128), :].rearrange("p (f c) -> p (f c)") if False else w["l1a_p1wT_bf"][ts(dc, 128), :]).then_inc(ld2, 16)
            sy.dma_start(out=p2w_sb[:, ds_(dc, 1024)], in_=w["l1a_p2wT_bf"][ts(dc, 128), :]).then_inc(ld2, 16)
            n2 += 32
        sy.dma_start(out=p2b_row[:], in_=w["l1a_p2b_bf"]).then_inc(ld2, 16)
        n2 += 16
        sy.wait_ge(ld2, n2)
        p.barrier()

        if _PH < 4:
            p.free_scope(mk2)
            continue
        NH_DBG = int(os.environ.get("L1A_NH", "16"))
        mha = p.sem(); eva = p.sem(); qrdy = p.sem()
        ldq = [p.sem(), p.sem()]
        r_v_a = p.reg(v, 0); r_t_ev2 = p.reg(t, 0); r_t_qr = p.reg(t, 0)
        for h in range(NH_DBG):
            hp = h % 2
            if h >= 2:
                sy.wait_ge(mha, (h - 1) * SST)
            sy.dma_start(out=qh_sb[:, hp, :], in_=dram["qT"][b, h * 64:(h + 1) * 64, :]).then_inc(ldq[hp], 16)
            sy.wait_ge(ldq[hp], (h // 2 + 1) * 16)
            sy.sem_inc(qrdy, 1)
            t.reg_add(r_t_qr, r_t_qr, 1)
            t.wait_ge(qrdy, r_t_qr)
            for sti in range(SST):
                t.wait_ge(eva, r_t_ev2)
                t.matmul(ps_att[:], kv_bf[:, ts(h, 64)],
                         qh_sb[:, hp, sti * 512:(sti + 1) * 512],
                         start=True, stop=True).then_inc(mha, 1)
                t.reg_add(r_t_ev2, r_t_ev2, 1)
                v.reg_add(r_v_a, r_v_a, 1)
                v.wait_ge(mha, r_v_a)
                dr = (h % 2) * 64
                off = (h // 2) * S + sti * 512
                v.tensor_tensor(attT_sb[dr:dr + 64, off:off + 512],
                                ps_att[:],
                                zT_b[dr:dr + 64, off:off + 512],
                                ALU.add).then_inc(eva, 1)
        p.barrier()

        if _PH < 5:
            p.free_scope(mk2)
            continue
        mm1 = p.sem(); ev1 = p.sem()
        r_s_1 = p.reg(sc, 0); r_t_1 = p.reg(t, 0)
        for fc in range(8):
            with nc.Fori(0, SST) as sti:
                t.wait_ge(ev1, r_t_1)
                for dc in range(8):
                    ins = t.matmul(ps_p1[:], p1w_sb[:, (dc * 8 + fc) * 128:(dc * 8 + fc + 1) * 128],
                                   attT_sb[:, bass.ds(dc * S + sti * 512, 512)],
                                   start=(dc == 0), stop=(dc == 7))
                ins.then_inc(mm1, 1)
                t.reg_add(r_t_1, r_t_1, 1)
                sc.reg_add(r_s_1, r_s_1, 1)
                sc.wait_ge(mm1, r_s_1)
                sc.activation(out=h1T_sb[:, bass.ds(fc * S + sti * 512, 512)], in_=ps_p1[:],
                              func=AF.Gelu, bias=p1b_t[:, fc:fc + 1]).then_inc(ev1, 1)
        p.barrier()

        if _PH < 6:
            p.free_scope(mk2)
            continue
        mm2 = p.sem(); cp2 = p.sem(); ev2 = p.sem(); st2 = p.sem(); po2 = p.sem()
        ldx = [p.sem(), p.sem()]
        xrdy = p.sem(); sfree = p.sem()
        r_v_cp = p.reg(v, 0)
        r_t_cp2 = p.reg(t, 0); r_v_m2 = p.reg(v, 0); r_g_e2 = p.reg(g, 0)
        r_sy_st2 = p.reg(sy, 0); r_v_xr = p.reg(v, 0); r_sy_po = p.reg(sy, 0)
        r_sy_po2 = p.reg(sy, 0); r_t_ev3 = p.reg(t, 0)
        r_ldx = [p.reg(sy, 32), p.reg(sy, 0)]
        r_g_sf = p.reg(g, 0)
        sy.sem_inc(sfree, 2)
        sy.dma_start(out=zt_f[:, 0, :], in_=z0_r[b * SPT]).then_inc(ldx[0], 16)
        sy.dma_start(out=xt_f[:, 0, :], in_=xi_r[b * SPT]).then_inc(ldx[0], 16)
        with nc.Fori(0, SPT // 2) as th:
            for par in (0, 1):
                tt = th * 2 + par
                nxt = 1 - par
                sy.wait_ge(po2, r_sy_po)
                sy.reg_add(r_sy_po, r_sy_po, 1)
                sy.dma_start(out=zt_f[:, nxt, :], in_=z0_r[smin(b * SPT + tt + 1, NT - 1)]).then_inc(ldx[nxt], 16)
                sy.dma_start(out=xt_f[:, nxt, :], in_=xi_r[smin(b * SPT + tt + 1, NT - 1)]).then_inc(ldx[nxt], 16)
                sy.reg_add(r_ldx[nxt], r_ldx[nxt], 32)
                sy.wait_ge(ldx[par], r_ldx[par])
                sy.sem_inc(xrdy, 1)
                v.wait_ge(mm2, r_v_cp)
                v.tensor_copy(h1stage[:, par, :],
                              h1T_sb[:].rearrange("p (c s) -> p c s", c=8)[:, :, bass.ds(tt * 128, 128)]).then_inc(cp2, 1)
                v.reg_add(r_v_cp, r_v_cp, 1)
                t.reg_add(r_t_cp2, r_t_cp2, 1)
                t.wait_ge(cp2, r_t_cp2)
                t.wait_ge(ev2, r_t_ev3)
                for half in (0, 1):
                    for hc in range(8):
                        t.matmul(ps_o[:, half, :], h1stage[:, par, ts(hc, 128)],
                                 p2w_sb[:, hc * 1024 + half * 512: hc * 1024 + (half + 1) * 512],
                                 start=(hc == 0), stop=False)
                    ins = t.matmul(ps_o[:, half, :], ones_row[:],
                                   p2b_row[0:1, half * 512:(half + 1) * 512],
                                   start=False, stop=True)
                ins.then_inc(mm2, 1)
                t.reg_add(r_t_ev3, r_t_ev3, 1)
                v.reg_add(r_v_m2, r_v_m2, 1)
                v.wait_ge(mm2, r_v_m2)
                v.reg_add(r_v_xr, r_v_xr, 1)
                v.wait_ge(xrdy, r_v_xr)
                v.tensor_tensor(tmpo[:, par, :], ps_o[:].rearrange("p c f -> p (c f)"),
                                zt_f[:, par, :], ALU.add).then_inc(ev2, 1)
                g.reg_add(r_g_e2, r_g_e2, 1)
                g.wait_ge(ev2, r_g_e2)
                g.reg_add(r_g_sf, r_g_sf, 1)
                g.wait_ge(sfree, r_g_sf)
                g.tensor_tensor(xo_t[:, par, :], tmpo[:, par, :], xt_f[:, par, :], ALU.add).then_inc(po2, 1)
                sy.reg_add(r_sy_po2, r_sy_po2, 1)
                sy.wait_ge(po2, r_sy_po2)
                sy.dma_start(out=xo_r[b * SPT + tt], in_=xo_t[:, par, :]).then_inc(st2, 16)
                sy.reg_add(r_sy_st2, r_sy_st2, 16)
                sy.wait_ge(st2, r_sy_st2)
                sy.sem_inc(sfree, 1)
        sy.wait_ge(ldx[0], r_ldx[0])
        sy.wait_ge(ldx[1], r_ldx[1])
        p.barrier()
        p.free_scope(mk2)
    p.free_scope(mk_all)
    p.barrier()



def emit_l0attn(p, dram, w):
    nc = p.nc
    v, sc, g, t, sy = nc.vector, nc.scalar, nc.gpsimd, nc.tensor, nc.sync
    SST = S // 512

    mk = p.scoped()
    inw_sb = p.sbuf([128, 8 * 2048], BF16, "sa_inw")
    xpw_sb = p.sbuf([128, 16 * 65], BF16, "sa_xpw")
    inb_sb = p.sbuf([128, 16], F32, "sa_inb")
    convw_sb = p.sbuf([128, 16 * 4], F32, "sa_convw")
    convb_sb = p.sbuf([128, 16], F32, "sa_convb")
    xpba_sb = p.sbuf([16, 1], F32, "sa_xpba")
    xpbb_sb = p.sbuf([16, 1], F32, "sa_xpbb")
    outw_aug = p.sbuf([17, 1024], BF16, "sa_outw")
    onescol = p.sbuf([128, 1], BF16, "sa_ones")
    onescol_f = p.sbuf([1, 16], F32, "sa_onesf")
    zT_b = p.sbuf([128, 8 * S], BF16, "sa_zT")
    hx = p.sbuf([128, 2, 3 + S], F32, "sa_hx")
    cacc = p.sbuf([128, 2, S], F32, "sa_cacc")
    hs_bf = p.sbuf([128, 2, S], BF16, "sa_hs")
    e1 = p.sbuf([16, S], F32, "sa_e1")
    e1n = p.sbuf([16, S], F32, "sa_e1n")
    scanA = p.sbuf([16, S], F32, "sa_scanA")
    sb1 = p.sbuf([16, S], F32, "sa_sb1")
    rs_sb = p.sbuf([1, S], F32, "sa_rs")
    scanB = p.sbuf([16, S], F32, "sa_scanB")
    ys_f = p.sbuf([16, S], F32, "sa_ysf")
    ysbf = p.sbuf([17, S], BF16, "sa_ysbf")
    xt_f = p.sbuf([128, 2, D], F32, "sa_xt")
    xo_t = p.sbuf([128, 2, D], F32, "sa_xo")
    ps_h = p.psum([128, 2, 512], F32, "sa_psh")
    ps_xpF = p.psum([128, S], F32, "sa_psxp")
    ps_xp = ps_xpF[0:65, :]
    ps_bc = ps_h[0:16, 0, :]

    s0 = p.sem()
    n0 = 0
    for dc in range(8):
        sy.dma_start(out=inw_sb[:, ds_(dc, 2048)], in_=w["l0a_inwT_bf"][ts(dc, 128), :]).then_inc(s0, 16)
        n0 += 16
    for c in range(16):
        sy.dma_start(out=xpw_sb[:, ds_(c, 65)],
                     in_=w["l0a_xpwT_pad_bf"][ts(c, 128), :]).then_inc(s0, 16)
        n0 += 16
    sy.dma_start(out=inb_sb[:], in_=w["l0a_inb"]).then_inc(s0, 16)
    n0 += 16
    for c in range(16):
        sy.dma_start(out=convw_sb[:, ds_(c, 4)], in_=w["l0a_convw"][c]).then_inc(s0, 16)
        n0 += 16
    sy.dma_start(out=convb_sb[:], in_=w["l0a_convb"]).then_inc(s0, 16)
    n0 += 16
    sy.dma_start(out=xpba_sb[:], in_=w["l0a_xpb"][0:16, :]).then_inc(s0, 16)
    n0 += 16
    sy.dma_start(out=xpbb_sb[:], in_=w["l0a_xpb"][16:32, :]).then_inc(s0, 16)
    n0 += 16
    sy.dma_start(out=outw_aug[0:16, :], in_=w["l0a_outwT_bf"]).then_inc(s0, 16)
    n0 += 16
    sy.dma_start(out=outw_aug[16:17, :], in_=w["l0a_outb_bf"]).then_inc(s0, 16)
    n0 += 16
    sy.wait_ge(s0, n0)
    g.memset(onescol[:], 1.0)
    g.memset(onescol_f[:], 1.0)
    g.memset(hx[:, 0, 0:3], 0.0)
    g.memset(hx[:, 1, 0:3], 0.0)
    g.memset(ysbf[:], 1.0)
    p.barrier()

    xi_r = dram["x_in"].rearrange("(n p) d -> n p d", p=128)
    xo_r = dram["x_out"].rearrange("(n p) d -> n p d", p=128)

    for b in range(B_LOC):
        mkb = p.scoped()
        ld = p.sem(); mmh = p.sem(); evh = p.sem(); cvA = p.sem(); cvB = p.sem()
        svu = p.sem(); xpd = p.sem()
        nld = 0
        for c in range(8):
            sy.dma_start(out=zT_b[:, ds_(c, S)], in_=dram["zlnT"][ts(c, 128), b * S:(b + 1) * S]).then_inc(ld, 16)
            nld += 16
        sy.wait_ge(ld, nld)
        p.barrier()

        n_mmh = 0; n_evh = 0; n_cvA = 0; n_cvB = 0; n_svu = 0
        for fc in range(16):
            par = fc % 2
            for st in range(SST):
                if n_evh - 2 + 1 > 0:
                    t.wait_ge(evh, n_evh - 1)
                for dc in range(8):
                    ins = t.matmul(ps_h[:, st % 2, :],
                                   inw_sb[:, dc * 2048 + fc * 128: dc * 2048 + (fc + 1) * 128],
                                   zT_b[:, dc * S + st * 512: dc * S + (st + 1) * 512],
                                   start=(dc == 0), stop=(dc == 7))
                ins.then_inc(mmh, 1)
                n_mmh += 1
                sc.wait_ge(mmh, n_mmh)
                sc.activation(out=hx[:, par, 3 + st * 512: 3 + (st + 1) * 512],
                              in_=ps_h[:, st % 2, :], func=AF.Identity,
                              bias=inb_sb[:, fc:fc + 1]).then_inc(evh, 1)
                n_evh += 1
            v.wait_ge(evh, n_evh)
            if fc >= 1:
                v.wait_ge(svu, n_svu)
            v.tensor_scalar(cacc[:, 0, :], hx[:, par, 0:S], convw_sb[:, fc * 4 + 0: fc * 4 + 1],
                            None, ALU.mult).then_inc(cvA, 1)
            n_cvA += 1
            v.wait_ge(cvA, n_cvA)
            v.scalar_tensor_tensor(cacc[:, 1, :], hx[:, par, 1:1 + S],
                                   convw_sb[:, fc * 4 + 1: fc * 4 + 2], cacc[:, 0, :],
                                   ALU.mult, ALU.add).then_inc(cvB, 1)
            n_cvB += 1
            v.wait_ge(cvB, n_cvB)
            v.scalar_tensor_tensor(cacc[:, 0, :], hx[:, par, 2:2 + S],
                                   convw_sb[:, fc * 4 + 2: fc * 4 + 3], cacc[:, 1, :],
                                   ALU.mult, ALU.add).then_inc(cvA, 1)
            n_cvA += 1
            v.wait_ge(cvA, n_cvA)
            v.scalar_tensor_tensor(cacc[:, 1, :], hx[:, par, 3:3 + S],
                                   convw_sb[:, fc * 4 + 3: fc * 4 + 4], cacc[:, 0, :],
                                   ALU.mult, ALU.add).then_inc(cvB, 1)
            n_cvB += 1
            sc.wait_ge(cvB, n_cvB)
            if fc >= 2:
                sc.wait_ge(xpd, fc - 1)
            sc.activation(out=hs_bf[:, par, :], in_=cacc[:, 1, :], func=AF.Silu,
                          bias=convb_sb[:, fc:fc + 1]).then_inc(svu, 1)
            n_svu += 1
            t.wait_ge(svu, n_svu)
            for st in range(SST):
                ins = t.matmul(ps_xpF[0:65, st * 512:(st + 1) * 512],
                               xpw_sb[:, fc * 65:(fc + 1) * 65],
                               hs_bf[:, par, st * 512:(st + 1) * 512],
                               start=(fc == 0), stop=(fc == 15))
            ins.then_inc(xpd, 1)

        s_rs = p.sem(); s_e1 = p.sem(); s_e1n = p.sem(); s_A = p.sem()
        s_sb1 = p.sem(); mmbc = p.sem(); s_B = p.sem(); scn = p.sem()
        v.wait_ge(xpd, 16)
        v.tensor_copy(rs_sb[:], ps_xp[64:65, :]).then_inc(s_rs, 1)
        sc.wait_ge(xpd, 16)
        sc.activation(out=e1[:], in_=ps_xp[0:16, :], func=AF.Exp,
                      bias=xpba_sb[:, 0:1]).then_inc(s_e1, 1)
        v.wait_ge(s_e1, 1)
        v.tensor_scalar(e1n[:], e1[:], -1.0, None, ALU.mult).then_inc(s_e1n, 1)
        sc.wait_ge(s_e1n, 1)
        sc.activation(out=scanA[:], in_=e1n[:], func=AF.Exp).then_inc(s_A, 1)
        sc.activation(out=sb1[:], in_=ps_xp[32:48, :], func=AF.Silu,
                      bias=xpbb_sb[:, 0:1]).then_inc(s_sb1, 1)
        t.wait_ge(s_rs, 1)
        t.wait_ge(evh, n_evh)
        for st in range(SST):
            if st > 0:
                t.wait_ge(s_B, st)
            t.matmul(ps_bc, onescol_f[0:1, :], rs_sb[0:1, st * 512:(st + 1) * 512],
                     start=True, stop=True).then_inc(mmbc, 1)
            v.wait_ge(mmbc, st + 1)
            v.wait_ge(s_sb1, 1)
            v.tensor_tensor(scanB[:, st * 512:(st + 1) * 512],
                            sb1[:, st * 512:(st + 1) * 512], ps_bc,
                            ALU.mult).then_inc(s_B, 1)
        v.wait_ge(s_A, 1)
        v.wait_ge(s_B, SST)
        v.tensor_tensor_scan(ys_f[:], scanA[:], scanB[:], 0.0, ALU.mult,
                             ALU.add).then_inc(scn, 1)
        g.wait_ge(scn, 1)
        g.tensor_copy(ysbf[0:16, :], ys_f[:]).then_inc(scn, 1)
        p.barrier()

        ldx = [p.sem(), p.sem()]
        mmo = p.sem(); evo = p.sem(); po = p.sem(); st2 = p.sem(); sfree = p.sem()
        nldx = [0, 0]
        n_mmo = 0; n_evo = 0; n_st2 = 0; n_po = 0
        sy.sem_inc(sfree, 2)
        sy.dma_start(out=xt_f[:, 0, :], in_=xi_r[b * 16]).then_inc(ldx[0], 16)
        nldx[0] += 16
        for tt in range(16):
            par = tt % 2
            nxt = 1 - par
            if tt < 15:
                sy.dma_start(out=xt_f[:, nxt, :], in_=xi_r[b * 16 + tt + 1]).then_inc(ldx[nxt], 16)
                nldx[nxt] += 16
            sy.wait_ge(ldx[par], nldx[par])
            sy.sem_inc(po, 1)
            n_po += 1
            if tt >= 2:
                t.wait_ge(evo, n_evo - 1)
            pso = ps_h[:].rearrange("p c f -> p (c f)") if tt % 2 == 0 else ps_xpF[:, 0:1024]
            for half in (0, 1):
                ins = t.matmul(pso[:, half * 512:(half + 1) * 512],
                               ysbf[0:17, tt * 128:(tt + 1) * 128],
                               outw_aug[0:17, half * 512:(half + 1) * 512],
                               start=True, stop=True)
            ins.then_inc(mmo, 1)
            n_mmo += 1
            v.wait_ge(mmo, n_mmo)
            v.wait_ge(po, n_po)
            v.wait_ge(sfree, tt)
            v.tensor_tensor(xo_t[:, par, :], pso,
                            xt_f[:, par, :], ALU.add).then_inc(evo, 1)
            n_evo += 1
            sy.wait_ge(evo, n_evo)
            sy.dma_start(out=xo_r[b * 16 + tt], in_=xo_t[:, par, :]).then_inc(st2, 16)
            n_st2 += 16
            sy.wait_ge(st2, n_st2)
            sy.sem_inc(sfree, 1)
        p.barrier()
        p.free_scope(mkb)
    p.free_scope(mk)
    p.barrier()



def emit_l1ffn_qkv(p, dram, w, consts):
    nc = p.nc
    v, sc, g, t, sy = nc.vector, nc.scalar, nc.gpsimd, nc.tensor, nc.sync
    SST = S // 512
    mk = p.scoped()
    qkvw_sb = p.sbuf([128, 8 * 3072], BF16, "lf_qkvw")
    zT_b = p.sbuf([128, 8 * S], BF16, "lf_zT")
    qtmp = p.sbuf([128, 2, 512], BF16, "lf_qtmp")
    t1s = p.sbuf([128, 2, 512], F32, "lf_t1")
    t2s = p.sbuf([128, 2, 512], F32, "lf_t2")
    ostg = p.sbuf([128, 2, 512], BF16, "lf_ostg")
    zstage = p.sbuf([128, 2, 1024], BF16, "lf_zstg")
    vstg = p.sbuf([128, 2, 1024], BF16, "lf_vstg")
    qkvb_h = p.sbuf([128, 24], F32, "lf_qkvbh")
    qkvb_row = p.sbuf([1, 3 * D], BF16, "lf_qkvbr")
    ones_row = consts["ones_row"]
    rot_t = p.sbuf([128, 128], BF16, "lf_rot")
    rc_t = p.sbuf([128, S], BF16, "lf_rc")
    rs_t = p.sbuf([128, S], BF16, "lf_rs")
    rcq_t = p.sbuf([128, S], BF16, "lf_rcq")
    rsq_t = p.sbuf([128, S], BF16, "lf_rsq")
    ps_q = p.psum([128, 2, 512], F32, "lf_psq")
    ps_rot = p.psum([128, 2, 512], F32, "lf_psrot")
    ps_kv = p.psum([128, 2, 1024], F32, "lf_pskv")

    s0 = p.sem()
    s0g = p.sem()
    sy.dma_start(out=qkvb_h[:], in_=w["l1f_qkvb_h"]).then_inc(s0, 16)
    sy.dma_start(out=qkvb_row[:], in_=w["l1f_qkvb_row"]).then_inc(s0, 16)
    g.dma_start(out=rot_t[:], in_=w["rotmat"]).then_inc(s0g, 16)
    g.dma_start(out=rc_t[:], in_=w["rope_cos"]).then_inc(s0g, 16)
    g.dma_start(out=rs_t[:], in_=w["rope_sin"]).then_inc(s0g, 16)
    g.dma_start(out=rcq_t[:], in_=w["rope_cos_q"]).then_inc(s0g, 16)
    g.dma_start(out=rsq_t[:], in_=w["rope_sin_q"]).then_inc(s0g, 16)
    sy.wait_ge(s0, 32)
    g.wait_ge(s0g, 80)
    p.barrier()
    lconsts = {"rotmat": rot_t[:], "rope_cos": rc_t[:], "rope_sin": rs_t[:],
               "ropeq_cos": rcq_t[:], "ropeq_sin": rsq_t[:]}

    for b in range(B_LOC):
        mkb = p.scoped()
        ld = p.sem()
        nld = 0
        for dc in range(8):
            sy.dma_start(out=qkvw_sb[:, ds_(dc, 3072)], in_=w["l1f_qkvwT_bf"][ts(dc, 128), :]).then_inc(ld, 16)
            nld += 16
        for c in range(8):
            sy.dma_start(out=zT_b[:, ds_(c, S)], in_=dram["zlnT"][ts(c, 128), b * S:(b + 1) * S]).then_inc(ld, 16)
            nld += 16
        sy.wait_ge(ld, nld)
        p.barrier()

        for which, dst, cosname, sinname in [("q", dram["qT"], "ropeq_cos", "ropeq_sin"),
                                             ("k", dram["kT"], "rope_cos", "rope_sin")]:
            fbase = 0 if which == "q" else D
            cos_t, sin_t = lconsts[cosname], lconsts[sinname]
            mmq = p.sem(); eva = p.sem(); evr = p.sem(); mr = p.sem()
            evf = p.sem(); stq = p.sem(); stdone = p.sem()
            r_t_ev = p.reg(t, 0); r_s_mm = p.reg(sc, 0); r_t_ea = p.reg(t, 0)
            r_v_mr = p.reg(v, 0); r_g_ev = p.reg(g, 0); r_sy_f = p.reg(sy, 0)
            r_sy_st = p.reg(sy, 0); r_v_std = p.reg(v, 0); r_s_evf = p.reg(sc, 0)
            sy.sem_inc(stdone, 2)
            for fc in range(8):
                bcol = fbase // 128 + fc
                with nc.Fori(0, SST) as sti:
                    par = sti % 2
                    t.wait_ge(evf, r_t_ev)
                    for dc in range(8):
                        ins = t.matmul(ps_q[:, par, :],
                                       qkvw_sb[:, dc * 3072 + fbase + fc * 128: dc * 3072 + fbase + (fc + 1) * 128],
                                       zT_b[:, bass.ds(dc * S + sti * 512, 512)],
                                       start=(dc == 0), stop=(dc == 7))
                    ins.then_inc(mmq, 1)
                    sc.reg_add(r_s_mm, r_s_mm, 1)
                    sc.wait_ge(mmq, r_s_mm)
                    sc.wait_ge(evf, r_s_evf)
                    sc.reg_add(r_s_evf, r_s_evf, 2)
                    sc.activation(out=qtmp[:, par, :], in_=ps_q[:, par, :], func=AF.Identity,
                                  bias=qkvb_h[:, bcol:bcol + 1]).then_inc(eva, 1)
                    t.reg_add(r_t_ea, r_t_ea, 1)
                    t.wait_ge(eva, r_t_ea)
                    t.matmul(ps_rot[:, par, :], lconsts["rotmat"], qtmp[:, par, :],
                             start=True, stop=True).then_inc(mr, 1)
                    t.reg_add(r_t_ev, r_t_ev, 2)
                    v.reg_add(r_v_mr, r_v_mr, 1)
                    v.wait_ge(mr, r_v_mr)
                    v.reg_add(r_v_std, r_v_std, 1)
                    v.wait_ge(stdone, r_v_std)
                    v.tensor_tensor(t1s[:, par, :], qtmp[:, par, :],
                                    cos_t[:, bass.ds(sti * 512, 512)], ALU.mult)
                    v.tensor_tensor(t2s[:, par, :], ps_rot[:, par, :],
                                    sin_t[:, bass.ds(sti * 512, 512)], ALU.mult).then_inc(evr, 2)
                    g.reg_add(r_g_ev, r_g_ev, 2)
                    g.wait_ge(evr, r_g_ev)
                    g.tensor_tensor(ostg[:, par, :], t1s[:, par, :], t2s[:, par, :],
                                    ALU.add).then_inc(evf, 2)
                    sy.reg_add(r_sy_f, r_sy_f, 2)
                    sy.wait_ge(evf, r_sy_f)
                    sy.dma_start(out=dst[b, ts(fc, 128)][:, bass.ds(sti * 512, 512)],
                                 in_=ostg[:, par, :]).then_inc(stq, 16)
                    sy.reg_add(r_sy_st, r_sy_st, 16)
                    sy.wait_ge(stq, r_sy_st)
                    sy.sem_inc(stdone, 1)
            p.barrier()

        sem_z = p.sem(); mmv = p.sem(); evv = p.sem(); stv = p.sem(); svd = p.sem()
        r_t_z = p.reg(t, 1); r_t_ev = p.reg(t, 0); r_s_mm = p.reg(sc, 0)
        r_sy_ev = p.reg(sy, 0); r_sy_st = p.reg(sy, 0); r_s_svd = p.reg(sc, 0)
        sy.sem_inc(svd, 2)
        r_v_mm = p.reg(v, 0)
        with nc.Fori(0, S // 128 // 2) as th:
            for par in (0, 1):
                tt = th * 2 + par
                v.wait_ge(mmv, r_v_mm)
                v.reg_add(r_v_mm, r_v_mm, 1)
                v.tensor_copy(zstage[:, par, :],
                              zT_b[:].rearrange("p (c s) -> p c s", c=8)[:, :, bass.ds(tt * 128, 128)]).then_inc(sem_z, 1)
                t.wait_ge(sem_z, r_t_z)
                t.wait_ge(evv, r_t_ev)
                for half in range(2):
                    for dc in range(8):
                        t.matmul(ps_kv[:, par, ts(half, 512)], zstage[:, par, ts(dc, 128)],
                                 qkvw_sb[:, dc * 3072 + 2 * D + half * 512: dc * 3072 + 2 * D + (half + 1) * 512],
                                 start=(dc == 0), stop=False)
                    ins = t.matmul(ps_kv[:, par, ts(half, 512)], ones_row,
                                   qkvb_row[0:1, 2 * D + half * 512: 2 * D + (half + 1) * 512],
                                   start=False, stop=True)
                ins.then_inc(mmv, 1)
                t.reg_add(r_t_z, r_t_z, 1)
                t.reg_add(r_t_ev, r_t_ev, 1)
                sc.reg_add(r_s_mm, r_s_mm, 1)
                sc.wait_ge(mmv, r_s_mm)
                sc.reg_add(r_s_svd, r_s_svd, 1)
                sc.wait_ge(svd, r_s_svd)
                sc.activation(out=vstg[:, par, :], in_=ps_kv[:, par, :],
                              func=AF.Identity).then_inc(evv, 1)
                sy.reg_add(r_sy_ev, r_sy_ev, 1)
                sy.wait_ge(evv, r_sy_ev)
                sy.dma_start(out=dram["v_tok"][b, bass.ds(tt * 128, 128), :],
                             in_=vstg[:, par, :]).then_inc(stv, 16)
                sy.reg_add(r_sy_st, r_sy_st, 16)
                sy.wait_ge(stv, r_sy_st)
                sy.sem_inc(svd, 1)
        p.barrier()
        p.free_scope(mkb)
    p.free_scope(mk)
    p.barrier()


def emit_l1ffn_attn(p, dram, w, consts):
    nc = p.nc
    v, sc, g, t, sy = nc.vector, nc.scalar, nc.gpsimd, nc.tensor, nc.sync
    mk = p.scoped()
    q_aug = p.sbuf([72, S], BF16, "lf_qaug")
    k_aug = p.sbuf([72, S], BF16, "lf_kaug")
    v_sb = p.sbuf([128, 16 * 65], BF16, "lf_vsb")
    sqv = p.sbuf([64, 2, S], F32, "lf_sqv")
    nrm = p.sbuf([1, S + 16], F32, "lf_nrm")
    expT = p.sbuf([128, 16 * 512], BF16, "lf_expT")
    attnout = p.sbuf([128, 16 * 1024], BF16, "lf_ao")
    rcp = p.sbuf([128, 4], F32, "lf_rcp")
    astage = p.sbuf([128, 1024], BF16, "lf_astg")
    aT_st = p.sbuf([128, 1024], BF16, "lf_aTst")
    outw_sb = p.sbuf([128, 8 * 1024], BF16, "lf_outw")
    outb_row = p.sbuf([1, D], BF16, "lf_outbr")
    zt_f = p.sbuf([128, 2, D], F32, "lf_ztf")
    z1_t = p.sbuf([128, 2, D], F32, "lf_z1t")
    ps_e = p.psum([128, 2, 512], F32, "lf_pse")
    ps_pv = p.psum([128, 4, 65], F32, "lf_pspv")
    ps_n = p.psum([33, 512], F32, "lf_psn")
    ps_tr = p.psum([128, 8, 128], BF16, "lf_pstr")
    ps_o = p.psum([128, 2, 512], F32, "lf_pso")

    s0 = p.sem()
    n0 = 0
    for dc in range(8):
        sy.dma_start(out=outw_sb[:, ds_(dc, 1024)], in_=w["l1f_outwT_bf"][ts(dc, 128), :]).then_inc(s0, 16)
        n0 += 16
    sy.dma_start(out=outb_row[:], in_=w["l1f_outb_row"]).then_inc(s0, 16)
    n0 += 16
    sy.wait_ge(s0, n0)
    g.memset(v_sb[:].rearrange("p (c w) -> p c w", c=16)[:, :, 64:65], 1.0)
    g.memset(k_aug[64:65, :], 1.0)
    p.barrier()

    z0_r = dram["z0"].rearrange("(n p) d -> n p d", p=128)
    z1_r = dram["z1"].rearrange("(n p) d -> n p d", p=128)

    for b in range(B_LOC):
        mkb = p.scoped()
        ld = p.sem(); hrdy = p.sem()
        mmn = p.sem(); s_sq = p.sem(); s_km = p.sem(); s_rt = p.sem(); vbr = p.sem()
        sq1 = p.sem(); sq2 = p.sem()
        mme = p.sem(); eve = [p.sem(), p.sem()]; mmpv = p.sem(); v_rc = p.sem(); evpv = p.sem()
        r_sy_pv = p.reg(sy, 0); r_sy_ld = p.reg(sy, 0)
        r_s_hr = p.reg(sc, 0)
        r_t_sq1 = p.reg(t, 0); r_t_sq2 = p.reg(t, 0); r_t_nc = p.reg(t, 0)
        r_t_nc2 = p.reg(t, 0); r_v_ssq = p.reg(v, 0); r_s_vbr = p.reg(sc, 0)
        r_s_mmn = p.reg(sc, 0); r_v_mmn = p.reg(v, 0)
        r_v_skc = p.reg(v, 0); r_s_km = p.reg(sc, 0); r_v_rt = p.reg(v, 0)
        r_s_nq = p.reg(sc, 0)
        r_t_vbr = p.reg(t, 0)
        r_t_eve = [p.reg(t, 0), p.reg(t, 0)]; r_s_mme = p.reg(sc, 0); r_s_pv = p.reg(sc, 0)
        r_t_pve = [p.reg(t, 1), p.reg(t, 1)]; r_t_evp = p.reg(t, 0)
        r_v_pv = p.reg(v, 0); r_s_rc = p.reg(sc, 0)
        with nc.Fori(0, 16) as h:
            sy.wait_ge(mmpv, r_sy_pv)
            sy.reg_add(r_sy_pv, r_sy_pv, 16)
            sy.dma_start(out=q_aug[0:64, :], in_=dram["qT"][b, bass.ds(h * 64, 64), :]).then_inc(ld, 16)
            sy.dma_start(out=k_aug[0:64, :], in_=dram["kT"][b, bass.ds(h * 64, 64), :]).then_inc(ld, 16)
            sy.dma_start(out=v_sb[:].rearrange("p (c w) -> p c w", c=16)[:, :, 0:64],
                         in_=dram["v_tok"][b][:, bass.ds(h * 64, 64)].rearrange("(c p) w -> p c w", p=128)).then_inc(ld, 16)
            sy.reg_add(r_sy_ld, r_sy_ld, 48)
            sy.wait_ge(ld, r_sy_ld)
            sy.sem_inc(hrdy, 1)
            sc.reg_add(r_s_hr, r_s_hr, 1)
            sc.wait_ge(hrdy, r_s_hr)
            sc.activation(out=sqv[:, 0, :], in_=q_aug[0:64, :], func=AF.Square).then_inc(sq1, 1)
            sc.activation(out=sqv[:, 1, :], in_=k_aug[0:64, :], func=AF.Square).then_inc(sq2, 1)
            t.reg_add(r_t_sq1, r_t_sq1, 1)
            t.wait_ge(sq1, r_t_sq1)
            t.reg_add(r_t_sq2, r_t_sq2, 1)
            t.wait_ge(sq2, r_t_sq2)
            for stq in range(4):
                t.wait_ge(s_sq, r_t_nc)
                t.wait_ge(s_km, r_t_nc2)
                t.reg_add(r_t_nc, r_t_nc, 1)
                t.reg_add(r_t_nc2, r_t_nc2, 1)
                t.matmul(ps_n[0:1, :], consts["ones64c"], sqv[0:64, 0, :][:, ts(stq, 512)],
                         start=True, stop=True)
                t.matmul(ps_n[32:33, :], consts["ones64c"], sqv[0:64, 1, :][:, ts(stq, 512)],
                         start=True, stop=True).then_inc(mmn, 1)
                sc.reg_add(r_s_mmn, r_s_mmn, 1)
                sc.wait_ge(mmn, r_s_mmn)
                if stq == 0:
                    sc.wait_ge(vbr, r_s_vbr)
                    sc.reg_add(r_s_vbr, r_s_vbr, 1)
                sc.activation(out=nrm[0:1, ts(stq, 512)], in_=ps_n[0:1, :],
                              func=AF.Sqrt).then_inc(s_sq, 1)
                v.reg_add(r_v_mmn, r_v_mmn, 1)
                v.wait_ge(mmn, r_v_mmn)
                v.tensor_reduce(nrm[0:1, S + stq:S + stq + 1], ps_n[32:33, :], AX.X,
                                ALU.max).then_inc(s_km, 1)
            t.reg_add(r_t_nc2, r_t_nc2, 1)
            v.reg_add(r_v_skc, r_v_skc, 4)
            v.wait_ge(s_km, r_v_skc)
            v.tensor_reduce(nrm[0:1, S + 8:S + 9], nrm[0:1, S:S + 4], AX.X,
                            ALU.max).then_inc(s_km, 1)
            v.reg_add(r_v_skc, r_v_skc, 1)
            sc.reg_add(r_s_km, r_s_km, 5)
            sc.wait_ge(s_km, r_s_km)
            sc.activation(out=nrm[0:1, S + 9:S + 10], in_=nrm[0:1, S + 8:S + 9],
                          func=AF.Sqrt).then_inc(s_rt, 1)
            v.reg_add(r_v_rt, r_v_rt, 1)
            v.wait_ge(s_rt, r_v_rt)
            v.reg_add(r_v_ssq, r_v_ssq, 4)
            v.wait_ge(s_sq, r_v_ssq)
            v.tensor_scalar(q_aug[64:65, :], nrm[0:1, 0:S], nrm[0:1, S + 9:S + 10],
                            -1.0, ALU.mult, ALU.mult).then_inc(vbr, 1)
            t.reg_add(r_t_vbr, r_t_vbr, 1)
            t.wait_ge(vbr, r_t_vbr)
            for qg in range(4):
                sc.wait_ge(mmpv, r_s_pv)
                for kc in range(16):
                    t.wait_ge(eve[kc % 2], r_t_eve[kc % 2])
                    t.reg_add(r_t_eve[kc % 2], r_t_eve[kc % 2], 1)
                    t.matmul(ps_e[:, kc % 2, :], k_aug[0:65, ts(kc, 128)],
                             q_aug[0:65, ts(qg, 512)], start=True, stop=True).then_inc(mme, 1)
                    sc.reg_add(r_s_mme, r_s_mme, 1)
                    sc.wait_ge(mme, r_s_mme)
                    sc.activation(out=expT[:, ds_(kc, 512)], in_=ps_e[:, kc % 2, :],
                                  func=AF.Exp).then_inc(eve[kc % 2], 1)
                sc.reg_add(r_s_pv, r_s_pv, 4)
                t.wait_ge(evpv, r_t_evp)
                t.reg_add(r_t_evp, r_t_evp, 4)
                for kc in range(16):
                    t.wait_ge(eve[kc % 2], r_t_pve[kc % 2])
                    t.reg_add(r_t_pve[kc % 2], r_t_pve[kc % 2], 1)
                    for qs in range(4):
                        ins = t.matmul(ps_pv[:, qs, :], expT[:, kc * 512 + qs * 128: kc * 512 + (qs + 1) * 128],
                                       v_sb[:, ds_(kc, 65)],
                                       start=(kc == 0 and qs == 0),
                                       stop=(kc == 15 and qs == 3))
                        if kc == 15:
                            ins.then_inc(mmpv, 1)
                v.reg_add(r_v_pv, r_v_pv, 4)
                v.wait_ge(mmpv, r_v_pv)
                for qs in range(4):
                    v.reciprocal(rcp[:, qs:qs + 1], ps_pv[:, qs, 64:65]).then_inc(v_rc, 1)
                sc.reg_add(r_s_rc, r_s_rc, 4)
                sc.wait_ge(v_rc, r_s_rc)
                for qs in range(4):
                    sc.activation(out=attnout[:, bass.ds((qg * 4 + qs) * 1024 + h * 64, 64)],
                                  in_=ps_pv[:, qs, 0:64], func=AF.Identity,
                                  scale=rcp[:, qs:qs + 1]).then_inc(evpv, 1)
        p.barrier()

        ldz = [p.sem(), p.sem()]
        zrdy = p.sem(); cpa = p.sem(); tpa = p.sem(); cpt = p.sem()
        mmo = p.sem(); evz = p.sem(); stz = p.sem(); szf = p.sem()
        r_ldz = [p.reg(sy, 16), p.reg(sy, 0)]
        r_sy_ev = p.reg(sy, 0)
        r_sy_st = p.reg(sy, 0)
        r_v_mmo = p.reg(v, 0)
        r_t_cpa = p.reg(t, 0); r_v_tpa = p.reg(v, 0); r_t_cpt = p.reg(t, 0)
        r_v_zr = p.reg(v, 0); r_v_szf = p.reg(v, 0)
        r_t_evz = p.reg(t, 0)
        sy.sem_inc(szf, 2)
        sy.dma_start(out=zt_f[:, 0, :], in_=z0_r[b * 16]).then_inc(ldz[0], 16)
        with nc.Fori(0, 8) as th:
            for par in (0, 1):
                qt = th * 2 + par
                nxt = 1 - par
                sy.dma_start(out=zt_f[:, nxt, :], in_=z0_r[smin(b * 16 + qt + 1, NT - 1)]).then_inc(ldz[nxt], 16)
                sy.reg_add(r_ldz[nxt], r_ldz[nxt], 16)
                sy.wait_ge(ldz[par], r_ldz[par])
                sy.sem_inc(zrdy, 1)
                v.wait_ge(mmo, r_v_mmo)
                v.tensor_copy(astage[:], attnout[:, bass.ds(qt * 1024, 1024)]).then_inc(cpa, 1)
                t.reg_add(r_t_cpa, r_t_cpa, 1)
                t.wait_ge(cpa, r_t_cpa)
                t.wait_ge(cpt, r_t_cpt)
                for c in range(8):
                    ins = t.transpose(ps_tr[:, c, :], astage[:, ts(c, 128)], consts["ident_bf"])
                ins.then_inc(tpa, 1)
                v.reg_add(r_v_tpa, r_v_tpa, 1)
                v.wait_ge(tpa, r_v_tpa)
                v.tensor_copy(aT_st[:], ps_tr[:].rearrange("p c f -> p (c f)")).then_inc(cpt, 1)
                t.reg_add(r_t_cpt, r_t_cpt, 1)
                t.wait_ge(cpt, r_t_cpt)
                t.wait_ge(evz, r_t_evz)
                for half in (0, 1):
                    for c in range(8):
                        t.matmul(ps_o[:, half, :], aT_st[:, ts(c, 128)],
                                 outw_sb[:, c * 1024 + half * 512: c * 1024 + (half + 1) * 512],
                                 start=(c == 0), stop=False)
                    ins = t.matmul(ps_o[:, half, :], consts["ones_row"],
                                   outb_row[0:1, half * 512:(half + 1) * 512],
                                   start=False, stop=True)
                ins.then_inc(mmo, 1)
                t.reg_add(r_t_evz, r_t_evz, 1)
                v.reg_add(r_v_mmo, r_v_mmo, 1)
                v.wait_ge(mmo, r_v_mmo)
                v.reg_add(r_v_zr, r_v_zr, 1)
                v.wait_ge(zrdy, r_v_zr)
                v.wait_ge(szf, r_v_szf)
                v.reg_add(r_v_szf, r_v_szf, 1)
                v.tensor_tensor(z1_t[:, par, :], ps_o[:].rearrange("p c f -> p (c f)"),
                                zt_f[:, par, :], ALU.add).then_inc(evz, 1)
                sy.reg_add(r_sy_ev, r_sy_ev, 1)
                sy.wait_ge(evz, r_sy_ev)
                sy.dma_start(out=z1_r[b * 16 + qt], in_=z1_t[:, par, :]).then_inc(stz, 16)
                sy.reg_add(r_sy_st, r_sy_st, 16)
                sy.wait_ge(stz, r_sy_st)
                sy.sem_inc(szf, 1)
        sy.wait_ge(ldz[0], r_ldz[0])
        sy.wait_ge(ldz[1], r_ldz[1])
        p.barrier()
        p.free_scope(mkb)
    p.free_scope(mk)
    p.barrier()


def emit_l1ffn_moe(p, dram, w, consts):
    nc = p.nc
    v, sc, g, t, sy = nc.vector, nc.scalar, nc.gpsimd, nc.tensor, nc.sync
    mk = p.scoped()
    gstage = p.sbuf([128, 2, 1024], BF16, "lm_gstg")
    gw_sb = p.sbuf([128, 8 * 8], BF16, "lm_gw")
    gsc = p.sbuf([128, 48], F32, "lm_gsc")
    comb_sb = p.sbuf([128, 64 * 8], F32, "lm_comb")
    w1_sb = p.sbuf([128, 8 * 2048], BF16, "lm_w1")
    w2_sb = p.sbuf([128, 8 * 2048], BF16, "lm_w2")
    w3_sb = p.sbuf([128, 16 * 1024], BF16, "lm_w3")
    b1_sb = p.sbuf([128, 16], F32, "lm_b1")
    b2_sb = p.sbuf([128, 16], F32, "lm_b2")
    b3_row = p.sbuf([1, 1024], BF16, "lm_b3")
    zst = p.sbuf([128, 2, 8 * 512], BF16, "lm_zst")
    s1_bf = p.sbuf([128, 2, 512], BF16, "lm_s1")
    s2_bf = p.sbuf([128, 2, 512], BF16, "lm_s2")
    hh_sb = p.sbuf([128, 16 * 512], BF16, "lm_hh")
    hstage = p.sbuf([128, 2, 16 * 128], BF16, "lm_hstg")
    yt = p.sbuf([128, 4, 1024], F32, "lm_yt")
    xt = p.sbuf([128, 2, 1024], F32, "lm_xt")
    xo = p.sbuf([128, 2, 1024], F32, "lm_xo")
    ps1 = p.psum([128, 2, 512], F32, "lm_ps1")
    ps2 = p.psum([128, 2, 512], F32, "lm_ps2")
    ps3 = p.psum([128, 2, 2, 512], F32, "lm_ps3")
    ps_g = ps1[:, 0, 0:8]

    s0 = p.sem()
    sy.dma_start(out=gw_sb[:].rearrange("p (c e) -> p c e", c=8),
                 in_=w["l1f_gatewT_bf"].rearrange("(c p) e -> p c e", p=128)).then_inc(s0, 16)
    sy.wait_ge(s0, 16)
    p.barrier()

    ldg = [p.sem(), p.sem()]
    grdy = p.sem(); mmg = p.sem(); sv = p.sem(); sg = p.sem(); ss = p.sem()
    r_ldg = [p.reg(sy, 0), p.reg(sy, 0)]
    r_sy_g = p.reg(sy, 0)
    r_t_gr = p.reg(t, 0); r_t_sv = p.reg(t, 0)
    r_v_mmg = p.reg(v, 0); r_vs = p.reg(v, 1); r_v_ss = p.reg(v, 0); r_v_sg2 = p.reg(v, 0)
    r_g_sv = p.reg(g, 10); r_s_sv = p.reg(sc, 7)
    zr = dram["zlnT"]
    LG, OH1, LGM, OH2, WS = 0, 8, 16, 24, 32
    M1, M2, DM, G2, G1 = 40, 41, 42, 43, 44
    with nc.Fori(0, NT // 2) as th:
        for par in (0, 1):
            tl = th * 2 + par
            sy.wait_ge(mmg, r_sy_g)
            sy.reg_add(r_sy_g, r_sy_g, 1)
            sy.dma_start(out=gstage[:, par, :].rearrange("p (c q) -> p c q", c=8),
                         in_=zr.rearrange("(c p) t -> p c t", p=128)[:, :, bass.ds(tl * 128, 128)]).then_inc(ldg[par], 16)
            sy.reg_add(r_ldg[par], r_ldg[par], 16)
            sy.wait_ge(ldg[par], r_ldg[par])
            sy.sem_inc(grdy, 1)
            t.reg_add(r_t_gr, r_t_gr, 1)
            t.wait_ge(grdy, r_t_gr)
            t.wait_ge(sv, r_t_sv)
            t.reg_add(r_t_sv, r_t_sv, 10)
            for dc in range(8):
                ins = t.matmul(ps_g, gstage[:, par, ts(dc, 128)], gw_sb[:, ds_(dc, 8)],
                               start=(dc == 0), stop=(dc == 7))
            ins.then_inc(mmg, 1)
            v.reg_add(r_v_mmg, r_v_mmg, 1); v.wait_ge(mmg, r_v_mmg)
            v.tensor_copy(gsc[:, LG:LG + 8], ps_g).then_inc(sv, 1)
            v.wait_ge(sv, r_vs)
            v.tensor_reduce(gsc[:, M1:M1 + 1], gsc[:, LG:LG + 8], AX.X, ALU.max).then_inc(sv, 1)
            v.reg_add(r_vs, r_vs, 1); v.wait_ge(sv, r_vs)
            v.tensor_scalar(gsc[:, OH1:OH1 + 8], gsc[:, LG:LG + 8], gsc[:, M1:M1 + 1],
                            None, ALU.is_equal).then_inc(sv, 1)
            v.reg_add(r_vs, r_vs, 1); v.wait_ge(sv, r_vs)
            v.scalar_tensor_tensor(gsc[:, LGM:LGM + 8], gsc[:, OH1:OH1 + 8], -1e30,
                                   gsc[:, LG:LG + 8], ALU.mult, ALU.add).then_inc(sv, 1)
            v.reg_add(r_vs, r_vs, 1); v.wait_ge(sv, r_vs)
            v.tensor_reduce(gsc[:, M2:M2 + 1], gsc[:, LGM:LGM + 8], AX.X, ALU.max).then_inc(sv, 1)
            v.reg_add(r_vs, r_vs, 1); v.wait_ge(sv, r_vs)
            v.tensor_scalar(gsc[:, OH2:OH2 + 8], gsc[:, LGM:LGM + 8], gsc[:, M2:M2 + 1],
                            None, ALU.is_equal).then_inc(sv, 1)
            v.reg_add(r_vs, r_vs, 1); v.wait_ge(sv, r_vs)
            v.tensor_scalar(gsc[:, DM:DM + 1], gsc[:, M2:M2 + 1], gsc[:, M1:M1 + 1],
                            None, ALU.subtract).then_inc(sv, 1)
            sc.wait_ge(sv, r_s_sv)
            sc.reg_add(r_s_sv, r_s_sv, 10)
            sc.activation(out=gsc[:, G2:G2 + 1], in_=gsc[:, DM:DM + 1],
                          func=AF.Sigmoid).then_inc(ss, 1)
            v.reg_add(r_v_ss, r_v_ss, 1); v.wait_ge(ss, r_v_ss)
            v.tensor_scalar(gsc[:, G1:G1 + 1], gsc[:, G2:G2 + 1], -1.0, 1.0,
                            ALU.mult, ALU.add).then_inc(sv, 1)
            v.reg_add(r_vs, r_vs, 2); v.wait_ge(sv, r_vs)
            v.tensor_scalar(gsc[:, LGM:LGM + 8], gsc[:, OH2:OH2 + 8], gsc[:, G2:G2 + 1],
                            None, ALU.mult).then_inc(sv, 1)
            v.reg_add(r_vs, r_vs, 1); v.wait_ge(sv, r_vs)
            v.wait_ge(sg, r_v_sg2)
            v.reg_add(r_v_sg2, r_v_sg2, 1)
            v.scalar_tensor_tensor(gsc[:, WS:WS + 8], gsc[:, OH1:OH1 + 8], gsc[:, G1:G1 + 1],
                                   gsc[:, LGM:LGM + 8], ALU.mult, ALU.add).then_inc(sv, 1)
            g.wait_ge(sv, r_g_sv)
            g.reg_add(r_g_sv, r_g_sv, 10)
            g.tensor_copy(comb_sb[:, bass.ds(tl * 8, 8)], gsc[:, WS:WS + 8]).then_inc(sg, 1)
            v.reg_add(r_vs, r_vs, 2)
    p.barrier()

    cps = p.sem(); cpr = p.sem()
    r_cps = p.reg(sy, 0)
    z1r_ = dram["z1"].rearrange("(n p) d -> n p d", p=128)
    yar_ = dram["yacc"].rearrange("(n p) d -> n p d", p=128)
    with nc.Fori(0, NT // 2) as th:
        for par in (0, 1):
            tl = th * 2 + par
            sy.dma_start(out=yt[:, par, :], in_=z1r_[tl]).then_inc(cps, 16)
            sy.reg_add(r_cps, r_cps, 16)
            sy.wait_ge(cps, r_cps)
            sy.dma_start(out=yar_[tl], in_=yt[:, par, :]).then_inc(cps, 16)
            sy.reg_add(r_cps, r_cps, 16)
            sy.wait_ge(cps, r_cps)
    p.barrier()

    lde = p.sem(); wrdy = p.sem(); mmw = p.sem(); mm12 = p.sem()
    s1d = p.sem(); s2d = p.sem(); hhd = p.sem(); cph = p.sem(); mm3 = p.sem()
    evy = p.sem(); yrdy = p.sem(); sty = p.sem(); ldy = p.sem()
    r_sy_lde = p.reg(sy, 0)
    r_sy_w = p.reg(sy, 0)
    r_s_hh = p.reg(sc, 0); r_v_hh2 = p.reg(v, 0)
    r_sty_c = p.reg(sy, 16)
    r_sc_ev = p.reg(sc, 1); r_sc_styw = p.reg(sc, 16)
    r_s_12 = p.reg(sc, 32); r_v_12 = p.reg(v, 32); r_g_12 = p.reg(g, 0)
    r_t_s1 = p.reg(t, 0); r_t_s2 = p.reg(t, 0)
    r_g_m3 = p.reg(g, 0)
    r_v_hh = p.reg(v, 0); r_v_m3 = p.reg(v, 0)
    r_t_cph = p.reg(t, 0); r_t_evy = p.reg(t, 0)
    r_v_ldy = p.reg(v, 0); r_v_3 = p.reg(v, 1)
    r_sy_sty = p.reg(sy, 1)
    yr = dram["yacc"].rearrange("(n p) d -> n p d", p=128)
    v.sem_inc(mm12, 32)
    v.sem_inc(evy, 1)
    v.sem_inc(sty, 16)
    v.sem_inc(mm3, 1)
    with nc.Fori(0, E) as e:
        nc.all_engine_barrier()
        nl = 0
        for dc in range(8):
            sy.dma_start(out=w1_sb[:, ds_(dc, 2048)],
                         in_=w["l1f_w1T_bf"][e, bass.ds(dc * 128, 128), :]).then_inc(lde, 16)
            sy.dma_start(out=w2_sb[:, ds_(dc, 2048)],
                         in_=w["l1f_w2T_bf"][e, bass.ds(dc * 128, 128), :]).then_inc(lde, 16)
            nl += 32
        for hc in range(16):
            sy.dma_start(out=w3_sb[:, ds_(hc, 1024)],
                         in_=w["l1f_w3T_bf"][e, bass.ds(hc * 128, 128), :]).then_inc(lde, 16)
            nl += 16
        sy.dma_start(out=b1_sb[:], in_=w["l1f_b1"][e]).then_inc(lde, 16)
        sy.dma_start(out=b2_sb[:], in_=w["l1f_b2"][e]).then_inc(lde, 16)
        sy.dma_start(out=b3_row[:], in_=w["l1f_b3_bf"][bass.ds(e, 1), :]).then_inc(lde, 16)
        nl += 48
        sy.reg_add(r_sy_lde, r_sy_lde, nl)
        sy.wait_ge(lde, r_sy_lde)
        nc.all_engine_barrier()
        with nc.Fori(0, T // 512) as st:
            stp = st % 2
            sy.wait_ge(mm12, r_sy_w)
            sy.reg_add(r_sy_w, r_sy_w, 32)
            sy.dma_start(out=zst[:, stp, :].rearrange("p (c q) -> p c q", c=8),
                         in_=zr.rearrange("(c p) t -> p c t", p=128)[:, :, bass.ds(st * 512, 512)]).then_inc(ldg[0], 16)
            sy.reg_add(r_ldg[0], r_ldg[0], 16)
            sy.wait_ge(ldg[0], r_ldg[0])
            sy.sem_inc(grdy, 1)
            t.reg_add(r_t_gr, r_t_gr, 1)
            t.wait_ge(grdy, r_t_gr)
            for fc in range(16):
                fp = fc % 2
                t.wait_ge(s1d, r_t_s1)
                t.reg_add(r_t_s1, r_t_s1, 1)
                for dc in range(8):
                    ins = t.matmul(ps1[:, fp, :], w1_sb[:, dc * 2048 + fc * 128: dc * 2048 + (fc + 1) * 128],
                                   zst[:, stp, bass.ds(dc * 512, 512)],
                                   start=(dc == 0), stop=(dc == 7))
                ins.then_inc(mm12, 1)
                t.wait_ge(s2d, r_t_s2)
                t.reg_add(r_t_s2, r_t_s2, 1)
                for dc in range(8):
                    ins = t.matmul(ps2[:, fp, :], w2_sb[:, dc * 2048 + fc * 128: dc * 2048 + (fc + 1) * 128],
                                   zst[:, stp, bass.ds(dc * 512, 512)],
                                   start=(dc == 0), stop=(dc == 7))
                ins.then_inc(mm12, 1)
                sc.reg_add(r_s_12, r_s_12, 2)
                sc.wait_ge(mm12, r_s_12)
                sc.wait_ge(hhd, r_s_hh)
                sc.reg_add(r_s_hh, r_s_hh, 1)
                sc.activation(out=s1_bf[:, fp, :], in_=ps1[:, fp, :], func=AF.Silu,
                              bias=b1_sb[:, fc:fc + 1]).then_inc(s1d, 1)
                v.reg_add(r_v_12, r_v_12, 2)
                v.wait_ge(mm12, r_v_12)
                v.wait_ge(hhd, r_v_hh2)
                v.reg_add(r_v_hh2, r_v_hh2, 1)
                v.tensor_scalar(s2_bf[:, fp, :], ps2[:, fp, :], b2_sb[:, fc:fc + 1],
                                None, ALU.add).then_inc(s2d, 1)
                g.reg_add(r_g_12, r_g_12, 1)
                g.wait_ge(s1d, r_g_12 + 1 if False else r_g_12)
                g.wait_ge(s2d, r_g_12)
                if fc == 0:
                    g.wait_ge(cph, r_g_m3)
                    g.reg_add(r_g_m3, r_g_m3, 4)
                g.tensor_tensor(hh_sb[:, ds_(fc, 512)], s1_bf[:, fp, :], s2_bf[:, fp, :],
                                ALU.mult).then_inc(hhd, 1)
                if fc == 12:
                    sc.wait_ge(evy, r_sc_ev)
                    sc.reg_add(r_sc_ev, r_sc_ev, 4)
                    sc.wait_ge(sty, r_sc_styw)
                    sc.reg_add(r_sc_styw, r_sc_styw, 64)
                if fc >= 12:
                    sc.dma_start(out=yt[:, fc - 12, :],
                                 in_=yr[st * 4 + (fc - 12)]).then_inc(ldy, 16)
            for tokt in range(4):
                if tokt == 0:
                    v.reg_add(r_v_hh, r_v_hh, 16)
                v.wait_ge(hhd, r_v_hh)
                v.wait_ge(mm3, r_v_m3)
                v.reg_add(r_v_m3, r_v_m3, 1)
                v.tensor_copy(hstage[:, tokt % 2, :],
                              hh_sb[:].rearrange("p (c q) -> p c q", c=16)[:, :, bass.ds(tokt * 128, 128)]).then_inc(cph, 1)
                t.reg_add(r_t_cph, r_t_cph, 1)
                t.wait_ge(cph, r_t_cph)
                t.wait_ge(evy, r_t_evy)
                t.reg_add(r_t_evy, r_t_evy, 1)
                for half in (0, 1):
                    t.matmul(ps3[:, tokt % 2, half, :], consts["ones_row"],
                             b3_row[0:1, ts(half, 512)], start=True, stop=False)
                    for hc in range(16):
                        ins = t.matmul(ps3[:, tokt % 2, half, :], hstage[:, tokt % 2, ts(hc, 128)],
                                       w3_sb[:, hc * 1024 + half * 512: hc * 1024 + (half + 1) * 512],
                                       start=False, stop=(hc == 15))
                ins.then_inc(mm3, 1)
                if tokt == 0:
                    v.reg_add(r_v_ldy, r_v_ldy, 64)
                v.wait_ge(ldy, r_v_ldy)
                v.reg_add(r_v_3, r_v_3, 1)
                v.wait_ge(mm3, r_v_3)
                v.scalar_tensor_tensor(yt[:, tokt, 0:512], ps3[:, tokt % 2, 0, :],
                                       comb_sb[:, bass.ds((st * 4 + tokt) * 8 + e, 1)],
                                       yt[:, tokt, 0:512], ALU.mult, ALU.add)
                v.scalar_tensor_tensor(yt[:, tokt, 512:1024], ps3[:, tokt % 2, 1, :],
                                       comb_sb[:, bass.ds((st * 4 + tokt) * 8 + e, 1)],
                                       yt[:, tokt, 512:1024], ALU.mult, ALU.add).then_inc(evy, 1)
                sy.reg_add(r_sy_sty, r_sy_sty, 1)
                sy.wait_ge(evy, r_sy_sty)
                sy.dma_start(out=yr[st * 4 + tokt], in_=yt[:, tokt, :]).then_inc(sty, 16)
                sy.reg_add(r_sty_c, r_sty_c, 16)
        sy.wait_ge(sty, r_sty_c)
        nc.all_engine_barrier()
    p.barrier()

    ldf = [p.sem(), p.sem()]
    frdy = p.sem(); evf = p.sem(); stf = p.sem(); sff = p.sem()
    r_ldf = [p.reg(sy, 0), p.reg(sy, 0)]
    r_sy_e = p.reg(sy, 0); r_sy_sf = p.reg(sy, 0)
    r_v_fr = p.reg(v, 0); r_v_sf = p.reg(v, 0)
    xi_r = dram["x_in"].rearrange("(n p) d -> n p d", p=128)
    xo_r = dram["x_out"].rearrange("(n p) d -> n p d", p=128)
    sy.sem_inc(sff, 2)
    with nc.Fori(0, NT // 2) as th:
        for par in (0, 1):
            tl = th * 2 + par
            sy.wait_ge(evf, r_sy_e)
            sy.dma_start(out=xt[:, par, :], in_=xi_r[tl]).then_inc(ldf[par], 16)
            sy.dma_start(out=yt[:, par, :], in_=yr[tl]).then_inc(ldf[par], 16)
            sy.reg_add(r_ldf[par], r_ldf[par], 32)
            sy.wait_ge(ldf[par], r_ldf[par])
            sy.sem_inc(frdy, 1)
            v.reg_add(r_v_fr, r_v_fr, 1)
            v.wait_ge(frdy, r_v_fr)
            v.wait_ge(sff, r_v_sf)
            v.reg_add(r_v_sf, r_v_sf, 1)
            v.scalar_tensor_tensor(xo[:, par, :], yt[:, par, :], 0.5, xt[:, par, :],
                                   ALU.mult, ALU.add).then_inc(evf, 1)
            sy.reg_add(r_sy_e, r_sy_e, 1)
            sy.wait_ge(evf, r_sy_e)
            sy.dma_start(out=xo_r[tl], in_=xo[:, par, :]).then_inc(stf, 16)
            sy.reg_add(r_sy_sf, r_sy_sf, 16)
            sy.wait_ge(stf, r_sy_sf)
            sy.sem_inc(sff, 1)
    p.barrier()
    p.free_scope(mk)
    p.barrier()


def emit_l1ffn(p, dram, w, consts):
    import os
    ph = int(os.environ.get("L1F_PHASES", "9"))
    emit_norm_pass(p, dram["x_in"], dram["z0"], None, None, "ln")
    emit_norm_pass(p, dram["z0"], None, None, dram["zlnT"], "rms",
                   gamma_rep=w["l1f_n1"], ident_bf=consts["ident_bf"], eps=RMSEPS)
    if ph < 2:
        return
    emit_l1ffn_qkv(p, dram, w, consts)
    if ph < 3:
        return
    emit_l1ffn_attn(p, dram, w, consts)
    if ph < 4:
        return
    emit_norm_pass(p, dram["z1"], None, None, dram["zlnT"], "rms",
                   gamma_rep=w["l1f_n2"], ident_bf=consts["ident_bf"], eps=RMSEPS)
    if ph < 5:
        return
    emit_l1ffn_moe(p, dram, w, consts)



def emit_l0ffn(p, dram, w, consts, nocc=False):
    nc = p.nc
    v, sc, g, t, sy = nc.vector, nc.scalar, nc.gpsimd, nc.tensor, nc.sync
    mk = p.scoped()
    gstage = p.sbuf([128, 2, 1024], BF16, "f0_gstg")
    rw_sb = p.sbuf([128, 8 * 8], BF16, "f0_rw")
    gsc = p.sbuf([128, 64], F32, "f0_gsc")
    Sall = p.sbuf([128, NT * 8], BF16, "f0_S")
    prefx = p.sbuf([128, NT * 8], F32, "f0_pref")
    oh1s = p.sbuf([128, NT * 8], F32, "f0_oh1")
    oh2s = p.sbuf([128, NT * 8], F32, "f0_oh2")
    g12s = p.sbuf([128, NT * 2], F32, "f0_g12")
    comb_sb = p.sbuf([128, NT * 8], F32, "f0_comb")
    totrow = p.sbuf([1, NT * 8], F32, "f0_tot")
    tri_bf = p.sbuf([128, 128], BF16, "f0_tri")
    onescol = p.sbuf([128, 1], BF16, "f0_onesc")
    onesrow_f = p.sbuf([1, 128], F32, "f0_onesrf")
    ident8 = p.sbuf([8, 8], F32, "f0_id8")
    onestile = p.sbuf([8, NT], F32, "f0_onet")
    csum = p.sbuf([8, NT + 2], F32, "f0_csum")
    tot8 = p.sbuf([8, NT], F32, "f0_tot8")
    corehot = p.sbuf([1, 64], F32, "f0_chot")
    coremask = p.sbuf([64, 1], F32, "f0_cmask")
    allc = p.sbuf([64, 8], F32, "f0_allc")
    coff = p.sbuf([1, 8], F32, "f0_coff")
    offs_st = p.sbuf([64, 8], F32, "f0_offs")
    totr8 = p.sbuf([1, 8], F32, "f0_totr8")
    arbuf = p.sbuf([64, 1024], F32, "f0_ar")
    offrow = p.sbuf([1, 8], F32, "f0_offrow")
    s0 = p.sem()
    n0 = 0
    g.dma_start(out=tri_bf[:], in_=w["tri"]).then_inc(s0, 16)
    g.dma_start(out=ident8[:], in_=w["ident"][0:8, 0:8]).then_inc(s0, 16)
    g.dma_start(out=corehot[:], in_=w["core_onehot"]).then_inc(s0, 16)
    g.dma_start(out=coremask[:], in_=w["core_mask"]).then_inc(s0, 16)
    g.dma_start(out=rw_sb[:].rearrange("p (c e) -> p c e", c=8),
                in_=w["l0f_router_bf"].rearrange("(c p) e -> p c e", p=128)).then_inc(s0, 16)
    n0 = 5 * 16
    g.wait_ge(s0, n0)
    g.memset(onescol[:], 1.0)
    g.memset(onesrow_f[:], 1.0)
    g.memset(onestile[:], 1.0)
    g.memset(csum[:, 0:1], 0.0)
    g.memset(arbuf[:], 0.0)
    p.barrier()
    mk_route = p.scoped()
    ps_g = p.psum([128, 8], F32, "f0_psg")
    ps_pf = p.psum([128, 8], F32, "f0_pspf")
    ps_t = p.psum([1, 8], F32, "f0_pst")
    ps_o8 = p.psum([64, 8], F32, "f0_pso8")
    ps_off = p.psum([128, 8], F32, "f0_psoff")

    zr = dram["zlnT"]
    LG, OH1, LGM, OH2, WS = 0, 8, 16, 24, 32
    M1, M2, DM, G2, G1 = 40, 41, 42, 43, 44

    ldg = [p.sem(), p.sem()]
    grdy = p.sem(); mmg = p.sem(); sv = p.sem(); sg = p.sem(); ss = p.sem()
    mmp = p.sem()
    r_ldg = [p.reg(sy, 0), p.reg(sy, 0)]
    r_sy_g = p.reg(sy, 0)
    r_t_gr = p.reg(t, 0); r_t_sv = p.reg(t, 0); r_t_sv2 = p.reg(t, 0)
    r_v_mmg = p.reg(v, 0); r_vs = p.reg(v, 1); r_v_ss = p.reg(v, 0); r_v_mmp = p.reg(v, 0)
    r_s_sv = p.reg(sc, 7)
    with nc.Fori(0, NT // 2) as th:
        for par in (0, 1):
            tl = th * 2 + par
            sy.wait_ge(mmg, r_sy_g)
            sy.reg_add(r_sy_g, r_sy_g, 2)
            sy.dma_start(out=gstage[:, par, :].rearrange("p (c q) -> p c q", c=8),
                         in_=zr.rearrange("(c p) t -> p c t", p=128)[:, :, bass.ds(tl * 128, 128)]).then_inc(ldg[par], 16)
            sy.reg_add(r_ldg[par], r_ldg[par], 16)
            sy.wait_ge(ldg[par], r_ldg[par])
            sy.sem_inc(grdy, 1)
            t.reg_add(r_t_gr, r_t_gr, 1)
            t.wait_ge(grdy, r_t_gr)
            t.wait_ge(sv, r_t_sv)
            t.reg_add(r_t_sv, r_t_sv, 12)
            for dc in range(8):
                ins = t.matmul(ps_g[:], gstage[:, par, ts(dc, 128)], rw_sb[:, ds_(dc, 8)],
                               start=(dc == 0), stop=(dc == 7))
            ins.then_inc(mmg, 1)
            v.reg_add(r_v_mmg, r_v_mmg, 1); v.wait_ge(mmg, r_v_mmg)
            v.tensor_copy(gsc[:, LG:LG + 8], ps_g[:]).then_inc(sv, 1)
            v.wait_ge(sv, r_vs)
            v.tensor_reduce(gsc[:, M1:M1 + 1], gsc[:, LG:LG + 8], AX.X, ALU.max).then_inc(sv, 1)
            v.reg_add(r_vs, r_vs, 1); v.wait_ge(sv, r_vs)
            v.tensor_scalar(gsc[:, OH1:OH1 + 8], gsc[:, LG:LG + 8], gsc[:, M1:M1 + 1],
                            None, ALU.is_equal).then_inc(sv, 1)
            v.reg_add(r_vs, r_vs, 1); v.wait_ge(sv, r_vs)
            v.scalar_tensor_tensor(gsc[:, LGM:LGM + 8], gsc[:, OH1:OH1 + 8], -1e30,
                                   gsc[:, LG:LG + 8], ALU.mult, ALU.add).then_inc(sv, 1)
            v.reg_add(r_vs, r_vs, 1); v.wait_ge(sv, r_vs)
            v.tensor_reduce(gsc[:, M2:M2 + 1], gsc[:, LGM:LGM + 8], AX.X, ALU.max).then_inc(sv, 1)
            v.reg_add(r_vs, r_vs, 1); v.wait_ge(sv, r_vs)
            v.tensor_scalar(gsc[:, OH2:OH2 + 8], gsc[:, LGM:LGM + 8], gsc[:, M2:M2 + 1],
                            None, ALU.is_equal).then_inc(sv, 1)
            v.reg_add(r_vs, r_vs, 1); v.wait_ge(sv, r_vs)
            v.tensor_scalar(gsc[:, DM:DM + 1], gsc[:, M2:M2 + 1], gsc[:, M1:M1 + 1],
                            None, ALU.subtract).then_inc(sv, 1)
            sc.wait_ge(sv, r_s_sv)
            sc.reg_add(r_s_sv, r_s_sv, 12)
            sc.activation(out=gsc[:, G2:G2 + 1], in_=gsc[:, DM:DM + 1],
                          func=AF.Sigmoid).then_inc(ss, 1)
            v.reg_add(r_v_ss, r_v_ss, 1); v.wait_ge(ss, r_v_ss)
            v.tensor_scalar(gsc[:, G1:G1 + 1], gsc[:, G2:G2 + 1], -1.0, 1.0,
                            ALU.mult, ALU.add).then_inc(sv, 1)
            v.reg_add(r_vs, r_vs, 2); v.wait_ge(sv, r_vs)
            v.tensor_copy(oh1s[:, bass.ds(tl * 8, 8)], gsc[:, OH1:OH1 + 8]).then_inc(sv, 1)
            v.reg_add(r_vs, r_vs, 1); v.wait_ge(sv, r_vs)
            v.tensor_copy(oh2s[:, bass.ds(tl * 8, 8)], gsc[:, OH2:OH2 + 8]).then_inc(sv, 1)
            v.reg_add(r_vs, r_vs, 1); v.wait_ge(sv, r_vs)
            v.tensor_copy(g12s[:, bass.ds(tl * 2, 2)], gsc[:, G2:G2 + 2]).then_inc(sv, 1)
            v.reg_add(r_vs, r_vs, 1); v.wait_ge(sv, r_vs)
            v.tensor_tensor(Sall[:, bass.ds(tl * 8, 8)], gsc[:, OH1:OH1 + 8],
                            gsc[:, OH2:OH2 + 8], ALU.add).then_inc(sv, 1)
            v.reg_add(r_vs, r_vs, 2)
            t.wait_ge(sv, r_t_sv)
            t.wait_ge(mmp, r_t_sv2)
            t.reg_add(r_t_sv2, r_t_sv2, 2)
            t.matmul(ps_pf[:], tri_bf[:], Sall[:, bass.ds(tl * 8, 8)],
                     start=True, stop=True)
            t.matmul(ps_t[:], onescol[:], Sall[:, bass.ds(tl * 8, 8)],
                     start=True, stop=True).then_inc(mmg, 1)
            v.reg_add(r_v_mmg, r_v_mmg, 1)
            v.wait_ge(mmg, r_v_mmg)
            v.tensor_copy(prefx[:, bass.ds(tl * 8, 8)], ps_pf[:]).then_inc(mmp, 1)
            v.tensor_copy(totrow[0:1, bass.ds(tl * 8, 8)], ps_t[0:1, :]).then_inc(mmp, 1)
    p.barrier()

    so = p.sem()
    sy.dma_start(out=dram["totdram"], in_=totrow[0:1, :]).then_inc(so, 16)
    sy.wait_ge(so, 16)
    sy.dma_start(out=tot8[:], in_=dram["totdram"].rearrange("t e -> e t"),
                 allow_slow_non_contiguous=True).then_inc(so, 16)
    sy.wait_ge(so, 32)
    p.barrier()
    v.tensor_tensor_scan(csum[:, 1:1 + NT], onestile[:], tot8[:], 0.0,
                         ALU.mult, ALU.add)
    p.barrier()
    t.matmul(ps_t[:], csum[0:8, NT:NT + 1], ident8[:], start=True, stop=True)
    p.barrier()
    v.tensor_copy(totr8[:], ps_t[0:1, :])
    p.barrier()
    t.matmul(ps_o8[:], corehot[:], totr8[:], start=True, stop=True)
    p.barrier()
    v.tensor_copy(arbuf[:, 0:8], ps_o8[:])
    p.barrier()
    sy.dma_start(out=dram["cc_in"], in_=arbuf[:]).then_inc(so, 16)
    sy.wait_ge(so, 48)
    if nocc:
        sy.dma_start(out=dram["cc_out"], in_=dram["cc_in"]).then_inc(so, 16)
        sy.wait_ge(so, 64)
    else:
        ccs = p.sem()
        g.collective_compute("AllReduce", ALU.add, replica_groups=[list(range(NCORES))],
                             ins=[dram["cc_in"]], outs=[dram["cc_out"]]).then_inc(ccs)
        g.wait_ge(ccs, 1)
    p.barrier()
    sy.dma_start(out=allc[:], in_=dram["cc_out"][:, 0:8]).then_inc(so, 16)
    sy.wait_ge(so, 80 if nocc else 64)
    p.barrier()
    t.matmul(ps_t[:], coremask[:], allc[:], start=True, stop=True)
    p.barrier()
    v.tensor_copy(coff[:], ps_t[0:1, :])
    p.barrier()
    t.matmul(ps_o8[0:NT, :], csum[0:8, 0:NT], consts["ident_f32"][0:8, 0:8],
             start=True, stop=False)
    t.matmul(ps_o8[0:NT, :], onesrow_f[0:1, 0:NT], coff[:], start=False, stop=True)
    p.barrier()
    v.tensor_copy(offs_st[0:NT, :], ps_o8[0:NT, :])
    p.barrier()
    sy.dma_start(out=dram["offs_dram"], in_=offs_st[0:NT, :]).then_inc(so, 16)
    sy.wait_ge(so, 96 if nocc else 80)
    p.barrier()

    ldo = p.sem(); ordy = p.sem(); mmo = p.sem(); fv = p.sem()
    r_ldo = p.reg(sy, 0); r_sy_fv = p.reg(sy, 0)
    r_t_or = p.reg(t, 0); r_t_fv = p.reg(t, 0)
    r_v_mmo = p.reg(v, 0); r_fv = p.reg(v, 1)
    FK = 11
    with nc.Fori(0, NT) as tl:
        sy.wait_ge(mmo, r_sy_fv)
        sy.reg_add(r_sy_fv, r_sy_fv, 1)
        sy.dma_start(out=offrow[:], in_=dram["offs_dram"][bass.ds(tl, 1), :]).then_inc(ldo, 16)
        sy.reg_add(r_ldo, r_ldo, 16)
        sy.wait_ge(ldo, r_ldo)
        sy.sem_inc(ordy, 1)
        t.reg_add(r_t_or, r_t_or, 1)
        t.wait_ge(ordy, r_t_or)
        t.wait_ge(fv, r_t_fv)
        t.reg_add(r_t_fv, r_t_fv, FK)
        t.matmul(ps_off[:], onesrow_f[:], offrow[:], start=True, stop=True).then_inc(mmo, 1)
        v.reg_add(r_v_mmo, r_v_mmo, 1)
        v.wait_ge(mmo, r_v_mmo)
        v.tensor_tensor(gsc[:, 0:8], prefx[:, bass.ds(tl * 8, 8)], ps_off[:],
                        ALU.add).then_inc(fv, 1)
        v.wait_ge(fv, r_fv)
        v.tensor_tensor(gsc[:, 8:16], gsc[:, 0:8], oh1s[:, bass.ds(tl * 8, 8)],
                        ALU.mult).then_inc(fv, 1)
        v.reg_add(r_fv, r_fv, 1); v.wait_ge(fv, r_fv)
        v.tensor_reduce(gsc[:, 40:41], gsc[:, 8:16], AX.X, ALU.add).then_inc(fv, 1)
        v.reg_add(r_fv, r_fv, 1); v.wait_ge(fv, r_fv)
        v.tensor_tensor(gsc[:, 16:24], gsc[:, 0:8], oh2s[:, bass.ds(tl * 8, 8)],
                        ALU.mult).then_inc(fv, 1)
        v.reg_add(r_fv, r_fv, 1); v.wait_ge(fv, r_fv)
        v.tensor_reduce(gsc[:, 41:42], gsc[:, 16:24], AX.X, ALU.add).then_inc(fv, 1)
        v.reg_add(r_fv, r_fv, 1); v.wait_ge(fv, r_fv)
        v.tensor_scalar(gsc[:, 42:43], gsc[:, 40:41], float(CAP), None, ALU.is_lt).then_inc(fv, 1)
        v.reg_add(r_fv, r_fv, 1); v.wait_ge(fv, r_fv)
        v.tensor_scalar(gsc[:, 43:44], gsc[:, 41:42], float(CAP), None, ALU.is_lt).then_inc(fv, 1)
        v.reg_add(r_fv, r_fv, 1); v.wait_ge(fv, r_fv)
        v.tensor_tensor(gsc[:, 44:45], g12s[:, bass.ds(tl * 2 + 1, 1)], gsc[:, 42:43],
                        ALU.mult).then_inc(fv, 1)
        v.reg_add(r_fv, r_fv, 1); v.wait_ge(fv, r_fv)
        v.tensor_tensor(gsc[:, 45:46], g12s[:, bass.ds(tl * 2, 1)], gsc[:, 43:44],
                        ALU.mult).then_inc(fv, 1)
        v.reg_add(r_fv, r_fv, 1); v.wait_ge(fv, r_fv)
        v.tensor_scalar(gsc[:, 24:32], oh2s[:, bass.ds(tl * 8, 8)], gsc[:, 45:46],
                        None, ALU.mult).then_inc(fv, 1)
        v.reg_add(r_fv, r_fv, 1); v.wait_ge(fv, r_fv)
        v.scalar_tensor_tensor(comb_sb[:, bass.ds(tl * 8, 8)], oh1s[:, bass.ds(tl * 8, 8)],
                               gsc[:, 44:45], gsc[:, 24:32], ALU.mult, ALU.add).then_inc(fv, 1)
        v.reg_add(r_fv, r_fv, 2)
    p.barrier()

    p.free_scope(mk_route)
    yt0 = p.sbuf([128, 4, 1024], F32, "f0_yt")
    xt0 = p.sbuf([128, 2, 1024], F32, "f0_xt")
    xo0 = p.sbuf([128, 2, 1024], F32, "f0_xo")
    zcp = p.sem(); r_zcp = p.reg(sy, 0)
    g.memset(yt0[:, 0, :], 0.0)
    p.barrier()
    yr = dram["yacc"].rearrange("(n p) d -> n p d", p=128)
    with nc.Fori(0, NT) as tl:
        sy.dma_start(out=yr[tl], in_=yt0[:, 0, :]).then_inc(zcp, 16)
        sy.reg_add(r_zcp, r_zcp, 16)
    sy.wait_ge(zcp, r_zcp)
    p.barrier()

    w1_sb = p.sbuf([128, 8 * 2048], BF16, "f0_w1")
    w2_sb = p.sbuf([128, 16 * 1024], BF16, "f0_w2")
    b1_sb = p.sbuf([128, 16], F32, "f0_b1")
    b2_row = p.sbuf([1, 1024], BF16, "f0_b2")
    zst = p.sbuf([128, 2, 8 * 512], BF16, "f0_zst")
    hh_sb = p.sbuf([128, 16 * 512], BF16, "f0_hh")
    hstage = p.sbuf([128, 2, 16 * 128], BF16, "f0_hstg")
    ps1 = p.psum([128, 2, 512], F32, "f0_ps1")
    ps3 = p.psum([128, 2, 2, 512], F32, "f0_ps3")
    for hh in range(2):
        lde = p.sem(); wrdy2 = p.sem(); mm12 = p.sem(); s1d = p.sem()
        hhd = p.sem(); cph = p.sem(); mm3 = p.sem(); evy = p.sem(); yrdy = p.sem()
        sty = p.sem(); zrd = p.sem(); ldz2 = p.sem(); ldy2 = p.sem()
        r_sy_lde = p.reg(sy, 0); r_sy_w = p.reg(sy, 0); r_ldz2 = p.reg(sy, 0)
        r_t_zr = p.reg(t, 0); r_s_12 = p.reg(sc, 16); r_s_hh = p.reg(sc, 0)
        r_t_s1 = p.reg(t, 0); r_g_m3 = p.reg(g, 0)
        r_v_hh = p.reg(v, 0); r_v_m3 = p.reg(v, 0)
        r_t_cph = p.reg(t, 0); r_t_evy = p.reg(t, 0)
        r_v_ldy = p.reg(v, 0); r_v_3 = p.reg(v, 1)
        r_sc_ev = p.reg(sc, 1); r_sc_styw = p.reg(sc, 16)
        r_sty_c = p.reg(sy, 1); r_styd = p.reg(sy, 16)
        v.sem_inc(mm12, 16)
        v.sem_inc(evy, 1)
        v.sem_inc(sty, 16)
        v.sem_inc(mm3, 1)
        with nc.Fori(0, E) as e:
            nc.all_engine_barrier()
            nl = 0
            for dc in range(8):
                sy.dma_start(out=w1_sb[:, ds_(dc, 2048)],
                             in_=w["l0f_w1_bf"][e, bass.ds(dc * 128, 128),
                                                bass.ds(hh * 2048, 2048)]).then_inc(lde, 16)
                nl += 16
            for hc in range(16):
                sy.dma_start(out=w2_sb[:, ds_(hc, 1024)],
                             in_=w["l0f_w2_bf"][e, bass.ds(hh * 2048 + hc * 128, 128), :]).then_inc(lde, 16)
                nl += 16
            sy.dma_start(out=b1_sb[:], in_=w["l0f_b1"][e][:, bass.ds(hh * 16, 16)]).then_inc(lde, 16)
            sy.dma_start(out=b2_row[:], in_=w["l0f_b2_bf"][bass.ds(e, 1), :]).then_inc(lde, 16)
            nl += 32
            sy.reg_add(r_sy_lde, r_sy_lde, nl)
            sy.wait_ge(lde, r_sy_lde)
            nc.all_engine_barrier()
            with nc.Fori(0, T // 512) as st:
                stp = st % 2
                sy.wait_ge(mm12, r_sy_w)
                sy.reg_add(r_sy_w, r_sy_w, 16)
                sy.dma_start(out=zst[:, stp, :].rearrange("p (c q) -> p c q", c=8),
                             in_=zr.rearrange("(c p) t -> p c t", p=128)[:, :, bass.ds(st * 512, 512)]).then_inc(ldz2, 16)
                sy.reg_add(r_ldz2, r_ldz2, 16)
                sy.wait_ge(ldz2, r_ldz2)
                sy.sem_inc(zrd, 1)
                t.reg_add(r_t_zr, r_t_zr, 1)
                t.wait_ge(zrd, r_t_zr)
                for fc in range(16):
                    fp = fc % 2
                    t.wait_ge(s1d, r_t_s1)
                    t.reg_add(r_t_s1, r_t_s1, 1)
                    for dc in range(8):
                        ins = t.matmul(ps1[:, fp, :],
                                       w1_sb[:, dc * 2048 + fc * 128: dc * 2048 + (fc + 1) * 128],
                                       zst[:, stp, bass.ds(dc * 512, 512)],
                                       start=(dc == 0), stop=(dc == 7))
                    ins.then_inc(mm12, 1)
                    sc.reg_add(r_s_12, r_s_12, 1)
                    sc.wait_ge(mm12, r_s_12)
                    if fc == 0:
                        sc.wait_ge(cph, r_s_hh)
                        sc.reg_add(r_s_hh, r_s_hh, 4)
                    sc.activation(out=hh_sb[:, ds_(fc, 512)], in_=ps1[:, fp, :], func=AF.Relu,
                                  bias=b1_sb[:, fc:fc + 1]).then_inc(s1d, 1)
                    if fc == 12:
                        sc.wait_ge(evy, r_sc_ev)
                        sc.reg_add(r_sc_ev, r_sc_ev, 4)
                        sc.wait_ge(sty, r_sc_styw)
                        sc.reg_add(r_sc_styw, r_sc_styw, 64)
                    if fc >= 12:
                        sc.dma_start(out=yt0[:, fc - 12, :],
                                     in_=yr[st * 4 + (fc - 12)]).then_inc(ldy2, 16)
                for tokt in range(4):
                    if tokt == 0:
                        v.reg_add(r_v_hh, r_v_hh, 16)
                    v.wait_ge(s1d, r_v_hh)
                    v.wait_ge(mm3, r_v_m3)
                    v.reg_add(r_v_m3, r_v_m3, 1)
                    v.tensor_copy(hstage[:, tokt % 2, :],
                                  hh_sb[:].rearrange("p (c q) -> p c q", c=16)[:, :, bass.ds(tokt * 128, 128)]).then_inc(cph, 1)
                    t.reg_add(r_t_cph, r_t_cph, 1)
                    t.wait_ge(cph, r_t_cph)
                    t.wait_ge(evy, r_t_evy)
                    t.reg_add(r_t_evy, r_t_evy, 1)
                    for half in (0, 1):
                        if hh == 0:
                            t.matmul(ps3[:, tokt % 2, half, :], consts["ones_row"],
                                     b2_row[0:1, ts(half, 512)], start=True, stop=False)
                        for hc in range(16):
                            ins = t.matmul(ps3[:, tokt % 2, half, :], hstage[:, tokt % 2, ts(hc, 128)],
                                           w2_sb[:, hc * 1024 + half * 512: hc * 1024 + (half + 1) * 512],
                                           start=(hh != 0 and hc == 0), stop=(hc == 15))
                    ins.then_inc(mm3, 1)
                    if tokt == 0:
                        v.reg_add(r_v_ldy, r_v_ldy, 64)
                    v.wait_ge(ldy2, r_v_ldy)
                    v.reg_add(r_v_3, r_v_3, 1)
                    v.wait_ge(mm3, r_v_3)
                    v.scalar_tensor_tensor(yt0[:, tokt, 0:512], ps3[:, tokt % 2, 0, :],
                                           comb_sb[:, bass.ds((st * 4 + tokt) * 8 + e, 1)],
                                           yt0[:, tokt, 0:512], ALU.mult, ALU.add)
                    v.scalar_tensor_tensor(yt0[:, tokt, 512:1024], ps3[:, tokt % 2, 1, :],
                                           comb_sb[:, bass.ds((st * 4 + tokt) * 8 + e, 1)],
                                           yt0[:, tokt, 512:1024], ALU.mult, ALU.add).then_inc(evy, 1)
                    sy.reg_add(r_sty_c, r_sty_c, 1)
                    sy.wait_ge(evy, r_sty_c)
                    sy.dma_start(out=yr[st * 4 + tokt], in_=yt0[:, tokt, :]).then_inc(sty, 16)
                    sy.reg_add(r_styd, r_styd, 16)
            sy.wait_ge(sty, r_styd)
            nc.all_engine_barrier()
        p.barrier()
    p.barrier()

    ldf = [p.sem(), p.sem()]
    frdy = p.sem(); evf = p.sem(); stf = p.sem(); sff = p.sem()
    r_ldf = [p.reg(sy, 0), p.reg(sy, 0)]
    r_sy_e2 = p.reg(sy, 0); r_sy_sf = p.reg(sy, 0)
    r_v_fr = p.reg(v, 0); r_v_sf = p.reg(v, 0)
    xi_r = dram["x_in"].rearrange("(n p) d -> n p d", p=128)
    xo_r = dram["x_out"].rearrange("(n p) d -> n p d", p=128)
    sy.sem_inc(sff, 2)
    with nc.Fori(0, NT // 2) as th:
        for par in (0, 1):
            tl = th * 2 + par
            sy.wait_ge(evf, r_sy_e2)
            sy.dma_start(out=xt0[:, par, :], in_=xi_r[tl]).then_inc(ldf[par], 16)
            sy.dma_start(out=yt0[:, par, :], in_=yr[tl]).then_inc(ldf[par], 16)
            sy.reg_add(r_ldf[par], r_ldf[par], 32)
            sy.wait_ge(ldf[par], r_ldf[par])
            sy.sem_inc(frdy, 1)
            v.reg_add(r_v_fr, r_v_fr, 1)
            v.wait_ge(frdy, r_v_fr)
            v.wait_ge(sff, r_v_sf)
            v.reg_add(r_v_sf, r_v_sf, 1)
            v.scalar_tensor_tensor(xo0[:, par, :], yt0[:, par, :], 0.5, xt0[:, par, :],
                                   ALU.mult, ALU.add).then_inc(evf, 1)
            sy.reg_add(r_sy_e2, r_sy_e2, 1)
            sy.wait_ge(evf, r_sy_e2)
            sy.dma_start(out=xo_r[tl], in_=xo0[:, par, :]).then_inc(stf, 16)
            sy.reg_add(r_sy_sf, r_sy_sf, 16)
            sy.wait_ge(stf, r_sy_sf)
            sy.sem_inc(sff, 1)
    p.barrier()
    p.free_scope(mk)
    p.barrier()



def _declare_weight_drams(nc, g):
    w = {}
    for name, arr in g.items():
        w[name] = nc.dram_tensor(name, list(arr.shape), F32, kind="ExternalInput").ap()
    w["core_onehot"] = nc.dram_tensor("core_onehot", [1, 64], F32, kind="ExternalInput").ap()
    w["core_mask"] = nc.dram_tensor("core_mask", [64, 1], F32, kind="ExternalInput").ap()
    BFS = ["l0f_router", "l0f_w1", "l0f_w2", "l0f_b2",
           "l0a_inwT", "l0a_xpwT_pad", "l0a_outwT", "l0a_outb",
           "l1f_qkvwT", "l1f_qkvb", "l1f_outwT", "l1f_outb",
           "l1f_gatewT", "l1f_w1T", "l1f_w2T", "l1f_w3T", "l1f_b3",
           "l1a_qkvwT", "l1a_qkvb", "l1a_p1wT", "l1a_p2wT", "l1a_p2b"]
    ren = {"l1f_qkvb": "l1f_qkvb_row", "l1a_qkvb": "l1a_qkvb_row",
           "l1f_outb": "l1f_outb_row", "l1a_p2b": "l1a_p2b_bf"}
    for name in BFS:
        arr = g[name]
        tgt = ren.get(name, name + "_bf")
        w[tgt] = nc.dram_tensor(tgt, list(arr.shape), BF16, kind="Internal").ap()
    return w


_CAST_PAIRS = [("l0f_router", "l0f_router_bf"), ("l0f_b2", "l0f_b2_bf"),
               ("l0a_inwT", "l0a_inwT_bf"), ("l0a_xpwT_pad", "l0a_xpwT_pad_bf"),
               ("l0a_outwT", "l0a_outwT_bf"), ("l0a_outb", "l0a_outb_bf"),
               ("l1f_qkvwT", "l1f_qkvwT_bf"), ("l1f_qkvb", "l1f_qkvb_row"),
               ("l1f_outwT", "l1f_outwT_bf"), ("l1f_outb", "l1f_outb_row"),
               ("l1f_gatewT", "l1f_gatewT_bf"), ("l1f_b3", "l1f_b3_bf"),
               ("l1a_qkvwT", "l1a_qkvwT_bf"), ("l1a_p1wT", "l1a_p1wT_bf"),
               ("l1a_p2wT", "l1a_p2wT_bf"), ("l1a_p2b", "l1a_p2b_bf"),
               ("l1a_qkvb", "l1a_qkvb_row")]
_CAST_3D = ["l0f_w1", "l0f_w2", "l1f_w1T", "l1f_w2T", "l1f_w3T"]


def _emit_casts_and_consts(p, nc, w):
    pairs = [(w[a], w[b]) for a, b in _CAST_PAIRS]
    for name in _CAST_3D:
        for e in range(E):
            pairs.append((w[name][e], w[name + "_bf"][e]))
    emit_prologue_casts(p, pairs)
    consts = {}
    identb = p.sbuf([128, 128], BF16, "c_identb")
    identf = p.sbuf([128, 128], F32, "c_identf")
    onesr = p.sbuf([1, 128], BF16, "c_ones")
    ones64 = p.sbuf([64, 1], F32, "c_ones64")
    sld = p.sem()
    gg = nc.gpsimd
    gg.dma_start(out=identb[:], in_=w["ident"]).then_inc(sld, 16)
    gg.dma_start(out=identf[:], in_=w["ident"]).then_inc(sld, 16)
    gg.memset(onesr[:], 1.0)
    gg.memset(ones64[:], 1.0)
    gg.wait_ge(sld, 2 * 16)
    p.barrier()
    consts.update({"ident_bf": identb[:], "ident_f32": identf[:], "ones_row": onesr[:],
                   "ones64c": ones64[:]})
    return consts


def _declare_scratch(nc):
    d = {}
    for nm, shape, dt in [("xA", [T, D], F32), ("xB", [T, D], F32),
                          ("z0", [T, D], F32), ("z1", [T, D], F32),
                          ("yacc", [T, D], F32), ("zlnT", [D, T], BF16),
                          ("qT", [B_LOC, D, S], BF16), ("kT", [B_LOC, D, S], BF16),
                          ("v_tok", [B_LOC, S, D], BF16),
                          ("totdram", [NT, 8], F32), ("offs_dram", [NT, 8], F32)]:
        d[nm] = nc.dram_tensor(nm, shape, dt, kind="Internal").ap()
    d["cc_in"] = nc.dram_tensor("cc_in", [64, 1024], F32, kind="Internal").ap()
    d["cc_out"] = nc.dram_tensor("cc_out", [64, 1024], F32, kind="Internal").ap()
    return d


def _emit_sublayer(p, nc, which, x_in, x_out, dram, w, consts, nocc):
    dd = dict(dram)
    dd["x_in"] = x_in
    dd["x_out"] = x_out
    if which == "l0ffn":
        emit_norm_pass(p, x_in, None, None, dd["zlnT"], "ln", ident_bf=consts["ident_bf"])
        emit_l0ffn(p, dd, w, consts, nocc=nocc)
    elif which == "l0attn":
        emit_norm_pass(p, x_in, None, None, dd["zlnT"], "ln", ident_bf=consts["ident_bf"])
        emit_l0attn(p, dd, w)
    elif which == "l1ffn":
        emit_l1ffn(p, dd, w, consts)
    elif which == "l1attn":
        emit_norm_pass(p, x_in, dd["z0"], None, dd["zlnT"], "ln", ident_bf=consts["ident_bf"])
        emit_l1attn(p, dd, w)


def _build_program(sublayers, nocc=False):
    nc = bacc.Bacc("TRN2", target_bir_lowering=False)
    x_in = nc.dram_tensor("x_in", [T, D], F32, kind="ExternalInput").ap()
    x_out = nc.dram_tensor("x_out", [T, D], F32, kind="ExternalOutput").ap()
    z = np_inputs_placeholder = None
    g = _HOST_G
    w = _declare_weight_drams(nc, g)
    dram = _declare_scratch(nc)
    dram["x_in_ext"] = x_in
    dram["x_out_ext"] = x_out
    p = P(nc)
    consts = _emit_casts_and_consts(p, nc, w)
    name2ap = {"x_in": x_in, "x_out": x_out, "xA": dram["xA"], "xB": dram["xB"]}
    for which, a, b in sublayers:
        _emit_sublayer(p, nc, which, name2ap[a], name2ap[b], dram, w, consts, nocc)
    p.close()
    nc.compile()
    return nc


_HOST_G = None
_PROG_CACHE = {}


def kernel(**inputs):
    global _HOST_G
    g, per_core = _prep(inputs)
    _HOST_G = g
    seq = [("l0ffn", "x_in", "xA"), ("l0attn", "xA", "xB"), ("l0ffn", "xB", "xA"),
           ("l1ffn", "xA", "xB"), ("l1attn", "xB", "xA"), ("l1ffn", "xA", "x_out")]
    import os
    fused = os.environ.get("MK_UNFUSED", "0") != "1"
    outs = []
    if fused:
        key = "full"
        if key not in _PROG_CACHE:
            _PROG_CACHE[key] = _build_program(seq)
        nc = _PROG_CACHE[key]
        in_maps = []
        for j in range(NCORES):
            m = {k: v for k, v in g.items()}
            m.update(per_core[j])
            m["x_in"] = m.pop("x_in")
            in_maps.append(m)
        res = run_bass_kernel_spmd(nc, in_maps, core_ids=list(range(NCORES)))
        outs = [res.results[j]["x_out"] for j in range(NCORES)]
    else:
        cur = [per_core[j]["x_in"] for j in range(NCORES)]
        for which, a, b in seq:
            key = which
            if key not in _PROG_CACHE:
                _PROG_CACHE[key] = _build_program([(which, "x_in", "x_out")])
            nc = _PROG_CACHE[key]
            in_maps = []
            for j in range(NCORES):
                m = {k: v for k, v in g.items()}
                m.update(per_core[j])
                m["x_in"] = cur[j]
                in_maps.append(m)
            res = run_bass_kernel_spmd(nc, in_maps, core_ids=list(range(NCORES)))
            cur = [res.results[j]["x_out"] for j in range(NCORES)]
        outs = cur
    return _unshard(outs)


# version 06 (final)
# speedup vs baseline: 1.4037x; 1.2512x over previous
import numpy as np
import ml_dtypes
import concourse.bacc as bacc
import concourse.bass as bass
import concourse.mybir as mybir
from concourse.bass import ts, IndirectOffsetOnAxis
from concourse.bass_utils import run_bass_kernel_spmd
from concourse.bass import IndirectOffsetOnAxis
from concourse.expressions import smin

F32 = mybir.dt.float32
BF16 = mybir.dt.bfloat16
I32 = mybir.dt.int32
I32 = mybir.dt.int32
AF = mybir.ActivationFunctionType
ALU = mybir.AluOpType
AX = mybir.AxisListType

NCORES = 8
B_LOC = 4
S = 2048
D = 1024
T = B_LOC * S
NT = T // 128
H1 = 16
HD = 64
E = 8
CAP = 3072
LNEPS = 1e-5
RMSEPS = 1e-6
KVEPS = 1e-5

BF = np.dtype(ml_dtypes.bfloat16)



def _rope_tables():
    inv = 1.0 / 10000.0 ** (np.arange(0, HD, 2, dtype=np.float32) / HD)
    tpos = np.arange(S, dtype=np.float32)
    fr = np.outer(tpos, inv)
    emb = np.concatenate([fr, fr], axis=1)
    cos = np.cos(emb).T.astype(np.float32)
    sin = np.sin(emb).T.astype(np.float32)
    cos128 = np.concatenate([cos, cos], axis=0)
    sin128 = np.concatenate([sin, sin], axis=0)
    return np.ascontiguousarray(cos128), np.ascontiguousarray(sin128)


def _rot_mat128():
    P = np.zeros((64, 64), np.float32)
    for i in range(32):
        P[i, i + 32] = -1.0
        P[i + 32, i] = 1.0
    blk = P.T
    M = np.zeros((128, 128), np.float32)
    M[:64, :64] = blk
    M[64:, 64:] = blk
    return M


def _prep(inputs):
    g = {}
    f32 = np.float32

    def c(a):
        return np.ascontiguousarray(a, dtype=f32)

    g["ident"] = np.eye(128, dtype=f32)
    g["iota8"] = np.tile(np.arange(8, dtype=np.float32), (128, 1))
    g["tri"] = np.triu(np.ones((128, 128), f32), 1)
    cos128, sin128 = _rope_tables()
    g["rope_cos"] = cos128
    g["rope_sin"] = sin128
    g["rope_cos_q"] = c(cos128 * (HD ** -0.5))
    g["rope_sin_q"] = c(sin128 * (HD ** -0.5))
    g["rotmat"] = _rot_mat128()
    g["iota8"] = np.broadcast_to(np.arange(8, dtype=f32), (128, 8)).copy()

    g["l0f_router"] = c(inputs["L0_ffn_w_router"])
    g["l0f_w1"] = c(inputs["L0_ffn_w1"])
    g["l0f_b1"] = c(inputs["L0_ffn_b1"].reshape(E, 32, 128).transpose(0, 2, 1))
    g["l0f_w2"] = c(inputs["L0_ffn_w2"])
    g["l0f_b2"] = c(inputs["L0_ffn_b2"])
    g["l0a_inwT"] = c(inputs["L0_attn_in_w"].T)
    g["l0a_inb"] = c(inputs["L0_attn_in_b"].reshape(16, 128).T)
    g["l0a_convw"] = c(inputs["L0_attn_conv_w"][:, 0, :].reshape(16, 128, 4))
    g["l0a_convb"] = c(inputs["L0_attn_conv_b"].reshape(16, 128).T)
    g["l0a_xpwT"] = c(inputs["L0_attn_xp_w"].T)
    xpp = np.zeros((2048, 65), np.float32)
    xpp[:, 0:16] = inputs["L0_attn_xp_w"].T[:, 0:16]
    xpp[:, 32:48] = inputs["L0_attn_xp_w"].T[:, 16:32]
    xpp[:, 64] = 1.0
    g["l0a_xpwT_pad"] = xpp
    g["l0a_xpb"] = c(inputs["L0_attn_xp_b"].reshape(32, 1))
    g["l0a_outwT"] = c(inputs["L0_attn_out_w"].T)
    g["l0a_outb"] = c(inputs["L0_attn_out_b"].reshape(1, D))
    g["l1f_n1"] = c(np.broadcast_to(inputs["L1_ffn_norm1_w"], (128, D)))
    g["l1f_n2"] = c(np.broadcast_to(inputs["L1_ffn_norm2_w"], (128, D)))
    g["l1f_qkvwT"] = c(inputs["L1_ffn_qkv_w"].T)
    g["l1f_qkvb"] = c(inputs["L1_ffn_qkv_b"].reshape(1, 3 * D))
    g["l1f_qkvb_h"] = c(inputs["L1_ffn_qkv_b"].reshape(24, 128).T)
    g["l1f_outwT"] = c(inputs["L1_ffn_out_w"].T)
    g["l1f_outb"] = c(inputs["L1_ffn_out_b"].reshape(1, D))
    g["l1f_gatewT"] = c(inputs["L1_ffn_gate_w"].T)
    g["l1f_w1T"] = c(inputs["L1_ffn_w1"].transpose(0, 2, 1))
    g["l1f_w2T"] = c(inputs["L1_ffn_w2"].transpose(0, 2, 1))
    g["l1f_b1"] = c(inputs["L1_ffn_b1"].reshape(E, 16, 128).transpose(0, 2, 1))
    g["l1f_b2"] = c(inputs["L1_ffn_b2"].reshape(E, 16, 128).transpose(0, 2, 1))
    g["l1f_w3T"] = c(inputs["L1_ffn_w3"].transpose(0, 2, 1))
    g["l1f_b3"] = c(inputs["L1_ffn_b3"])
    g["l1a_qkvwT"] = c(inputs["L1_attn_qkv_w"].T)
    g["l1a_qkvb"] = c(inputs["L1_attn_qkv_b"].reshape(1, 3 * D))
    g["l1a_qkvb_h"] = c(inputs["L1_attn_qkv_b"].reshape(24, 128).T)
    g["l1a_kng"] = c(np.broadcast_to(np.tile(inputs["L1_attn_kn_g"], H1), (128, D)))
    g["l1a_knb"] = c(np.broadcast_to(np.tile(inputs["L1_attn_kn_b"], H1), (128, D)))
    g["l1a_vng"] = c(np.broadcast_to(np.tile(inputs["L1_attn_vn_g"], H1), (128, D)))
    g["l1a_vnb"] = c(np.broadcast_to(np.tile(inputs["L1_attn_vn_b"], H1), (128, D)))
    g["l1a_temp"] = c(inputs["L1_attn_temperature"].reshape(1, H1))
    g["l1a_bias"] = c(inputs["L1_attn_attn_bias"])
    g["l1a_p1wT"] = c(inputs["L1_attn_p1_w"].T)
    g["l1a_p1b"] = c(inputs["L1_attn_p1_b"].reshape(8, 128).T)
    g["l1a_p2wT"] = c(inputs["L1_attn_p2_w"].T)
    g["l1a_p2b"] = c(inputs["L1_attn_p2_b"].reshape(1, D))

    x = np.ascontiguousarray(inputs["L0_attn_x"], dtype=f32)
    per_core = []
    for j in range(NCORES):
        bsel = [4 * j, 4 * j + 1, 4 * j + 2, 4 * j + 3]
        m = {"x_in": np.ascontiguousarray(x[bsel].reshape(T, D))}
        onehot = np.zeros((1, 64), f32)
        onehot[0, j] = 1.0
        mask = np.zeros((64, 1), f32)
        mask[:j, 0] = 1.0
        m["core_onehot"] = onehot
        m["core_mask"] = mask
        per_core.append(m)
    return g, per_core


def _unshard(outs):
    full = np.empty((32, S, D), np.float32)
    for j in range(NCORES):
        r = outs[j].reshape(B_LOC, S, D)
        for bi, b in enumerate([4 * j, 4 * j + 1, 4 * j + 2, 4 * j + 3]):
            full[b] = r[bi]
    return full



class P:

    def __init__(self, nc):
        self.nc = nc
        self._exit = []
        self._sem_i = 0
        self._reg_i = 0

    def sem(self):
        cm = self.nc.semaphore(f"s{self._sem_i}")
        self._sem_i += 1
        h = cm.__enter__()
        self._exit.append(("sem", cm, h))
        return h

    def sbuf(self, shape, dtype, name=None):
        self._name_i = getattr(self, "_name_i", 0) + 1
        cm = self.nc.sbuf_tensor(f"{name or 'sb'}_{self._name_i}", shape, dtype)
        h = cm.__enter__()
        self._exit.append(("buf", cm, h))
        return h

    def psum(self, shape, dtype=F32, name=None):
        self._name_i = getattr(self, "_name_i", 0) + 1
        cm = self.nc.psum_tensor(f"{name or 'ps'}_{self._name_i}", shape, dtype)
        h = cm.__enter__()
        self._exit.append(("buf", cm, h))
        return h

    def reg(self, eng, val=None):
        r = eng.alloc_register(f"r{self._reg_i}")
        self._reg_i += 1
        if val is not None:
            eng.reg_mov(r, val)
        return r

    def close(self):
        self.free_scope(0)

    def scoped(self):
        return len(self._exit)

    def free_scope(self, mark):
        sems = [h for (kind, cm, h) in self._exit[mark:] if kind == "sem"]
        if sems:
            self.barrier()
            self.nc.gpsimd.dma_reset()
            for h in sems:
                self.nc.gpsimd.sem_clear(h)
            self.barrier()
        for kind, cm, h in reversed(self._exit[mark:]):
            cm.__exit__(None, None, None)
        del self._exit[mark:]

    def barrier(self):
        self.nc.all_engine_barrier()


def emit_prologue_casts(p, pairs):
    nc = p.nc
    mk_pc = p.scoped()
    ld = p.sem()
    st = p.sem()
    nld = 0
    nst = 0
    stage = p.sbuf([128, 4096], BF16, "cast_stage")
    cols_max = 4096
    for src, dst in pairs:
        rows, cols = src.shape
        assert dst.shape == (rows, cols)
        r = 0
        while r < rows:
            rr = min(128, rows - r)
            cc0 = 0
            while cc0 < cols:
                cc = min(cols_max, cols - cc0)
                buf = stage[0:rr, 0:cc]
                if nst > 0:
                    nc.gpsimd.wait_ge(st, nst * 16)
                nc.gpsimd.dma_start(out=buf, in_=src[r:r + rr, cc0:cc0 + cc]).then_inc(ld, 16)
                nld += 1
                nc.gpsimd.wait_ge(ld, nld * 16)
                nc.gpsimd.dma_start(out=dst[r:r + rr, cc0:cc0 + cc], in_=buf).then_inc(st, 16)
                nst += 1
                cc0 += cc
            r += rr
    nc.gpsimd.wait_ge(st, nst * 16)
    p.barrier()
    p.free_scope(mk_pc)




def emit_norm_pass(p, x_in, z_f32, zln, zlnT, kind, gamma_rep=None, ident_bf=None,
                   eps=LNEPS, nt=None):
    nc = p.nc
    v, sc, g, t, sy = nc.vector, nc.scalar, nc.gpsimd, nc.tensor, nc.sync
    if nt is None:
        nt = NT
    mk_np = p.scoped()
    xr = x_in.rearrange("(n p) d -> n p d", p=128)
    ld = [p.sem(), p.sem()]
    rdy = p.sem()
    stok = p.sem()
    sc_mean = p.sem(); vc_xc = p.sem(); sc_sq = p.sem(); vc_v = p.sem()
    sc_rt = p.sem(); vc_rc = p.sem(); sc_zf = p.sem(); vn = p.sem()
    tp = p.sem(); ev = p.sem(); st = p.sem(); evok = p.sem()

    xt = p.sbuf([128, 2, D], F32, "np_x")
    xc = p.sbuf([128, 2, D], F32, "np_xc")
    zf = p.sbuf([128, 2, D], F32, "np_zf")
    sqs = p.sbuf([128, D], F32, "np_sqs")
    stat = p.sbuf([128, 8], F32, "np_stat")
    lnt = p.sbuf([128, 2, D], BF16, "np_ln")
    g_t = p.sbuf([128, D], F32, "np_gam") if gamma_rep is not None else None
    zT_sb = p.sbuf([128, 8, S], BF16, "np_zT") if zlnT is not None else None
    ps_tr = p.psum([128, 8, 128], BF16, "np_tr") if zlnT is not None else None

    if gamma_rep is not None:
        sy.dma_start(out=g_t[:], in_=gamma_rep).then_inc(ld[0], 16)
        sy.wait_ge(ld[0], 16)
    p.barrier()
    ld_base = [16 if gamma_rep is not None else 0, 0]
    sy.dma_start(out=xt[:, 0, :], in_=xr[0]).then_inc(ld[0], 16)
    r_ld = [p.reg(sy, ld_base[0] + 16), p.reg(sy, ld_base[1])]
    r_g = p.reg(sy, 0)
    r_sv = p.reg(sy, 1)
    r_st = p.reg(sy, 0)

    r_s_rdy = p.reg(sc, 0)
    r_v_m = p.reg(v, 0)
    r_s_xc = p.reg(sc, 0)
    r_v_sq = p.reg(v, 0)
    r_s_v = p.reg(sc, 0)
    r_v_rt = p.reg(v, 0)
    r_s_rc = p.reg(sc, 0)
    r_v_zf = p.reg(v, 0)
    r_v_stok = p.reg(v, 0)
    r_t_vn = p.reg(t, 0) if zlnT is not None else None
    r_t_ev = p.reg(t, 0) if zlnT is not None else None
    r_v_tp = p.reg(v, 0) if zlnT is not None else None
    r_sy_ev = p.reg(sy, 0)

    TPB = S // 128
    ln = kind == "ln"
    nst_per = (1 if zln is not None else 0) + (1 if z_f32 is not None else 0)

    def tile_body(tg_expr, t_expr, par):
        nxt = 1 - par
        sy.wait_ge(vn, r_g)
        sy.reg_add(r_g, r_g, 1)
        sy.dma_start(out=xt[:, nxt, :], in_=xr[smin(tg_expr + 1, nt - 1)]).then_inc(ld[nxt], 16)
        sy.reg_add(r_ld[nxt], r_ld[nxt], 16)
        sy.wait_ge(ld[par], r_ld[par])
        sy.sem_inc(rdy, 1)

        sc.reg_add(r_s_rdy, r_s_rdy, 1)
        sc.wait_ge(rdy, r_s_rdy)
        if ln:
            sc.activation(out=sqs[:], in_=xt[:, par, :], func=AF.Identity,
                          scale=1.0 / D, accum_out=stat[:, 0:1]).then_inc(sc_mean, 1)
            v.reg_add(r_v_m, r_v_m, 1)
            v.wait_ge(sc_mean, r_v_m)
            v.tensor_scalar(xc[:, par, :], xt[:, par, :], stat[:, 0:1], None,
                            ALU.subtract).then_inc(vc_xc, 1)
            csrc = xc
            sc.reg_add(r_s_xc, r_s_xc, 1)
            sc.wait_ge(vc_xc, r_s_xc)
        else:
            csrc = xt
        sc.activation(out=sqs[:], in_=csrc[:, par, :], func=AF.Square,
                      accum_out=stat[:, 1:2]).then_inc(sc_sq, 1)
        v.reg_add(r_v_sq, r_v_sq, 1)
        v.wait_ge(sc_sq, r_v_sq)
        v.tensor_scalar(stat[:, 2:3], stat[:, 1:2], 1.0 / D, eps, ALU.mult,
                        ALU.add).then_inc(vc_v, 1)
        sc.reg_add(r_s_v, r_s_v, 1)
        sc.wait_ge(vc_v, r_s_v)
        sc.activation(out=stat[:, 3:4], in_=stat[:, 2:3], func=AF.Sqrt).then_inc(sc_rt, 1)
        v.reg_add(r_v_rt, r_v_rt, 1)
        v.wait_ge(sc_rt, r_v_rt)
        v.reciprocal(stat[:, 4:5], stat[:, 3:4]).then_inc(vc_rc, 1)
        sc.reg_add(r_s_rc, r_s_rc, 1)
        sc.wait_ge(vc_rc, r_s_rc)
        sc.activation(out=zf[:, par, :], in_=csrc[:, par, :], func=AF.Identity,
                      scale=stat[:, 4:5]).then_inc(sc_zf, 1)
        v.reg_add(r_v_zf, r_v_zf, 1)
        v.wait_ge(sc_zf, r_v_zf)
        if nst_per:
            v.wait_ge(stok, r_v_stok)
            v.reg_add(r_v_stok, r_v_stok, 1)
        if gamma_rep is not None:
            v.tensor_tensor(lnt[:, par, :], zf[:, par, :], g_t[:], ALU.mult).then_inc(vn, 1)
        else:
            v.tensor_copy(lnt[:, par, :], zf[:, par, :]).then_inc(vn, 1)

        if zlnT is not None:
            t.reg_add(r_t_vn, r_t_vn, 1)
            t.wait_ge(vn, r_t_vn)
            t.wait_ge(ev, r_t_ev)
            for c in range(8):
                ins = t.transpose(ps_tr[:, c, :], lnt[:, par, ts(c, 128)], ident_bf[:])
            ins.then_inc(tp, 1)
            t.reg_add(r_t_ev, r_t_ev, 1)
            v.reg_add(r_v_tp, r_v_tp, 1)
            v.wait_ge(tp, r_v_tp)
            v.tensor_copy(zT_sb[:, :, bass.ds(t_expr * 128, 128)], ps_tr[:]).then_inc(ev, 1)
        sy.wait_ge(vn, r_sv)
        sy.reg_add(r_sv, r_sv, 1)
        sy_waited = False
        if zln is not None:
            sy.dma_start(out=zln.rearrange("(n p) d -> n p d", p=128)[tg_expr],
                         in_=lnt[:, par, :]).then_inc(st, 16)
            sy.reg_add(r_st, r_st, 16)
            sy_waited = True
        if z_f32 is not None:
            sy.dma_start(out=z_f32.rearrange("(n p) d -> n p d", p=128)[tg_expr],
                         in_=zf[:, par, :]).then_inc(st, 16)
            sy.reg_add(r_st, r_st, 16)
            sy_waited = True
        if sy_waited:
            sy.wait_ge(st, r_st)
            sy.sem_inc(stok, 1)
        else:
            sy.sem_inc(stok, 1)

    nb = nt // TPB
    for b in range(nb):
        with nc.Fori(0, TPB // 2) as th:
            for par in (0, 1):
                tile_body(b * TPB + th * 2 + par, th * 2 + par, par)
        if zlnT is not None:
            sy.reg_add(r_sy_ev, r_sy_ev, TPB)
            sy.wait_ge(ev, r_sy_ev)
            for c in range(8):
                sy.dma_start(out=zlnT[ts(c, 128), b * S:(b + 1) * S],
                             in_=zT_sb[:, c, :]).then_inc(st, 16)
                sy.reg_add(r_st, r_st, 16)
            sy.wait_ge(st, r_st)
            p.barrier()
    sy.wait_ge(st, r_st)
    sy.wait_ge(ld[0], r_ld[0])
    sy.wait_ge(ld[1], r_ld[1])
    p.barrier()
    p.free_scope(mk_np)

def ds_(i, w_):
    return bass.ds(i * w_, w_)


def emit_head_ln(p, nc, src, dst_bf, gam_t, bet_t, aux):
    v, s, g = nc.vector, nc.scalar, nc.gpsimd
    xcg, sqg, t1g, st16 = aux["xcg"], aux["sqg"], aux["t1g"], aux["st16"]
    sem_v, sem_s, sem_g = aux["sem_v"], aux["sem_s"], aux["sem_g"]
    rv, rs, rg, rv2, rvg = aux["rv"], aux["rs"], aux["rg"], aux["rv2"], aux["rvg"]
    gsrc = src.rearrange("p (g w) -> p g w", g=16)
    v.wait_ge(sem_g, rv2)
    v.reg_add(rv2, rv2, 2)
    v.tensor_reduce(st16[:, 0:16], gsrc, AX.X, ALU.add).then_inc(sem_v, 1)
    s.reg_add(rs, rs, 1); s.wait_ge(sem_v, rs)
    s.activation(out=st16[:, 16:32], in_=st16[:, 0:16], func=AF.Identity,
                 scale=1.0 / HD).then_inc(sem_s, 1)
    v.reg_add(rv, rv, 1); v.wait_ge(sem_s, rv)
    v.tensor_tensor(xcg[:].rearrange("p (g w) -> p g w", g=16), gsrc,
                    st16[:, 16:32].broadcast_to([128, 16, HD]),
                    ALU.subtract).then_inc(sem_v, 1)
    s.reg_add(rs, rs, 1); s.wait_ge(sem_v, rs)
    s.activation(out=sqg[:], in_=xcg[:], func=AF.Square).then_inc(sem_s, 1)
    v.reg_add(rv, rv, 1); v.wait_ge(sem_s, rv)
    v.tensor_reduce(st16[:, 32:48], sqg[:].rearrange("p (g w) -> p g w", g=16),
                    AX.X, ALU.add).then_inc(sem_v, 1)
    s.reg_add(rs, rs, 1); s.wait_ge(sem_v, rs)
    s.activation(out=st16[:, 48:64], in_=st16[:, 32:48], func=AF.Sqrt,
                 scale=1.0 / HD, bias=aux["eps_col"][:, 0:1]).then_inc(sem_s, 1)
    s.reg_add(rs, rs, 2)
    v.reg_add(rv, rv, 1); v.wait_ge(sem_s, rv)
    v.reciprocal(st16[:, 64:80], st16[:, 48:64]).then_inc(sem_v, 1)
    g.reg_add(rg, rg, 4); g.wait_ge(sem_v, rg)
    g.tensor_tensor(t1g[:].rearrange("p (g w) -> p g w", g=16),
                    xcg[:].rearrange("p (g w) -> p g w", g=16),
                    st16[:, 64:80].broadcast_to([128, 16, HD]), ALU.mult).then_inc(sem_g, 1)
    v.wait_ge(sem_g, rvg)
    v.reg_add(rvg, rvg, 2)
    v.tensor_tensor(xcg[:], t1g[:], gam_t[:], ALU.mult).then_inc(sem_v, 1)
    g.reg_add(rg, rg, 1); g.wait_ge(sem_v, rg)
    g.tensor_tensor(dst_bf, xcg[:], bet_t[:], ALU.add).then_inc(sem_g, 1)


def emit_l1attn(p, dram, w):
    import os
    _PH = int(os.environ.get("L1A_PHASES", "9"))
    nc = p.nc
    v, sc, g, t, sy = nc.vector, nc.scalar, nc.gpsimd, nc.tensor, nc.sync
    SPT = S // 128
    SST = S // 512

    mk_all = p.scoped()
    ident_bf = p.sbuf([128, 128], BF16, "la_ident")
    ones_row = p.sbuf([1, 128], BF16, "la_ones")
    qkvb_h = p.sbuf([128, 24], F32, "la_qkvbh")
    p1b_t = p.sbuf([128, 8], F32, "la_p1b")
    eps_col = p.sbuf([128, 1], F32, "la_eps")
    kng_t = p.sbuf([128, D], BF16, "la_kng")
    knb_t = p.sbuf([128, D], BF16, "la_knb")
    vng_t = p.sbuf([128, D], BF16, "la_vng")
    vnb_t = p.sbuf([128, D], BF16, "la_vnb")
    temp_sb = p.sbuf([1, 16], F32, "la_temp")
    tempcol = p.sbuf([64, 16], F32, "la_tempcol")
    biasall = p.sbuf([64, 16 * 64], F32, "la_biasall")
    kv_bf = p.sbuf([64, 16 * 64], BF16, "la_kv")
    zT_b = p.sbuf([128, 8 * S], BF16, "la_zT")
    onesq_col = p.sbuf([1, 64], BF16, "la_onesqc")
    qkvb_row_sb = p.sbuf([1, 3 * D], BF16, "la_qkvbrow")
    tmp_bf = p.sbuf([1, 16], BF16, "la_tmpbf")
    ps_misc = p.psum([64, 16], F32, "la_psmisc")

    s0 = p.sem()
    g.dma_start(out=ident_bf[:], in_=w["ident"]).then_inc(s0, 16)
    g.memset(ones_row[:], 1.0)
    g.memset(eps_col[:], KVEPS)
    g.memset(onesq_col[:], 1.0)
    g.dma_start(out=qkvb_h[:], in_=w["l1a_qkvb_h"]).then_inc(s0, 16)
    g.dma_start(out=p1b_t[:], in_=w["l1a_p1b"]).then_inc(s0, 16)
    g.dma_start(out=kng_t[:], in_=w["l1a_kng"]).then_inc(s0, 16)
    g.dma_start(out=knb_t[:], in_=w["l1a_knb"]).then_inc(s0, 16)
    g.dma_start(out=vng_t[:], in_=w["l1a_vng"]).then_inc(s0, 16)
    g.dma_start(out=vnb_t[:], in_=w["l1a_vnb"]).then_inc(s0, 16)
    g.dma_start(out=temp_sb[:], in_=w["l1a_temp"]).then_inc(s0, 16)
    g.dma_start(out=biasall[:].rearrange("p (h e) -> p h e", h=16), in_=w["l1a_bias"].rearrange("h d e -> d h e")).then_inc(s0, 16)
    g.dma_start(out=qkvb_row_sb[:], in_=w["l1a_qkvb_row"]).then_inc(s0, 16)
    g.wait_ge(s0, 10 * 16)
    p.barrier()
    v.tensor_scalar(tmp_bf[:], temp_sb[:], 1.0 / S, None, ALU.mult)
    p.barrier()
    t.matmul(ps_misc[:], onesq_col[:], tmp_bf[:], start=True, stop=True)
    p.barrier()
    v.tensor_copy(tempcol[:], ps_misc[:])
    p.barrier()

    xi_r = dram["x_in"].rearrange("(n p) d -> n p d", p=128)
    z0_r = dram["z0"].rearrange("(n p) d -> n p d", p=128)
    xo_r = dram["x_out"].rearrange("(n p) d -> n p d", p=128)

    for b in range(B_LOC):
        mk1 = p.scoped()
        qkvw_sb = p.sbuf([128, 8 * 2048], BF16, f"la_qkvw{b}")
        kln_sb = p.sbuf([128, SPT * D], BF16, f"la_kln{b}")
        vln_sb = p.sbuf([128, SPT * D], BF16, f"la_vln{b}")
        zstage = p.sbuf([128, 2, 1024], BF16, f"la_zstg{b}")
        kstage = p.sbuf([128, 2, D], BF16, f"la_kstg{b}")
        xcg = p.sbuf([128, D], F32, f"la_xcg{b}")
        sqg = p.sbuf([128, D], F32, f"la_sqg{b}")
        t1g = p.sbuf([128, D], F32, f"la_t1g{b}")
        st16 = p.sbuf([128, 96], F32, f"la_st16{b}")
        qstg = p.sbuf([128, 2, 512], BF16, f"la_qstg{b}")
        ps_q = p.psum([128, 512], F32, f"la_psq{b}")
        ps_kv4 = p.psum([128, 4, 512], F32, f"la_pskv{b}")
        ps_ein = p.psum([64, 16 * 64], F32, f"la_psein{b}")
        aux = {"xcg": xcg, "sqg": sqg, "t1g": t1g, "st16": st16, "eps_col": eps_col}

        ld = p.sem()
        nld = 0
        for dc in range(8):
            sy.dma_start(out=qkvw_sb[:, ds_(dc, 1024)][:, 0:1024],
                         in_=w["l1a_qkvwT_bf"][ts(dc, 128), 0:D]).then_inc(ld, 16)
            nld += 16
        for c in range(8):
            sy.dma_start(out=zT_b[:, ds_(c, S)], in_=dram["zlnT"][ts(c, 128), b * S:(b + 1) * S]).then_inc(ld, 16)
            nld += 16
        sy.wait_ge(ld, nld)
        p.barrier()

        mmq = p.sem(); evq = p.sem(); stq = p.sem(); stfree = p.sem()
        r_v_q = p.reg(v, 0); r_sy_q = p.reg(sy, 0); r_t_q = p.reg(t, 0)
        r_v_stf = p.reg(v, 0); r_sy_st = p.reg(sy, 0)
        sy.sem_inc(stfree, 1)
        for fc in range(8):
            with nc.Fori(0, SST) as sti:
                t.wait_ge(evq, r_t_q)
                for dc in range(8):
                    ins = t.matmul(ps_q[:], qkvw_sb[:, dc * 1024 + fc * 128: dc * 1024 + (fc + 1) * 128],
                                   zT_b[:, bass.ds(dc * S + sti * 512, 512)],
                                   start=(dc == 0), stop=(dc == 7))
                ins.then_inc(mmq, 1)
                t.reg_add(r_t_q, r_t_q, 1)
                v.reg_add(r_v_stf, r_v_stf, 1)
                v.wait_ge(stfree, r_v_stf)
                v.reg_add(r_v_q, r_v_q, 1)
                v.wait_ge(mmq, r_v_q)
                v.tensor_scalar(qstg[:].rearrange("p c f -> p (c f)")[:, bass.ds((sti % 2) * 512, 512)],
                                ps_q[:], qkvb_h[:, fc:fc + 1], None, ALU.add).then_inc(evq, 1)
                sy.reg_add(r_sy_q, r_sy_q, 1)
                sy.wait_ge(evq, r_sy_q)
                sy.dma_start(out=dram["qT"][b, ts(fc, 128)][:, bass.ds(sti * 512, 512)],
                             in_=qstg[:].rearrange("p c f -> p (c f)")[:, bass.ds((sti % 2) * 512, 512)]).then_inc(stq, 16)
                sy.reg_add(r_sy_st, r_sy_st, 16)
                sy.wait_ge(stq, r_sy_st)
                sy.sem_inc(stfree, 1)
        p.barrier()

        if _PH < 2:
            p.free_scope(mk1)
            continue
        ldkv = p.sem()
        for dc in range(8):
            sy.dma_start(out=qkvw_sb[:, ds_(dc, 2048)],
                         in_=w["l1a_qkvwT_bf"][ts(dc, 128), D:3 * D]).then_inc(ldkv, 16)
        sy.wait_ge(ldkv, 8 * 16)
        p.barrier()
        sem_v = p.sem(); sem_s = p.sem(); sem_g = p.sem(); mmkv = p.sem(); sem_z = p.sem()
        rv = p.reg(v, 0); rs = p.reg(sc, 0); rg = p.reg(g, 0)
        rv2 = p.reg(v, 0); rvg = p.reg(v, 1)
        r_v_mm = p.reg(v, 0); r_t_z = p.reg(t, 1); r_t_sv = p.reg(t, 0)
        aux.update({"sem_v": sem_v, "sem_s": sem_s, "sem_g": sem_g,
                    "rv": rv, "rs": rs, "rg": rg, "rv2": rv2, "rvg": rvg})
        with nc.Fori(0, SPT // 2) as th:
            for par in (0, 1):
                tt = th * 2 + par
                v.tensor_copy(zstage[:, par, :],
                              zT_b[:].rearrange("p (c s) -> p c s", c=8)[:, :, bass.ds(tt * 128, 128)]).then_inc(sem_z, 1)
                t.wait_ge(sem_z, r_t_z)
                t.wait_ge(sem_v, r_t_sv)
                for part in range(4):
                    for dc in range(8):
                        t.matmul(ps_kv4[:, part, :], zstage[:, par, ts(dc, 128)],
                                 qkvw_sb[:, dc * 2048 + part * 512: dc * 2048 + (part + 1) * 512],
                                 start=(dc == 0), stop=False)
                    ins = t.matmul(ps_kv4[:, part, :], ones_row[:],
                                   qkvb_row_sb[0:1, D + part * 512: D + (part + 1) * 512],
                                   start=False, stop=True)
                ins.then_inc(mmkv, 1)
                t.reg_add(r_t_z, r_t_z, 1)
                t.reg_add(r_t_sv, r_t_sv, 10)
                v.reg_add(r_v_mm, r_v_mm, 1)
                v.wait_ge(mmkv, r_v_mm)
                emit_head_ln(p, nc, ps_kv4[:, 0:2, :].rearrange("p c f -> p (c f)"),
                             kln_sb[:, bass.ds(tt * D, D)], kng_t, knb_t, aux)
                emit_head_ln(p, nc, ps_kv4[:, 2:4, :].rearrange("p c f -> p (c f)"),
                             vln_sb[:, bass.ds(tt * D, D)], vng_t, vnb_t, aux)
        p.barrier()

        if _PH < 3:
            p.free_scope(mk1)
            continue
        mm_e = p.sem(); cp_e = p.sem()
        for c in range(SPT):
            if c >= 2:
                v.wait_ge(mm_e, c - 1)
            v.tensor_copy(kstage[:, c % 2, :], kln_sb[:, ds_(c, D)]).then_inc(cp_e, 1)
            t.wait_ge(cp_e, c + 1)
            for h in range(16):
                ins = t.matmul(ps_ein[:, ts(h, 64)], kstage[:, c % 2, ts(h, 64)],
                               vln_sb[:, c * D + h * 64: c * D + (h + 1) * 64],
                               start=(c == 0 and h % 8 == 0),
                               stop=(c == SPT - 1 and h % 8 == 7))
            ins.then_inc(mm_e, 1)
        v.wait_ge(mm_e, SPT)
        for h in range(16):
            v.scalar_tensor_tensor(kv_bf[:, ts(h, 64)], ps_ein[:, ts(h, 64)],
                                   tempcol[:, h:h + 1], biasall[:, ts(h, 64)],
                                   ALU.mult, ALU.add)
        p.barrier()
        p.free_scope(mk1)

        mk2 = p.scoped()
        attT_sb = p.sbuf([128, 8 * S], BF16, f"la_att{b}")
        h1T_sb = p.sbuf([128, 8 * S], BF16, f"la_h1{b}")
        p1w_sb = p.sbuf([128, 64 * 128], BF16, f"la_p1w{b}")
        p2w_sb = p.sbuf([128, 8 * 1024], BF16, f"la_p2w{b}")
        p2b_row = p.sbuf([1, D], BF16, f"la_p2b{b}")
        qh_sb = p.sbuf([64, 2, S], BF16, f"la_qh{b}")
        h1stage = p.sbuf([128, 2, 1024], BF16, f"la_h1s{b}")
        zt_f = p.sbuf([128, 2, D], F32, f"la_ztf{b}")
        xt_f = p.sbuf([128, 2, D], F32, f"la_xtf{b}")
        tmpo = p.sbuf([128, 2, D], F32, f"la_tmpo{b}")
        xo_t = p.sbuf([128, 2, D], F32, f"la_xo{b}")
        ps_att = p.psum([64, 512], F32, f"la_psa{b}")
        ps_p1 = p.psum([128, 512], F32, f"la_psp1{b}")
        ps_o = p.psum([128, 2, 512], F32, f"la_pso{b}")

        ld2 = p.sem()
        n2 = 0
        for dc in range(8):
            sy.dma_start(out=p1w_sb[:, ds_(dc, 1024)],
                         in_=w["l1a_p1wT_bf"][ts(dc, 128), :].rearrange("p (f c) -> p (f c)") if False else w["l1a_p1wT_bf"][ts(dc, 128), :]).then_inc(ld2, 16)
            sy.dma_start(out=p2w_sb[:, ds_(dc, 1024)], in_=w["l1a_p2wT_bf"][ts(dc, 128), :]).then_inc(ld2, 16)
            n2 += 32
        sy.dma_start(out=p2b_row[:], in_=w["l1a_p2b_bf"]).then_inc(ld2, 16)
        n2 += 16
        sy.wait_ge(ld2, n2)
        p.barrier()

        if _PH < 4:
            p.free_scope(mk2)
            continue
        NH_DBG = int(os.environ.get("L1A_NH", "16"))
        mha = p.sem(); eva = p.sem(); qrdy = p.sem()
        ldq = [p.sem(), p.sem()]
        r_v_a = p.reg(v, 0); r_t_ev2 = p.reg(t, 0); r_t_qr = p.reg(t, 0)
        for h in range(NH_DBG):
            hp = h % 2
            if h >= 2:
                sy.wait_ge(mha, (h - 1) * SST)
            sy.dma_start(out=qh_sb[:, hp, :], in_=dram["qT"][b, h * 64:(h + 1) * 64, :]).then_inc(ldq[hp], 16)
            sy.wait_ge(ldq[hp], (h // 2 + 1) * 16)
            sy.sem_inc(qrdy, 1)
            t.reg_add(r_t_qr, r_t_qr, 1)
            t.wait_ge(qrdy, r_t_qr)
            for sti in range(SST):
                t.wait_ge(eva, r_t_ev2)
                t.matmul(ps_att[:], kv_bf[:, ts(h, 64)],
                         qh_sb[:, hp, sti * 512:(sti + 1) * 512],
                         start=True, stop=True).then_inc(mha, 1)
                t.reg_add(r_t_ev2, r_t_ev2, 1)
                v.reg_add(r_v_a, r_v_a, 1)
                v.wait_ge(mha, r_v_a)
                dr = (h % 2) * 64
                off = (h // 2) * S + sti * 512
                v.tensor_tensor(attT_sb[dr:dr + 64, off:off + 512],
                                ps_att[:],
                                zT_b[dr:dr + 64, off:off + 512],
                                ALU.add).then_inc(eva, 1)
        p.barrier()

        if _PH < 5:
            p.free_scope(mk2)
            continue
        mm1 = p.sem(); ev1 = p.sem()
        r_s_1 = p.reg(sc, 0); r_t_1 = p.reg(t, 0)
        for fc in range(8):
            with nc.Fori(0, SST) as sti:
                t.wait_ge(ev1, r_t_1)
                for dc in range(8):
                    ins = t.matmul(ps_p1[:], p1w_sb[:, (dc * 8 + fc) * 128:(dc * 8 + fc + 1) * 128],
                                   attT_sb[:, bass.ds(dc * S + sti * 512, 512)],
                                   start=(dc == 0), stop=(dc == 7))
                ins.then_inc(mm1, 1)
                t.reg_add(r_t_1, r_t_1, 1)
                sc.reg_add(r_s_1, r_s_1, 1)
                sc.wait_ge(mm1, r_s_1)
                sc.activation(out=h1T_sb[:, bass.ds(fc * S + sti * 512, 512)], in_=ps_p1[:],
                              func=AF.Gelu, bias=p1b_t[:, fc:fc + 1]).then_inc(ev1, 1)
        p.barrier()

        if _PH < 6:
            p.free_scope(mk2)
            continue
        mm2 = p.sem(); cp2 = p.sem(); ev2 = p.sem(); st2 = p.sem(); po2 = p.sem()
        ldx = [p.sem(), p.sem()]
        xrdy = p.sem(); sfree = p.sem()
        r_v_cp = p.reg(v, 0)
        r_t_cp2 = p.reg(t, 0); r_v_m2 = p.reg(v, 0); r_g_e2 = p.reg(g, 0)
        r_sy_st2 = p.reg(sy, 0); r_v_xr = p.reg(v, 0); r_sy_po = p.reg(sy, 0)
        r_sy_po2 = p.reg(sy, 0); r_t_ev3 = p.reg(t, 0)
        r_ldx = [p.reg(sy, 32), p.reg(sy, 0)]
        r_g_sf = p.reg(g, 0)
        sy.sem_inc(sfree, 2)
        sy.dma_start(out=zt_f[:, 0, :], in_=z0_r[b * SPT]).then_inc(ldx[0], 16)
        sy.dma_start(out=xt_f[:, 0, :], in_=xi_r[b * SPT]).then_inc(ldx[0], 16)
        with nc.Fori(0, SPT // 2) as th:
            for par in (0, 1):
                tt = th * 2 + par
                nxt = 1 - par
                sy.wait_ge(po2, r_sy_po)
                sy.reg_add(r_sy_po, r_sy_po, 1)
                sy.dma_start(out=zt_f[:, nxt, :], in_=z0_r[smin(b * SPT + tt + 1, NT - 1)]).then_inc(ldx[nxt], 16)
                sy.dma_start(out=xt_f[:, nxt, :], in_=xi_r[smin(b * SPT + tt + 1, NT - 1)]).then_inc(ldx[nxt], 16)
                sy.reg_add(r_ldx[nxt], r_ldx[nxt], 32)
                sy.wait_ge(ldx[par], r_ldx[par])
                sy.sem_inc(xrdy, 1)
                v.wait_ge(mm2, r_v_cp)
                v.tensor_copy(h1stage[:, par, :],
                              h1T_sb[:].rearrange("p (c s) -> p c s", c=8)[:, :, bass.ds(tt * 128, 128)]).then_inc(cp2, 1)
                v.reg_add(r_v_cp, r_v_cp, 1)
                t.reg_add(r_t_cp2, r_t_cp2, 1)
                t.wait_ge(cp2, r_t_cp2)
                t.wait_ge(ev2, r_t_ev3)
                for half in (0, 1):
                    for hc in range(8):
                        t.matmul(ps_o[:, half, :], h1stage[:, par, ts(hc, 128)],
                                 p2w_sb[:, hc * 1024 + half * 512: hc * 1024 + (half + 1) * 512],
                                 start=(hc == 0), stop=False)
                    ins = t.matmul(ps_o[:, half, :], ones_row[:],
                                   p2b_row[0:1, half * 512:(half + 1) * 512],
                                   start=False, stop=True)
                ins.then_inc(mm2, 1)
                t.reg_add(r_t_ev3, r_t_ev3, 1)
                v.reg_add(r_v_m2, r_v_m2, 1)
                v.wait_ge(mm2, r_v_m2)
                v.reg_add(r_v_xr, r_v_xr, 1)
                v.wait_ge(xrdy, r_v_xr)
                v.tensor_tensor(tmpo[:, par, :], ps_o[:].rearrange("p c f -> p (c f)"),
                                zt_f[:, par, :], ALU.add).then_inc(ev2, 1)
                g.reg_add(r_g_e2, r_g_e2, 1)
                g.wait_ge(ev2, r_g_e2)
                g.reg_add(r_g_sf, r_g_sf, 1)
                g.wait_ge(sfree, r_g_sf)
                g.tensor_tensor(xo_t[:, par, :], tmpo[:, par, :], xt_f[:, par, :], ALU.add).then_inc(po2, 1)
                sy.reg_add(r_sy_po2, r_sy_po2, 1)
                sy.wait_ge(po2, r_sy_po2)
                sy.dma_start(out=xo_r[b * SPT + tt], in_=xo_t[:, par, :]).then_inc(st2, 16)
                sy.reg_add(r_sy_st2, r_sy_st2, 16)
                sy.wait_ge(st2, r_sy_st2)
                sy.sem_inc(sfree, 1)
        sy.wait_ge(ldx[0], r_ldx[0])
        sy.wait_ge(ldx[1], r_ldx[1])
        p.barrier()
        p.free_scope(mk2)
    p.free_scope(mk_all)
    p.barrier()



def emit_l0attn(p, dram, w):
    nc = p.nc
    v, sc, g, t, sy = nc.vector, nc.scalar, nc.gpsimd, nc.tensor, nc.sync
    SST = S // 512

    mk = p.scoped()
    inw_sb = p.sbuf([128, 8 * 2048], BF16, "sa_inw")
    xpw_sb = p.sbuf([128, 16 * 65], BF16, "sa_xpw")
    inb_sb = p.sbuf([128, 16], F32, "sa_inb")
    convw_sb = p.sbuf([128, 16 * 4], F32, "sa_convw")
    convb_sb = p.sbuf([128, 16], F32, "sa_convb")
    xpba_sb = p.sbuf([16, 1], F32, "sa_xpba")
    xpbb_sb = p.sbuf([16, 1], F32, "sa_xpbb")
    outw_aug = p.sbuf([17, 1024], BF16, "sa_outw")
    onescol = p.sbuf([128, 1], BF16, "sa_ones")
    onescol_f = p.sbuf([1, 16], F32, "sa_onesf")
    zT_b = p.sbuf([128, 8 * S], BF16, "sa_zT")
    hx = p.sbuf([128, 2, 3 + S], F32, "sa_hx")
    cacc = p.sbuf([128, 2, S], F32, "sa_cacc")
    hs_bf = p.sbuf([128, 2, S], BF16, "sa_hs")
    e1 = p.sbuf([16, S], F32, "sa_e1")
    e1n = p.sbuf([16, S], F32, "sa_e1n")
    scanA = p.sbuf([16, S], F32, "sa_scanA")
    sb1 = p.sbuf([16, S], F32, "sa_sb1")
    rs_sb = p.sbuf([1, S], F32, "sa_rs")
    scanB = p.sbuf([16, S], F32, "sa_scanB")
    ys_f = p.sbuf([16, S], F32, "sa_ysf")
    ysbf = p.sbuf([17, S], BF16, "sa_ysbf")
    xt_f = p.sbuf([128, 2, D], F32, "sa_xt")
    xo_t = p.sbuf([128, 2, D], F32, "sa_xo")
    ps_h = p.psum([128, 2, 512], F32, "sa_psh")
    ps_xpF = p.psum([128, S], F32, "sa_psxp")
    ps_xp = ps_xpF[0:65, :]
    ps_bc = ps_h[0:16, 0, :]

    s0 = p.sem()
    n0 = 0
    for dc in range(8):
        sy.dma_start(out=inw_sb[:, ds_(dc, 2048)], in_=w["l0a_inwT_bf"][ts(dc, 128), :]).then_inc(s0, 16)
        n0 += 16
    for c in range(16):
        sy.dma_start(out=xpw_sb[:, ds_(c, 65)],
                     in_=w["l0a_xpwT_pad_bf"][ts(c, 128), :]).then_inc(s0, 16)
        n0 += 16
    sy.dma_start(out=inb_sb[:], in_=w["l0a_inb"]).then_inc(s0, 16)
    n0 += 16
    for c in range(16):
        sy.dma_start(out=convw_sb[:, ds_(c, 4)], in_=w["l0a_convw"][c]).then_inc(s0, 16)
        n0 += 16
    sy.dma_start(out=convb_sb[:], in_=w["l0a_convb"]).then_inc(s0, 16)
    n0 += 16
    sy.dma_start(out=xpba_sb[:], in_=w["l0a_xpb"][0:16, :]).then_inc(s0, 16)
    n0 += 16
    sy.dma_start(out=xpbb_sb[:], in_=w["l0a_xpb"][16:32, :]).then_inc(s0, 16)
    n0 += 16
    sy.dma_start(out=outw_aug[0:16, :], in_=w["l0a_outwT_bf"]).then_inc(s0, 16)
    n0 += 16
    sy.dma_start(out=outw_aug[16:17, :], in_=w["l0a_outb_bf"]).then_inc(s0, 16)
    n0 += 16
    sy.wait_ge(s0, n0)
    g.memset(onescol[:], 1.0)
    g.memset(onescol_f[:], 1.0)
    g.memset(hx[:, 0, 0:3], 0.0)
    g.memset(hx[:, 1, 0:3], 0.0)
    g.memset(ysbf[:], 1.0)
    p.barrier()

    xi_r = dram["x_in"].rearrange("(n p) d -> n p d", p=128)
    xo_r = dram["x_out"].rearrange("(n p) d -> n p d", p=128)

    for b in range(B_LOC):
        mkb = p.scoped()
        ld = p.sem(); mmh = p.sem(); evh = p.sem(); cvA = p.sem(); cvB = p.sem()
        svu = p.sem(); xpd = p.sem()
        nld = 0
        for c in range(8):
            sy.dma_start(out=zT_b[:, ds_(c, S)], in_=dram["zlnT"][ts(c, 128), b * S:(b + 1) * S]).then_inc(ld, 16)
            nld += 16
        sy.wait_ge(ld, nld)
        p.barrier()

        n_mmh = 0; n_evh = 0; n_cvA = 0; n_cvB = 0; n_svu = 0
        for fc in range(16):
            par = fc % 2
            for st in range(SST):
                if n_evh - 2 + 1 > 0:
                    t.wait_ge(evh, n_evh - 1)
                for dc in range(8):
                    ins = t.matmul(ps_h[:, st % 2, :],
                                   inw_sb[:, dc * 2048 + fc * 128: dc * 2048 + (fc + 1) * 128],
                                   zT_b[:, dc * S + st * 512: dc * S + (st + 1) * 512],
                                   start=(dc == 0), stop=(dc == 7))
                ins.then_inc(mmh, 1)
                n_mmh += 1
                sc.wait_ge(mmh, n_mmh)
                sc.activation(out=hx[:, par, 3 + st * 512: 3 + (st + 1) * 512],
                              in_=ps_h[:, st % 2, :], func=AF.Identity,
                              bias=inb_sb[:, fc:fc + 1]).then_inc(evh, 1)
                n_evh += 1
            v.wait_ge(evh, n_evh)
            if fc >= 1:
                v.wait_ge(svu, n_svu)
            v.tensor_scalar(cacc[:, 0, :], hx[:, par, 0:S], convw_sb[:, fc * 4 + 0: fc * 4 + 1],
                            None, ALU.mult).then_inc(cvA, 1)
            n_cvA += 1
            v.wait_ge(cvA, n_cvA)
            v.scalar_tensor_tensor(cacc[:, 1, :], hx[:, par, 1:1 + S],
                                   convw_sb[:, fc * 4 + 1: fc * 4 + 2], cacc[:, 0, :],
                                   ALU.mult, ALU.add).then_inc(cvB, 1)
            n_cvB += 1
            v.wait_ge(cvB, n_cvB)
            v.scalar_tensor_tensor(cacc[:, 0, :], hx[:, par, 2:2 + S],
                                   convw_sb[:, fc * 4 + 2: fc * 4 + 3], cacc[:, 1, :],
                                   ALU.mult, ALU.add).then_inc(cvA, 1)
            n_cvA += 1
            v.wait_ge(cvA, n_cvA)
            v.scalar_tensor_tensor(cacc[:, 1, :], hx[:, par, 3:3 + S],
                                   convw_sb[:, fc * 4 + 3: fc * 4 + 4], cacc[:, 0, :],
                                   ALU.mult, ALU.add).then_inc(cvB, 1)
            n_cvB += 1
            sc.wait_ge(cvB, n_cvB)
            if fc >= 2:
                sc.wait_ge(xpd, fc - 1)
            sc.activation(out=hs_bf[:, par, :], in_=cacc[:, 1, :], func=AF.Silu,
                          bias=convb_sb[:, fc:fc + 1]).then_inc(svu, 1)
            n_svu += 1
            t.wait_ge(svu, n_svu)
            for st in range(SST):
                ins = t.matmul(ps_xpF[0:65, st * 512:(st + 1) * 512],
                               xpw_sb[:, fc * 65:(fc + 1) * 65],
                               hs_bf[:, par, st * 512:(st + 1) * 512],
                               start=(fc == 0), stop=(fc == 15))
            ins.then_inc(xpd, 1)

        s_rs = p.sem(); s_e1 = p.sem(); s_e1n = p.sem(); s_A = p.sem()
        s_sb1 = p.sem(); mmbc = p.sem(); s_B = p.sem(); scn = p.sem()
        v.wait_ge(xpd, 16)
        v.tensor_copy(rs_sb[:], ps_xp[64:65, :]).then_inc(s_rs, 1)
        sc.wait_ge(xpd, 16)
        sc.activation(out=e1[:], in_=ps_xp[0:16, :], func=AF.Exp,
                      bias=xpba_sb[:, 0:1]).then_inc(s_e1, 1)
        v.wait_ge(s_e1, 1)
        v.tensor_scalar(e1n[:], e1[:], -1.0, None, ALU.mult).then_inc(s_e1n, 1)
        sc.wait_ge(s_e1n, 1)
        sc.activation(out=scanA[:], in_=e1n[:], func=AF.Exp).then_inc(s_A, 1)
        sc.activation(out=sb1[:], in_=ps_xp[32:48, :], func=AF.Silu,
                      bias=xpbb_sb[:, 0:1]).then_inc(s_sb1, 1)
        t.wait_ge(s_rs, 1)
        t.wait_ge(evh, n_evh)
        for st in range(SST):
            if st > 0:
                t.wait_ge(s_B, st)
            t.matmul(ps_bc, onescol_f[0:1, :], rs_sb[0:1, st * 512:(st + 1) * 512],
                     start=True, stop=True).then_inc(mmbc, 1)
            v.wait_ge(mmbc, st + 1)
            v.wait_ge(s_sb1, 1)
            v.tensor_tensor(scanB[:, st * 512:(st + 1) * 512],
                            sb1[:, st * 512:(st + 1) * 512], ps_bc,
                            ALU.mult).then_inc(s_B, 1)
        v.wait_ge(s_A, 1)
        v.wait_ge(s_B, SST)
        v.tensor_tensor_scan(ys_f[:], scanA[:], scanB[:], 0.0, ALU.mult,
                             ALU.add).then_inc(scn, 1)
        g.wait_ge(scn, 1)
        g.tensor_copy(ysbf[0:16, :], ys_f[:]).then_inc(scn, 1)
        p.barrier()

        ldx = [p.sem(), p.sem()]
        mmo = p.sem(); evo = p.sem(); po = p.sem(); st2 = p.sem(); sfree = p.sem()
        nldx = [0, 0]
        n_mmo = 0; n_evo = 0; n_st2 = 0; n_po = 0
        sy.sem_inc(sfree, 2)
        sy.dma_start(out=xt_f[:, 0, :], in_=xi_r[b * 16]).then_inc(ldx[0], 16)
        nldx[0] += 16
        for tt in range(16):
            par = tt % 2
            nxt = 1 - par
            if tt < 15:
                sy.dma_start(out=xt_f[:, nxt, :], in_=xi_r[b * 16 + tt + 1]).then_inc(ldx[nxt], 16)
                nldx[nxt] += 16
            sy.wait_ge(ldx[par], nldx[par])
            sy.sem_inc(po, 1)
            n_po += 1
            if tt >= 2:
                t.wait_ge(evo, n_evo - 1)
            pso = ps_h[:].rearrange("p c f -> p (c f)") if tt % 2 == 0 else ps_xpF[:, 0:1024]
            for half in (0, 1):
                ins = t.matmul(pso[:, half * 512:(half + 1) * 512],
                               ysbf[0:17, tt * 128:(tt + 1) * 128],
                               outw_aug[0:17, half * 512:(half + 1) * 512],
                               start=True, stop=True)
            ins.then_inc(mmo, 1)
            n_mmo += 1
            v.wait_ge(mmo, n_mmo)
            v.wait_ge(po, n_po)
            v.wait_ge(sfree, tt)
            v.tensor_tensor(xo_t[:, par, :], pso,
                            xt_f[:, par, :], ALU.add).then_inc(evo, 1)
            n_evo += 1
            sy.wait_ge(evo, n_evo)
            sy.dma_start(out=xo_r[b * 16 + tt], in_=xo_t[:, par, :]).then_inc(st2, 16)
            n_st2 += 16
            sy.wait_ge(st2, n_st2)
            sy.sem_inc(sfree, 1)
        p.barrier()
        p.free_scope(mkb)
    p.free_scope(mk)
    p.barrier()



def emit_l1ffn_qkv(p, dram, w, consts):
    nc = p.nc
    v, sc, g, t, sy = nc.vector, nc.scalar, nc.gpsimd, nc.tensor, nc.sync
    SST = S // 512
    mk = p.scoped()
    qkvw_sb = p.sbuf([128, 8 * 3072], BF16, "lf_qkvw")
    zT_b = p.sbuf([128, 8 * S], BF16, "lf_zT")
    qtmp = p.sbuf([128, 2, 512], BF16, "lf_qtmp")
    t1s = p.sbuf([128, 2, 512], F32, "lf_t1")
    t2s = p.sbuf([128, 2, 512], F32, "lf_t2")
    ostg = p.sbuf([128, 2, 512], BF16, "lf_ostg")
    zstage = p.sbuf([128, 2, 1024], BF16, "lf_zstg")
    vstg = p.sbuf([128, 2, 1024], BF16, "lf_vstg")
    qkvb_h = p.sbuf([128, 24], F32, "lf_qkvbh")
    qkvb_row = p.sbuf([1, 3 * D], BF16, "lf_qkvbr")
    ones_row = consts["ones_row"]
    rot_t = p.sbuf([128, 128], BF16, "lf_rot")
    rc_t = p.sbuf([128, S], BF16, "lf_rc")
    rs_t = p.sbuf([128, S], BF16, "lf_rs")
    rcq_t = p.sbuf([128, S], BF16, "lf_rcq")
    rsq_t = p.sbuf([128, S], BF16, "lf_rsq")
    ps_q = p.psum([128, 2, 512], F32, "lf_psq")
    ps_rot = p.psum([128, 2, 512], F32, "lf_psrot")
    ps_kv = p.psum([128, 2, 1024], F32, "lf_pskv")

    s0 = p.sem()
    s0g = p.sem()
    sy.dma_start(out=qkvb_h[:], in_=w["l1f_qkvb_h"]).then_inc(s0, 16)
    sy.dma_start(out=qkvb_row[:], in_=w["l1f_qkvb_row"]).then_inc(s0, 16)
    g.dma_start(out=rot_t[:], in_=w["rotmat"]).then_inc(s0g, 16)
    g.dma_start(out=rc_t[:], in_=w["rope_cos"]).then_inc(s0g, 16)
    g.dma_start(out=rs_t[:], in_=w["rope_sin"]).then_inc(s0g, 16)
    g.dma_start(out=rcq_t[:], in_=w["rope_cos_q"]).then_inc(s0g, 16)
    g.dma_start(out=rsq_t[:], in_=w["rope_sin_q"]).then_inc(s0g, 16)
    sy.wait_ge(s0, 32)
    g.wait_ge(s0g, 80)
    p.barrier()
    lconsts = {"rotmat": rot_t[:], "rope_cos": rc_t[:], "rope_sin": rs_t[:],
               "ropeq_cos": rcq_t[:], "ropeq_sin": rsq_t[:]}

    for b in range(B_LOC):
        mkb = p.scoped()
        ld = p.sem()
        nld = 0
        for dc in range(8):
            sy.dma_start(out=qkvw_sb[:, ds_(dc, 3072)], in_=w["l1f_qkvwT_bf"][ts(dc, 128), :]).then_inc(ld, 16)
            nld += 16
        for c in range(8):
            sy.dma_start(out=zT_b[:, ds_(c, S)], in_=dram["zlnT"][ts(c, 128), b * S:(b + 1) * S]).then_inc(ld, 16)
            nld += 16
        sy.wait_ge(ld, nld)
        p.barrier()

        for which, dst, cosname, sinname in [("q", dram["qT"], "ropeq_cos", "ropeq_sin"),
                                             ("k", dram["kT"], "rope_cos", "rope_sin")]:
            fbase = 0 if which == "q" else D
            cos_t, sin_t = lconsts[cosname], lconsts[sinname]
            mmq = p.sem(); eva = p.sem(); evr = p.sem(); mr = p.sem()
            evf = p.sem(); stq = p.sem(); stdone = p.sem()
            r_t_ev = p.reg(t, 0); r_s_mm = p.reg(sc, 0); r_t_ea = p.reg(t, 0)
            r_v_mr = p.reg(v, 0); r_g_ev = p.reg(g, 0); r_sy_f = p.reg(sy, 0)
            r_sy_st = p.reg(sy, 0); r_v_std = p.reg(v, 0); r_s_evf = p.reg(sc, 0)
            sy.sem_inc(stdone, 2)
            for fc in range(8):
                bcol = fbase // 128 + fc
                with nc.Fori(0, SST) as sti:
                    par = sti % 2
                    t.wait_ge(evf, r_t_ev)
                    for dc in range(8):
                        ins = t.matmul(ps_q[:, par, :],
                                       qkvw_sb[:, dc * 3072 + fbase + fc * 128: dc * 3072 + fbase + (fc + 1) * 128],
                                       zT_b[:, bass.ds(dc * S + sti * 512, 512)],
                                       start=(dc == 0), stop=(dc == 7))
                    ins.then_inc(mmq, 1)
                    sc.reg_add(r_s_mm, r_s_mm, 1)
                    sc.wait_ge(mmq, r_s_mm)
                    sc.wait_ge(evf, r_s_evf)
                    sc.reg_add(r_s_evf, r_s_evf, 2)
                    sc.activation(out=qtmp[:, par, :], in_=ps_q[:, par, :], func=AF.Identity,
                                  bias=qkvb_h[:, bcol:bcol + 1]).then_inc(eva, 1)
                    t.reg_add(r_t_ea, r_t_ea, 1)
                    t.wait_ge(eva, r_t_ea)
                    t.matmul(ps_rot[:, par, :], lconsts["rotmat"], qtmp[:, par, :],
                             start=True, stop=True).then_inc(mr, 1)
                    t.reg_add(r_t_ev, r_t_ev, 2)
                    v.reg_add(r_v_mr, r_v_mr, 1)
                    v.wait_ge(mr, r_v_mr)
                    v.reg_add(r_v_std, r_v_std, 1)
                    v.wait_ge(stdone, r_v_std)
                    v.tensor_tensor(t1s[:, par, :], qtmp[:, par, :],
                                    cos_t[:, bass.ds(sti * 512, 512)], ALU.mult)
                    v.tensor_tensor(t2s[:, par, :], ps_rot[:, par, :],
                                    sin_t[:, bass.ds(sti * 512, 512)], ALU.mult).then_inc(evr, 2)
                    g.reg_add(r_g_ev, r_g_ev, 2)
                    g.wait_ge(evr, r_g_ev)
                    g.tensor_tensor(ostg[:, par, :], t1s[:, par, :], t2s[:, par, :],
                                    ALU.add).then_inc(evf, 2)
                    sy.reg_add(r_sy_f, r_sy_f, 2)
                    sy.wait_ge(evf, r_sy_f)
                    sy.dma_start(out=dst[b, ts(fc, 128)][:, bass.ds(sti * 512, 512)],
                                 in_=ostg[:, par, :]).then_inc(stq, 16)
                    sy.reg_add(r_sy_st, r_sy_st, 16)
                    sy.wait_ge(stq, r_sy_st)
                    sy.sem_inc(stdone, 1)
            p.barrier()

        sem_z = p.sem(); mmv = p.sem(); evv = p.sem(); stv = p.sem(); svd = p.sem()
        r_t_z = p.reg(t, 1); r_t_ev = p.reg(t, 0); r_s_mm = p.reg(sc, 0)
        r_sy_ev = p.reg(sy, 0); r_sy_st = p.reg(sy, 0); r_s_svd = p.reg(sc, 0)
        sy.sem_inc(svd, 2)
        r_v_mm = p.reg(v, 0)
        with nc.Fori(0, S // 128 // 2) as th:
            for par in (0, 1):
                tt = th * 2 + par
                v.wait_ge(mmv, r_v_mm)
                v.reg_add(r_v_mm, r_v_mm, 1)
                v.tensor_copy(zstage[:, par, :],
                              zT_b[:].rearrange("p (c s) -> p c s", c=8)[:, :, bass.ds(tt * 128, 128)]).then_inc(sem_z, 1)
                t.wait_ge(sem_z, r_t_z)
                t.wait_ge(evv, r_t_ev)
                for half in range(2):
                    for dc in range(8):
                        t.matmul(ps_kv[:, par, ts(half, 512)], zstage[:, par, ts(dc, 128)],
                                 qkvw_sb[:, dc * 3072 + 2 * D + half * 512: dc * 3072 + 2 * D + (half + 1) * 512],
                                 start=(dc == 0), stop=False)
                    ins = t.matmul(ps_kv[:, par, ts(half, 512)], ones_row,
                                   qkvb_row[0:1, 2 * D + half * 512: 2 * D + (half + 1) * 512],
                                   start=False, stop=True)
                ins.then_inc(mmv, 1)
                t.reg_add(r_t_z, r_t_z, 1)
                t.reg_add(r_t_ev, r_t_ev, 1)
                sc.reg_add(r_s_mm, r_s_mm, 1)
                sc.wait_ge(mmv, r_s_mm)
                sc.reg_add(r_s_svd, r_s_svd, 1)
                sc.wait_ge(svd, r_s_svd)
                sc.activation(out=vstg[:, par, :], in_=ps_kv[:, par, :],
                              func=AF.Identity).then_inc(evv, 1)
                sy.reg_add(r_sy_ev, r_sy_ev, 1)
                sy.wait_ge(evv, r_sy_ev)
                sy.dma_start(out=dram["v_tok"][b, bass.ds(tt * 128, 128), :],
                             in_=vstg[:, par, :]).then_inc(stv, 16)
                sy.reg_add(r_sy_st, r_sy_st, 16)
                sy.wait_ge(stv, r_sy_st)
                sy.sem_inc(svd, 1)
        p.barrier()
        p.free_scope(mkb)
    p.free_scope(mk)
    p.barrier()


def emit_l1ffn_attn(p, dram, w, consts):
    nc = p.nc
    v, sc, g, t, sy = nc.vector, nc.scalar, nc.gpsimd, nc.tensor, nc.sync
    mk = p.scoped()
    q_aug = p.sbuf([72, S], BF16, "lf_qaug")
    k_aug = p.sbuf([72, S], BF16, "lf_kaug")
    v_sb = p.sbuf([128, 16 * 65], BF16, "lf_vsb")
    sqv = p.sbuf([64, 2, S], F32, "lf_sqv")
    nrm = p.sbuf([1, S + 16], F32, "lf_nrm")
    expT = p.sbuf([128, 16 * 512], BF16, "lf_expT")
    attnout = p.sbuf([128, 16 * 1024], BF16, "lf_ao")
    rcp = p.sbuf([128, 4], F32, "lf_rcp")
    astage = p.sbuf([128, 1024], BF16, "lf_astg")
    aT_st = p.sbuf([128, 1024], BF16, "lf_aTst")
    outw_sb = p.sbuf([128, 8 * 1024], BF16, "lf_outw")
    outb_row = p.sbuf([1, D], BF16, "lf_outbr")
    zt_f = p.sbuf([128, 2, D], F32, "lf_ztf")
    z1_t = p.sbuf([128, 2, D], F32, "lf_z1t")
    ps_e = p.psum([128, 2, 512], F32, "lf_pse")
    ps_pv = p.psum([128, 4, 65], F32, "lf_pspv")
    ps_n = p.psum([33, 512], F32, "lf_psn")
    ps_tr = p.psum([128, 8, 128], BF16, "lf_pstr")
    ps_o = p.psum([128, 2, 512], F32, "lf_pso")

    s0 = p.sem()
    n0 = 0
    for dc in range(8):
        sy.dma_start(out=outw_sb[:, ds_(dc, 1024)], in_=w["l1f_outwT_bf"][ts(dc, 128), :]).then_inc(s0, 16)
        n0 += 16
    sy.dma_start(out=outb_row[:], in_=w["l1f_outb_row"]).then_inc(s0, 16)
    n0 += 16
    sy.wait_ge(s0, n0)
    g.memset(v_sb[:].rearrange("p (c w) -> p c w", c=16)[:, :, 64:65], 1.0)
    g.memset(k_aug[64:65, :], 1.0)
    p.barrier()

    z0_r = dram["z0"].rearrange("(n p) d -> n p d", p=128)
    z1_r = dram["z1"].rearrange("(n p) d -> n p d", p=128)

    for b in range(B_LOC):
        mkb = p.scoped()
        ld = p.sem(); hrdy = p.sem()
        mmn = p.sem(); s_sq = p.sem(); s_km = p.sem(); s_rt = p.sem(); vbr = p.sem()
        sq1 = p.sem(); sq2 = p.sem()
        mme = p.sem(); eve = [p.sem(), p.sem()]; mmpv = p.sem(); v_rc = p.sem(); evpv = p.sem()
        r_sy_pv = p.reg(sy, 0); r_sy_ld = p.reg(sy, 0)
        r_s_hr = p.reg(sc, 0)
        r_t_sq1 = p.reg(t, 0); r_t_sq2 = p.reg(t, 0); r_t_nc = p.reg(t, 0)
        r_t_nc2 = p.reg(t, 0); r_v_ssq = p.reg(v, 0); r_s_vbr = p.reg(sc, 0)
        r_s_mmn = p.reg(sc, 0); r_v_mmn = p.reg(v, 0)
        r_v_skc = p.reg(v, 0); r_s_km = p.reg(sc, 0); r_v_rt = p.reg(v, 0)
        r_s_nq = p.reg(sc, 0)
        r_t_vbr = p.reg(t, 0)
        r_t_eve = [p.reg(t, 0), p.reg(t, 0)]; r_s_mme = p.reg(sc, 0); r_s_pv = p.reg(sc, 0)
        r_t_pve = [p.reg(t, 1), p.reg(t, 1)]; r_t_evp = p.reg(t, 0)
        r_v_pv = p.reg(v, 0); r_s_rc = p.reg(sc, 0)
        with nc.Fori(0, 16) as h:
            sy.wait_ge(mmpv, r_sy_pv)
            sy.reg_add(r_sy_pv, r_sy_pv, 16)
            sy.dma_start(out=q_aug[0:64, :], in_=dram["qT"][b, bass.ds(h * 64, 64), :]).then_inc(ld, 16)
            sy.dma_start(out=k_aug[0:64, :], in_=dram["kT"][b, bass.ds(h * 64, 64), :]).then_inc(ld, 16)
            sy.dma_start(out=v_sb[:].rearrange("p (c w) -> p c w", c=16)[:, :, 0:64],
                         in_=dram["v_tok"][b][:, bass.ds(h * 64, 64)].rearrange("(c p) w -> p c w", p=128)).then_inc(ld, 16)
            sy.reg_add(r_sy_ld, r_sy_ld, 48)
            sy.wait_ge(ld, r_sy_ld)
            sy.sem_inc(hrdy, 1)
            sc.reg_add(r_s_hr, r_s_hr, 1)
            sc.wait_ge(hrdy, r_s_hr)
            sc.activation(out=sqv[:, 0, :], in_=q_aug[0:64, :], func=AF.Square).then_inc(sq1, 1)
            sc.activation(out=sqv[:, 1, :], in_=k_aug[0:64, :], func=AF.Square).then_inc(sq2, 1)
            t.reg_add(r_t_sq1, r_t_sq1, 1)
            t.wait_ge(sq1, r_t_sq1)
            t.reg_add(r_t_sq2, r_t_sq2, 1)
            t.wait_ge(sq2, r_t_sq2)
            for stq in range(4):
                t.wait_ge(s_sq, r_t_nc)
                t.wait_ge(s_km, r_t_nc2)
                t.reg_add(r_t_nc, r_t_nc, 1)
                t.reg_add(r_t_nc2, r_t_nc2, 1)
                t.matmul(ps_n[0:1, :], consts["ones64c"], sqv[0:64, 0, :][:, ts(stq, 512)],
                         start=True, stop=True)
                t.matmul(ps_n[32:33, :], consts["ones64c"], sqv[0:64, 1, :][:, ts(stq, 512)],
                         start=True, stop=True).then_inc(mmn, 1)
                sc.reg_add(r_s_mmn, r_s_mmn, 1)
                sc.wait_ge(mmn, r_s_mmn)
                if stq == 0:
                    sc.wait_ge(vbr, r_s_vbr)
                    sc.reg_add(r_s_vbr, r_s_vbr, 1)
                sc.activation(out=nrm[0:1, ts(stq, 512)], in_=ps_n[0:1, :],
                              func=AF.Sqrt).then_inc(s_sq, 1)
                v.reg_add(r_v_mmn, r_v_mmn, 1)
                v.wait_ge(mmn, r_v_mmn)
                v.tensor_reduce(nrm[0:1, S + stq:S + stq + 1], ps_n[32:33, :], AX.X,
                                ALU.max).then_inc(s_km, 1)
            t.reg_add(r_t_nc2, r_t_nc2, 1)
            v.reg_add(r_v_skc, r_v_skc, 4)
            v.wait_ge(s_km, r_v_skc)
            v.tensor_reduce(nrm[0:1, S + 8:S + 9], nrm[0:1, S:S + 4], AX.X,
                            ALU.max).then_inc(s_km, 1)
            v.reg_add(r_v_skc, r_v_skc, 1)
            sc.reg_add(r_s_km, r_s_km, 5)
            sc.wait_ge(s_km, r_s_km)
            sc.activation(out=nrm[0:1, S + 9:S + 10], in_=nrm[0:1, S + 8:S + 9],
                          func=AF.Sqrt).then_inc(s_rt, 1)
            v.reg_add(r_v_rt, r_v_rt, 1)
            v.wait_ge(s_rt, r_v_rt)
            v.reg_add(r_v_ssq, r_v_ssq, 4)
            v.wait_ge(s_sq, r_v_ssq)
            v.tensor_scalar(q_aug[64:65, :], nrm[0:1, 0:S], nrm[0:1, S + 9:S + 10],
                            -1.0, ALU.mult, ALU.mult).then_inc(vbr, 1)
            t.reg_add(r_t_vbr, r_t_vbr, 1)
            t.wait_ge(vbr, r_t_vbr)
            for qg in range(4):
                sc.wait_ge(mmpv, r_s_pv)
                for kc in range(16):
                    t.wait_ge(eve[kc % 2], r_t_eve[kc % 2])
                    t.reg_add(r_t_eve[kc % 2], r_t_eve[kc % 2], 1)
                    t.matmul(ps_e[:, kc % 2, :], k_aug[0:65, ts(kc, 128)],
                             q_aug[0:65, ts(qg, 512)], start=True, stop=True).then_inc(mme, 1)
                    sc.reg_add(r_s_mme, r_s_mme, 1)
                    sc.wait_ge(mme, r_s_mme)
                    sc.activation(out=expT[:, ds_(kc, 512)], in_=ps_e[:, kc % 2, :],
                                  func=AF.Exp).then_inc(eve[kc % 2], 1)
                sc.reg_add(r_s_pv, r_s_pv, 4)
                t.wait_ge(evpv, r_t_evp)
                t.reg_add(r_t_evp, r_t_evp, 4)
                for kc in range(16):
                    t.wait_ge(eve[kc % 2], r_t_pve[kc % 2])
                    t.reg_add(r_t_pve[kc % 2], r_t_pve[kc % 2], 1)
                    for qs in range(4):
                        ins = t.matmul(ps_pv[:, qs, :], expT[:, kc * 512 + qs * 128: kc * 512 + (qs + 1) * 128],
                                       v_sb[:, ds_(kc, 65)],
                                       start=(kc == 0 and qs == 0),
                                       stop=(kc == 15 and qs == 3))
                        if kc == 15:
                            ins.then_inc(mmpv, 1)
                v.reg_add(r_v_pv, r_v_pv, 4)
                v.wait_ge(mmpv, r_v_pv)
                for qs in range(4):
                    v.reciprocal(rcp[:, qs:qs + 1], ps_pv[:, qs, 64:65]).then_inc(v_rc, 1)
                sc.reg_add(r_s_rc, r_s_rc, 4)
                sc.wait_ge(v_rc, r_s_rc)
                for qs in range(4):
                    sc.activation(out=attnout[:, bass.ds((qg * 4 + qs) * 1024 + h * 64, 64)],
                                  in_=ps_pv[:, qs, 0:64], func=AF.Identity,
                                  scale=rcp[:, qs:qs + 1]).then_inc(evpv, 1)
        p.barrier()

        ldz = [p.sem(), p.sem()]
        zrdy = p.sem(); cpa = p.sem(); tpa = p.sem(); cpt = p.sem()
        mmo = p.sem(); evz = p.sem(); stz = p.sem(); szf = p.sem()
        r_ldz = [p.reg(sy, 16), p.reg(sy, 0)]
        r_sy_ev = p.reg(sy, 0)
        r_sy_st = p.reg(sy, 0)
        r_v_mmo = p.reg(v, 0)
        r_t_cpa = p.reg(t, 0); r_v_tpa = p.reg(v, 0); r_t_cpt = p.reg(t, 0)
        r_v_zr = p.reg(v, 0); r_v_szf = p.reg(v, 0)
        r_t_evz = p.reg(t, 0)
        sy.sem_inc(szf, 2)
        sy.dma_start(out=zt_f[:, 0, :], in_=z0_r[b * 16]).then_inc(ldz[0], 16)
        with nc.Fori(0, 8) as th:
            for par in (0, 1):
                qt = th * 2 + par
                nxt = 1 - par
                sy.dma_start(out=zt_f[:, nxt, :], in_=z0_r[smin(b * 16 + qt + 1, NT - 1)]).then_inc(ldz[nxt], 16)
                sy.reg_add(r_ldz[nxt], r_ldz[nxt], 16)
                sy.wait_ge(ldz[par], r_ldz[par])
                sy.sem_inc(zrdy, 1)
                v.wait_ge(mmo, r_v_mmo)
                v.tensor_copy(astage[:], attnout[:, bass.ds(qt * 1024, 1024)]).then_inc(cpa, 1)
                t.reg_add(r_t_cpa, r_t_cpa, 1)
                t.wait_ge(cpa, r_t_cpa)
                t.wait_ge(cpt, r_t_cpt)
                for c in range(8):
                    ins = t.transpose(ps_tr[:, c, :], astage[:, ts(c, 128)], consts["ident_bf"])
                ins.then_inc(tpa, 1)
                v.reg_add(r_v_tpa, r_v_tpa, 1)
                v.wait_ge(tpa, r_v_tpa)
                v.tensor_copy(aT_st[:], ps_tr[:].rearrange("p c f -> p (c f)")).then_inc(cpt, 1)
                t.reg_add(r_t_cpt, r_t_cpt, 1)
                t.wait_ge(cpt, r_t_cpt)
                t.wait_ge(evz, r_t_evz)
                for half in (0, 1):
                    for c in range(8):
                        t.matmul(ps_o[:, half, :], aT_st[:, ts(c, 128)],
                                 outw_sb[:, c * 1024 + half * 512: c * 1024 + (half + 1) * 512],
                                 start=(c == 0), stop=False)
                    ins = t.matmul(ps_o[:, half, :], consts["ones_row"],
                                   outb_row[0:1, half * 512:(half + 1) * 512],
                                   start=False, stop=True)
                ins.then_inc(mmo, 1)
                t.reg_add(r_t_evz, r_t_evz, 1)
                v.reg_add(r_v_mmo, r_v_mmo, 1)
                v.wait_ge(mmo, r_v_mmo)
                v.reg_add(r_v_zr, r_v_zr, 1)
                v.wait_ge(zrdy, r_v_zr)
                v.wait_ge(szf, r_v_szf)
                v.reg_add(r_v_szf, r_v_szf, 1)
                v.tensor_tensor(z1_t[:, par, :], ps_o[:].rearrange("p c f -> p (c f)"),
                                zt_f[:, par, :], ALU.add).then_inc(evz, 1)
                sy.reg_add(r_sy_ev, r_sy_ev, 1)
                sy.wait_ge(evz, r_sy_ev)
                sy.dma_start(out=z1_r[b * 16 + qt], in_=z1_t[:, par, :]).then_inc(stz, 16)
                sy.reg_add(r_sy_st, r_sy_st, 16)
                sy.wait_ge(stz, r_sy_st)
                sy.sem_inc(szf, 1)
        sy.wait_ge(ldz[0], r_ldz[0])
        sy.wait_ge(ldz[1], r_ldz[1])
        p.barrier()
        p.free_scope(mkb)
    p.free_scope(mk)
    p.barrier()


def emit_l1ffn_moe(p, dram, w, consts):
    nc = p.nc
    v, sc, g, t, sy = nc.vector, nc.scalar, nc.gpsimd, nc.tensor, nc.sync
    mk = p.scoped()
    gstage = p.sbuf([128, 2, 1024], BF16, "lm_gstg")
    gw_sb = p.sbuf([128, 8 * 8], BF16, "lm_gw")
    gsc = p.sbuf([128, 48], F32, "lm_gsc")
    comb_sb = p.sbuf([128, 64 * 8], F32, "lm_comb")
    w1_sb = p.sbuf([128, 8 * 2048], BF16, "lm_w1")
    w2_sb = p.sbuf([128, 8 * 2048], BF16, "lm_w2")
    w3_sb = p.sbuf([128, 16 * 1024], BF16, "lm_w3")
    b1_sb = p.sbuf([128, 16], F32, "lm_b1")
    b2_sb = p.sbuf([128, 16], F32, "lm_b2")
    b3_row = p.sbuf([1, 1024], BF16, "lm_b3")
    zst = p.sbuf([128, 2, 8 * 512], BF16, "lm_zst")
    s1_bf = p.sbuf([128, 2, 512], BF16, "lm_s1")
    s2_bf = p.sbuf([128, 2, 512], BF16, "lm_s2")
    hh_sb = p.sbuf([128, 16 * 512], BF16, "lm_hh")
    hstage = p.sbuf([128, 2, 16 * 128], BF16, "lm_hstg")
    yt = p.sbuf([128, 4, 1024], F32, "lm_yt")
    xt = p.sbuf([128, 2, 1024], F32, "lm_xt")
    xo = p.sbuf([128, 2, 1024], F32, "lm_xo")
    ps1 = p.psum([128, 2, 512], F32, "lm_ps1")
    ps2 = p.psum([128, 2, 512], F32, "lm_ps2")
    ps3 = p.psum([128, 2, 2, 512], F32, "lm_ps3")
    ps_g = ps1[:, 0, 0:8]

    s0 = p.sem()
    sy.dma_start(out=gw_sb[:].rearrange("p (c e) -> p c e", c=8),
                 in_=w["l1f_gatewT_bf"].rearrange("(c p) e -> p c e", p=128)).then_inc(s0, 16)
    sy.wait_ge(s0, 16)
    p.barrier()

    ldg = [p.sem(), p.sem()]
    grdy = p.sem(); mmg = p.sem(); sv = p.sem(); sg = p.sem(); ss = p.sem()
    r_ldg = [p.reg(sy, 0), p.reg(sy, 0)]
    r_sy_g = p.reg(sy, 0)
    r_t_gr = p.reg(t, 0); r_t_sv = p.reg(t, 0)
    r_v_mmg = p.reg(v, 0); r_vs = p.reg(v, 1); r_v_ss = p.reg(v, 0); r_v_sg2 = p.reg(v, 0)
    r_g_sv = p.reg(g, 10); r_s_sv = p.reg(sc, 7)
    zr = dram["zlnT"]
    LG, OH1, LGM, OH2, WS = 0, 8, 16, 24, 32
    M1, M2, DM, G2, G1 = 40, 41, 42, 43, 44
    with nc.Fori(0, NT // 2) as th:
        for par in (0, 1):
            tl = th * 2 + par
            sy.wait_ge(mmg, r_sy_g)
            sy.reg_add(r_sy_g, r_sy_g, 1)
            sy.dma_start(out=gstage[:, par, :].rearrange("p (c q) -> p c q", c=8),
                         in_=zr.rearrange("(c p) t -> p c t", p=128)[:, :, bass.ds(tl * 128, 128)]).then_inc(ldg[par], 16)
            sy.reg_add(r_ldg[par], r_ldg[par], 16)
            sy.wait_ge(ldg[par], r_ldg[par])
            sy.sem_inc(grdy, 1)
            t.reg_add(r_t_gr, r_t_gr, 1)
            t.wait_ge(grdy, r_t_gr)
            t.wait_ge(sv, r_t_sv)
            t.reg_add(r_t_sv, r_t_sv, 10)
            for dc in range(8):
                ins = t.matmul(ps_g, gstage[:, par, ts(dc, 128)], gw_sb[:, ds_(dc, 8)],
                               start=(dc == 0), stop=(dc == 7))
            ins.then_inc(mmg, 1)
            v.reg_add(r_v_mmg, r_v_mmg, 1); v.wait_ge(mmg, r_v_mmg)
            v.tensor_copy(gsc[:, LG:LG + 8], ps_g).then_inc(sv, 1)
            v.wait_ge(sv, r_vs)
            v.tensor_reduce(gsc[:, M1:M1 + 1], gsc[:, LG:LG + 8], AX.X, ALU.max).then_inc(sv, 1)
            v.reg_add(r_vs, r_vs, 1); v.wait_ge(sv, r_vs)
            v.tensor_scalar(gsc[:, OH1:OH1 + 8], gsc[:, LG:LG + 8], gsc[:, M1:M1 + 1],
                            None, ALU.is_equal).then_inc(sv, 1)
            v.reg_add(r_vs, r_vs, 1); v.wait_ge(sv, r_vs)
            v.scalar_tensor_tensor(gsc[:, LGM:LGM + 8], gsc[:, OH1:OH1 + 8], -1e30,
                                   gsc[:, LG:LG + 8], ALU.mult, ALU.add).then_inc(sv, 1)
            v.reg_add(r_vs, r_vs, 1); v.wait_ge(sv, r_vs)
            v.tensor_reduce(gsc[:, M2:M2 + 1], gsc[:, LGM:LGM + 8], AX.X, ALU.max).then_inc(sv, 1)
            v.reg_add(r_vs, r_vs, 1); v.wait_ge(sv, r_vs)
            v.tensor_scalar(gsc[:, OH2:OH2 + 8], gsc[:, LGM:LGM + 8], gsc[:, M2:M2 + 1],
                            None, ALU.is_equal).then_inc(sv, 1)
            v.reg_add(r_vs, r_vs, 1); v.wait_ge(sv, r_vs)
            v.tensor_scalar(gsc[:, DM:DM + 1], gsc[:, M2:M2 + 1], gsc[:, M1:M1 + 1],
                            None, ALU.subtract).then_inc(sv, 1)
            sc.wait_ge(sv, r_s_sv)
            sc.reg_add(r_s_sv, r_s_sv, 10)
            sc.activation(out=gsc[:, G2:G2 + 1], in_=gsc[:, DM:DM + 1],
                          func=AF.Sigmoid).then_inc(ss, 1)
            v.reg_add(r_v_ss, r_v_ss, 1); v.wait_ge(ss, r_v_ss)
            v.tensor_scalar(gsc[:, G1:G1 + 1], gsc[:, G2:G2 + 1], -1.0, 1.0,
                            ALU.mult, ALU.add).then_inc(sv, 1)
            v.reg_add(r_vs, r_vs, 2); v.wait_ge(sv, r_vs)
            v.tensor_scalar(gsc[:, LGM:LGM + 8], gsc[:, OH2:OH2 + 8], gsc[:, G2:G2 + 1],
                            None, ALU.mult).then_inc(sv, 1)
            v.reg_add(r_vs, r_vs, 1); v.wait_ge(sv, r_vs)
            v.wait_ge(sg, r_v_sg2)
            v.reg_add(r_v_sg2, r_v_sg2, 1)
            v.scalar_tensor_tensor(gsc[:, WS:WS + 8], gsc[:, OH1:OH1 + 8], gsc[:, G1:G1 + 1],
                                   gsc[:, LGM:LGM + 8], ALU.mult, ALU.add).then_inc(sv, 1)
            g.wait_ge(sv, r_g_sv)
            g.reg_add(r_g_sv, r_g_sv, 10)
            g.tensor_copy(comb_sb[:, bass.ds(tl * 8, 8)], gsc[:, WS:WS + 8]).then_inc(sg, 1)
            v.reg_add(r_vs, r_vs, 2)
    p.barrier()

    cps = p.sem(); cpr = p.sem()
    r_cps = p.reg(sy, 0)
    z1r_ = dram["z1"].rearrange("(n p) d -> n p d", p=128)
    yar_ = dram["yacc"].rearrange("(n p) d -> n p d", p=128)
    with nc.Fori(0, NT // 2) as th:
        for par in (0, 1):
            tl = th * 2 + par
            sy.dma_start(out=yt[:, par, :], in_=z1r_[tl]).then_inc(cps, 16)
            sy.reg_add(r_cps, r_cps, 16)
            sy.wait_ge(cps, r_cps)
            sy.dma_start(out=yar_[tl], in_=yt[:, par, :]).then_inc(cps, 16)
            sy.reg_add(r_cps, r_cps, 16)
            sy.wait_ge(cps, r_cps)
    p.barrier()

    lde = p.sem(); wrdy = p.sem(); mmw = p.sem(); mm12 = p.sem()
    s1d = p.sem(); s2d = p.sem(); hhd = p.sem(); cph = p.sem(); mm3 = p.sem()
    evy = p.sem(); yrdy = p.sem(); sty = p.sem(); ldy = p.sem()
    r_sy_lde = p.reg(sy, 0)
    r_sy_w = p.reg(sy, 0)
    r_s_hh = p.reg(sc, 0); r_v_hh2 = p.reg(v, 0)
    r_sty_c = p.reg(sy, 16)
    r_sc_ev = p.reg(sc, 1); r_sc_styw = p.reg(sc, 16)
    r_s_12 = p.reg(sc, 32); r_v_12 = p.reg(v, 32); r_g_12 = p.reg(g, 0)
    r_t_s1 = p.reg(t, 0); r_t_s2 = p.reg(t, 0)
    r_g_m3 = p.reg(g, 0)
    r_v_hh = p.reg(v, 0); r_v_m3 = p.reg(v, 0)
    r_t_cph = p.reg(t, 0); r_t_evy = p.reg(t, 0)
    r_v_ldy = p.reg(v, 0); r_v_3 = p.reg(v, 1)
    r_sy_sty = p.reg(sy, 1)
    yr = dram["yacc"].rearrange("(n p) d -> n p d", p=128)
    v.sem_inc(mm12, 32)
    v.sem_inc(evy, 1)
    v.sem_inc(sty, 16)
    v.sem_inc(mm3, 1)
    with nc.Fori(0, E) as e:
        nc.all_engine_barrier()
        nl = 0
        for dc in range(8):
            sy.dma_start(out=w1_sb[:, ds_(dc, 2048)],
                         in_=w["l1f_w1T_bf"][e, bass.ds(dc * 128, 128), :]).then_inc(lde, 16)
            sy.dma_start(out=w2_sb[:, ds_(dc, 2048)],
                         in_=w["l1f_w2T_bf"][e, bass.ds(dc * 128, 128), :]).then_inc(lde, 16)
            nl += 32
        for hc in range(16):
            sy.dma_start(out=w3_sb[:, ds_(hc, 1024)],
                         in_=w["l1f_w3T_bf"][e, bass.ds(hc * 128, 128), :]).then_inc(lde, 16)
            nl += 16
        sy.dma_start(out=b1_sb[:], in_=w["l1f_b1"][e]).then_inc(lde, 16)
        sy.dma_start(out=b2_sb[:], in_=w["l1f_b2"][e]).then_inc(lde, 16)
        sy.dma_start(out=b3_row[:], in_=w["l1f_b3_bf"][bass.ds(e, 1), :]).then_inc(lde, 16)
        nl += 48
        sy.reg_add(r_sy_lde, r_sy_lde, nl)
        sy.wait_ge(lde, r_sy_lde)
        nc.all_engine_barrier()
        with nc.Fori(0, T // 512) as st:
            stp = st % 2
            sy.wait_ge(mm12, r_sy_w)
            sy.reg_add(r_sy_w, r_sy_w, 32)
            sy.dma_start(out=zst[:, stp, :].rearrange("p (c q) -> p c q", c=8),
                         in_=zr.rearrange("(c p) t -> p c t", p=128)[:, :, bass.ds(st * 512, 512)]).then_inc(ldg[0], 16)
            sy.reg_add(r_ldg[0], r_ldg[0], 16)
            sy.wait_ge(ldg[0], r_ldg[0])
            sy.sem_inc(grdy, 1)
            t.reg_add(r_t_gr, r_t_gr, 1)
            t.wait_ge(grdy, r_t_gr)
            for fc in range(16):
                fp = fc % 2
                t.wait_ge(s1d, r_t_s1)
                t.reg_add(r_t_s1, r_t_s1, 1)
                for dc in range(8):
                    ins = t.matmul(ps1[:, fp, :], w1_sb[:, dc * 2048 + fc * 128: dc * 2048 + (fc + 1) * 128],
                                   zst[:, stp, bass.ds(dc * 512, 512)],
                                   start=(dc == 0), stop=(dc == 7))
                ins.then_inc(mm12, 1)
                t.wait_ge(s2d, r_t_s2)
                t.reg_add(r_t_s2, r_t_s2, 1)
                for dc in range(8):
                    ins = t.matmul(ps2[:, fp, :], w2_sb[:, dc * 2048 + fc * 128: dc * 2048 + (fc + 1) * 128],
                                   zst[:, stp, bass.ds(dc * 512, 512)],
                                   start=(dc == 0), stop=(dc == 7))
                ins.then_inc(mm12, 1)
                sc.reg_add(r_s_12, r_s_12, 2)
                sc.wait_ge(mm12, r_s_12)
                sc.wait_ge(hhd, r_s_hh)
                sc.reg_add(r_s_hh, r_s_hh, 1)
                sc.activation(out=s1_bf[:, fp, :], in_=ps1[:, fp, :], func=AF.Silu,
                              bias=b1_sb[:, fc:fc + 1]).then_inc(s1d, 1)
                v.reg_add(r_v_12, r_v_12, 2)
                v.wait_ge(mm12, r_v_12)
                v.wait_ge(hhd, r_v_hh2)
                v.reg_add(r_v_hh2, r_v_hh2, 1)
                v.tensor_scalar(s2_bf[:, fp, :], ps2[:, fp, :], b2_sb[:, fc:fc + 1],
                                None, ALU.add).then_inc(s2d, 1)
                g.reg_add(r_g_12, r_g_12, 1)
                g.wait_ge(s1d, r_g_12 + 1 if False else r_g_12)
                g.wait_ge(s2d, r_g_12)
                if fc == 0:
                    g.wait_ge(cph, r_g_m3)
                    g.reg_add(r_g_m3, r_g_m3, 4)
                g.tensor_tensor(hh_sb[:, ds_(fc, 512)], s1_bf[:, fp, :], s2_bf[:, fp, :],
                                ALU.mult).then_inc(hhd, 1)
                if fc == 12:
                    sc.wait_ge(evy, r_sc_ev)
                    sc.reg_add(r_sc_ev, r_sc_ev, 4)
                    sc.wait_ge(sty, r_sc_styw)
                    sc.reg_add(r_sc_styw, r_sc_styw, 64)
                if fc >= 12:
                    sc.dma_start(out=yt[:, fc - 12, :],
                                 in_=yr[st * 4 + (fc - 12)]).then_inc(ldy, 16)
            for tokt in range(4):
                if tokt == 0:
                    v.reg_add(r_v_hh, r_v_hh, 16)
                v.wait_ge(hhd, r_v_hh)
                v.wait_ge(mm3, r_v_m3)
                v.reg_add(r_v_m3, r_v_m3, 1)
                v.tensor_copy(hstage[:, tokt % 2, :],
                              hh_sb[:].rearrange("p (c q) -> p c q", c=16)[:, :, bass.ds(tokt * 128, 128)]).then_inc(cph, 1)
                t.reg_add(r_t_cph, r_t_cph, 1)
                t.wait_ge(cph, r_t_cph)
                t.wait_ge(evy, r_t_evy)
                t.reg_add(r_t_evy, r_t_evy, 1)
                for half in (0, 1):
                    t.matmul(ps3[:, tokt % 2, half, :], consts["ones_row"],
                             b3_row[0:1, ts(half, 512)], start=True, stop=False)
                    for hc in range(16):
                        ins = t.matmul(ps3[:, tokt % 2, half, :], hstage[:, tokt % 2, ts(hc, 128)],
                                       w3_sb[:, hc * 1024 + half * 512: hc * 1024 + (half + 1) * 512],
                                       start=False, stop=(hc == 15))
                ins.then_inc(mm3, 1)
                if tokt == 0:
                    v.reg_add(r_v_ldy, r_v_ldy, 64)
                v.wait_ge(ldy, r_v_ldy)
                v.reg_add(r_v_3, r_v_3, 1)
                v.wait_ge(mm3, r_v_3)
                v.scalar_tensor_tensor(yt[:, tokt, 0:512], ps3[:, tokt % 2, 0, :],
                                       comb_sb[:, bass.ds((st * 4 + tokt) * 8 + e, 1)],
                                       yt[:, tokt, 0:512], ALU.mult, ALU.add)
                v.scalar_tensor_tensor(yt[:, tokt, 512:1024], ps3[:, tokt % 2, 1, :],
                                       comb_sb[:, bass.ds((st * 4 + tokt) * 8 + e, 1)],
                                       yt[:, tokt, 512:1024], ALU.mult, ALU.add).then_inc(evy, 1)
                sy.reg_add(r_sy_sty, r_sy_sty, 1)
                sy.wait_ge(evy, r_sy_sty)
                sy.dma_start(out=yr[st * 4 + tokt], in_=yt[:, tokt, :]).then_inc(sty, 16)
                sy.reg_add(r_sty_c, r_sty_c, 16)
        sy.wait_ge(sty, r_sty_c)
        nc.all_engine_barrier()
    p.barrier()

    ldf = [p.sem(), p.sem()]
    frdy = p.sem(); evf = p.sem(); stf = p.sem(); sff = p.sem()
    r_ldf = [p.reg(sy, 0), p.reg(sy, 0)]
    r_sy_e = p.reg(sy, 0); r_sy_sf = p.reg(sy, 0)
    r_v_fr = p.reg(v, 0); r_v_sf = p.reg(v, 0)
    xi_r = dram["x_in"].rearrange("(n p) d -> n p d", p=128)
    xo_r = dram["x_out"].rearrange("(n p) d -> n p d", p=128)
    sy.sem_inc(sff, 2)
    with nc.Fori(0, NT // 2) as th:
        for par in (0, 1):
            tl = th * 2 + par
            sy.wait_ge(evf, r_sy_e)
            sy.dma_start(out=xt[:, par, :], in_=xi_r[tl]).then_inc(ldf[par], 16)
            sy.dma_start(out=yt[:, par, :], in_=yr[tl]).then_inc(ldf[par], 16)
            sy.reg_add(r_ldf[par], r_ldf[par], 32)
            sy.wait_ge(ldf[par], r_ldf[par])
            sy.sem_inc(frdy, 1)
            v.reg_add(r_v_fr, r_v_fr, 1)
            v.wait_ge(frdy, r_v_fr)
            v.wait_ge(sff, r_v_sf)
            v.reg_add(r_v_sf, r_v_sf, 1)
            v.scalar_tensor_tensor(xo[:, par, :], yt[:, par, :], 0.5, xt[:, par, :],
                                   ALU.mult, ALU.add).then_inc(evf, 1)
            sy.reg_add(r_sy_e, r_sy_e, 1)
            sy.wait_ge(evf, r_sy_e)
            sy.dma_start(out=xo_r[tl], in_=xo[:, par, :]).then_inc(stf, 16)
            sy.reg_add(r_sy_sf, r_sy_sf, 16)
            sy.wait_ge(stf, r_sy_sf)
            sy.sem_inc(sff, 1)
    p.barrier()
    p.free_scope(mk)
    p.barrier()


def emit_l1ffn(p, dram, w, consts):
    import os
    ph = int(os.environ.get("L1F_PHASES", "9"))
    emit_norm_pass(p, dram["x_in"], dram["z0"], None, None, "ln")
    emit_norm_pass(p, dram["z0"], None, None, dram["zlnT"], "rms",
                   gamma_rep=w["l1f_n1"], ident_bf=consts["ident_bf"], eps=RMSEPS)
    if ph < 2:
        return
    emit_l1ffn_qkv(p, dram, w, consts)
    if ph < 3:
        return
    emit_l1ffn_attn(p, dram, w, consts)
    if ph < 4:
        return
    emit_norm_pass(p, dram["z1"], None, None, dram["zlnT"], "rms",
                   gamma_rep=w["l1f_n2"], ident_bf=consts["ident_bf"], eps=RMSEPS)
    if ph < 5:
        return
    emit_l1ffn_moe(p, dram, w, consts)



def emit_l0ffn(p, dram, w, consts, nocc=False):
    nc = p.nc
    v, sc, g, t, sy = nc.vector, nc.scalar, nc.gpsimd, nc.tensor, nc.sync
    mk = p.scoped()
    gstage = p.sbuf([128, 2, 1024], BF16, "f0_gstg")
    rw_sb = p.sbuf([128, 8 * 8], BF16, "f0_rw")
    gsc = p.sbuf([128, 64], F32, "f0_gsc")
    Sall = p.sbuf([128, NT * 8], BF16, "f0_S")
    prefx = p.sbuf([128, NT * 8], F32, "f0_pref")
    oh1s = p.sbuf([128, NT * 8], F32, "f0_oh1")
    oh2s = p.sbuf([128, NT * 8], F32, "f0_oh2")
    g12s = p.sbuf([128, NT * 2], F32, "f0_g12")
    comb_sb = p.sbuf([128, NT * 8], F32, "f0_comb")
    slots_f = p.sbuf([128, NT * 2], F32, "f0_slotf")
    slots_i = p.sbuf([128, NT * 2], I32, "f0_sloti")
    gks_sb = p.sbuf([128, NT * 2], F32, "f0_gks")
    iota8_t = p.sbuf([128, 8], F32, "f0_iota8")
    totrow = p.sbuf([1, NT * 8], F32, "f0_tot")
    tri_bf = p.sbuf([128, 128], BF16, "f0_tri")
    onescol = p.sbuf([128, 1], BF16, "f0_onesc")
    onesrow_f = p.sbuf([1, 128], F32, "f0_onesrf")
    ident8 = p.sbuf([8, 8], F32, "f0_id8")
    onestile = p.sbuf([8, NT], F32, "f0_onet")
    csum = p.sbuf([8, NT + 2], F32, "f0_csum")
    tot8 = p.sbuf([8, NT], F32, "f0_tot8")
    corehot = p.sbuf([1, 64], F32, "f0_chot")
    coremask = p.sbuf([64, 1], F32, "f0_cmask")
    allc = p.sbuf([64, 8], F32, "f0_allc")
    coff = p.sbuf([1, 8], F32, "f0_coff")
    offs_st = p.sbuf([64, 8], F32, "f0_offs")
    totr8 = p.sbuf([1, 8], F32, "f0_totr8")
    arbuf = p.sbuf([64, 1024], F32, "f0_ar")
    offrow = p.sbuf([1, 8], F32, "f0_offrow")
    s0 = p.sem()
    n0 = 0
    g.dma_start(out=tri_bf[:], in_=w["tri"]).then_inc(s0, 16)
    g.dma_start(out=ident8[:], in_=w["ident"][0:8, 0:8]).then_inc(s0, 16)
    g.dma_start(out=corehot[:], in_=w["core_onehot"]).then_inc(s0, 16)
    g.dma_start(out=coremask[:], in_=w["core_mask"]).then_inc(s0, 16)
    g.dma_start(out=rw_sb[:].rearrange("p (c e) -> p c e", c=8),
                in_=w["l0f_router_bf"].rearrange("(c p) e -> p c e", p=128)).then_inc(s0, 16)
    g.dma_start(out=iota8_t[:], in_=w["iota8"]).then_inc(s0, 16)
    n0 = 6 * 16
    g.wait_ge(s0, n0)
    g.memset(onescol[:], 1.0)
    g.memset(onesrow_f[:], 1.0)
    g.memset(onestile[:], 1.0)
    g.memset(csum[:, 0:1], 0.0)
    g.memset(arbuf[:], 0.0)
    p.barrier()
    mk_route = p.scoped()
    ps_g = p.psum([128, 8], F32, "f0_psg")
    ps_pf = p.psum([128, 8], F32, "f0_pspf")
    ps_t = p.psum([1, 8], F32, "f0_pst")
    ps_o8 = p.psum([64, 8], F32, "f0_pso8")
    ps_off = p.psum([128, 8], F32, "f0_psoff")

    zr = dram["zlnT"]
    LG, OH1, LGM, OH2, WS = 0, 8, 16, 24, 32
    M1, M2, DM, G2, G1 = 40, 41, 42, 43, 44

    ldg = [p.sem(), p.sem()]
    grdy = p.sem(); mmg = p.sem(); sv = p.sem(); sg = p.sem(); ss = p.sem()
    mmp = p.sem()
    r_ldg = [p.reg(sy, 0), p.reg(sy, 0)]
    r_sy_g = p.reg(sy, 0)
    r_t_gr = p.reg(t, 0); r_t_sv = p.reg(t, 0); r_t_sv2 = p.reg(t, 0)
    r_v_mmg = p.reg(v, 0); r_vs = p.reg(v, 1); r_v_ss = p.reg(v, 0); r_v_mmp = p.reg(v, 0)
    r_s_sv = p.reg(sc, 7)
    with nc.Fori(0, NT // 2) as th:
        for par in (0, 1):
            tl = th * 2 + par
            sy.wait_ge(mmg, r_sy_g)
            sy.reg_add(r_sy_g, r_sy_g, 2)
            sy.dma_start(out=gstage[:, par, :].rearrange("p (c q) -> p c q", c=8),
                         in_=zr.rearrange("(c p) t -> p c t", p=128)[:, :, bass.ds(tl * 128, 128)]).then_inc(ldg[par], 16)
            sy.reg_add(r_ldg[par], r_ldg[par], 16)
            sy.wait_ge(ldg[par], r_ldg[par])
            sy.sem_inc(grdy, 1)
            t.reg_add(r_t_gr, r_t_gr, 1)
            t.wait_ge(grdy, r_t_gr)
            t.wait_ge(sv, r_t_sv)
            t.reg_add(r_t_sv, r_t_sv, 12)
            for dc in range(8):
                ins = t.matmul(ps_g[:], gstage[:, par, ts(dc, 128)], rw_sb[:, ds_(dc, 8)],
                               start=(dc == 0), stop=(dc == 7))
            ins.then_inc(mmg, 1)
            v.reg_add(r_v_mmg, r_v_mmg, 1); v.wait_ge(mmg, r_v_mmg)
            v.tensor_copy(gsc[:, LG:LG + 8], ps_g[:]).then_inc(sv, 1)
            v.wait_ge(sv, r_vs)
            v.tensor_reduce(gsc[:, M1:M1 + 1], gsc[:, LG:LG + 8], AX.X, ALU.max).then_inc(sv, 1)
            v.reg_add(r_vs, r_vs, 1); v.wait_ge(sv, r_vs)
            v.tensor_scalar(gsc[:, OH1:OH1 + 8], gsc[:, LG:LG + 8], gsc[:, M1:M1 + 1],
                            None, ALU.is_equal).then_inc(sv, 1)
            v.reg_add(r_vs, r_vs, 1); v.wait_ge(sv, r_vs)
            v.scalar_tensor_tensor(gsc[:, LGM:LGM + 8], gsc[:, OH1:OH1 + 8], -1e30,
                                   gsc[:, LG:LG + 8], ALU.mult, ALU.add).then_inc(sv, 1)
            v.reg_add(r_vs, r_vs, 1); v.wait_ge(sv, r_vs)
            v.tensor_reduce(gsc[:, M2:M2 + 1], gsc[:, LGM:LGM + 8], AX.X, ALU.max).then_inc(sv, 1)
            v.reg_add(r_vs, r_vs, 1); v.wait_ge(sv, r_vs)
            v.tensor_scalar(gsc[:, OH2:OH2 + 8], gsc[:, LGM:LGM + 8], gsc[:, M2:M2 + 1],
                            None, ALU.is_equal).then_inc(sv, 1)
            v.reg_add(r_vs, r_vs, 1); v.wait_ge(sv, r_vs)
            v.tensor_scalar(gsc[:, DM:DM + 1], gsc[:, M2:M2 + 1], gsc[:, M1:M1 + 1],
                            None, ALU.subtract).then_inc(sv, 1)
            sc.wait_ge(sv, r_s_sv)
            sc.reg_add(r_s_sv, r_s_sv, 12)
            sc.activation(out=gsc[:, G2:G2 + 1], in_=gsc[:, DM:DM + 1],
                          func=AF.Sigmoid).then_inc(ss, 1)
            v.reg_add(r_v_ss, r_v_ss, 1); v.wait_ge(ss, r_v_ss)
            v.tensor_scalar(gsc[:, G1:G1 + 1], gsc[:, G2:G2 + 1], -1.0, 1.0,
                            ALU.mult, ALU.add).then_inc(sv, 1)
            v.reg_add(r_vs, r_vs, 2); v.wait_ge(sv, r_vs)
            v.tensor_copy(oh1s[:, bass.ds(tl * 8, 8)], gsc[:, OH1:OH1 + 8]).then_inc(sv, 1)
            v.reg_add(r_vs, r_vs, 1); v.wait_ge(sv, r_vs)
            v.tensor_copy(oh2s[:, bass.ds(tl * 8, 8)], gsc[:, OH2:OH2 + 8]).then_inc(sv, 1)
            v.reg_add(r_vs, r_vs, 1); v.wait_ge(sv, r_vs)
            v.tensor_copy(g12s[:, bass.ds(tl * 2, 2)], gsc[:, G2:G2 + 2]).then_inc(sv, 1)
            v.reg_add(r_vs, r_vs, 1); v.wait_ge(sv, r_vs)
            v.tensor_tensor(Sall[:, bass.ds(tl * 8, 8)], gsc[:, OH1:OH1 + 8],
                            gsc[:, OH2:OH2 + 8], ALU.add).then_inc(sv, 1)
            v.reg_add(r_vs, r_vs, 2)
            t.wait_ge(sv, r_t_sv)
            t.wait_ge(mmp, r_t_sv2)
            t.reg_add(r_t_sv2, r_t_sv2, 2)
            t.matmul(ps_pf[:], tri_bf[:], Sall[:, bass.ds(tl * 8, 8)],
                     start=True, stop=True)
            t.matmul(ps_t[:], onescol[:], Sall[:, bass.ds(tl * 8, 8)],
                     start=True, stop=True).then_inc(mmg, 1)
            v.reg_add(r_v_mmg, r_v_mmg, 1)
            v.wait_ge(mmg, r_v_mmg)
            v.tensor_copy(prefx[:, bass.ds(tl * 8, 8)], ps_pf[:]).then_inc(mmp, 1)
            v.tensor_copy(totrow[0:1, bass.ds(tl * 8, 8)], ps_t[0:1, :]).then_inc(mmp, 1)
    p.barrier()

    so = p.sem()
    sy.dma_start(out=dram["totdram"], in_=totrow[0:1, :]).then_inc(so, 16)
    sy.wait_ge(so, 16)
    sy.dma_start(out=tot8[:], in_=dram["totdram"].rearrange("t e -> e t"),
                 allow_slow_non_contiguous=True).then_inc(so, 16)
    sy.wait_ge(so, 32)
    p.barrier()
    v.tensor_tensor_scan(csum[:, 1:1 + NT], onestile[:], tot8[:], 0.0,
                         ALU.mult, ALU.add)
    p.barrier()
    t.matmul(ps_t[:], csum[0:8, NT:NT + 1], ident8[:], start=True, stop=True)
    p.barrier()
    v.tensor_copy(totr8[:], ps_t[0:1, :])
    p.barrier()
    t.matmul(ps_o8[:], corehot[:], totr8[:], start=True, stop=True)
    p.barrier()
    v.tensor_copy(arbuf[:, 0:8], ps_o8[:])
    p.barrier()
    sy.dma_start(out=dram["cc_in"], in_=arbuf[:]).then_inc(so, 16)
    sy.wait_ge(so, 48)
    if nocc:
        sy.dma_start(out=dram["cc_out"], in_=dram["cc_in"]).then_inc(so, 16)
        sy.wait_ge(so, 64)
    else:
        ccs = p.sem()
        g.collective_compute("AllReduce", ALU.add, replica_groups=[list(range(NCORES))],
                             ins=[dram["cc_in"]], outs=[dram["cc_out"]]).then_inc(ccs)
        g.wait_ge(ccs, 1)
    p.barrier()
    sy.dma_start(out=allc[:], in_=dram["cc_out"][:, 0:8]).then_inc(so, 16)
    sy.wait_ge(so, 80 if nocc else 64)
    p.barrier()
    t.matmul(ps_t[:], coremask[:], allc[:], start=True, stop=True)
    p.barrier()
    v.tensor_copy(coff[:], ps_t[0:1, :])
    p.barrier()
    t.matmul(ps_o8[0:NT, :], csum[0:8, 0:NT], consts["ident_f32"][0:8, 0:8],
             start=True, stop=False)
    t.matmul(ps_o8[0:NT, :], onesrow_f[0:1, 0:NT], coff[:], start=False, stop=True)
    p.barrier()
    v.tensor_copy(offs_st[0:NT, :], ps_o8[0:NT, :])
    p.barrier()
    sy.dma_start(out=dram["offs_dram"], in_=offs_st[0:NT, :]).then_inc(so, 16)
    sy.wait_ge(so, 96 if nocc else 80)
    p.barrier()

    ldo = p.sem(); ordy = p.sem(); mmo = p.sem(); fv = p.sem()
    r_ldo = p.reg(sy, 0); r_sy_fv = p.reg(sy, 0)
    r_t_or = p.reg(t, 0); r_t_fv = p.reg(t, 0)
    r_v_mmo = p.reg(v, 0); r_fv = p.reg(v, 1)
    FK = 24
    with nc.Fori(0, NT) as tl:
        sy.wait_ge(mmo, r_sy_fv)
        sy.reg_add(r_sy_fv, r_sy_fv, 1)
        sy.dma_start(out=offrow[:], in_=dram["offs_dram"][bass.ds(tl, 1), :]).then_inc(ldo, 16)
        sy.reg_add(r_ldo, r_ldo, 16)
        sy.wait_ge(ldo, r_ldo)
        sy.sem_inc(ordy, 1)
        t.reg_add(r_t_or, r_t_or, 1)
        t.wait_ge(ordy, r_t_or)
        t.wait_ge(fv, r_t_fv)
        t.reg_add(r_t_fv, r_t_fv, FK)
        t.matmul(ps_off[:], onesrow_f[:], offrow[:], start=True, stop=True).then_inc(mmo, 1)
        v.reg_add(r_v_mmo, r_v_mmo, 1)
        v.wait_ge(mmo, r_v_mmo)
        v.tensor_tensor(gsc[:, 0:8], prefx[:, bass.ds(tl * 8, 8)], ps_off[:],
                        ALU.add).then_inc(fv, 1)
        v.wait_ge(fv, r_fv)
        v.tensor_tensor(gsc[:, 8:16], gsc[:, 0:8], oh1s[:, bass.ds(tl * 8, 8)],
                        ALU.mult).then_inc(fv, 1)
        v.reg_add(r_fv, r_fv, 1); v.wait_ge(fv, r_fv)
        v.tensor_reduce(gsc[:, 40:41], gsc[:, 8:16], AX.X, ALU.add).then_inc(fv, 1)
        v.reg_add(r_fv, r_fv, 1); v.wait_ge(fv, r_fv)
        v.tensor_tensor(gsc[:, 16:24], gsc[:, 0:8], oh2s[:, bass.ds(tl * 8, 8)],
                        ALU.mult).then_inc(fv, 1)
        v.reg_add(r_fv, r_fv, 1); v.wait_ge(fv, r_fv)
        v.tensor_reduce(gsc[:, 41:42], gsc[:, 16:24], AX.X, ALU.add).then_inc(fv, 1)
        v.reg_add(r_fv, r_fv, 1); v.wait_ge(fv, r_fv)
        v.tensor_scalar(gsc[:, 42:43], gsc[:, 40:41], float(CAP), None, ALU.is_lt).then_inc(fv, 1)
        v.reg_add(r_fv, r_fv, 1); v.wait_ge(fv, r_fv)
        v.tensor_scalar(gsc[:, 43:44], gsc[:, 41:42], float(CAP), None, ALU.is_lt).then_inc(fv, 1)
        v.reg_add(r_fv, r_fv, 1); v.wait_ge(fv, r_fv)
        v.tensor_tensor(gsc[:, 44:45], g12s[:, bass.ds(tl * 2 + 1, 1)], gsc[:, 42:43],
                        ALU.mult).then_inc(fv, 1)
        v.reg_add(r_fv, r_fv, 1); v.wait_ge(fv, r_fv)
        v.tensor_tensor(gsc[:, 45:46], g12s[:, bass.ds(tl * 2, 1)], gsc[:, 43:44],
                        ALU.mult).then_inc(fv, 1)
        v.reg_add(r_fv, r_fv, 1); v.wait_ge(fv, r_fv)
        v.tensor_scalar(gsc[:, 24:32], oh2s[:, bass.ds(tl * 8, 8)], gsc[:, 45:46],
                        None, ALU.mult).then_inc(fv, 1)
        v.reg_add(r_fv, r_fv, 1); v.wait_ge(fv, r_fv)
        v.scalar_tensor_tensor(comb_sb[:, bass.ds(tl * 8, 8)], oh1s[:, bass.ds(tl * 8, 8)],
                               gsc[:, 44:45], gsc[:, 24:32], ALU.mult, ALU.add).then_inc(fv, 1)
        v.reg_add(r_fv, r_fv, 1); v.wait_ge(fv, r_fv)
        v.tensor_tensor(gsc[:, 46:54], oh1s[:, bass.ds(tl * 8, 8)], iota8_t[:],
                        ALU.mult).then_inc(fv, 1)
        v.reg_add(r_fv, r_fv, 1); v.wait_ge(fv, r_fv)
        v.tensor_reduce(gsc[:, 54:55], gsc[:, 46:54], AX.X, ALU.add).then_inc(fv, 1)
        v.reg_add(r_fv, r_fv, 1); v.wait_ge(fv, r_fv)
        v.scalar_tensor_tensor(gsc[:, 55:56], gsc[:, 54:55], float(CAP), gsc[:, 40:41],
                               ALU.mult, ALU.add).then_inc(fv, 1)
        v.reg_add(r_fv, r_fv, 1); v.wait_ge(fv, r_fv)
        v.tensor_scalar(gsc[:, 56:57], gsc[:, 42:43], -1e6, 1e6, ALU.mult,
                        ALU.add).then_inc(fv, 1)
        v.reg_add(r_fv, r_fv, 1); v.wait_ge(fv, r_fv)
        v.tensor_tensor(slots_f[:, bass.ds(tl * 2, 1)], gsc[:, 55:56], gsc[:, 56:57],
                        ALU.add).then_inc(fv, 1)
        v.reg_add(r_fv, r_fv, 1); v.wait_ge(fv, r_fv)
        v.tensor_tensor(gsc[:, 46:54], oh2s[:, bass.ds(tl * 8, 8)], iota8_t[:],
                        ALU.mult).then_inc(fv, 1)
        v.reg_add(r_fv, r_fv, 1); v.wait_ge(fv, r_fv)
        v.tensor_reduce(gsc[:, 54:55], gsc[:, 46:54], AX.X, ALU.add).then_inc(fv, 1)
        v.reg_add(r_fv, r_fv, 1); v.wait_ge(fv, r_fv)
        v.scalar_tensor_tensor(gsc[:, 57:58], gsc[:, 54:55], float(CAP), gsc[:, 41:42],
                               ALU.mult, ALU.add).then_inc(fv, 1)
        v.reg_add(r_fv, r_fv, 1); v.wait_ge(fv, r_fv)
        v.tensor_scalar(gsc[:, 58:59], gsc[:, 43:44], -1e6, 1e6, ALU.mult,
                        ALU.add).then_inc(fv, 1)
        v.reg_add(r_fv, r_fv, 1); v.wait_ge(fv, r_fv)
        v.tensor_tensor(slots_f[:, bass.ds(tl * 2 + 1, 1)], gsc[:, 57:58], gsc[:, 58:59],
                        ALU.add).then_inc(fv, 1)
        v.reg_add(r_fv, r_fv, 1); v.wait_ge(fv, r_fv)
        v.tensor_copy(gks_sb[:, bass.ds(tl * 2, 1)], gsc[:, 44:45]).then_inc(fv, 1)
        v.reg_add(r_fv, r_fv, 1); v.wait_ge(fv, r_fv)
        v.tensor_copy(gks_sb[:, bass.ds(tl * 2 + 1, 1)], gsc[:, 45:46]).then_inc(fv, 1)
        v.reg_add(r_fv, r_fv, 1); v.wait_ge(fv, r_fv)
        v.tensor_copy(slots_i[:, bass.ds(tl * 2, 2)],
                      slots_f[:, bass.ds(tl * 2, 2)]).then_inc(fv, 1)
        v.reg_add(r_fv, r_fv, 2)
    p.barrier()

    p.free_scope(mk_route)
    tokid_t = p.sbuf([128, 2, 1], I32, "f0_tokid")
    zero_i = p.sbuf([128, E * CAP // 128], I32, "f0_zeroi")
    sct = p.sem()
    sci = p.sem()
    nsc = 0
    nio = 0
    g.iota(zero_i[:], pattern=[[0, E * CAP // 128]], base=0,
           channel_multiplier=0).then_inc(sci, 1)
    nio += 1
    g.wait_ge(sci, nio)
    g.dma_start(out=dram["idx"].rearrange("(p a) one -> p (a one)", p=128),
                in_=zero_i[:]).then_inc(sct, 16)
    nsc += 16
    g.wait_ge(sct, nsc)
    for tl in range(NT):
        for k in (0, 1):
            g.iota(tokid_t[:, k, :], pattern=[[1, 1]], base=tl * 128,
                   channel_multiplier=1).then_inc(sci, 1)
            nio += 1
            g.wait_ge(sci, nio)
            g.indirect_dma_start(
                out=dram["idx"],
                out_offset=IndirectOffsetOnAxis(ap=slots_i[:, 2 * tl + k: 2 * tl + k + 1], axis=0),
                in_=tokid_t[:, k, :], in_offset=None,
                bounds_check=E * CAP - 1, oob_is_err=False).then_inc(sct, 16)
            nsc += 16
        g.wait_ge(sct, nsc)
    p.barrier()

    w1_sb = p.sbuf([128, 8 * 2048], BF16, "f0_w1")
    w2_sb = p.sbuf([128, 16 * 1024], BF16, "f0_w2")
    b1_sb = p.sbuf([128, 16], F32, "f0_b1")
    b2_row = p.sbuf([1, 1024], BF16, "f0_b2")
    zst = p.sbuf([128, 2, 8 * 512], BF16, "f0_zst")
    zg = p.sbuf([128, 2, 4, 1024], BF16, "f0_zg")
    idxt = p.sbuf([128, 2, 4], I32, "f0_idxt")
    hh_sb = p.sbuf([128, 16 * 512], BF16, "f0_hh")
    hstage = p.sbuf([128, 2, 16 * 128], BF16, "f0_hstg")
    obf = p.sbuf([128, 2, 1024], BF16, "f0_obf")
    oin = p.sbuf([128, 4, 1024], BF16, "f0_oin")
    ps1 = p.psum([128, 2, 512], F32, "f0_ps1")
    ps3 = p.psum([128, 2, 2, 512], F32, "f0_ps3")
    ps_tr = p.psum([128, 8, 128], BF16, "f0_pstr")
    QT = CAP // 512
    for hh in range(2):
        lde = p.sem(); mm12 = p.sem(); s1d = p.sem(); cph = p.sem(); mm3 = p.sem()
        evo = p.sem(); sto = p.sem(); ldix = p.sem(); ixr = p.sem()
        gsem = p.sem(); zgr = p.sem(); trd = p.sem(); evz = p.sem(); ldo = p.sem()
        r_sy_lde = p.reg(sy, 0); r_sy_ix = p.reg(sy, 0); r_sy_ldix = p.reg(sy, 0)
        r_g_ix = p.reg(g, 0); r_g_own = p.reg(g, 0); r_g_trd = p.reg(g, 0)
        r_t_zgr = p.reg(t, 1); r_t_evz = p.reg(t, 0); r_t_evz2 = p.reg(t, 0)
        r_t_s1 = p.reg(t, 0); r_t_cph = p.reg(t, 0); r_t_evo = p.reg(t, 0)
        r_s_12 = p.reg(sc, 16); r_s_hh = p.reg(sc, 0); r_sc_evo = p.reg(sc, 1)
        r_v_trd = p.reg(v, 32); r_v_zw = p.reg(v, 0); r_v_s1 = p.reg(v, 0)
        r_v_m3 = p.reg(v, 0); r_v_3 = p.reg(v, 1); r_v_sto = p.reg(v, 0)
        r_v_ldo = p.reg(v, 0)
        r_sy_evo = p.reg(sy, 1); r_sy_sto = p.reg(sy, 16)
        v.sem_inc(mm12, 16)
        v.sem_inc(zgr, 1)
        v.sem_inc(trd, 32)
        v.sem_inc(mm3, 1)
        v.sem_inc(evo, 1)
        v.sem_inc(sto, 16)
        with nc.Fori(0, E) as e:
            nc.all_engine_barrier()
            nl = 0
            for dc in range(8):
                sy.dma_start(out=w1_sb[:, ds_(dc, 2048)],
                             in_=w["l0f_w1_bf"][e, bass.ds(dc * 128, 128),
                                                bass.ds(hh * 2048, 2048)]).then_inc(lde, 16)
                nl += 16
            for hc in range(16):
                sy.dma_start(out=w2_sb[:, ds_(hc, 1024)],
                             in_=w["l0f_w2_bf"][e, bass.ds(hh * 2048 + hc * 128, 128), :]).then_inc(lde, 16)
                nl += 16
            sy.dma_start(out=b1_sb[:], in_=w["l0f_b1"][e][:, bass.ds(hh * 16, 16)]).then_inc(lde, 16)
            sy.dma_start(out=b2_row[:], in_=w["l0f_b2_bf"][bass.ds(e, 1), :]).then_inc(lde, 16)
            nl += 32
            sy.reg_add(r_sy_lde, r_sy_lde, nl)
            sy.wait_ge(lde, r_sy_lde)
            nc.all_engine_barrier()
            for qt in range(QT):
                qp = qt % 2
                sy.wait_ge(zgr, r_sy_ix)
                sy.reg_add(r_sy_ix, r_sy_ix, 1)
                for c in range(4):
                    sy.dma_start(out=idxt[:, qp, c:c + 1],
                                 in_=dram["idx"][bass.ds(e * CAP + qt * 512 + c * 128, 128), :]).then_inc(ldix, 16)
                sy.reg_add(r_sy_ldix, r_sy_ldix, 64)
                sy.wait_ge(ldix, r_sy_ldix)
                sy.sem_inc(ixr, 1)
                g.reg_add(r_g_ix, r_g_ix, 1)
                g.wait_ge(ixr, r_g_ix)
                g.wait_ge(trd, r_g_trd)
                g.reg_add(r_g_trd, r_g_trd, 32)
                for c in range(4):
                    g.indirect_dma_start(
                        out=zg[:, qp, c, :], out_offset=None, in_=dram["zln"],
                        in_offset=IndirectOffsetOnAxis(ap=idxt[:, qp, c:c + 1], axis=0)).then_inc(gsem, 16)
                g.reg_add(r_g_own, r_g_own, 64)
                g.wait_ge(gsem, r_g_own)
                g.sem_inc(zgr, 1)
                t.reg_add(r_t_zgr, r_t_zgr, 1)
                t.wait_ge(zgr, r_t_zgr)
                for c in range(4):
                    t.wait_ge(evz, r_t_evz)
                    t.reg_add(r_t_evz, r_t_evz, 1)
                    for dc in range(8):
                        ins = t.transpose(ps_tr[:, dc, :], zg[:, qp, c, ts(dc, 128)],
                                          consts["ident_bf"])
                        ins.then_inc(trd, 1)
                    v.reg_add(r_v_trd, r_v_trd, 8)
                    v.wait_ge(trd, r_v_trd)
                    if c == 0:
                        v.wait_ge(mm12, r_v_zw)
                        v.reg_add(r_v_zw, r_v_zw, 16)
                    v.tensor_copy(zst[:, qp, :].rearrange("p (c q) -> p c q", c=8)[:, :, bass.ds(c * 128, 128)],
                                  ps_tr[:]).then_inc(evz, 1)
                t.reg_add(r_t_evz2, r_t_evz2, 4)
                t.wait_ge(evz, r_t_evz2)
                for fc in range(16):
                    fp = fc % 2
                    t.wait_ge(s1d, r_t_s1)
                    t.reg_add(r_t_s1, r_t_s1, 1)
                    for dc in range(8):
                        ins = t.matmul(ps1[:, fp, :],
                                       w1_sb[:, dc * 2048 + fc * 128: dc * 2048 + (fc + 1) * 128],
                                       zst[:, qp, bass.ds(dc * 512, 512)],
                                       start=(dc == 0), stop=(dc == 7))
                    ins.then_inc(mm12, 1)
                    sc.reg_add(r_s_12, r_s_12, 1)
                    sc.wait_ge(mm12, r_s_12)
                    if fc == 0:
                        sc.wait_ge(cph, r_s_hh)
                        sc.reg_add(r_s_hh, r_s_hh, 4)
                    sc.activation(out=hh_sb[:, ds_(fc, 512)], in_=ps1[:, fp, :], func=AF.Relu,
                                  bias=b1_sb[:, fc:fc + 1]).then_inc(s1d, 1)
                    if hh == 1:
                        if fc == 12:
                            sc.wait_ge(evo, r_sc_evo)
                            sc.reg_add(r_sc_evo, r_sc_evo, 4)
                        if fc >= 12:
                            sc.dma_start(out=oin[:, fc - 12, :],
                                         in_=dram["odram"][bass.ds(e * CAP + qt * 512 + (fc - 12) * 128, 128), :]).then_inc(ldo, 16)
                for tokt in range(4):
                    if tokt == 0:
                        v.reg_add(r_v_s1, r_v_s1, 16)
                    v.wait_ge(s1d, r_v_s1)
                    v.wait_ge(mm3, r_v_m3)
                    v.reg_add(r_v_m3, r_v_m3, 1)
                    v.tensor_copy(hstage[:, tokt % 2, :],
                                  hh_sb[:].rearrange("p (c q) -> p c q", c=16)[:, :, bass.ds(tokt * 128, 128)]).then_inc(cph, 1)
                    t.reg_add(r_t_cph, r_t_cph, 1)
                    t.wait_ge(cph, r_t_cph)
                    t.wait_ge(evo, r_t_evo)
                    t.reg_add(r_t_evo, r_t_evo, 1)
                    for half in (0, 1):
                        if hh == 0:
                            t.matmul(ps3[:, tokt % 2, half, :], consts["ones_row"],
                                     b2_row[0:1, ts(half, 512)], start=True, stop=False)
                        for hc in range(16):
                            ins = t.matmul(ps3[:, tokt % 2, half, :], hstage[:, tokt % 2, ts(hc, 128)],
                                           w2_sb[:, hc * 1024 + half * 512: hc * 1024 + (half + 1) * 512],
                                           start=(hh != 0 and hc == 0), stop=(hc == 15))
                    ins.then_inc(mm3, 1)
                    if hh == 1:
                        if tokt == 0:
                            v.reg_add(r_v_ldo, r_v_ldo, 64)
                        v.wait_ge(ldo, r_v_ldo)
                    v.reg_add(r_v_3, r_v_3, 1)
                    v.wait_ge(mm3, r_v_3)
                    v.wait_ge(sto, r_v_sto)
                    v.reg_add(r_v_sto, r_v_sto, 16)
                    if hh == 0:
                        v.tensor_copy(obf[:, tokt % 2, 0:512], ps3[:, tokt % 2, 0, :])
                        v.tensor_copy(obf[:, tokt % 2, 512:1024],
                                      ps3[:, tokt % 2, 1, :]).then_inc(evo, 1)
                    else:
                        v.tensor_tensor(obf[:, tokt % 2, 0:512], ps3[:, tokt % 2, 0, :],
                                        oin[:, tokt, 0:512], ALU.add)
                        v.tensor_tensor(obf[:, tokt % 2, 512:1024], ps3[:, tokt % 2, 1, :],
                                        oin[:, tokt, 512:1024], ALU.add).then_inc(evo, 1)
                    sy.reg_add(r_sy_evo, r_sy_evo, 1)
                    sy.wait_ge(evo, r_sy_evo)
                    sy.dma_start(out=dram["odram"][bass.ds(e * CAP + qt * 512 + tokt * 128, 128), :],
                                 in_=obf[:, tokt % 2, :]).then_inc(sto, 16)
                    sy.reg_add(r_sy_sto, r_sy_sto, 16)
                    sy.wait_ge(sto, r_sy_sto)
            nc.all_engine_barrier()
        p.barrier()
    p.barrier()

    y1t = p.sbuf([128, 2, 1024], BF16, "f0_y1")
    y2t = p.sbuf([128, 2, 1024], BF16, "f0_y2")
    ytmp = p.sbuf([128, 2, 1024], F32, "f0_ytmp")
    xt0 = p.sbuf([128, 2, 1024], F32, "f0_xt")
    xo0 = p.sbuf([128, 2, 1024], F32, "f0_xo")
    ldf = p.sem(); frdy = p.sem(); evf = p.sem(); stf = p.sem()
    gco = p.sem(); gcr = p.sem(); fv2 = p.sem(); mci = p.sem()
    r_ldf = p.reg(sy, 0)
    r_sy_e2 = p.reg(sy, 0); r_sy_sf = p.reg(sy, 16); r_sy_ev3 = p.reg(sy, 1)
    r_v_fr = p.reg(v, 0); r_v_gcr = p.reg(v, 0); r_v_sf2 = p.reg(v, 0); r_v_f2 = p.reg(v, 0)
    r_g_ev = p.reg(g, 0); r_g_own2 = p.reg(g, 0); r_g_mc = p.reg(g, 0)
    xi_r = dram["x_in"].rearrange("(n p) d -> n p d", p=128)
    xo_r = dram["x_out"].rearrange("(n p) d -> n p d", p=128)
    v.sem_inc(evf, 1)
    v.sem_inc(stf, 16)
    for tl in range(NT):
        par = tl % 2
        sy.wait_ge(evf, r_sy_e2)
        sy.reg_add(r_sy_e2, r_sy_e2, 1)
        sy.dma_start(out=xt0[:, par, :], in_=xi_r[tl]).then_inc(ldf, 16)
        sy.reg_add(r_ldf, r_ldf, 16)
        sy.wait_ge(ldf, r_ldf)
        sy.sem_inc(frdy, 1)
        g.wait_ge(evf, r_g_ev)
        g.reg_add(r_g_ev, r_g_ev, 1)
        g.memset(y1t[:, par, :], 0.0).then_inc(mci, 1)
        g.memset(y2t[:, par, :], 0.0).then_inc(mci, 1)
        g.reg_add(r_g_mc, r_g_mc, 2)
        g.wait_ge(mci, r_g_mc)
        g.indirect_dma_start(
            out=y1t[:, par, :], out_offset=None, in_=dram["odram"],
            in_offset=IndirectOffsetOnAxis(ap=slots_i[:, 2 * tl: 2 * tl + 1], axis=0),
            bounds_check=E * CAP - 1, oob_is_err=False).then_inc(gco, 16)
        g.indirect_dma_start(
            out=y2t[:, par, :], out_offset=None, in_=dram["odram"],
            in_offset=IndirectOffsetOnAxis(ap=slots_i[:, 2 * tl + 1: 2 * tl + 2], axis=0),
            bounds_check=E * CAP - 1, oob_is_err=False).then_inc(gco, 16)
        g.reg_add(r_g_own2, r_g_own2, 32)
        g.wait_ge(gco, r_g_own2)
        g.sem_inc(gcr, 1)
        v.reg_add(r_v_fr, r_v_fr, 1)
        v.wait_ge(frdy, r_v_fr)
        v.reg_add(r_v_gcr, r_v_gcr, 1)
        v.wait_ge(gcr, r_v_gcr)
        v.wait_ge(stf, r_v_sf2)
        v.reg_add(r_v_sf2, r_v_sf2, 16)
        v.tensor_scalar(ytmp[:, par, :], y1t[:, par, :], gks_sb[:, 2 * tl: 2 * tl + 1],
                        None, ALU.mult).then_inc(fv2, 1)
        v.reg_add(r_v_f2, r_v_f2, 1)
        v.wait_ge(fv2, r_v_f2)
        v.scalar_tensor_tensor(ytmp[:, par, :], y2t[:, par, :], gks_sb[:, 2 * tl + 1: 2 * tl + 2],
                               ytmp[:, par, :], ALU.mult, ALU.add).then_inc(fv2, 1)
        v.reg_add(r_v_f2, r_v_f2, 1)
        v.wait_ge(fv2, r_v_f2)
        v.scalar_tensor_tensor(xo0[:, par, :], ytmp[:, par, :], 0.5, xt0[:, par, :],
                               ALU.mult, ALU.add).then_inc(evf, 1)
        sy.reg_add(r_sy_ev3, r_sy_ev3, 1)
        sy.wait_ge(evf, r_sy_ev3)
        sy.dma_start(out=xo_r[tl], in_=xo0[:, par, :]).then_inc(stf, 16)
        sy.reg_add(r_sy_sf, r_sy_sf, 16)
        sy.wait_ge(stf, r_sy_sf)
    p.barrier()
    p.free_scope(mk)
    p.barrier()



def _declare_weight_drams(nc, g):
    w = {}
    for name, arr in g.items():
        w[name] = nc.dram_tensor(name, list(arr.shape), F32, kind="ExternalInput").ap()
    w["core_onehot"] = nc.dram_tensor("core_onehot", [1, 64], F32, kind="ExternalInput").ap()
    w["core_mask"] = nc.dram_tensor("core_mask", [64, 1], F32, kind="ExternalInput").ap()
    BFS = ["l0f_router", "l0f_w1", "l0f_w2", "l0f_b2",
           "l0a_inwT", "l0a_xpwT_pad", "l0a_outwT", "l0a_outb",
           "l1f_qkvwT", "l1f_qkvb", "l1f_outwT", "l1f_outb",
           "l1f_gatewT", "l1f_w1T", "l1f_w2T", "l1f_w3T", "l1f_b3",
           "l1a_qkvwT", "l1a_qkvb", "l1a_p1wT", "l1a_p2wT", "l1a_p2b"]
    ren = {"l1f_qkvb": "l1f_qkvb_row", "l1a_qkvb": "l1a_qkvb_row",
           "l1f_outb": "l1f_outb_row", "l1a_p2b": "l1a_p2b_bf"}
    for name in BFS:
        arr = g[name]
        tgt = ren.get(name, name + "_bf")
        w[tgt] = nc.dram_tensor(tgt, list(arr.shape), BF16, kind="Internal").ap()
    return w


_CAST_PAIRS = [("l0f_router", "l0f_router_bf"), ("l0f_b2", "l0f_b2_bf"),
               ("l0a_inwT", "l0a_inwT_bf"), ("l0a_xpwT_pad", "l0a_xpwT_pad_bf"),
               ("l0a_outwT", "l0a_outwT_bf"), ("l0a_outb", "l0a_outb_bf"),
               ("l1f_qkvwT", "l1f_qkvwT_bf"), ("l1f_qkvb", "l1f_qkvb_row"),
               ("l1f_outwT", "l1f_outwT_bf"), ("l1f_outb", "l1f_outb_row"),
               ("l1f_gatewT", "l1f_gatewT_bf"), ("l1f_b3", "l1f_b3_bf"),
               ("l1a_qkvwT", "l1a_qkvwT_bf"), ("l1a_p1wT", "l1a_p1wT_bf"),
               ("l1a_p2wT", "l1a_p2wT_bf"), ("l1a_p2b", "l1a_p2b_bf"),
               ("l1a_qkvb", "l1a_qkvb_row")]
_CAST_3D = ["l0f_w1", "l0f_w2", "l1f_w1T", "l1f_w2T", "l1f_w3T"]


def _emit_casts_and_consts(p, nc, w):
    pairs = [(w[a], w[b]) for a, b in _CAST_PAIRS]
    for name in _CAST_3D:
        for e in range(E):
            pairs.append((w[name][e], w[name + "_bf"][e]))
    emit_prologue_casts(p, pairs)
    consts = {}
    identb = p.sbuf([128, 128], BF16, "c_identb")
    identf = p.sbuf([128, 128], F32, "c_identf")
    onesr = p.sbuf([1, 128], BF16, "c_ones")
    ones64 = p.sbuf([64, 1], F32, "c_ones64")
    sld = p.sem()
    gg = nc.gpsimd
    gg.dma_start(out=identb[:], in_=w["ident"]).then_inc(sld, 16)
    gg.dma_start(out=identf[:], in_=w["ident"]).then_inc(sld, 16)
    gg.memset(onesr[:], 1.0)
    gg.memset(ones64[:], 1.0)
    gg.wait_ge(sld, 2 * 16)
    p.barrier()
    consts.update({"ident_bf": identb[:], "ident_f32": identf[:], "ones_row": onesr[:],
                   "ones64c": ones64[:]})
    return consts


def _declare_scratch(nc):
    d = {}
    for nm, shape, dt in [("xA", [T, D], F32), ("xB", [T, D], F32),
                          ("z0", [T, D], F32), ("z1", [T, D], F32),
                          ("yacc", [T, D], F32), ("zlnT", [D, T], BF16),
                          ("qT", [B_LOC, D, S], BF16), ("kT", [B_LOC, D, S], BF16),
                          ("v_tok", [B_LOC, S, D], BF16),
                          ("totdram", [NT, 8], F32), ("offs_dram", [NT, 8], F32),
                          ("zln", [T, D], BF16)]:
        d[nm] = nc.dram_tensor(nm, shape, dt, kind="Internal").ap()
    d["cc_in"] = nc.dram_tensor("cc_in", [64, 1024], F32, kind="Internal").ap()
    d["cc_out"] = nc.dram_tensor("cc_out", [64, 1024], F32, kind="Internal").ap()
    d["idx"] = nc.dram_tensor("idx", [E * CAP, 1], I32, kind="Internal").ap()
    d["odram"] = nc.dram_tensor("odram", [E * CAP, D], BF16, kind="Internal").ap()
    return d


def _emit_sublayer(p, nc, which, x_in, x_out, dram, w, consts, nocc):
    dd = dict(dram)
    dd["x_in"] = x_in
    dd["x_out"] = x_out
    if which == "l0ffn":
        emit_norm_pass(p, x_in, None, dd["zln"], dd["zlnT"], "ln",
                       ident_bf=consts["ident_bf"])
        emit_l0ffn(p, dd, w, consts, nocc=nocc)
    elif which == "l0attn":
        emit_norm_pass(p, x_in, None, None, dd["zlnT"], "ln", ident_bf=consts["ident_bf"])
        emit_l0attn(p, dd, w)
    elif which == "l1ffn":
        emit_l1ffn(p, dd, w, consts)
    elif which == "l1attn":
        emit_norm_pass(p, x_in, dd["z0"], None, dd["zlnT"], "ln", ident_bf=consts["ident_bf"])
        emit_l1attn(p, dd, w)


def _build_program(sublayers, nocc=False):
    nc = bacc.Bacc("TRN2", target_bir_lowering=False)
    x_in = nc.dram_tensor("x_in", [T, D], F32, kind="ExternalInput").ap()
    x_out = nc.dram_tensor("x_out", [T, D], F32, kind="ExternalOutput").ap()
    z = np_inputs_placeholder = None
    g = _HOST_G
    w = _declare_weight_drams(nc, g)
    dram = _declare_scratch(nc)
    dram["x_in_ext"] = x_in
    dram["x_out_ext"] = x_out
    p = P(nc)
    consts = _emit_casts_and_consts(p, nc, w)
    name2ap = {"x_in": x_in, "x_out": x_out, "xA": dram["xA"], "xB": dram["xB"]}
    for which, a, b in sublayers:
        _emit_sublayer(p, nc, which, name2ap[a], name2ap[b], dram, w, consts, nocc)
    p.close()
    nc.compile()
    return nc


_HOST_G = None
_PROG_CACHE = {}


def kernel(**inputs):
    global _HOST_G
    g, per_core = _prep(inputs)
    _HOST_G = g
    seq = [("l0ffn", "x_in", "xA"), ("l0attn", "xA", "xB"), ("l0ffn", "xB", "xA"),
           ("l1ffn", "xA", "xB"), ("l1attn", "xB", "xA"), ("l1ffn", "xA", "x_out")]
    import os
    fused = os.environ.get("MK_UNFUSED", "0") != "1"
    outs = []
    if fused:
        key = "full"
        if key not in _PROG_CACHE:
            _PROG_CACHE[key] = _build_program(seq)
        nc = _PROG_CACHE[key]
        in_maps = []
        for j in range(NCORES):
            m = {k: v for k, v in g.items()}
            m.update(per_core[j])
            m["x_in"] = m.pop("x_in")
            in_maps.append(m)
        res = run_bass_kernel_spmd(nc, in_maps, core_ids=list(range(NCORES)))
        outs = [res.results[j]["x_out"] for j in range(NCORES)]
    else:
        cur = [per_core[j]["x_in"] for j in range(NCORES)]
        for which, a, b in seq:
            key = which
            if key not in _PROG_CACHE:
                _PROG_CACHE[key] = _build_program([(which, "x_in", "x_out")])
            nc = _PROG_CACHE[key]
            in_maps = []
            for j in range(NCORES):
                m = {k: v for k, v in g.items()}
                m.update(per_core[j])
                m["x_in"] = cur[j]
                in_maps.append(m)
            res = run_bass_kernel_spmd(nc, in_maps, core_ids=list(range(NCORES)))
            cur = [res.results[j]["x_out"] for j in range(NCORES)]
        outs = cur
    return _unshard(outs)
